# version 02
# baseline (speedup 1.0000x reference)
import numpy as np
from contextlib import ExitStack

import concourse.bass as bass
import concourse.mybir as mybir
from concourse.bass_utils import run_bass_kernel_spmd

F32 = mybir.dt.float32
BF16 = mybir.dt.bfloat16
U8 = mybir.dt.uint8
AF = mybir.ActivationFunctionType
ALU = mybir.AluOpType
AX = mybir.AxisListType

ENGS = ("tensor", "vector", "scalar", "gpsimd", "sync")


class Cfg:
    def __init__(self, NB=4, S=2048, D=1024, DQ=256, E=8, F=4096, FG=4, TBW=512):
        self.NB, self.S, self.D, self.DQ, self.E, self.F = NB, S, D, DQ, E, F
        self.KD = D // 128
        self.KQ = DQ // 128
        self.NT = S // 128
        self.NFL = F // 128
        self.FG = FG
        self.NFG = self.NFL // FG
        self.TBW = TBW
        self.NTB = S // TBW
        self.TPB = TBW // 128
        self.eps_ln = 1e-5
        self.eps_rms = float(np.finfo(np.float32).eps)
        self.scale = 1.0 / float(np.sqrt(DQ))
        self.debug = False
        self.only_a = False
        self.only_ln = False


class Sem:
    def __init__(self, h):
        self.h = h
        self.n = 0

    def inc(self, amt=1):
        self.n += amt
        return (self.h, amt)


class Q:
    def __init__(self):
        self.ops = {e: [] for e in ENGS}

    def op(self, eng, fn, waits=(), inc=None):
        self.ops[eng].append((fn, [x for x in waits if x is not None], inc))

    def run_block(self, nc):
        with nc.Block() as block:
            for eng in ENGS:
                ops = self.ops[eng]
                if not ops:
                    continue

                def make(ops):
                    def prog(e):
                        for fn, waits, inc in ops:
                            for (sem, val) in waits:
                                e.wait_ge(sem, val)
                            ins = fn(e)
                            if inc is not None:
                                ins.then_inc(inc[0], inc[1])
                    return prog

                getattr(block, eng)(make(ops))
        self.ops = {e: [] for e in ENGS}


def build_program(cfg: Cfg):
    nc = bass.Bass("TRN2", target_bir_lowering=False)
    NB, S, D, DQ, E, F = cfg.NB, cfg.S, cfg.D, cfg.DQ, cfg.E, cfg.F
    KD, KQ, NT, NFL, FG, NFG = cfg.KD, cfg.KQ, cfg.NT, cfg.NFL, cfg.FG, cfg.NFG
    TBW, NTB, TPB = cfg.TBW, cfg.NTB, cfg.TPB

    x_h = nc.dram_tensor("x", [NB * S, D], F32, kind="ExternalInput").ap()
    wq_h = nc.dram_tensor("wq", [D, DQ], F32, kind="ExternalInput").ap()
    wk_h = nc.dram_tensor("wk", [D, DQ], F32, kind="ExternalInput").ap()
    wvd_h = nc.dram_tensor("wvd", [D, DQ], F32, kind="ExternalInput").ap()
    wvu_h = nc.dram_tensor("wvu", [DQ, D], F32, kind="ExternalInput").ap()
    lnw_h = nc.dram_tensor("lnw", [D], F32, kind="ExternalInput").ap()
    wg_h = nc.dram_tensor("wg", [D, E], F32, kind="ExternalInput").ap()
    bg_h = nc.dram_tensor("bg", [E], F32, kind="ExternalInput").ap()
    w1a_h = nc.dram_tensor("w1a", [E, D, F], F32, kind="ExternalInput").ap()
    b1a_h = nc.dram_tensor("b1a", [128, E * NFL], F32, kind="ExternalInput").ap()
    w1b_h = nc.dram_tensor("w1b", [E, D, F], F32, kind="ExternalInput").ap()
    b1b_h = nc.dram_tensor("b1b", [128, E * NFL], F32, kind="ExternalInput").ap()
    w2_h = nc.dram_tensor("w2", [E, F, D], F32, kind="ExternalInput").ap()
    b2_h = nc.dram_tensor("b2", [E, D], F32, kind="ExternalInput").ap()
    ident_h = nc.dram_tensor("ident", [128, 128], F32, kind="ExternalInput").ap()
    amask_h = nc.dram_tensor("amask", [128, 128], U8, kind="ExternalInput").ap()
    o_h = nc.dram_tensor("o", [NB * S, D], F32, kind="ExternalOutput").ap()
    dbgk = "ExternalOutput" if cfg.debug else "Internal"
    xn2s_h = nc.dram_tensor("xn2s", [KD, 128, NB * S], BF16, kind=dbgk).ap()
    s_h = nc.dram_tensor("sscr", [NB * S, D], F32, kind=dbgk).ap()
    if cfg.debug:
        weo_h = nc.dram_tensor("weo", [128, NB * NT * E], F32, kind="ExternalOutput").ap()
        go_h = nc.dram_tensor("go", [NB * S, D], F32, kind="ExternalOutput").ap()
        ao_h = nc.dram_tensor("ao", [NB * S, S], F32, kind="ExternalOutput").ap()
        xnso_h = nc.dram_tensor("xnso", [KD, 128, NB * S], BF16, kind="ExternalOutput").ap()
        colso_h = nc.dram_tensor("colso", [NB * NT, 128, 8], F32, kind="ExternalOutput").ap()
        lgo_h = nc.dram_tensor("lgo", [NB, 128, NT * E], F32, kind="ExternalOutput").ap()
        vno_h = nc.dram_tensor("vno", [NB, 128, S // 128 * D], BF16, kind="ExternalOutput").ap()
        qo_h = nc.dram_tensor("qo", [NB, 128, (DQ // 128) * S], BF16, kind="ExternalOutput").ap()
        ko_h = nc.dram_tensor("ko", [NB, 128, (DQ // 128) * S], BF16, kind="ExternalOutput").ap()

    st = ExitStack()

    def sbuf(name, shape, dt):
        return st.enter_context(nc.sbuf_tensor(name, shape, dt))

    def psum(name, shape, dt=F32):
        return st.enter_context(nc.psum_tensor(name, shape, dt))

    def sem(name):
        return Sem(st.enter_context(nc.semaphore(name)))

    we_all = sbuf("we_all", [128, NB * NT * E], F32)
    weT_all = sbuf("weT_all", [E, NB * S], BF16)
    b1a_sb = sbuf("b1a_sb", [128, E * NFL], F32)
    b1b_sb = sbuf("b1b_sb", [128, E * NFL], F32)
    b2_sb = sbuf("b2_sb", [E, D], BF16)
    ident_sb = sbuf("ident_sb", [128, 128], F32)
    amask_sb = sbuf("amask_sb", [128, 128], U8)
    neg_sb = sbuf("neg_sb", [128, 128], F32)
    cols = sbuf("colsA", [128, 32], F32)
    epsln_c = sbuf("epsln_c", [128, 1], F32)
    epsrms_c = sbuf("epsrms_c", [128, 1], F32)

    ps_big = psum("ps_big", [128, 2048])
    ps_o0 = psum("ps_o0", [128, 1024])
    ps_o1 = psum("ps_o1", [128, 1024])

    s_pe = sem("s_pe")
    s_dv = sem("s_dv")
    s_ac = sem("s_ac")
    s_wc = sem("s_wc")
    s_x = [sem("s_x0"), sem("s_x1")]
    s_se = [sem("s_se0"), sem("s_se1")]
    s_spx = sem("s_spx")
    s_w1a = [sem("s_w1a0"), sem("s_w1a1")]
    s_w1b = [sem("s_w1b0"), sem("s_w1b1")]
    s_w2 = [sem("s_w20"), sem("s_w21")]
    s_xn2r = sem("s_xn2r")
    s_xc = [sem("s_xc0"), sem("s_xc1")]
    s_sc = [sem("s_sc0"), sem("s_sc1")]
    s_out = [sem("s_out0"), sem("s_out1")]
    s_dbg = sem("s_dbg")
    s_wqkv = sem("s_wqkv")
    s_ms = sem("s_ms")

    q = Q()

    def w(s, val):
        return (s.h, val) if val > 0 else None

    psg = {r: {"dv": 0, "ac": 0} for r in
           ("bigA", "ba0", "bb0", "ba1", "bb1", "o0h0", "o0h1", "o1h0", "o1h1")}

    def pwait(*regions):
        rs = set(regions)
        quarters = {"ba0", "bb0", "ba1", "bb1"}
        if "bigA" in rs:
            rs |= quarters
        elif rs & quarters:
            rs.add("bigA")
        dv = max(psg[r]["dv"] for r in rs)
        ac = max(psg[r]["ac"] for r in rs)
        return [w(s_dv, dv), w(s_ac, ac)]

    def pread(region, kind, cnt):
        psg[region][kind] = max(psg[region][kind], cnt)

    stA = ExitStack()

    def sbufA(name, shape, dt):
        return stA.enter_context(nc.sbuf_tensor(name, shape, dt))

    assert KD * DQ == KQ * D
    wqkv_sb = sbufA("wqkv_sb", [128, KD * DQ], BF16)
    wg_sb = sbufA("wg_sb", [128, KD * E], BF16)
    lnw_bc = sbufA("lnw_bc", [128, D], F32)
    bg_bc = sbufA("bg_bc", [128, E], F32)
    xin = [sbufA("xin0", [128, D], F32), sbufA("xin1", [128, D], F32)]
    sqs = sbufA("sqs", [128, D], BF16)
    xnf = [sbufA("xnf0", [128, D], F32), sbufA("xnf1", [128, D], F32)]
    xn2f = [sbufA("xn2f0", [128, D], F32), sbufA("xn2f1", [128, D], F32)]
    xnT = sbufA("xnT", [128, KD * S], BF16)
    xn2T = sbufA("xn2T_A", [128, KD * S], BF16)
    logits = sbufA("logits", [128, NT * E], F32)
    glt = sbufA("glt", [128, 4 * NT], F32)
    geq = sbufA("geq", [128, 2 * NT * E], F32)
    qT_sb = sbufA("qT", [128, KQ * S], BF16)
    kT_sb = sbufA("kT", [128, KQ * S], BF16)
    vdT_sb = sbufA("vdT", [128, KQ * S], BF16)
    vnat = sbufA("vnat", [128, NT * D], BF16)
    expb = sbufA("expb", [128, S], F32)
    attnT = sbufA("attnT", [128, S], BF16)
    sev = [sbufA("sev0", [128, D], F32), sbufA("sev1", [128, D], F32)]

    def ld3(dst, src, kk):
        return lambda g: g.dma_start(
            out=dst[:].rearrange("p (k m) -> p k m", k=kk),
            in_=src.rearrange("(k p) m -> p k m", p=128))

    q.op("gpsimd", ld3(wg_sb, wg_h, KD), inc=s_wc.inc(16))
    q.op("gpsimd", lambda g: g.dma_start(out=lnw_bc[:], in_=lnw_h[None, :].broadcast_to([128, D])),
         inc=s_wc.inc(16))
    q.op("gpsimd", lambda g: g.dma_start(out=bg_bc[:], in_=bg_h[None, :].broadcast_to([128, E])),
         inc=s_wc.inc(16))
    q.op("gpsimd", lambda g: g.dma_start(out=b1a_sb[:], in_=b1a_h[:]), inc=s_wc.inc(16))
    q.op("gpsimd", lambda g: g.dma_start(out=b1b_sb[:], in_=b1b_h[:]), inc=s_wc.inc(16))
    q.op("gpsimd", lambda g: g.dma_start(out=b2_sb[:], in_=b2_h[:]), inc=s_wc.inc(16))
    q.op("gpsimd", lambda g: g.dma_start(out=ident_sb[:], in_=ident_h[:]), inc=s_wc.inc(16))
    q.op("gpsimd", lambda g: g.dma_start(out=amask_sb[:], in_=amask_h[:]), inc=s_wc.inc(16))
    q.op("gpsimd", lambda g: g.memset(neg_sb[:], -1e30), inc=s_ms.inc(1))
    q.op("gpsimd", lambda g: g.memset(epsln_c[:], float(cfg.eps_ln)), inc=s_ms.inc(1))
    q.op("gpsimd", lambda g: g.memset(epsrms_c[:], float(cfg.eps_rms)), inc=s_ms.inc(1))
    WC = s_wc.n
    wcw = (s_wc.h, WC)
    wcm = (s_ms.h, s_ms.n)

    dbg_cols = [0, 0]
    dbg_xnt = 0
    x_free = [0, 0]
    xnf_free = [0, 0]
    xn2f_free = [0, 0]
    sev_rec = [0, 0]
    we_done = [0] * NB
    smm_last = [0] * NB
    wqkv_carry = [0]

    for b in range(NB):
        logit_guard = 0

        for tt in range(NT):
            sl = tt % 2
            par = tt % 2
            row0 = b * S + tt * 128
            cb = 16 * par
            q.op("sync", (lambda r, sl: lambda e: e.dma_start(
                out=xin[sl][:], in_=x_h[r:r + 128, :]))(r=row0, sl=sl),
                waits=[w(s_dv, x_free[sl])], inc=s_x[sl].inc(16))
            xw = [(s_x[sl].h, s_x[sl].n)]
            q.op("scalar", (lambda sl, cb: lambda e: e.activation(
                sqs[:], xin[sl][:], AF.Square, accum_out=cols[:, cb + 1:cb + 2]))(sl, cb),
                waits=xw + [w(s_dv, x_free[sl])] +
                ([w(s_dbg, dbg_cols[par])] if cfg.debug else []), inc=s_ac.inc())
            ac_sq = s_ac.n
            q.op("vector", (lambda sl, cb: lambda e: e.reduce_sum(
                cols[:, cb:cb + 1], xin[sl][:], axis=AX.X))(sl, cb),
                waits=xw + ([w(s_dbg, dbg_cols[par])] if cfg.debug else []),
                inc=s_dv.inc())
            dv_rs = s_dv.n
            q.op("vector", (lambda cb: lambda e: e.tensor_scalar_mul(
                cols[:, cb + 2:cb + 3], cols[:, cb:cb + 1], 1.0 / D))(cb),
                waits=[w(s_dv, dv_rs)], inc=s_dv.inc())
            dv_mean = s_dv.n
            q.op("vector", (lambda cb: lambda e: e.tensor_scalar_mul(
                cols[:, cb + 5:cb + 6], cols[:, cb + 1:cb + 2], 1.0 / D))(cb),
                waits=[w(s_ac, ac_sq)], inc=s_dv.inc())
            dv_ex2 = s_dv.n
            q.op("vector", (lambda cb: lambda e: e.scalar_tensor_tensor(
                cols[:, cb + 6:cb + 7], cols[:, cb + 2:cb + 3], cols[:, cb + 2:cb + 3],
                cols[:, cb + 5:cb + 6], ALU.mult, ALU.subtract))(cb),
                waits=[w(s_dv, dv_mean), w(s_dv, dv_ex2)], inc=s_dv.inc())
            dv_nv = s_dv.n
            q.op("scalar", (lambda cb: lambda e: e.activation(
                cols[:, cb + 3:cb + 4], cols[:, cb + 6:cb + 7], AF.Sqrt,
                scale=-1.0, bias=epsln_c[:]))(cb),
                waits=[w(s_dv, dv_nv), wcw, wcm], inc=s_ac.inc())
            ac_sd = s_ac.n
            q.op("vector", (lambda cb: lambda e: e.reciprocal(
                cols[:, cb + 3:cb + 4], cols[:, cb + 3:cb + 4]))(cb),
                waits=[w(s_ac, ac_sd)], inc=s_dv.inc())
            dv_rstd = s_dv.n
            q.op("vector", (lambda sl, par, cb: lambda e: e.tensor_scalar(
                xnf[par][:], xin[sl][:], cols[:, cb + 2:cb + 3], cols[:, cb + 3:cb + 4],
                ALU.subtract, ALU.mult))(sl, par, cb),
                waits=[w(s_pe, xnf_free[par]), w(s_dv, dv_rstd)], inc=s_dv.inc())
            dv_xn = s_dv.n
            x_free[sl] = dv_xn
            q.op("scalar", (lambda par, cb: lambda e: e.activation(
                sqs[:], xnf[par][:], AF.Square, accum_out=cols[:, cb + 7:cb + 8]))(par, cb),
                waits=[w(s_dv, dv_xn)], inc=s_ac.inc())
            ac_s2 = s_ac.n
            q.op("scalar", (lambda cb: lambda e: e.activation(
                cols[:, cb + 4:cb + 5], cols[:, cb + 7:cb + 8], AF.Sqrt,
                scale=1.0 / D, bias=epsrms_c[:]))(cb),
                waits=[w(s_ac, ac_s2)], inc=s_ac.inc())
            ac_sd2 = s_ac.n
            q.op("vector", (lambda cb: lambda e: e.reciprocal(
                cols[:, cb + 4:cb + 5], cols[:, cb + 4:cb + 5]))(cb),
                waits=[w(s_ac, ac_sd2)], inc=s_dv.inc())
            dv_rstd2 = s_dv.n
            q.op("vector", (lambda par, cb: lambda e: e.scalar_tensor_tensor(
                xn2f[par][:], xnf[par][:], cols[:, cb + 4:cb + 5], lnw_bc[:],
                ALU.mult, ALU.mult))(par, cb),
                waits=[wcw, wcm, w(s_pe, xn2f_free[par]), w(s_dv, dv_rstd2)], inc=s_dv.inc())
            dv_xn2 = s_dv.n

            if cfg.debug:
                q.op("sync", (lambda b, tt, cb: lambda e: e.dma_start(
                    out=colso_h[b * NT + tt], in_=cols[:, cb:cb + 8]))(b, tt, cb),
                    waits=[w(s_dv, dv_xn2)], inc=s_dbg.inc(16))
                dbg_cols[par] = s_dbg.n

            for which, srcf, dstT, dvready in ((0, xnf[par], xnT, dv_xn),
                                               (1, xn2f[par], xn2T, dv_xn2)):
                for c0 in range(0, KD, 4):
                    half = (c0 // 4) % 2
                    reg = "o0h%d" % half
                    n = min(4, KD - c0)
                    for j in range(n):
                        q.op("tensor", (lambda srcf, c0, j, half: lambda e: e.transpose(
                            ps_o0[:, half * 512 + j * 128: half * 512 + (j + 1) * 128],
                            srcf[:, (c0 + j) * 128: (c0 + j + 1) * 128], ident_sb[:]))(
                                srcf, c0, j, half),
                            waits=([w(s_dv, dvready), wcw, wcm] + pwait(reg) +
                                   ([w(s_dbg, dbg_xnt)] if cfg.debug else [])) if j == 0 else [],
                            inc=s_pe.inc() if j == n - 1 else None)
                    pe_tr = s_pe.n
                    extra = []
                    if which == 1 and c0 == 0:
                        extra = [w(s_spx, 16 * KD * b)]
                    q.op("vector", (lambda dstT, c0, n, half, tt: lambda e: e.tensor_copy(
                        dstT[:].rearrange("p (k s) -> p k s", k=KD)[
                            :, c0:c0 + n, tt * 128:(tt + 1) * 128],
                        ps_o0[:, half * 512: half * 512 + n * 128].rearrange(
                            "p (k s) -> p k s", k=n)))(dstT, c0, n, half, tt),
                        waits=[w(s_pe, pe_tr)] + extra, inc=s_dv.inc())
                    pread(reg, "dv", s_dv.n)
                if which == 0:
                    xnf_free[par] = pe_tr
                    xnT_done = s_dv.n
                else:
                    xn2f_free[par] = pe_tr
                    xn2T_done = s_dv.n

            for k in range(KD):
                q.op("tensor", (lambda k, tt: lambda e: e.matmul(
                    ps_o1[:, 0:E],
                    xn2T[:, k * S + tt * 128: k * S + (tt + 1) * 128],
                    wg_sb[:, k * E:(k + 1) * E],
                    start=(k == 0), stop=(k == KD - 1)))(k, tt),
                    waits=([w(s_dv, xn2T_done)] + pwait("o1h0")) if k == 0 else [],
                    inc=s_pe.inc() if k == KD - 1 else None)
            pe_lg = s_pe.n
            q.op("vector", (lambda tt: lambda e: e.tensor_add(
                logits[:, tt * E:(tt + 1) * E], ps_o1[:, 0:E], bg_bc[:]))(tt),
                waits=[w(s_pe, pe_lg), wcw, wcm], inc=s_dv.inc())
            pread("o1h0", "dv", s_dv.n)
            lg_done = s_dv.n

        if cfg.debug:
            q.op("sync", (lambda b: lambda e: e.dma_start(
                out=lgo_h[b], in_=logits[:]))(b),
                waits=[w(s_dv, s_dv.n)], inc=s_dbg.inc(16))
            dbg_lg = s_dbg.n
        lg3 = logits[:].rearrange("p (t e) -> p t e", e=E)
        m1 = glt[:, 0 * NT:1 * NT]
        m2 = glt[:, 1 * NT:2 * NT]
        w1c = glt[:, 2 * NT:3 * NT]
        w2c = glt[:, 3 * NT:4 * NT]
        eq1 = geq[:, 0:NT * E]
        eq2 = geq[:, NT * E:2 * NT * E]
        eq13 = eq1.rearrange("p (t e) -> p t e", e=E)
        eq23 = eq2.rearrange("p (t e) -> p t e", e=E)
        q.op("vector", lambda e: e.reduce_max(m1, lg3, axis=AX.X),
             waits=[w(s_dv, lg_done)] + ([w(s_dbg, dbg_lg)] if cfg.debug else []),
             inc=s_dv.inc())
        dv_m1 = s_dv.n
        q.op("vector", lambda e: e.tensor_tensor(
            eq13, lg3, m1.unsqueeze(2).broadcast_to([128, NT, E]), ALU.is_equal),
            waits=[w(s_dv, dv_m1)], inc=s_dv.inc())
        dv_eq1 = s_dv.n
        q.op("vector", lambda e: e.scalar_tensor_tensor(
            eq2, eq1, -1e30, logits[:], ALU.mult, ALU.add),
            waits=[w(s_dv, dv_eq1)], inc=s_dv.inc())
        dv_msk = s_dv.n
        q.op("vector", lambda e: e.reduce_max(m2, eq23, axis=AX.X),
             waits=[w(s_dv, dv_msk)], inc=s_dv.inc())
        dv_m2 = s_dv.n
        q.op("vector", lambda e: e.tensor_sub(w1c, m1, m2), waits=[w(s_dv, dv_m2)],
             inc=s_dv.inc())
        dv_w1 = s_dv.n
        q.op("scalar", lambda e: e.activation(w1c, w1c, AF.Sigmoid),
             waits=[w(s_dv, dv_w1)], inc=s_ac.inc())
        ac_w1 = s_ac.n
        q.op("vector", lambda e: e.tensor_tensor(
            eq23, eq23, m2.unsqueeze(2).broadcast_to([128, NT, E]), ALU.is_equal),
            waits=[w(s_ac, ac_w1)], inc=s_dv.inc())
        dv_eq2 = s_dv.n
        q.op("vector", lambda e: e.tensor_scalar(w2c, w1c, -1.0, 1.0, ALU.mult, ALU.add),
             inc=s_dv.inc())
        dv_w2 = s_dv.n
        q.op("vector", lambda e: e.tensor_tensor(
            eq13, eq13, w1c.unsqueeze(2).broadcast_to([128, NT, E]), ALU.mult),
            waits=[w(s_dv, dv_eq2)], inc=s_dv.inc())
        dv_p1 = s_dv.n
        q.op("vector", lambda e: e.tensor_tensor(
            eq23, eq23, w2c.unsqueeze(2).broadcast_to([128, NT, E]), ALU.mult),
            waits=[w(s_dv, dv_w2)], inc=s_dv.inc())
        dv_p2 = s_dv.n
        q.op("vector", (lambda b: lambda e: e.tensor_add(
            we_all[:, b * NT * E:(b + 1) * NT * E], eq1, eq2))(b),
            waits=[w(s_dv, dv_p1), w(s_dv, dv_p2)], inc=s_dv.inc())
        dv_we = s_dv.n

        for c0 in range(0, NT, 4):
            half = (c0 // 4) % 2
            reg = "o0h%d" % half
            n = min(4, NT - c0)
            for j in range(n):
                q.op("tensor", (lambda b, c0, j, half: lambda e: e.transpose(
                    ps_o0[0:E, half * 512 + j * 128: half * 512 + (j + 1) * 128],
                    we_all[:, (b * NT + c0 + j) * E:(b * NT + c0 + j + 1) * E],
                    ident_sb[:]))(b, c0, j, half),
                    waits=([w(s_dv, dv_we)] + pwait(reg)) if j == 0 else [],
                    inc=s_pe.inc() if j == n - 1 else None)
            pe_wt = s_pe.n
            q.op("vector", (lambda b, c0, n, half: lambda e: e.tensor_copy(
                weT_all[0:E, b * S + c0 * 128: b * S + (c0 + n) * 128],
                ps_o0[0:E, half * 512: half * 512 + n * 128]))(b, c0, n, half),
                waits=[w(s_pe, pe_wt)], inc=s_dv.inc())
            pread(reg, "dv", s_dv.n)
        we_done[b] = s_dv.n

        if cfg.debug:
            for k in range(KD):
                q.op("sync", (lambda b, k: lambda e: e.dma_start(
                    out=xnso_h[k, :, b * S:(b + 1) * S],
                    in_=xnT[:, k * S:(k + 1) * S]))(b, k),
                    waits=[w(s_dv, xnT_done)] if k == 0 else [], inc=s_dbg.inc(16))
            dbg_xnt = s_dbg.n
        for k in range(KD):
            q.op("sync", (lambda b, k: lambda e: e.dma_start(
                out=xn2s_h[k, :, b * S:(b + 1) * S],
                in_=xn2T[:, k * S:(k + 1) * S]))(b, k),
                waits=[w(s_dv, xn2T_done)] if k == 0 else [], inc=s_spx.inc(16))

        wqkv_last_pe = wqkv_carry[0]
        if not cfg.only_ln:
            for (wsrc, kk, dstT) in ((wq_h, KD, qT_sb), (wk_h, KD, kT_sb), (wvd_h, KD, vdT_sb)):
                q.op("gpsimd", (lambda wsrc, kk: ld3(wqkv_sb, wsrc, kk))(wsrc, kk),
                     waits=[w(s_pe, wqkv_last_pe)], inc=s_wqkv.inc(16))
                wq_n = s_wqkv.n
                for m in range(KQ):
                    for c in range(0, S, 512):
                        cw = min(512, S - c)
                        half = (c // 512) % 2
                        reg = "o1h%d" % half
                        for k in range(KD):
                            q.op("tensor", (lambda m, c, cw, k, half: lambda e: e.matmul(
                                ps_o1[:, half * 512: half * 512 + cw],
                                wqkv_sb[:, k * DQ + m * 128: k * DQ + (m + 1) * 128],
                                xnT[:, k * S + c: k * S + c + cw],
                                start=(k == 0), stop=(k == KD - 1)))(m, c, cw, k, half),
                                waits=([w(s_dv, xnT_done), (s_wqkv.h, wq_n)] + pwait(reg))
                                if k == 0 else [],
                                inc=s_pe.inc() if k == KD - 1 else None)
                        pe_g = s_pe.n
                        q.op("vector", (lambda dstT, m, c, cw, half: lambda e: e.tensor_copy(
                            dstT[:, m * S + c: m * S + c + cw],
                            ps_o1[:, half * 512: half * 512 + cw]))(dstT, m, c, cw, half),
                            waits=[w(s_pe, pe_g)], inc=s_dv.inc())
                        pread(reg, "dv", s_dv.n)
                wqkv_last_pe = s_pe.n
        qkv_done = s_dv.n

        if not cfg.only_ln:
            q.op("gpsimd", ld3(wqkv_sb, wvu_h, KQ),
                 waits=[w(s_pe, wqkv_last_pe)], inc=s_wqkv.inc(16))
            wvu_n = s_wqkv.n
        for tt in range(0 if cfg.only_ln else NT):
            regs = ("o0h0", "o0h1") if tt % 2 == 0 else ("o1h0", "o1h1")
            pst = ps_o0 if tt % 2 == 0 else ps_o1
            nmm = 0
            tot = ((D + 511) // 512) * KQ
            for h in range(0, D, 512):
                hw = min(512, D - h)
                for kq in range(KQ):
                    nmm += 1
                    q.op("tensor", (lambda tt, h, hw, kq, pst: lambda e: e.matmul(
                        pst[:, h:h + hw],
                        vdT_sb[:, kq * S + tt * 128: kq * S + (tt + 1) * 128],
                        wqkv_sb[:, kq * D + h: kq * D + h + hw],
                        start=(kq == 0), stop=(kq == KQ - 1)))(tt, h, hw, kq, pst),
                        waits=(pwait(*regs) + [w(s_dv, qkv_done), (s_wqkv.h, wvu_n)])
                        if nmm == 1 else [],
                        inc=s_pe.inc() if nmm == tot else None)
            pe_v = s_pe.n
            q.op("vector", (lambda tt, pst: lambda e: e.tensor_copy(
                vnat[:, tt * D:(tt + 1) * D], pst[:, 0:D]))(tt, pst),
                waits=[w(s_pe, pe_v)], inc=s_dv.inc())
            for r in regs:
                pread(r, "dv", s_dv.n)
        if not cfg.only_ln:
            wqkv_carry[0] = s_pe.n

        expb_free = 0

        def emit_scores(tt):
            ncols = (tt + 1) * 128
            nblk = (ncols + 511) // 512
            for cbk in range(nblk):
                cw = min(512, ncols - cbk * 512)
                for kq in range(KQ):
                    q.op("tensor", (lambda tt, cbk, cw, kq: lambda e: e.matmul(
                        ps_big[:, cbk * 512: cbk * 512 + cw],
                        qT_sb[:, kq * S + tt * 128: kq * S + (tt + 1) * 128],
                        kT_sb[:, kq * S + cbk * 512: kq * S + cbk * 512 + cw],
                        start=(kq == 0), stop=(kq == KQ - 1)))(tt, cbk, cw, kq),
                        waits=pwait("bigA") if (cbk == 0 and kq == 0) else [],
                        inc=s_pe.inc() if (cbk == nblk - 1 and kq == KQ - 1) else None)
            return s_pe.n

        def emit_softmax(tt, pe_sc):
            ncols = (tt + 1) * 128
            q.op("vector", (lambda tt: lambda e: e.copy_predicated(
                ps_big[:, tt * 128:(tt + 1) * 128], amask_sb[:], neg_sb[:]))(tt),
                waits=[w(s_pe, pe_sc), wcw, wcm], inc=s_dv.inc())
            dv_mask = s_dv.n
            pread("bigA", "dv", dv_mask)
            q.op("vector", (lambda ncols: lambda e: e.reduce_max(
                cols[:, 16:17], ps_big[:, 0:ncols], axis=AX.X, negate=True))(ncols),
                waits=[w(s_dv, dv_mask)], inc=s_dv.inc())
            dv_mx = s_dv.n
            pread("bigA", "dv", dv_mx)
            q.op("scalar", lambda e: e.activation(
                cols[:, 17:18], cols[:, 16:17], AF.Copy, scale=float(cfg.scale)),
                waits=[w(s_dv, dv_mx)], inc=s_ac.inc())
            ac_cp = s_ac.n
            q.op("scalar", (lambda ncols: lambda e: e.activation(
                expb[:, 0:ncols], ps_big[:, 0:ncols], AF.Exp,
                bias=cols[:, 17:18], scale=float(cfg.scale),
                accum_out=cols[:, 18:19]))(ncols),
                waits=[w(s_pe, expb_free), w(s_ac, ac_cp)] +
                      ([(s_dbg.h, s_dbg.n)] if cfg.debug else []), inc=s_ac.inc())
            ac_exp = s_ac.n
            pread("bigA", "ac", ac_exp)
            q.op("vector", lambda e: e.reciprocal(cols[:, 19:20], cols[:, 18:19]),
                 waits=[w(s_ac, ac_exp)], inc=s_dv.inc())
            dv_rc = s_dv.n
            q.op("vector", (lambda ncols: lambda e: e.tensor_scalar_mul(
                expb[:, 0:ncols], expb[:, 0:ncols], cols[:, 19:20]))(ncols),
                waits=[w(s_dv, dv_rc)], inc=s_dv.inc())
            return s_dv.n

        def emit_trans_smm(tt, dv_attn):
            nonlocal expb_free
            n_tk = tt + 1
            if cfg.debug:
                rr = b * S + tt * 128
                q.op("sync", (lambda rr, n_tk: lambda e: e.dma_start(
                    out=ao_h[rr:rr + 128, 0:n_tk * 128], in_=expb[:, 0:n_tk * 128]))(rr, n_tk),
                    waits=[w(s_dv, dv_attn)], inc=s_dbg.inc(16))
            for c0 in range(0, n_tk, 4):
                half = (c0 // 4) % 2
                reg = "o0h%d" % half
                n = min(4, n_tk - c0)
                for j in range(n):
                    q.op("tensor", (lambda c0, j, half: lambda e: e.transpose(
                        ps_o0[:, half * 512 + j * 128: half * 512 + (j + 1) * 128],
                        expb[:, (c0 + j) * 128: (c0 + j + 1) * 128], ident_sb[:]))(c0, j, half),
                        waits=([w(s_dv, dv_attn)] + pwait(reg)) if j == 0 else [],
                        inc=s_pe.inc() if j == n - 1 else None)
                pe_tr = s_pe.n
                q.op("vector", (lambda c0, n, half: lambda e: e.tensor_copy(
                    attnT[:, c0 * 128:(c0 + n) * 128],
                    ps_o0[:, half * 512: half * 512 + n * 128]))(c0, n, half),
                    waits=[w(s_pe, pe_tr)], inc=s_dv.inc())
                pread(reg, "dv", s_dv.n)
            expb_free = s_pe.n
            attnT_done = s_dv.n
            nmm = 0
            tot = n_tk * ((D + 511) // 512)
            for tk in range(n_tk):
                for h in range(0, D, 512):
                    hw = min(512, D - h)
                    nmm += 1
                    q.op("tensor", (lambda tk, h, hw, n_tk: lambda e: e.matmul(
                        ps_o1[:, h:h + hw],
                        attnT[:, tk * 128:(tk + 1) * 128],
                        vnat[:, tk * D + h: tk * D + h + hw],
                        start=(tk == 0), stop=(tk == n_tk - 1)))(tk, h, hw, n_tk),
                        waits=([w(s_dv, attnT_done)] + pwait("o1h0", "o1h1"))
                        if nmm == 1 else [],
                        inc=s_pe.inc() if nmm == tot else None)
            pe_s = s_pe.n
            sl = tt % 2
            q.op("vector", (lambda sl: lambda e: e.tensor_copy(
                sev[sl][:], ps_o1[:, 0:D]))(sl),
                waits=[w(s_pe, pe_s), (s_se[sl].h, sev_rec[sl]) if sev_rec[sl] else None],
                inc=s_dv.inc())
            pread("o1h0", "dv", s_dv.n)
            pread("o1h1", "dv", s_dv.n)
            dv_sev = s_dv.n
            row0 = b * S + tt * 128
            q.op("sync", (lambda sl, row0: lambda e: e.dma_start(
                out=s_h[row0:row0 + 128, :], in_=sev[sl][:]))(sl, row0),
                waits=[w(s_dv, dv_sev)], inc=s_se[sl].inc(16))
            sev_rec[sl] = s_se[sl].n
            return pe_s

        prev = None
        for tt in range(0 if cfg.only_ln else NT):
            pe_sc = emit_scores(tt)
            if prev is not None:
                smm_last[b] = emit_trans_smm(prev[0], prev[1])
            dv_attn = emit_softmax(tt, pe_sc)
            prev = (tt, dv_attn)
        if prev is not None:
            smm_last[b] = emit_trans_smm(prev[0], prev[1])
        if cfg.debug:
            vnat_dv = s_dv.n
            q.op("sync", (lambda b: lambda e: e.dma_start(
                out=vno_h[b], in_=vnat[:]))(b), waits=[w(s_dv, vnat_dv)], inc=s_dbg.inc(16))
            q.op("sync", (lambda b: lambda e: e.dma_start(
                out=qo_h[b], in_=qT_sb[:]))(b), inc=s_dbg.inc(16))
            q.op("sync", (lambda b: lambda e: e.dma_start(
                out=ko_h[b], in_=kT_sb[:]))(b), inc=s_dbg.inc(16))

    PE_END_A = s_pe.n
    DV_END_A = s_dv.n
    AC_END_A = s_ac.n
    SE_END = (s_se[0].n, s_se[1].n)

    q.run_block(nc)
    stA.close()

    stB = ExitStack()

    def sbufB(name, shape, dt):
        return stB.enter_context(nc.sbuf_tensor(name, shape, dt))

    xn2B = sbufB("xn2B", [128, KD * S], BF16)
    acc = sbufB("acc", [128, NT * D], F32)
    w1a_b = [sbufB("w1a_b0", [128, KD * FG * 128], BF16),
             sbufB("w1a_b1", [128, KD * FG * 128], BF16)]
    w1b_b = [sbufB("w1b_b0", [128, KD * FG * 128], BF16),
             sbufB("w1b_b1", [128, KD * FG * 128], BF16)]
    w2_b = [sbufB("w2_b0", [128, FG * D], BF16),
            sbufB("w2_b1", [128, FG * D], BF16)]
    yT = [sbufB("yT0", [128, FG * TBW], BF16), sbufB("yT1", [128, FG * TBW], BF16)]
    silu_sb = sbufB("silu_sb", [128, TBW], BF16)
    xcb = [sbufB("xc0", [128, D], F32), sbufB("xc1", [128, D], F32)]
    scb = [sbufB("scb0", [128, D], F32), sbufB("scb1", [128, D], F32)]
    gsb = sbufB("gsb", [128, D], F32)
    osb = [sbufB("osb0", [128, D], F32), sbufB("osb1", [128, D], F32)]

    barrier = [(s_pe.h, PE_END_A), (s_dv.h, DV_END_A), (s_ac.h, AC_END_A),
               (s_spx.h, 16 * KD * NB),
               (s_se[0].h, SE_END[0]) if SE_END[0] else None,
               (s_se[1].h, SE_END[1]) if SE_END[1] else None]
    barrier = [x for x in barrier if x is not None]

    state = dict(w1_free=[0, 0], w2_free=[0, 0], yT_free=[0, 0], yT_ready=[0, 0],
                 silu_free=0, fl_last_pe=0, c_done=0, g_free=0,
                 xc_rec=[0, 0], sc_rec=[0, 0], osb_rec=[0, 0],
                 acc_ready=[0] * NT, first=True)

    def emit_w_loads(ts, e_i, fg):
        gidx = (ts * E + e_i) * NFG + fg
        slot = gidx % 2
        extra = list(barrier) if state["first"] else []
        state["first"] = False
        q.op("gpsimd", (lambda e_i, fg, slot: lambda e: e.dma_start(
            out=w1a_b[slot][:].rearrange("p (k f) -> p k f", k=KD),
            in_=w1a_h[e_i, :, fg * FG * 128:(fg + 1) * FG * 128].rearrange(
                "(k p) f -> p k f", p=128)))(e_i, fg, slot),
            waits=extra + [w(s_pe, state["w1_free"][slot])], inc=s_w1a[slot].inc(16))
        q.op("gpsimd", (lambda e_i, fg, slot: lambda e: e.dma_start(
            out=w1b_b[slot][:].rearrange("p (k f) -> p k f", k=KD),
            in_=w1b_h[e_i, :, fg * FG * 128:(fg + 1) * FG * 128].rearrange(
                "(k p) f -> p k f", p=128)))(e_i, fg, slot),
            inc=s_w1b[slot].inc(16))
        q.op("gpsimd", (lambda e_i, fg, slot: lambda e: e.dma_start(
            out=w2_b[slot][:].rearrange("p (f d) -> p f d", f=FG),
            in_=w2_h[e_i, fg * FG * 128:(fg + 1) * FG * 128, :].rearrange(
                "(f p) d -> p f d", p=128)))(e_i, fg, slot),
            waits=[w(s_pe, state["w2_free"][slot])], inc=s_w2[slot].inc(16))
        return slot, s_w1a[slot].n, s_w1b[slot].n, s_w2[slot].n

    def emit_fl(ts, e_i, fg, tb, wslot, w1a_n, w1b_n, xn2_n, unit):
        ys = unit % 2
        for fl in range(FG):
            par = fl % 2
            ra, rb = "ba%d" % par, "bb%d" % par
            pa = ps_big[:, par * 1024: par * 1024 + TBW]
            pb = ps_big[:, par * 1024 + 512: par * 1024 + 512 + TBW]
            for k in range(KD):
                waits = []
                if k == 0:
                    waits = pwait(ra) + [(s_w1a[wslot].h, w1a_n), (s_xn2r.h, xn2_n)]
                q.op("tensor", (lambda fl, k, tb, wslot, pa: lambda e: e.matmul(
                    pa, w1a_b[wslot][:, k * FG * 128 + fl * 128: k * FG * 128 + (fl + 1) * 128],
                    xn2B[:, k * S + tb * TBW: k * S + (tb + 1) * TBW],
                    start=(k == 0), stop=(k == KD - 1)))(fl, k, tb, wslot, pa),
                    waits=waits, inc=s_pe.inc() if k == KD - 1 else None)
            pe_a = s_pe.n
            for k in range(KD):
                waits = []
                if k == 0:
                    waits = pwait(rb) + [(s_w1b[wslot].h, w1b_n)]
                q.op("tensor", (lambda fl, k, tb, wslot, pb: lambda e: e.matmul(
                    pb, w1b_b[wslot][:, k * FG * 128 + fl * 128: k * FG * 128 + (fl + 1) * 128],
                    xn2B[:, k * S + tb * TBW: k * S + (tb + 1) * TBW],
                    start=(k == 0), stop=(k == KD - 1)))(fl, k, tb, wslot, pb),
                    waits=waits, inc=s_pe.inc() if k == KD - 1 else None)
            pe_b = s_pe.n
            bcol = e_i * NFL + fg * FG + fl
            q.op("scalar", (lambda bcol, pa: lambda e: e.activation(
                silu_sb[:], pa, AF.Silu, bias=b1a_sb[:, bcol:bcol + 1]))(bcol, pa),
                waits=[w(s_pe, pe_a), w(s_dv, state["silu_free"]), wcw], inc=s_ac.inc())
            pread(ra, "ac", s_ac.n)
            ac_silu = s_ac.n
            q.op("vector", (lambda bcol, pb, ys, fl: lambda e: e.scalar_tensor_tensor(
                yT[ys][:, fl * TBW:(fl + 1) * TBW], pb, b1b_sb[:, bcol:bcol + 1],
                silu_sb[:], ALU.add, ALU.mult))(bcol, pb, ys, fl),
                waits=[w(s_pe, pe_b), w(s_ac, ac_silu), wcw] +
                      ([w(s_pe, state["yT_free"][ys])] if fl == 0 else []),
                inc=s_dv.inc())
            pread(rb, "dv", s_dv.n)
            state["silu_free"] = s_dv.n
        state["yT_ready"][ys] = s_dv.n
        state["fl_last_pe"] = s_pe.n
        state["w1_free"][wslot] = s_pe.n

    def emit_o(ts, e_i, fg, tb, wslot, w2_n, unit):
        ys = unit % 2
        for ttl in range(TPB):
            tt = tb * TPB + ttl
            po = ps_o0 if ttl % 2 == 0 else ps_o1
            regs = ("o0h0", "o0h1") if ttl % 2 == 0 else ("o1h0", "o1h1")
            nmm = 0
            tot = FG * ((D + 511) // 512)
            for fl in range(FG):
                for h in range(0, D, 512):
                    hw = min(512, D - h)
                    nmm += 1
                    waits = []
                    if nmm == 1:
                        waits = pwait(*regs) + [w(s_dv, state["yT_ready"][ys]),
                                                (s_w2[wslot].h, w2_n)]
                    q.op("tensor", (lambda fl, h, hw, ttl, ys, wslot, po: lambda e: e.matmul(
                        po[:, h:h + hw],
                        yT[ys][:, fl * TBW + ttl * 128: fl * TBW + (ttl + 1) * 128],
                        w2_b[wslot][:, fl * D + h: fl * D + h + hw],
                        start=(fl == 0), stop=(fl == FG - 1)))(fl, h, hw, ttl, ys, wslot, po),
                        waits=waits,
                        inc=s_pe.inc() if nmm == tot else None)
            pe_o = s_pe.n
            state["w2_free"][wslot] = pe_o
            state["yT_free"][ys] = pe_o
            q.op("vector", (lambda ts, e_i, tt, po: lambda e: e.scalar_tensor_tensor(
                acc[:, tt * D:(tt + 1) * D], po[:, 0:D],
                we_all[:, (ts * NT + tt) * E + e_i:(ts * NT + tt) * E + e_i + 1],
                acc[:, tt * D:(tt + 1) * D], ALU.mult, ALU.add))(ts, e_i, tt, po),
                waits=[w(s_pe, pe_o), w(s_dv, state["acc_ready"][tt])], inc=s_dv.inc())
            for r in regs:
                pread(r, "dv", s_dv.n)
            state["acc_ready"][tt] = s_dv.n

    def emit_phase_c(ts):
        for tt in range(NT):
            sl = tt % 2
            row0 = ts * S + tt * 128
            q.op("sync", (lambda row0, sl: lambda e: e.dma_start(
                out=xcb[sl][:], in_=x_h[row0:row0 + 128, :]))(row0, sl),
                waits=[w(s_dv, state["xc_rec"][sl])] +
                      (list(barrier) if (ts == 0 and tt < 2) else []),
                inc=s_xc[sl].inc(16))
            xn_ = s_xc[sl].n
            q.op("sync", (lambda row0, sl: lambda e: e.dma_start(
                out=scb[sl][:], in_=s_h[row0:row0 + 128, :]))(row0, sl),
                waits=[w(s_dv, state["sc_rec"][sl])], inc=s_sc[sl].inc(16))
            sn_ = s_sc[sl].n
            q.op("scalar", (lambda tt: lambda e: e.activation(
                gsb[:], acc[:, tt * D:(tt + 1) * D], AF.Silu))(tt),
                waits=[w(s_dv, state["acc_ready"][tt]), w(s_dv, state["g_free"])],
                inc=s_ac.inc())
            ac_g = s_ac.n
            gw = []
            if cfg.debug:
                q.op("sync", (lambda row0: lambda e: e.dma_start(
                    out=go_h[row0:row0 + 128, :], in_=gsb[:]))(row0),
                    waits=[w(s_ac, ac_g)], inc=s_dbg.inc(16))
                gw = [(s_dbg.h, s_dbg.n)]
            q.op("vector", (lambda sl: lambda e: e.tensor_mul(gsb[:], gsb[:], scb[sl][:]))(sl),
                 waits=[w(s_ac, ac_g), (s_sc[sl].h, sn_)] + gw, inc=s_dv.inc())
            dv_gm = s_dv.n
            q.op("vector", (lambda sl: lambda e: e.tensor_add(
                osb[sl][:], gsb[:], xcb[sl][:]))(sl),
                waits=[(s_xc[sl].h, xn_), w(s_dv, dv_gm),
                       (s_out[sl].h, state["osb_rec"][sl]) if state["osb_rec"][sl] else None],
                inc=s_dv.inc())
            dv_o = s_dv.n
            state["sc_rec"][sl] = dv_o
            state["xc_rec"][sl] = dv_o
            state["g_free"] = dv_o
            q.op("sync", (lambda row0, sl: lambda e: e.dma_start(
                out=o_h[row0:row0 + 128, :], in_=osb[sl][:]))(row0, sl),
                waits=[w(s_dv, dv_o)], inc=s_out[sl].inc(16))
            state["osb_rec"][sl] = s_out[sl].n
        state["c_done"] = s_dv.n

    for ts in range(NB if not cfg.only_a else 0):
        if ts == 0:
            waits0 = list(barrier)
        else:
            waits0 = [w(s_pe, state["fl_last_pe"])]
        for k in range(KD):
            q.op("sync", (lambda ts, k: lambda e: e.dma_start(
                out=xn2B[:, k * S:(k + 1) * S],
                in_=xn2s_h[k, :, ts * S:(ts + 1) * S]))(ts, k),
                waits=waits0 if k == 0 else [], inc=s_xn2r.inc(16))
        xn2_n = s_xn2r.n

        for tt in range(NT):
            po = ps_o0 if tt % 2 == 0 else ps_o1
            regs = ("o0h0", "o0h1") if tt % 2 == 0 else ("o1h0", "o1h1")
            nh = (D + 511) // 512
            for hi, h in enumerate(range(0, D, 512)):
                hw = min(512, D - h)
                waits = []
                if hi == 0:
                    waits = pwait(*regs) + [w(s_dv, we_done[ts]), wcw]
                    if ts == 0 and tt == 0:
                        waits = waits + list(barrier)
                q.op("tensor", (lambda ts, tt, h, hw, po: lambda e: e.matmul(
                    po[:, h:h + hw],
                    weT_all[0:E, ts * S + tt * 128: ts * S + (tt + 1) * 128],
                    b2_sb[0:E, h:h + hw],
                    start=True, stop=True))(ts, tt, h, hw, po),
                    waits=waits, inc=s_pe.inc() if hi == nh - 1 else None)
            pe_b2 = s_pe.n
            q.op("vector", (lambda tt, po: lambda e: e.tensor_copy(
                acc[:, tt * D:(tt + 1) * D], po[:, 0:D]))(tt, po),
                waits=[w(s_pe, pe_b2), w(s_dv, state["c_done"])], inc=s_dv.inc())
            for r in regs:
                pread(r, "dv", s_dv.n)
            state["acc_ready"][tt] = s_dv.n

        pending = None
        for e_i in range(E):
            for fg in range(NFG):
                wslot, w1a_n, w1b_n, w2_n = emit_w_loads(ts, e_i, fg)
                for tb in range(NTB):
                    unit = (e_i * NFG + fg) * NTB + tb
                    emit_fl(ts, e_i, fg, tb, wslot, w1a_n, w1b_n, xn2_n, unit)
                    if pending is not None:
                        emit_o(*pending)
                    pending = (ts, e_i, fg, tb, wslot, w2_n, unit)
        emit_o(*pending)

        emit_phase_c(ts)

    if cfg.debug:
        q.op("sync", lambda e: e.dma_start(out=weo_h[:], in_=we_all[:]),
             waits=[(s_dv.h, s_dv.n)], inc=s_dbg.inc(16))
        q.op("sync", lambda e: e.wait_ge(s_dbg.h, s_dbg.n))

    if not cfg.only_a:
        q.op("sync", lambda e: e.wait_ge(s_out[0].h, s_out[0].n),
             waits=[(s_out[1].h, s_out[1].n)])
    else:
        q.op("sync", lambda e: e.wait_ge(s_spx.h, s_spx.n),
             waits=[(s_se[0].h, s_se[0].n), (s_se[1].h, s_se[1].n),
                    (s_dv.h, s_dv.n), (s_ac.h, s_ac.n), (s_pe.h, s_pe.n)] +
                   ([(s_dbg.h, s_dbg.n)] if cfg.debug else []))

    q.run_block(nc)
    stB.close()
    st.close()
    return nc


_CFG = Cfg()
_NC_CACHE = {}


def _get_nc(cfg):
    key = id(cfg)
    if key not in _NC_CACHE:
        _NC_CACHE[key] = build_program(cfg)
    return _NC_CACHE[key]


def make_in_maps(cfg, n_cores, x, wq, wk, wvd, wvu, lnw, wg, bg, w1a, b1a, w1b, b1b, w2, b2):
    B, S, D = x.shape
    per = B // n_cores
    ident = np.eye(128, dtype=np.float32)
    amask = np.triu(np.ones((128, 128), dtype=np.uint8), 1)
    f32 = lambda a: np.ascontiguousarray(np.asarray(a, dtype=np.float32))

    def btr(bv):
        bv = f32(bv)
        E_, F_ = bv.shape
        return np.ascontiguousarray(
            bv.reshape(E_, F_ // 128, 128).transpose(2, 0, 1).reshape(128, E_ * (F_ // 128)))

    shared = dict(wq=f32(wq), wk=f32(wk), wvd=f32(wvd), wvu=f32(wvu), lnw=f32(lnw),
                  wg=f32(wg), bg=f32(bg), w1a=f32(w1a), b1a=btr(b1a), w1b=f32(w1b),
                  b1b=btr(b1b), w2=f32(w2), b2=f32(b2), ident=ident, amask=amask)
    in_maps = []
    for c in range(n_cores):
        m = dict(shared)
        m["x"] = np.ascontiguousarray(x[c * per:(c + 1) * per].reshape(per * S, D))
        in_maps.append(m)
    return in_maps


def kernel(L0_attn_inputs, L0_attn_Wq, L0_attn_Wk, L0_attn_Wvdown, L0_attn_Wvup,
           L0_ffn_ln_w, L0_ffn_Wg, L0_ffn_bg, L0_ffn_W1a, L0_ffn_b1a,
           L0_ffn_W1b, L0_ffn_b1b, L0_ffn_W2, L0_ffn_b2):
    cfg = _CFG
    n_cores = 8
    x = np.ascontiguousarray(np.asarray(L0_attn_inputs, dtype=np.float32))
    B, S, D = x.shape
    per = B // n_cores
    nc = _get_nc(cfg)
    in_maps = make_in_maps(cfg, n_cores, x, L0_attn_Wq, L0_attn_Wk, L0_attn_Wvdown,
                           L0_attn_Wvup, L0_ffn_ln_w, L0_ffn_Wg, L0_ffn_bg,
                           L0_ffn_W1a, L0_ffn_b1a, L0_ffn_W1b, L0_ffn_b1b,
                           L0_ffn_W2, L0_ffn_b2)
    res = run_bass_kernel_spmd(nc, in_maps, core_ids=list(range(n_cores)))
    out = np.concatenate([np.asarray(r["o"]).reshape(per, S, D) for r in res.results], axis=0)
    return out.astype(np.float32)


# version 04
# speedup vs baseline: 1.2939x; 1.2939x over previous
import numpy as np
from contextlib import ExitStack

import concourse.bass as bass
import concourse.mybir as mybir
from concourse.bass_utils import run_bass_kernel_spmd

F32 = mybir.dt.float32
BF16 = mybir.dt.bfloat16
U8 = mybir.dt.uint8
AF = mybir.ActivationFunctionType
ALU = mybir.AluOpType
AX = mybir.AxisListType

ENGS = ("tensor", "vector", "scalar", "gpsimd", "sync")


class Cfg:
    def __init__(self, NB=4, S=2048, D=1024, DQ=256, E=8, F=4096, FG=4, TBW=512,
                 GRP=512):
        self.NB, self.S, self.D, self.DQ, self.E, self.F = NB, S, D, DQ, E, F
        self.KD = D // 128
        self.KQ = DQ // 128
        self.NT = S // 128
        self.NFL = F // 128
        self.FG = FG
        self.NFG = self.NFL // FG
        self.TBW = TBW
        self.NTB = S // TBW
        self.TPB = TBW // 128
        self.eps_ln = 1e-5
        self.eps_rms = float(np.finfo(np.float32).eps)
        self.scale = 1.0 / float(np.sqrt(DQ))
        self.debug = False
        self.only_a = False
        self.only_ln = False
        self.sparse = True
        self.ng_limit = None
        self.skip_c = False
        self.GRP = GRP
        self.NGT = self.GRP // 128
        self.G = E + (2 * S) // self.GRP
        self.LG = int(np.log2(self.GRP))


class Sem:
    def __init__(self, h):
        self.h = h
        self.n = 0

    def inc(self, amt=1):
        self.n += amt
        return (self.h, amt)


class Q:
    def __init__(self):
        self.ops = {e: [] for e in ENGS}

    def op(self, eng, fn, waits=(), inc=None):
        self.ops[eng].append((fn, [x for x in waits if x is not None], inc))

    def run_block(self, nc):
        with nc.Block() as block:
            for eng in ENGS:
                ops = self.ops[eng]
                if not ops:
                    continue

                def make(ops):
                    def prog(e):
                        for fn, waits, inc in ops:
                            for (sem, val) in waits:
                                e.wait_ge(sem, val)
                            ins = fn(e)
                            if inc is not None:
                                ins.then_inc(inc[0], inc[1])
                    return prog

                getattr(block, eng)(make(ops))
        self.ops = {e: [] for e in ENGS}


def build_program(cfg: Cfg):
    nc = bass.Bass("TRN2", target_bir_lowering=False)
    NB, S, D, DQ, E, F = cfg.NB, cfg.S, cfg.D, cfg.DQ, cfg.E, cfg.F
    KD, KQ, NT, NFL, FG, NFG = cfg.KD, cfg.KQ, cfg.NT, cfg.NFL, cfg.FG, cfg.NFG
    TBW, NTB, TPB = cfg.TBW, cfg.NTB, cfg.TPB

    x_h = nc.dram_tensor("x", [NB * S, D], F32, kind="ExternalInput").ap()
    wq_h = nc.dram_tensor("wq", [D, DQ], F32, kind="ExternalInput").ap()
    wk_h = nc.dram_tensor("wk", [D, DQ], F32, kind="ExternalInput").ap()
    wvd_h = nc.dram_tensor("wvd", [D, DQ], F32, kind="ExternalInput").ap()
    wvu_h = nc.dram_tensor("wvu", [DQ, D], F32, kind="ExternalInput").ap()
    lnw_h = nc.dram_tensor("lnw", [D], F32, kind="ExternalInput").ap()
    wg_h = nc.dram_tensor("wg", [D, E], F32, kind="ExternalInput").ap()
    bg_h = nc.dram_tensor("bg", [E], F32, kind="ExternalInput").ap()
    w1a_h = nc.dram_tensor("w1a", [E, D, F], F32, kind="ExternalInput").ap()
    b1a_h = nc.dram_tensor("b1a", [128, E * NFL], F32, kind="ExternalInput").ap()
    w1b_h = nc.dram_tensor("w1b", [E, D, F], F32, kind="ExternalInput").ap()
    b1b_h = nc.dram_tensor("b1b", [128, E * NFL], F32, kind="ExternalInput").ap()
    w2_h = nc.dram_tensor("w2", [E, F, D], F32, kind="ExternalInput").ap()
    b2_h = nc.dram_tensor("b2", [E, D], F32, kind="ExternalInput").ap()
    ident_h = nc.dram_tensor("ident", [128, 128], F32, kind="ExternalInput").ap()
    amask_h = nc.dram_tensor("amask", [128, 128], U8, kind="ExternalInput").ap()
    o_h = nc.dram_tensor("o", [NB * S, D], F32, kind="ExternalOutput").ap()
    dbgk = "ExternalOutput" if cfg.debug else "Internal"
    xn2s_h = nc.dram_tensor("xn2s", [KD, 128, NB * S], BF16, kind=dbgk).ap()
    s_h = nc.dram_tensor("sscr", [NB * S, D], F32, kind=dbgk).ap()
    GRP, NGT, G = cfg.GRP, cfg.NGT, cfg.G
    NBS = NB * S
    if cfg.sparse:
        xn2n_h = nc.dram_tensor("xn2n", [NBS, D], F32, kind="Internal").ap()
        rt_h = nc.dram_tensor("rt", [NB, 4 * NT, 128], F32, kind="Internal").ap()
        tbl_h = [nc.dram_tensor("tbl%d" % i, [G * GRP, 4], F32, kind="Internal").ap()
                 for i in range(NB)]
        FGWc = FG * 128
        DHc = min(512, D)
        NHc = D // DHc
        obuf_hh = [nc.dram_tensor("obuf%d" % h, [2 * NBS, DHc], F32,
                                  kind="Internal").ap() for h in range(NHc)]
        w1a_sw = nc.dram_tensor("w1asw", [E * NFG * 128, KD * FGWc], BF16,
                                kind="Internal").ap()
        w1b_sw = nc.dram_tensor("w1bsw", [E * NFG * 128, KD * FGWc], BF16,
                                kind="Internal").ap()
        w2_sw = nc.dram_tensor("w2sw", [E * NHc * NFG * 128, FG * DHc], BF16,
                               kind="Internal").ap()
        b1ag_h = nc.dram_tensor("b1agh", [E * 128, NFL], F32, kind="ExternalInput").ap()
        b1bg_h = nc.dram_tensor("b1bgh", [E * 128, NFL], F32, kind="ExternalInput").ap()
    if cfg.debug:
        weo_h = nc.dram_tensor("weo", [128, NB * NT * E], F32, kind="ExternalOutput").ap()
        go_h = nc.dram_tensor("go", [NB * S, D], F32, kind="ExternalOutput").ap()
        ao_h = nc.dram_tensor("ao", [NB * S, S], F32, kind="ExternalOutput").ap()
        xnso_h = nc.dram_tensor("xnso", [KD, 128, NB * S], BF16, kind="ExternalOutput").ap()
        colso_h = nc.dram_tensor("colso", [NB * NT, 128, 8], F32, kind="ExternalOutput").ap()
        lgo_h = nc.dram_tensor("lgo", [NB, 128, NT * E], F32, kind="ExternalOutput").ap()
        vno_h = nc.dram_tensor("vno", [NB, 128, S // 128 * D], BF16, kind="ExternalOutput").ap()
        qo_h = nc.dram_tensor("qo", [NB, 128, (DQ // 128) * S], BF16, kind="ExternalOutput").ap()
        ko_h = nc.dram_tensor("ko", [NB, 128, (DQ // 128) * S], BF16, kind="ExternalOutput").ap()

    st = ExitStack()

    def sbuf(name, shape, dt):
        return st.enter_context(nc.sbuf_tensor(name, shape, dt))

    def psum(name, shape, dt=F32):
        return st.enter_context(nc.psum_tensor(name, shape, dt))

    def sem(name):
        return Sem(st.enter_context(nc.semaphore(name)))

    we_all = sbuf("we_all", [128, NB * NT * E], F32)
    weT_all = sbuf("weT_all", [E, NB * S], BF16)
    if not cfg.sparse:
        b1a_sb = sbuf("b1a_sb", [128, E * NFL], F32)
        b1b_sb = sbuf("b1b_sb", [128, E * NFL], F32)
    b2_sb = sbuf("b2_sb", [E, D], BF16)
    ident_sb = sbuf("ident_sb", [128, 128], F32)
    amask_sb = sbuf("amask_sb", [128, 128], U8)
    neg_sb = sbuf("neg_sb", [128, 128], F32)
    cols = sbuf("colsA", [128, 32], F32)
    epsln_c = sbuf("epsln_c", [128, 1], F32)
    epsrms_c = sbuf("epsrms_c", [128, 1], F32)
    if cfg.sparse:
        I32 = mybir.dt.int32
        SEG = 2 * S // 128
        sent_sb = sbuf("sent_sb", [128, 256], F32)
        pcv = [sbuf("pcv0", [128, 2048], BF16), sbuf("pcv1", [128, 2048], BF16)]
        egrp_i = sbuf("egrp_i", [1, NB * G], I32)
        e_iota = sbuf("e_iota", [128, E], F32)
        e12g = sbuf("e12g", [128, SEG], F32)
        w12g = sbuf("w12g", [128, SEG], F32)
        mskg = sbuf("mskg", [128, SEG], F32)
        scng = sbuf("scng", [128, SEG], F32)
        zrog = sbuf("zrog", [128, SEG], F32)
        slag = sbuf("slag", [128, SEG], F32)
        tmpg = sbuf("tmpg", [128, SEG], F32)
        ubase = sbuf("ubase", [128, SEG], F32)
        rankc = sbuf("rankc", [128, 1], F32)
        rankcS = sbuf("rankcS", [128, 1], F32)
        rankcN = sbuf("rankcN", [128, 1], F32)
        pcol = sbuf("pcol", [128, 1], F32)
        ebase = sbuf("ebase", [128, G], F32)
        eb2 = sbuf("eb2", [128, G], F32)
        widx_i = sbuf("widx_i", [128, NB * G * NFG], I32)
        w2x_i = sbuf("w2x_i", [128, NB * G * (D // min(512, D)) * NFG], I32)
        b1x_i = sbuf("b1x_i", [128, NB * G], I32)
        basec = sbuf("basec", [128, E], F32)
        etmp = sbuf("etmp", [1, G], F32)
        ones_r = sbuf("ones_r", [1, 128], F32)
        segoff = sbuf("segoff", [128, 1], F32)
        rowa = sbuf("rowa", [1, 128], F32)
        rowb = sbuf("rowb", [1, 128], F32)
        rzro = sbuf("rzro", [1, 128], F32)
        crow = sbuf("crow", [1, 64], F32)
        cri = sbuf("cri", [1, 16], I32)
        grow = sbuf("grow", [1, G], F32)
        egf = sbuf("egf", [1, G], F32)
        sloti = sbuf("sloti", [128, SEG], I32)
        pdg = sbuf("pdg", [128, SEG * 4], F32)

    ps_big = psum("ps_big", [128, 2048])
    ps_o0 = psum("ps_o0", [128, 1024])
    ps_o1 = psum("ps_o1", [128, 1024])

    s_pe = sem("s_pe")
    s_dv = sem("s_dv")
    s_ac = sem("s_ac")
    s_wc = sem("s_wc")
    s_x = [sem("s_x0"), sem("s_x1")]
    s_se = [sem("s_se0"), sem("s_se1")]
    s_spx = sem("s_spx")
    s_w1a = [sem("s_w1a0"), sem("s_w1a1")]
    s_w1b = [sem("s_w1b0"), sem("s_w1b1")]
    s_w2 = [sem("s_w20"), sem("s_w21")]
    s_xn2r = sem("s_xn2r")
    s_tb2s = [sem("s_tb2a"), sem("s_tb2b")]
    s_xc = [sem("s_xc0"), sem("s_xc1")]
    s_sc = [sem("s_sc0"), sem("s_sc1")]
    s_out = [sem("s_out0"), sem("s_out1")]
    s_dbg = sem("s_dbg")
    s_wqkv = sem("s_wqkv")
    s_ms = sem("s_ms")
    s_spn = [sem("s_spn0"), sem("s_spn1")]
    s_tbl = sem("s_tbl")
    s_pre = sem("s_pre")
    s_rt = sem("s_rt")
    s_gx = [sem("s_gx0"), sem("s_gx1")]
    s_tb2 = sem("s_tb2")
    s_osc = [sem("s_osc0"), sem("s_osc1")]
    s_fill = sem("s_fill")
    s_pri = [sem("s_pri0"), sem("s_pri1")]
    s_pro = [sem("s_pro0"), sem("s_pro1")]

    q = Q()
    _regc = [0]

    def _rname(p):
        _regc[0] += 1
        return "%s%d" % (p, _regc[0])

    def w(s, val):
        return (s.h, val) if val > 0 else None

    psg = {r: {"dv": 0, "ac": 0} for r in
           ("bigA", "ba0", "bb0", "ba1", "bb1", "o0h0", "o0h1", "o1h0", "o1h1")}

    def pwait(*regions):
        rs = set(regions)
        quarters = {"ba0", "bb0", "ba1", "bb1"}
        if "bigA" in rs:
            rs |= quarters
        elif rs & quarters:
            rs.add("bigA")
        dv = max(psg[r]["dv"] for r in rs)
        ac = max(psg[r]["ac"] for r in rs)
        return [w(s_dv, dv), w(s_ac, ac)]

    def pread(region, kind, cnt):
        psg[region][kind] = max(psg[region][kind], cnt)

    stA = ExitStack()

    def sbufA(name, shape, dt):
        return stA.enter_context(nc.sbuf_tensor(name, shape, dt))

    assert KD * DQ == KQ * D
    wqkv_sb = sbufA("wqkv_sb", [128, KD * DQ], BF16)
    wg_sb = sbufA("wg_sb", [128, KD * E], BF16)
    lnw_bc = sbufA("lnw_bc", [128, D], F32)
    bg_bc = sbufA("bg_bc", [128, E], F32)
    xin = [sbufA("xin0", [128, D], F32), sbufA("xin1", [128, D], F32)]
    sqs = sbufA("sqs", [128, D], BF16)
    xnf = [sbufA("xnf0", [128, D], F32), sbufA("xnf1", [128, D], F32)]
    xn2f = [sbufA("xn2f0", [128, D], F32), sbufA("xn2f1", [128, D], F32)]
    xnT = sbufA("xnT", [128, KD * S], BF16)
    SW2 = 256 if cfg.sparse else S
    xn2T = sbufA("xn2T_A", [128, KD * SW2], BF16)
    logits = sbufA("logits", [128, NT * E], F32)
    glt = sbufA("glt", [128, 4 * NT], F32)
    geq = sbufA("geq", [128, 2 * NT * E], F32)
    qT_sb = sbufA("qT", [128, KQ * S], BF16)
    kT_sb = sbufA("kT", [128, KQ * S], BF16)
    vdT_sb = sbufA("vdT", [128, KQ * S], BF16)
    vnat = sbufA("vnat", [128, NT * D], BF16)
    expb = sbufA("expb", [128, S], F32)
    attnT = sbufA("attnT", [128, S], BF16)
    sev = [sbufA("sev0", [128, D], F32), sbufA("sev1", [128, D], F32)]
    if cfg.sparse:
        gsc = sbufA("gsc", [128, NT * E], F32)
        gidx = sbufA("gidx", [128, 2 * NT], F32)
        rtn = sbufA("rtn", [128, 4 * NT], F32)
        rtv = sbufA("rtv", [4 * NT, 128], F32)

    def ld3(dst, src, kk):
        return lambda g: g.dma_start(
            out=dst[:].rearrange("p (k m) -> p k m", k=kk),
            in_=src.rearrange("(k p) m -> p k m", p=128))

    q.op("gpsimd", ld3(wg_sb, wg_h, KD), inc=s_wc.inc(16))
    q.op("gpsimd", lambda g: g.dma_start(out=lnw_bc[:], in_=lnw_h[None, :].broadcast_to([128, D])),
         inc=s_wc.inc(16))
    q.op("gpsimd", lambda g: g.dma_start(out=bg_bc[:], in_=bg_h[None, :].broadcast_to([128, E])),
         inc=s_wc.inc(16))
    if not cfg.sparse:
        q.op("gpsimd", lambda g: g.dma_start(out=b1a_sb[:], in_=b1a_h[:]), inc=s_wc.inc(16))
        q.op("gpsimd", lambda g: g.dma_start(out=b1b_sb[:], in_=b1b_h[:]), inc=s_wc.inc(16))
    q.op("gpsimd", lambda g: g.dma_start(out=b2_sb[:], in_=b2_h[:]), inc=s_wc.inc(16))
    q.op("gpsimd", lambda g: g.dma_start(out=ident_sb[:], in_=ident_h[:]), inc=s_wc.inc(16))
    q.op("gpsimd", lambda g: g.dma_start(out=amask_sb[:], in_=amask_h[:]), inc=s_wc.inc(16))
    q.op("gpsimd", lambda g: g.memset(neg_sb[:], -1e30), inc=s_ms.inc(1))
    q.op("gpsimd", lambda g: g.memset(epsln_c[:], float(cfg.eps_ln)), inc=s_ms.inc(1))
    q.op("gpsimd", lambda g: g.memset(epsrms_c[:], float(cfg.eps_rms)), inc=s_ms.inc(1))
    if cfg.sparse:
        q.op("gpsimd", lambda g: g.memset(sent_sb[:], float(2 * NBS)), inc=s_ms.inc(1))
        q.op("gpsimd", lambda g: g.iota(ubase[:], [[1, SEG]], channel_multiplier=SEG,
                                        allow_small_or_imprecise_dtypes=True), inc=s_ms.inc(1))
        q.op("gpsimd", lambda g: g.iota(e_iota[:], [[1, E]], channel_multiplier=0,
                                        allow_small_or_imprecise_dtypes=True), inc=s_ms.inc(1))
        q.op("gpsimd", lambda g: g.iota(rankc[:], [[0, 1]], channel_multiplier=1,
                                        allow_small_or_imprecise_dtypes=True), inc=s_ms.inc(1))
        q.op("gpsimd", lambda g: g.iota(pcol[:], [[0, 1]], channel_multiplier=1,
                                        allow_small_or_imprecise_dtypes=True), inc=s_ms.inc(1))
        q.op("gpsimd", lambda g: g.iota(grow[:], [[1, G]], channel_multiplier=0,
                                        allow_small_or_imprecise_dtypes=True), inc=s_ms.inc(1))
        q.op("gpsimd", lambda g: g.memset(zrog[:], 0.0), inc=s_ms.inc(1))
        q.op("gpsimd", lambda g: g.memset(rzro[:], 0.0), inc=s_ms.inc(1))
        q.op("gpsimd", lambda g: g.memset(pdg[:], 0.0), inc=s_ms.inc(1))
        q.op("gpsimd", lambda g: g.memset(ones_r[:], 1.0), inc=s_ms.inc(1))
        BND = {}
        _bnd_gen = [0]

        def _setup_bounds(e):
            vals = dict(tbl=G * GRP - 1, gath=NBS - 1, scat=2 * NBS - 1,
                        w1=E * NFG * 128 - 1,
                        w2=E * NHc * NFG * 128 - 1, b1=E * 128 - 1)
            gen = _bnd_gen[0]
            _bnd_gen[0] += 1
            ins = None
            for kk, vv in vals.items():
                r = e.register("bnd%d_%s" % (gen, kk)).__enter__()
                ins = e.reg_mov(r, vv)
                BND[kk] = e.snap(r)
            return ins
        q.op("gpsimd", _setup_bounds)
        ms_init = s_ms.n
        q.op("vector", lambda e: e.tensor_scalar(
            rankc[:], rankc[:], 63.5, 0.0, ALU.is_gt, ALU.add),
            waits=[(s_ms.h, ms_init)], inc=s_dv.inc())
        dvx = s_dv.n
        q.op("vector", lambda e: e.tensor_scalar_mul(rankcS[:], rankc[:], float(S)),
             waits=[w(s_dv, dvx)], inc=s_dv.inc())
        dvx = s_dv.n
        q.op("vector", lambda e: e.tensor_scalar_mul(rankcN[:], rankc[:], float(NBS)),
             waits=[w(s_dv, dvx)], inc=s_dv.inc())
        dvx = s_dv.n
        q.op("vector", lambda e: e.tensor_scalar_mul(grow[:], grow[:], float(GRP)),
             waits=[w(s_dv, dvx)], inc=s_dv.inc())
        RT_INIT_DV = s_dv.n
        for b0 in range(NB):
            nfill = (G * GRP * 4 + (128 * 256) - 1) // (128 * 256)
            flat = tbl_h[b0].rearrange("a b -> (a b)")
            for fi in range(nfill):
                lo = fi * 128 * 256
                hi = min((fi + 1) * 128 * 256, G * GRP * 4)
                q.op("sync", (lambda flat, lo, hi: lambda e: e.dma_start(
                    out=flat[lo:hi].rearrange("(p j) -> p j", p=128),
                    in_=sent_sb[:, 0:(hi - lo) // 128]))(flat, lo, hi),
                    waits=[(s_ms.h, ms_init)], inc=s_fill.inc(16))
        TBL_FILL = s_fill.n
        pre_jobs = []
        NKH = 2 if KD % 2 == 0 and KD * FGWc > 2048 else 1
        KH = KD // NKH
        for (srcw, dstw) in ((w1a_h, w1a_sw), (w1b_h, w1b_sw)):
            for ei in range(E):
                for fg in range(NFG):
                    for kh in range(NKH):
                        pre_jobs.append((
                            srcw[ei, kh * KH * 128:(kh + 1) * KH * 128,
                                 fg * FGWc:(fg + 1) * FGWc].rearrange(
                                "(k p) f -> p k f", p=128),
                            dstw[(ei * NFG + fg) * 128:(ei * NFG + fg + 1) * 128,
                                 kh * KH * FGWc:(kh + 1) * KH * FGWc],
                            KH * FGWc))
        for ei in range(E):
            for half in range(NHc):
                for fg in range(NFG):
                    row = ((ei * NHc + half) * NFG + fg) * 128
                    pre_jobs.append((
                        w2_h[ei, fg * FGWc:(fg + 1) * FGWc,
                             half * DHc:(half + 1) * DHc].rearrange(
                            "(fl p) d -> p fl d", p=128),
                        w2_sw[row:row + 128, :],
                        FG * DHc))
        pre_state = dict(free=[0, 0], n=[0, 0], i=0)

        def emit_pre(count):
            for _ in range(count):
                if pre_state["i"] >= len(pre_jobs):
                    return
                srcap, dstap, ncols = pre_jobs[pre_state["i"]]
                sl = pre_state["i"] % 2
                pre_state["i"] += 1
                q.op("gpsimd", (lambda srcap, ncols, sl: lambda e: e.dma_start(
                    out=pcv[sl][:, 0:ncols].rearrange(
                        "p (a b) -> p a b", a=srcap.shape[1]),
                    in_=srcap))(srcap, ncols, sl),
                    waits=[w(s_pro[sl], pre_state["free"][sl])], inc=s_pri[sl].inc(16))
                pre_state["n"][sl] = s_pri[sl].n
                q.op("sync", (lambda dstap, ncols, sl: lambda e: e.dma_start(
                    out=dstap, in_=pcv[sl][:, 0:ncols]))(dstap, ncols, sl),
                    waits=[(s_pri[sl].h, pre_state["n"][sl])], inc=s_pro[sl].inc(16))
                pre_state["free"][sl] = s_pro[sl].n
        PRE_TOTAL = None
    WC = s_wc.n
    wcw = (s_wc.h, WC)
    wcm = (s_ms.h, s_ms.n)

    dbg_cols = [0, 0]
    dbg_xnt = 0
    x_free = [0, 0]
    xnf_free = [0, 0]
    xn2f_free = [0, 0]
    xn2Tslot_free = [0, 0]
    sev_rec = [0, 0]
    we_done = [0] * NB
    smm_last = [0] * NB
    wqkv_carry = [0]

    TBL_DONE = [0] * NB
    EGRP_DV = [0] * NB
    SPN_END = [0, 0]

    def emit_routing(b):
        dvc = [s_dv.n]

        def vq(fn, extra=()):
            q.op("vector", fn, waits=[w(s_dv, dvc[0])] + [x for x in extra if x], inc=s_dv.inc())
            dvc[0] = s_dv.n

        vq(lambda e: e.tensor_copy(rtn[:, 0:2 * NT], gidx[:]), extra=[w(s_dv, dv_gidx)])
        vq(lambda e: e.tensor_copy(rtn[:, 2 * NT:3 * NT], glt[:, 2 * NT:3 * NT]),
           extra=[w(s_dv, dv_we)])
        vq(lambda e: e.tensor_copy(rtn[:, 3 * NT:4 * NT], glt[:, 3 * NT:4 * NT]))
        rdy = dvc[0]
        q.op("tensor", lambda e: e.transpose(
            ps_o0[0:4 * NT, 0:128], rtn[:], ident_sb[:]),
            waits=[w(s_dv, rdy), wcw] + pwait("o0h0"), inc=s_pe.inc())
        pe_rt = s_pe.n
        vq(lambda e: e.tensor_copy(rtv[:], ps_o0[0:4 * NT, 0:128]),
           extra=[w(s_pe, pe_rt)])
        pread("o0h0", "dv", s_dv.n)
        q.op("sync", (lambda b: lambda e: e.dma_start(
            out=rt_h[b], in_=rtv[:]))(b), waits=[w(s_dv, dvc[0])], inc=s_rt.inc(16))
        rt_st = s_rt.n
        flat = rt_h[b].rearrange("a b -> (a b)")
        q.op("sync", lambda e: e.dma_start(
            out=e12g[:], in_=flat[0:2 * S].rearrange("(p j) -> p j", p=128)),
            waits=[(s_rt.h, rt_st)], inc=s_rt.inc(16))
        q.op("sync", lambda e: e.dma_start(
            out=w12g[:], in_=flat[2 * S:4 * S].rearrange("(p j) -> p j", p=128)),
            inc=s_rt.inc(16))
        rt_ld = s_rt.n

        vq(lambda e: e.memset(slag[:], 0.0), extra=[(s_rt.h, rt_ld), (s_ms.h, s_ms.n),
                                                    w(s_dv, RT_INIT_DV)])
        for ei in range(E):
            vq((lambda ei: lambda e: e.tensor_single_scalar(
                mskg[:], e12g[:], float(ei), ALU.is_equal))(ei))
            vq(lambda e: e.tensor_tensor_scan(
                scng[:], mskg[:], zrog[:], 0.0, ALU.add, ALU.add))
            rdy = dvc[0]
            q.op("tensor", lambda e: e.transpose(
                ps_o0[0:1, 0:128], scng[:, SEG - 1:SEG], ident_sb[:]),
                waits=[w(s_dv, rdy)] + pwait("o0h0"), inc=s_pe.inc())
            pe_t = s_pe.n
            vq(lambda e: e.tensor_copy(rowa[:], ps_o0[0:1, 0:128]),
               extra=[w(s_pe, pe_t)])
            pread("o0h0", "dv", s_dv.n)
            vq(lambda e: e.tensor_tensor_scan(
                rowb[:], rowa[:], rzro[:], 0.0, ALU.add, ALU.add))
            vq((lambda ei: lambda e: e.tensor_copy(
                crow[0:1, ei:ei + 1], rowb[0:1, 127:128]))(ei))
            vq(lambda e: e.tensor_sub(rowa[:], rowb[:], rowa[:]))
            rdy = dvc[0]
            q.op("tensor", lambda e: e.transpose(
                ps_o0[:, 128:129], rowa[:], ident_sb[0:1, 0:1]),
                waits=[w(s_dv, rdy)] + pwait("o0h0"), inc=s_pe.inc())
            pe_t = s_pe.n
            vq(lambda e: e.tensor_copy(segoff[:], ps_o0[:, 128:129]),
               extra=[w(s_pe, pe_t)])
            pread("o0h0", "dv", s_dv.n)
            vq(lambda e: e.tensor_scalar(
                tmpg[:], scng[:], segoff[:], -1.0, ALU.add, ALU.add))
            vq(lambda e: e.tensor_mul(tmpg[:], tmpg[:], mskg[:]))
            vq(lambda e: e.tensor_add(slag[:], slag[:], tmpg[:]))

        vq(lambda e: e.tensor_copy(cri[0:1, 0:E], crow[0:1, 0:E]))
        vq(lambda e: e.tensor_scalar_add(cri[0:1, 0:E], cri[0:1, 0:E], GRP - 1))
        vq(lambda e: e.tensor_single_scalar(
            cri[0:1, 0:E], cri[0:1, 0:E], cfg.LG, ALU.arith_shift_right))
        vq(lambda e: e.tensor_single_scalar(
            cri[0:1, 0:E], cri[0:1, 0:E], cfg.LG, ALU.arith_shift_left))
        vq(lambda e: e.tensor_copy(crow[0:1, 8:8 + E], cri[0:1, 0:E]))
        vq(lambda e: e.tensor_tensor_scan(
            crow[0:1, 16:16 + E], crow[0:1, 8:8 + E], rzro[0:1, 0:E], 0.0,
            ALU.add, ALU.add))
        vq(lambda e: e.tensor_sub(crow[0:1, 24:24 + E], crow[0:1, 16:16 + E],
                                  crow[0:1, 8:8 + E]))
        rdy = dvc[0]
        q.op("tensor", lambda e: e.matmul(
            ps_o0[:, 0:E], ones_r[:], crow[0:1, 24:24 + E], start=True, stop=True),
            waits=[w(s_dv, rdy), (s_ms.h, s_ms.n)] + pwait("o0h0"), inc=s_pe.inc())
        pe_t = s_pe.n
        vq(lambda e: e.tensor_copy(basec[:, 0:E], ps_o0[:, 0:E]), extra=[w(s_pe, pe_t)])
        pread("o0h0", "dv", s_dv.n)

        vq(lambda e: e.memset(egf[:], -1.0))
        for ei in range(E):
            vq((lambda ei: lambda e: e.tensor_single_scalar(
                etmp[:], grow[:], crow[0:1, 24 + ei:25 + ei], ALU.is_ge))(ei))
            vq(lambda e: e.tensor_add(egf[:], egf[:], etmp[:]))
        vq((lambda b: lambda e: e.tensor_copy(
            egrp_i[0:1, b * G:(b + 1) * G], egf[:]))(b))
        rdy = dvc[0]
        q.op("tensor", lambda e: e.matmul(
            ps_o0[:, 256:256 + G], ones_r[:], egf[:], start=True, stop=True),
            waits=[w(s_dv, rdy), (s_ms.h, s_ms.n)] + pwait("o0h0"), inc=s_pe.inc())
        pe_t = s_pe.n
        vq(lambda e: e.tensor_copy(ebase[:], ps_o0[:, 256:256 + G]),
           extra=[w(s_pe, pe_t)])
        pread("o0h0", "dv", s_dv.n)
        NHl = D // min(512, D)
        DHl = min(512, D)
        vq(lambda e: e.tensor_scalar(eb2[:], ebase[:], 128.0, pcol[:],
                                     ALU.mult, ALU.add))
        vq((lambda b: lambda e: e.tensor_copy(b1x_i[:, b * G:(b + 1) * G], eb2[:]))(b))
        vq(lambda e: e.tensor_scalar(eb2[:], ebase[:], float(NFG * 128), pcol[:],
                                     ALU.mult, ALU.add))
        wv = widx_i[:, b * G * NFG:(b + 1) * G * NFG].rearrange(
            "p (g c) -> p g c", c=NFG)
        for fg in range(NFG):
            vq((lambda fg, wv: lambda e: e.tensor_scalar_add(
                wv[:, :, fg:fg + 1].squeeze(2), eb2[:], float(fg * 128)))(fg, wv))
        vq(lambda e: e.tensor_scalar(eb2[:], ebase[:], float(NHl * NFG * 128), pcol[:],
                                     ALU.mult, ALU.add))
        w2v = w2x_i[:, b * G * NHl * NFG:(b + 1) * G * NHl * NFG].rearrange(
            "p (g c) -> p g c", c=NHl * NFG)
        for c in range(NHl * NFG):
            vq((lambda c, w2v: lambda e: e.tensor_scalar_add(
                w2v[:, :, c:c + 1].squeeze(2), eb2[:], float(c * 128)))(c, w2v))
        EGRP_DV[b] = dvc[0]

        for ei in range(E):
            vq((lambda ei: lambda e: e.tensor_single_scalar(
                mskg[:], e12g[:], float(ei), ALU.is_equal))(ei))
            vq((lambda ei: lambda e: e.tensor_scalar_mul(
                tmpg[:], mskg[:], basec[:, ei:ei + 1]))(ei))
            vq(lambda e: e.tensor_add(slag[:], slag[:], tmpg[:]))
        vq(lambda e: e.tensor_copy(sloti[:], slag[:]),
           extra=[(s_tbl.h, TBL_DONE[b - 1]) if b > 0 and TBL_DONE[b - 1] else None])

        vq((lambda b: lambda e: e.tensor_scalar(
            tmpg[:], ubase[:], rankcS[:], float(b * S), ALU.subtract, ALU.add))(b))
        pdg4 = pdg[:].rearrange("p (j f) -> p j f", f=4)
        vq(lambda e: e.tensor_copy(pdg4[:, :, 0:1].squeeze(2), tmpg[:]))
        vq(lambda e: e.tensor_scalar(
            pdg4[:, :, 1:2].squeeze(2), tmpg[:], rankcN[:], 0.0, ALU.add, ALU.add))
        vq(lambda e: e.tensor_copy(pdg4[:, :, 2:3].squeeze(2), w12g[:]))
        rdy = dvc[0]
        for j in range(SEG):
            q.op("gpsimd", (lambda b, j: lambda e: e.indirect_dma_start(
                out=tbl_h[b],
                out_offset=bass.IndirectOffsetOnAxis(ap=sloti[:, j:j + 1], axis=0),
                in_=pdg[:, j * 4:(j + 1) * 4], in_offset=None,
                bounds_check=BND["tbl"], oob_is_err=False))(b, j),
                waits=[(s_fill.h, TBL_FILL), w(s_dv, rdy)] if j == 0 else [],
                inc=s_tbl.inc(16))
        TBL_DONE[b] = s_tbl.n

    for b in range(NB):
        logit_guard = 0

        for tt in range(NT):
            sl = tt % 2
            par = tt % 2
            row0 = b * S + tt * 128
            cb = 16 * par
            q.op("sync", (lambda r, sl: lambda e: e.dma_start(
                out=xin[sl][:], in_=x_h[r:r + 128, :]))(r=row0, sl=sl),
                waits=[w(s_dv, x_free[sl])], inc=s_x[sl].inc(16))
            xw = [(s_x[sl].h, s_x[sl].n)]
            q.op("scalar", (lambda sl, cb: lambda e: e.activation(
                sqs[:], xin[sl][:], AF.Square, accum_out=cols[:, cb + 1:cb + 2]))(sl, cb),
                waits=xw + [w(s_dv, x_free[sl])] +
                ([w(s_dbg, dbg_cols[par])] if cfg.debug else []), inc=s_ac.inc())
            ac_sq = s_ac.n
            q.op("vector", (lambda sl, cb: lambda e: e.reduce_sum(
                cols[:, cb:cb + 1], xin[sl][:], axis=AX.X))(sl, cb),
                waits=xw + ([w(s_dbg, dbg_cols[par])] if cfg.debug else []),
                inc=s_dv.inc())
            dv_rs = s_dv.n
            q.op("vector", (lambda cb: lambda e: e.tensor_scalar_mul(
                cols[:, cb + 2:cb + 3], cols[:, cb:cb + 1], 1.0 / D))(cb),
                waits=[w(s_dv, dv_rs)], inc=s_dv.inc())
            dv_mean = s_dv.n
            q.op("vector", (lambda cb: lambda e: e.tensor_scalar_mul(
                cols[:, cb + 5:cb + 6], cols[:, cb + 1:cb + 2], 1.0 / D))(cb),
                waits=[w(s_ac, ac_sq)], inc=s_dv.inc())
            dv_ex2 = s_dv.n
            q.op("vector", (lambda cb: lambda e: e.scalar_tensor_tensor(
                cols[:, cb + 6:cb + 7], cols[:, cb + 2:cb + 3], cols[:, cb + 2:cb + 3],
                cols[:, cb + 5:cb + 6], ALU.mult, ALU.subtract))(cb),
                waits=[w(s_dv, dv_mean), w(s_dv, dv_ex2)], inc=s_dv.inc())
            dv_nv = s_dv.n
            q.op("scalar", (lambda cb: lambda e: e.activation(
                cols[:, cb + 3:cb + 4], cols[:, cb + 6:cb + 7], AF.Sqrt,
                scale=-1.0, bias=epsln_c[:]))(cb),
                waits=[w(s_dv, dv_nv), wcw, wcm], inc=s_ac.inc())
            ac_sd = s_ac.n
            q.op("vector", (lambda cb: lambda e: e.reciprocal(
                cols[:, cb + 3:cb + 4], cols[:, cb + 3:cb + 4]))(cb),
                waits=[w(s_ac, ac_sd)], inc=s_dv.inc())
            dv_rstd = s_dv.n
            q.op("vector", (lambda sl, par, cb: lambda e: e.tensor_scalar(
                xnf[par][:], xin[sl][:], cols[:, cb + 2:cb + 3], cols[:, cb + 3:cb + 4],
                ALU.subtract, ALU.mult))(sl, par, cb),
                waits=[w(s_pe, xnf_free[par]), w(s_dv, dv_rstd)], inc=s_dv.inc())
            dv_xn = s_dv.n
            x_free[sl] = dv_xn
            q.op("scalar", (lambda par, cb: lambda e: e.activation(
                sqs[:], xnf[par][:], AF.Square, accum_out=cols[:, cb + 7:cb + 8]))(par, cb),
                waits=[w(s_dv, dv_xn)], inc=s_ac.inc())
            ac_s2 = s_ac.n
            q.op("scalar", (lambda cb: lambda e: e.activation(
                cols[:, cb + 4:cb + 5], cols[:, cb + 7:cb + 8], AF.Sqrt,
                scale=1.0 / D, bias=epsrms_c[:]))(cb),
                waits=[w(s_ac, ac_s2)], inc=s_ac.inc())
            ac_sd2 = s_ac.n
            q.op("vector", (lambda cb: lambda e: e.reciprocal(
                cols[:, cb + 4:cb + 5], cols[:, cb + 4:cb + 5]))(cb),
                waits=[w(s_ac, ac_sd2)], inc=s_dv.inc())
            dv_rstd2 = s_dv.n
            q.op("vector", (lambda par, cb: lambda e: e.scalar_tensor_tensor(
                xn2f[par][:], xnf[par][:], cols[:, cb + 4:cb + 5], lnw_bc[:],
                ALU.mult, ALU.mult))(par, cb),
                waits=[wcw, wcm, w(s_pe, xn2f_free[par]), w(s_dv, dv_rstd2)] +
                      ([(s_spn[par].h, s_spn[par].n)] if cfg.sparse else []),
                inc=s_dv.inc())
            dv_xn2 = s_dv.n
            if cfg.sparse:
                q.op("sync", (lambda par, row0: lambda e: e.dma_start(
                    out=xn2n_h[row0:row0 + 128, :], in_=xn2f[par][:]))(par, row0),
                    waits=[w(s_dv, dv_xn2)], inc=s_spn[par].inc(16))

            if cfg.debug:
                q.op("sync", (lambda b, tt, cb: lambda e: e.dma_start(
                    out=colso_h[b * NT + tt], in_=cols[:, cb:cb + 8]))(b, tt, cb),
                    waits=[w(s_dv, dv_xn2)], inc=s_dbg.inc(16))
                dbg_cols[par] = s_dbg.n

            for which, srcf, dstT, dvready in ((0, xnf[par], xnT, dv_xn),
                                               (1, xn2f[par], xn2T, dv_xn2)):
                for c0 in range(0, KD, 4):
                    half = (c0 // 4) % 2
                    reg = "o0h%d" % half
                    n = min(4, KD - c0)
                    for j in range(n):
                        q.op("tensor", (lambda srcf, c0, j, half: lambda e: e.transpose(
                            ps_o0[:, half * 512 + j * 128: half * 512 + (j + 1) * 128],
                            srcf[:, (c0 + j) * 128: (c0 + j + 1) * 128], ident_sb[:]))(
                                srcf, c0, j, half),
                            waits=([w(s_dv, dvready), wcw, wcm] + pwait(reg) +
                                   ([w(s_dbg, dbg_xnt)] if cfg.debug else [])) if j == 0 else [],
                            inc=s_pe.inc() if j == n - 1 else None)
                    pe_tr = s_pe.n
                    extra = []
                    if which == 1 and c0 == 0 and not cfg.sparse:
                        extra = [w(s_spx, 16 * KD * b)]
                    if which == 1 and c0 == 0 and cfg.sparse:
                        extra = [w(s_pe, xn2Tslot_free[par])]
                    ttc = (tt % 2) if (which == 1 and cfg.sparse) else tt
                    q.op("vector", (lambda dstT, c0, n, half, ttc: lambda e: e.tensor_copy(
                        dstT[:].rearrange("p (k s) -> p k s", k=KD)[
                            :, c0:c0 + n, ttc * 128:(ttc + 1) * 128],
                        ps_o0[:, half * 512: half * 512 + n * 128].rearrange(
                            "p (k s) -> p k s", k=n)))(dstT, c0, n, half, ttc),
                        waits=[w(s_pe, pe_tr)] + extra, inc=s_dv.inc())
                    pread(reg, "dv", s_dv.n)
                if which == 0:
                    xnf_free[par] = pe_tr
                    xnT_done = s_dv.n
                else:
                    xn2f_free[par] = pe_tr
                    xn2T_done = s_dv.n

            x2b = (tt % 2) * 128 if cfg.sparse else tt * 128
            for k in range(KD):
                q.op("tensor", (lambda k, x2b: lambda e: e.matmul(
                    ps_o1[:, 0:E],
                    xn2T[:, k * SW2 + x2b: k * SW2 + x2b + 128],
                    wg_sb[:, k * E:(k + 1) * E],
                    start=(k == 0), stop=(k == KD - 1)))(k, x2b),
                    waits=([w(s_dv, xn2T_done)] + pwait("o1h0")) if k == 0 else [],
                    inc=s_pe.inc() if k == KD - 1 else None)
            pe_lg = s_pe.n
            xn2Tslot_free[tt % 2] = pe_lg
            q.op("vector", (lambda tt: lambda e: e.tensor_add(
                logits[:, tt * E:(tt + 1) * E], ps_o1[:, 0:E], bg_bc[:]))(tt),
                waits=[w(s_pe, pe_lg), wcw, wcm], inc=s_dv.inc())
            pread("o1h0", "dv", s_dv.n)
            lg_done = s_dv.n

        if cfg.debug:
            q.op("sync", (lambda b: lambda e: e.dma_start(
                out=lgo_h[b], in_=logits[:]))(b),
                waits=[w(s_dv, s_dv.n)], inc=s_dbg.inc(16))
            dbg_lg = s_dbg.n
        lg3 = logits[:].rearrange("p (t e) -> p t e", e=E)
        m1 = glt[:, 0 * NT:1 * NT]
        m2 = glt[:, 1 * NT:2 * NT]
        w1c = glt[:, 2 * NT:3 * NT]
        w2c = glt[:, 3 * NT:4 * NT]
        eq1 = geq[:, 0:NT * E]
        eq2 = geq[:, NT * E:2 * NT * E]
        eq13 = eq1.rearrange("p (t e) -> p t e", e=E)
        eq23 = eq2.rearrange("p (t e) -> p t e", e=E)
        q.op("vector", lambda e: e.reduce_max(m1, lg3, axis=AX.X),
             waits=[w(s_dv, lg_done)] + ([w(s_dbg, dbg_lg)] if cfg.debug else []),
             inc=s_dv.inc())
        dv_m1 = s_dv.n
        q.op("vector", lambda e: e.tensor_tensor(
            eq13, lg3, m1.unsqueeze(2).broadcast_to([128, NT, E]), ALU.is_equal),
            waits=[w(s_dv, dv_m1)], inc=s_dv.inc())
        dv_eq1 = s_dv.n
        q.op("vector", lambda e: e.scalar_tensor_tensor(
            eq2, eq1, -1e30, logits[:], ALU.mult, ALU.add),
            waits=[w(s_dv, dv_eq1)], inc=s_dv.inc())
        dv_msk = s_dv.n
        q.op("vector", lambda e: e.reduce_max(m2, eq23, axis=AX.X),
             waits=[w(s_dv, dv_msk)], inc=s_dv.inc())
        dv_m2 = s_dv.n
        q.op("vector", lambda e: e.tensor_sub(w1c, m1, m2), waits=[w(s_dv, dv_m2)],
             inc=s_dv.inc())
        dv_w1 = s_dv.n
        q.op("scalar", lambda e: e.activation(w1c, w1c, AF.Sigmoid),
             waits=[w(s_dv, dv_w1)], inc=s_ac.inc())
        ac_w1 = s_ac.n
        q.op("vector", lambda e: e.tensor_tensor(
            eq23, eq23, m2.unsqueeze(2).broadcast_to([128, NT, E]), ALU.is_equal),
            waits=[w(s_ac, ac_w1)], inc=s_dv.inc())
        dv_eq2 = s_dv.n
        if cfg.sparse:
            eb = e_iota[:].unsqueeze(1).broadcast_to([128, NT, E])
            gsc3 = gsc[:].rearrange("p (t e) -> p t e", e=E)
            q.op("vector", lambda e: e.tensor_tensor(gsc3, eq13, eb, ALU.mult),
                 waits=[w(s_dv, dv_eq2), (s_ms.h, s_ms.n)], inc=s_dv.inc())
            dvtmp = s_dv.n
            q.op("vector", lambda e: e.reduce_sum(gidx[:, 0:NT], gsc3, axis=AX.X),
                 waits=[w(s_dv, dvtmp)], inc=s_dv.inc())
            dvtmp = s_dv.n
            q.op("vector", lambda e: e.tensor_tensor(gsc3, eq23, eb, ALU.mult),
                 waits=[w(s_dv, dvtmp)], inc=s_dv.inc())
            dvtmp = s_dv.n
            q.op("vector", lambda e: e.reduce_sum(gidx[:, NT:2 * NT], gsc3, axis=AX.X),
                 waits=[w(s_dv, dvtmp)], inc=s_dv.inc())
            dv_gidx = s_dv.n
        q.op("vector", lambda e: e.tensor_scalar(w2c, w1c, -1.0, 1.0, ALU.mult, ALU.add),
             inc=s_dv.inc())
        dv_w2 = s_dv.n
        q.op("vector", lambda e: e.tensor_tensor(
            eq13, eq13, w1c.unsqueeze(2).broadcast_to([128, NT, E]), ALU.mult),
            waits=[w(s_dv, dv_eq2)], inc=s_dv.inc())
        dv_p1 = s_dv.n
        q.op("vector", lambda e: e.tensor_tensor(
            eq23, eq23, w2c.unsqueeze(2).broadcast_to([128, NT, E]), ALU.mult),
            waits=[w(s_dv, dv_w2)], inc=s_dv.inc())
        dv_p2 = s_dv.n
        q.op("vector", (lambda b: lambda e: e.tensor_add(
            we_all[:, b * NT * E:(b + 1) * NT * E], eq1, eq2))(b),
            waits=[w(s_dv, dv_p1), w(s_dv, dv_p2)], inc=s_dv.inc())
        dv_we = s_dv.n

        for c0 in range(0, NT, 4):
            half = (c0 // 4) % 2
            reg = "o0h%d" % half
            n = min(4, NT - c0)
            for j in range(n):
                q.op("tensor", (lambda b, c0, j, half: lambda e: e.transpose(
                    ps_o0[0:E, half * 512 + j * 128: half * 512 + (j + 1) * 128],
                    we_all[:, (b * NT + c0 + j) * E:(b * NT + c0 + j + 1) * E],
                    ident_sb[:]))(b, c0, j, half),
                    waits=([w(s_dv, dv_we)] + pwait(reg)) if j == 0 else [],
                    inc=s_pe.inc() if j == n - 1 else None)
            pe_wt = s_pe.n
            q.op("vector", (lambda b, c0, n, half: lambda e: e.tensor_copy(
                weT_all[0:E, b * S + c0 * 128: b * S + (c0 + n) * 128],
                ps_o0[0:E, half * 512: half * 512 + n * 128]))(b, c0, n, half),
                waits=[w(s_pe, pe_wt)], inc=s_dv.inc())
            pread(reg, "dv", s_dv.n)
        we_done[b] = s_dv.n

        if cfg.sparse:
            emit_routing(b)
            emit_pre((len(pre_jobs) + NB - 1) // NB)

        if cfg.debug:
            for k in range(KD):
                q.op("sync", (lambda b, k: lambda e: e.dma_start(
                    out=xnso_h[k, :, b * S:(b + 1) * S],
                    in_=xnT[:, k * S:(k + 1) * S]))(b, k),
                    waits=[w(s_dv, xnT_done)] if k == 0 else [], inc=s_dbg.inc(16))
            dbg_xnt = s_dbg.n
        if not cfg.sparse:
            for k in range(KD):
                q.op("sync", (lambda b, k: lambda e: e.dma_start(
                    out=xn2s_h[k, :, b * S:(b + 1) * S],
                    in_=xn2T[:, k * S:(k + 1) * S]))(b, k),
                    waits=[w(s_dv, xn2T_done)] if k == 0 else [], inc=s_spx.inc(16))

        wqkv_last_pe = wqkv_carry[0]
        if not cfg.only_ln:
            for (wsrc, kk, dstT) in ((wq_h, KD, qT_sb), (wk_h, KD, kT_sb), (wvd_h, KD, vdT_sb)):
                q.op("gpsimd", (lambda wsrc, kk: ld3(wqkv_sb, wsrc, kk))(wsrc, kk),
                     waits=[w(s_pe, wqkv_last_pe)], inc=s_wqkv.inc(16))
                wq_n = s_wqkv.n
                for m in range(KQ):
                    for c in range(0, S, 512):
                        cw = min(512, S - c)
                        half = (c // 512) % 2
                        reg = "o1h%d" % half
                        for k in range(KD):
                            q.op("tensor", (lambda m, c, cw, k, half: lambda e: e.matmul(
                                ps_o1[:, half * 512: half * 512 + cw],
                                wqkv_sb[:, k * DQ + m * 128: k * DQ + (m + 1) * 128],
                                xnT[:, k * S + c: k * S + c + cw],
                                start=(k == 0), stop=(k == KD - 1)))(m, c, cw, k, half),
                                waits=([w(s_dv, xnT_done), (s_wqkv.h, wq_n)] + pwait(reg))
                                if k == 0 else [],
                                inc=s_pe.inc() if k == KD - 1 else None)
                        pe_g = s_pe.n
                        q.op("vector", (lambda dstT, m, c, cw, half: lambda e: e.tensor_copy(
                            dstT[:, m * S + c: m * S + c + cw],
                            ps_o1[:, half * 512: half * 512 + cw]))(dstT, m, c, cw, half),
                            waits=[w(s_pe, pe_g)], inc=s_dv.inc())
                        pread(reg, "dv", s_dv.n)
                wqkv_last_pe = s_pe.n
        qkv_done = s_dv.n

        if not cfg.only_ln:
            q.op("gpsimd", ld3(wqkv_sb, wvu_h, KQ),
                 waits=[w(s_pe, wqkv_last_pe)], inc=s_wqkv.inc(16))
            wvu_n = s_wqkv.n
        for tt in range(0 if cfg.only_ln else NT):
            regs = ("o0h0", "o0h1") if tt % 2 == 0 else ("o1h0", "o1h1")
            pst = ps_o0 if tt % 2 == 0 else ps_o1
            nmm = 0
            tot = ((D + 511) // 512) * KQ
            for h in range(0, D, 512):
                hw = min(512, D - h)
                for kq in range(KQ):
                    nmm += 1
                    q.op("tensor", (lambda tt, h, hw, kq, pst: lambda e: e.matmul(
                        pst[:, h:h + hw],
                        vdT_sb[:, kq * S + tt * 128: kq * S + (tt + 1) * 128],
                        wqkv_sb[:, kq * D + h: kq * D + h + hw],
                        start=(kq == 0), stop=(kq == KQ - 1)))(tt, h, hw, kq, pst),
                        waits=(pwait(*regs) + [w(s_dv, qkv_done), (s_wqkv.h, wvu_n)])
                        if nmm == 1 else [],
                        inc=s_pe.inc() if nmm == tot else None)
            pe_v = s_pe.n
            q.op("vector", (lambda tt, pst: lambda e: e.tensor_copy(
                vnat[:, tt * D:(tt + 1) * D], pst[:, 0:D]))(tt, pst),
                waits=[w(s_pe, pe_v)], inc=s_dv.inc())
            for r in regs:
                pread(r, "dv", s_dv.n)
        if not cfg.only_ln:
            wqkv_carry[0] = s_pe.n

        expb_free = 0

        def emit_scores(tt):
            ncols = (tt + 1) * 128
            nblk = (ncols + 511) // 512
            for cbk in range(nblk):
                cw = min(512, ncols - cbk * 512)
                for kq in range(KQ):
                    q.op("tensor", (lambda tt, cbk, cw, kq: lambda e: e.matmul(
                        ps_big[:, cbk * 512: cbk * 512 + cw],
                        qT_sb[:, kq * S + tt * 128: kq * S + (tt + 1) * 128],
                        kT_sb[:, kq * S + cbk * 512: kq * S + cbk * 512 + cw],
                        start=(kq == 0), stop=(kq == KQ - 1)))(tt, cbk, cw, kq),
                        waits=pwait("bigA") if (cbk == 0 and kq == 0) else [],
                        inc=s_pe.inc() if (cbk == nblk - 1 and kq == KQ - 1) else None)
            return s_pe.n

        def emit_softmax(tt, pe_sc):
            ncols = (tt + 1) * 128
            q.op("vector", (lambda tt: lambda e: e.copy_predicated(
                ps_big[:, tt * 128:(tt + 1) * 128], amask_sb[:], neg_sb[:]))(tt),
                waits=[w(s_pe, pe_sc), wcw, wcm], inc=s_dv.inc())
            dv_mask = s_dv.n
            pread("bigA", "dv", dv_mask)
            q.op("vector", (lambda ncols: lambda e: e.reduce_max(
                cols[:, 16:17], ps_big[:, 0:ncols], axis=AX.X, negate=True))(ncols),
                waits=[w(s_dv, dv_mask)], inc=s_dv.inc())
            dv_mx = s_dv.n
            pread("bigA", "dv", dv_mx)
            q.op("scalar", lambda e: e.activation(
                cols[:, 17:18], cols[:, 16:17], AF.Copy, scale=float(cfg.scale)),
                waits=[w(s_dv, dv_mx)], inc=s_ac.inc())
            ac_cp = s_ac.n
            q.op("scalar", (lambda ncols: lambda e: e.activation(
                expb[:, 0:ncols], ps_big[:, 0:ncols], AF.Exp,
                bias=cols[:, 17:18], scale=float(cfg.scale),
                accum_out=cols[:, 18:19]))(ncols),
                waits=[w(s_pe, expb_free), w(s_ac, ac_cp)] +
                      ([(s_dbg.h, s_dbg.n)] if cfg.debug else []), inc=s_ac.inc())
            ac_exp = s_ac.n
            pread("bigA", "ac", ac_exp)
            q.op("vector", lambda e: e.reciprocal(cols[:, 19:20], cols[:, 18:19]),
                 waits=[w(s_ac, ac_exp)], inc=s_dv.inc())
            dv_rc = s_dv.n
            q.op("vector", (lambda ncols: lambda e: e.tensor_scalar_mul(
                expb[:, 0:ncols], expb[:, 0:ncols], cols[:, 19:20]))(ncols),
                waits=[w(s_dv, dv_rc)], inc=s_dv.inc())
            return s_dv.n

        def emit_trans_smm(tt, dv_attn):
            nonlocal expb_free
            n_tk = tt + 1
            if cfg.debug:
                rr = b * S + tt * 128
                q.op("sync", (lambda rr, n_tk: lambda e: e.dma_start(
                    out=ao_h[rr:rr + 128, 0:n_tk * 128], in_=expb[:, 0:n_tk * 128]))(rr, n_tk),
                    waits=[w(s_dv, dv_attn)], inc=s_dbg.inc(16))
            for c0 in range(0, n_tk, 4):
                half = (c0 // 4) % 2
                reg = "o0h%d" % half
                n = min(4, n_tk - c0)
                for j in range(n):
                    q.op("tensor", (lambda c0, j, half: lambda e: e.transpose(
                        ps_o0[:, half * 512 + j * 128: half * 512 + (j + 1) * 128],
                        expb[:, (c0 + j) * 128: (c0 + j + 1) * 128], ident_sb[:]))(c0, j, half),
                        waits=([w(s_dv, dv_attn)] + pwait(reg)) if j == 0 else [],
                        inc=s_pe.inc() if j == n - 1 else None)
                pe_tr = s_pe.n
                q.op("vector", (lambda c0, n, half: lambda e: e.tensor_copy(
                    attnT[:, c0 * 128:(c0 + n) * 128],
                    ps_o0[:, half * 512: half * 512 + n * 128]))(c0, n, half),
                    waits=[w(s_pe, pe_tr)], inc=s_dv.inc())
                pread(reg, "dv", s_dv.n)
            expb_free = s_pe.n
            attnT_done = s_dv.n
            nmm = 0
            tot = n_tk * ((D + 511) // 512)
            for tk in range(n_tk):
                for h in range(0, D, 512):
                    hw = min(512, D - h)
                    nmm += 1
                    q.op("tensor", (lambda tk, h, hw, n_tk: lambda e: e.matmul(
                        ps_o1[:, h:h + hw],
                        attnT[:, tk * 128:(tk + 1) * 128],
                        vnat[:, tk * D + h: tk * D + h + hw],
                        start=(tk == 0), stop=(tk == n_tk - 1)))(tk, h, hw, n_tk),
                        waits=([w(s_dv, attnT_done)] + pwait("o1h0", "o1h1"))
                        if nmm == 1 else [],
                        inc=s_pe.inc() if nmm == tot else None)
            pe_s = s_pe.n
            sl = tt % 2
            q.op("vector", (lambda sl: lambda e: e.tensor_copy(
                sev[sl][:], ps_o1[:, 0:D]))(sl),
                waits=[w(s_pe, pe_s), (s_se[sl].h, sev_rec[sl]) if sev_rec[sl] else None],
                inc=s_dv.inc())
            pread("o1h0", "dv", s_dv.n)
            pread("o1h1", "dv", s_dv.n)
            dv_sev = s_dv.n
            row0 = b * S + tt * 128
            q.op("sync", (lambda sl, row0: lambda e: e.dma_start(
                out=s_h[row0:row0 + 128, :], in_=sev[sl][:]))(sl, row0),
                waits=[w(s_dv, dv_sev)], inc=s_se[sl].inc(16))
            sev_rec[sl] = s_se[sl].n
            return pe_s

        prev = None
        for tt in range(0 if cfg.only_ln else NT):
            pe_sc = emit_scores(tt)
            if prev is not None:
                smm_last[b] = emit_trans_smm(prev[0], prev[1])
            dv_attn = emit_softmax(tt, pe_sc)
            prev = (tt, dv_attn)
        if prev is not None:
            smm_last[b] = emit_trans_smm(prev[0], prev[1])
        if cfg.debug:
            vnat_dv = s_dv.n
            q.op("sync", (lambda b: lambda e: e.dma_start(
                out=vno_h[b], in_=vnat[:]))(b), waits=[w(s_dv, vnat_dv)], inc=s_dbg.inc(16))
            q.op("sync", (lambda b: lambda e: e.dma_start(
                out=qo_h[b], in_=qT_sb[:]))(b), inc=s_dbg.inc(16))
            q.op("sync", (lambda b: lambda e: e.dma_start(
                out=ko_h[b], in_=kT_sb[:]))(b), inc=s_dbg.inc(16))

    PE_END_A = s_pe.n
    DV_END_A = s_dv.n
    AC_END_A = s_ac.n
    SE_END = (s_se[0].n, s_se[1].n)

    q.run_block(nc)
    stA.close()

    stB = ExitStack()

    def sbufB(name, shape, dt):
        return stB.enter_context(nc.sbuf_tensor(name, shape, dt))

    w1a_b = [sbufB("w1a_b0", [128, KD * FG * 128], BF16),
             sbufB("w1a_b1", [128, KD * FG * 128], BF16)]
    w1b_b = [sbufB("w1b_b0", [128, KD * FG * 128], BF16),
             sbufB("w1b_b1", [128, KD * FG * 128], BF16)]
    xcb = [sbufB("xc0", [128, D], F32), sbufB("xc1", [128, D], F32)]
    scb = [sbufB("scb0", [128, D], F32), sbufB("scb1", [128, D], F32)]
    gsb = sbufB("gsb", [128, D], F32)
    osb = [sbufB("osb0", [128, D], F32), sbufB("osb1", [128, D], F32)]
    if not cfg.sparse:
        xn2B = sbufB("xn2B", [128, KD * S], BF16)
        acc = sbufB("acc", [128, NT * D], F32)
        w2_b = [sbufB("w2_b0", [128, FG * D], BF16),
                sbufB("w2_b1", [128, FG * D], BF16)]
        yT = [sbufB("yT0", [128, FG * TBW], BF16), sbufB("yT1", [128, FG * TBW], BF16)]
        silu_sb = sbufB("silu_sb", [128, TBW], BF16)
    else:
        DHp = min(512, D)
        w2_b = [sbufB("w2_b0", [128, FG * DHp], BF16),
                sbufB("w2_b1", [128, FG * DHp], BF16)]
        yT = [sbufB("yT0", [128, NFL * GRP], BF16), sbufB("yT1", [128, NFL * GRP], BF16)]
        silu_sb = sbufB("silu_sb", [128, GRP], BF16)
        xgT = [sbufB("xgT0", [128, KD * GRP], BF16), sbufB("xgT1", [128, KD * GRP], BF16)]
        xgf = [sbufB("xgf0", [128, D], F32), sbufB("xgf1", [128, D], F32)]
        tb_sb = [sbufB("tb0", [128, NGT * 4], F32), sbufB("tb1", [128, NGT * 4], F32)]
        gi32 = [sbufB("gi0", [128, NGT], mybir.dt.int32),
                sbufB("gi1", [128, NGT], mybir.dt.int32)]
        si32 = [sbufB("si0", [128, NGT], mybir.dt.int32),
                sbufB("si1", [128, NGT], mybir.dt.int32)]
        b1ag = [sbufB("b1ag0", [128, NFL], F32), sbufB("b1ag1", [128, NFL], F32)]
        b1bg = [sbufB("b1bg0", [128, NFL], F32), sbufB("b1bg1", [128, NFL], F32)]
        o1b = sbufB("o1b", [128, D], F32)
        o2b = sbufB("o2b", [128, D], F32)


    barrier = [(s_pe.h, PE_END_A), (s_dv.h, DV_END_A), (s_ac.h, AC_END_A),
               (s_spx.h, 16 * KD * NB) if not cfg.sparse else None,
               (s_se[0].h, SE_END[0]) if SE_END[0] else None,
               (s_se[1].h, SE_END[1]) if SE_END[1] else None]
    barrier = [x for x in barrier if x is not None]
    if cfg.sparse:
        barrier = barrier + [(s_spn[0].h, s_spn[0].n) if s_spn[0].n else None,
                             (s_spn[1].h, s_spn[1].n) if s_spn[1].n else None,
                             (s_pro[0].h, s_pro[0].n), (s_pro[1].h, s_pro[1].n)]
        barrier = [x for x in barrier if x is not None]

    state = dict(w1_free=[0, 0], w2_free=[0, 0], yT_free=[0, 0], yT_ready=[0, 0],
                 silu_free=0, fl_last_pe=0, c_done=0, g_free=0,
                 xc_rec=[0, 0], sc_rec=[0, 0], osb_rec=[0, 0], o1_rec=0,
                 acc_ready=[0] * NT, first=True)

    def emit_w_loads(ts, e_i, fg):
        gidx = (ts * E + e_i) * NFG + fg
        slot = gidx % 2
        extra = list(barrier) if state["first"] else []
        state["first"] = False
        q.op("gpsimd", (lambda e_i, fg, slot: lambda e: e.dma_start(
            out=w1a_b[slot][:].rearrange("p (k f) -> p k f", k=KD),
            in_=w1a_h[e_i, :, fg * FG * 128:(fg + 1) * FG * 128].rearrange(
                "(k p) f -> p k f", p=128)))(e_i, fg, slot),
            waits=extra + [w(s_pe, state["w1_free"][slot])], inc=s_w1a[slot].inc(16))
        q.op("gpsimd", (lambda e_i, fg, slot: lambda e: e.dma_start(
            out=w1b_b[slot][:].rearrange("p (k f) -> p k f", k=KD),
            in_=w1b_h[e_i, :, fg * FG * 128:(fg + 1) * FG * 128].rearrange(
                "(k p) f -> p k f", p=128)))(e_i, fg, slot),
            inc=s_w1b[slot].inc(16))
        q.op("gpsimd", (lambda e_i, fg, slot: lambda e: e.dma_start(
            out=w2_b[slot][:].rearrange("p (f d) -> p f d", f=FG),
            in_=w2_h[e_i, fg * FG * 128:(fg + 1) * FG * 128, :].rearrange(
                "(f p) d -> p f d", p=128)))(e_i, fg, slot),
            waits=[w(s_pe, state["w2_free"][slot])], inc=s_w2[slot].inc(16))
        return slot, s_w1a[slot].n, s_w1b[slot].n, s_w2[slot].n

    def emit_fl(ts, e_i, fg, tb, wslot, w1a_n, w1b_n, xn2_n, unit):
        ys = unit % 2
        for fl in range(FG):
            par = fl % 2
            ra, rb = "ba%d" % par, "bb%d" % par
            pa = ps_big[:, par * 1024: par * 1024 + TBW]
            pb = ps_big[:, par * 1024 + 512: par * 1024 + 512 + TBW]
            for k in range(KD):
                waits = []
                if k == 0:
                    waits = pwait(ra) + [(s_w1a[wslot].h, w1a_n), (s_xn2r.h, xn2_n)]
                q.op("tensor", (lambda fl, k, tb, wslot, pa: lambda e: e.matmul(
                    pa, w1a_b[wslot][:, k * FG * 128 + fl * 128: k * FG * 128 + (fl + 1) * 128],
                    xn2B[:, k * S + tb * TBW: k * S + (tb + 1) * TBW],
                    start=(k == 0), stop=(k == KD - 1)))(fl, k, tb, wslot, pa),
                    waits=waits, inc=s_pe.inc() if k == KD - 1 else None)
            pe_a = s_pe.n
            for k in range(KD):
                waits = []
                if k == 0:
                    waits = pwait(rb) + [(s_w1b[wslot].h, w1b_n)]
                q.op("tensor", (lambda fl, k, tb, wslot, pb: lambda e: e.matmul(
                    pb, w1b_b[wslot][:, k * FG * 128 + fl * 128: k * FG * 128 + (fl + 1) * 128],
                    xn2B[:, k * S + tb * TBW: k * S + (tb + 1) * TBW],
                    start=(k == 0), stop=(k == KD - 1)))(fl, k, tb, wslot, pb),
                    waits=waits, inc=s_pe.inc() if k == KD - 1 else None)
            pe_b = s_pe.n
            bcol = e_i * NFL + fg * FG + fl
            q.op("scalar", (lambda bcol, pa: lambda e: e.activation(
                silu_sb[:], pa, AF.Silu, bias=b1a_sb[:, bcol:bcol + 1]))(bcol, pa),
                waits=[w(s_pe, pe_a), w(s_dv, state["silu_free"]), wcw], inc=s_ac.inc())
            pread(ra, "ac", s_ac.n)
            ac_silu = s_ac.n
            q.op("vector", (lambda bcol, pb, ys, fl: lambda e: e.scalar_tensor_tensor(
                yT[ys][:, fl * TBW:(fl + 1) * TBW], pb, b1b_sb[:, bcol:bcol + 1],
                silu_sb[:], ALU.add, ALU.mult))(bcol, pb, ys, fl),
                waits=[w(s_pe, pe_b), w(s_ac, ac_silu), wcw] +
                      ([w(s_pe, state["yT_free"][ys])] if fl == 0 else []),
                inc=s_dv.inc())
            pread(rb, "dv", s_dv.n)
            state["silu_free"] = s_dv.n
        state["yT_ready"][ys] = s_dv.n
        state["fl_last_pe"] = s_pe.n
        state["w1_free"][wslot] = s_pe.n

    def emit_o(ts, e_i, fg, tb, wslot, w2_n, unit):
        ys = unit % 2
        for ttl in range(TPB):
            tt = tb * TPB + ttl
            po = ps_o0 if ttl % 2 == 0 else ps_o1
            regs = ("o0h0", "o0h1") if ttl % 2 == 0 else ("o1h0", "o1h1")
            nmm = 0
            tot = FG * ((D + 511) // 512)
            for fl in range(FG):
                for h in range(0, D, 512):
                    hw = min(512, D - h)
                    nmm += 1
                    waits = []
                    if nmm == 1:
                        waits = pwait(*regs) + [w(s_dv, state["yT_ready"][ys]),
                                                (s_w2[wslot].h, w2_n)]
                    q.op("tensor", (lambda fl, h, hw, ttl, ys, wslot, po: lambda e: e.matmul(
                        po[:, h:h + hw],
                        yT[ys][:, fl * TBW + ttl * 128: fl * TBW + (ttl + 1) * 128],
                        w2_b[wslot][:, fl * D + h: fl * D + h + hw],
                        start=(fl == 0), stop=(fl == FG - 1)))(fl, h, hw, ttl, ys, wslot, po),
                        waits=waits,
                        inc=s_pe.inc() if nmm == tot else None)
            pe_o = s_pe.n
            state["w2_free"][wslot] = pe_o
            state["yT_free"][ys] = pe_o
            q.op("vector", (lambda ts, e_i, tt, po: lambda e: e.scalar_tensor_tensor(
                acc[:, tt * D:(tt + 1) * D], po[:, 0:D],
                we_all[:, (ts * NT + tt) * E + e_i:(ts * NT + tt) * E + e_i + 1],
                acc[:, tt * D:(tt + 1) * D], ALU.mult, ALU.add))(ts, e_i, tt, po),
                waits=[w(s_pe, pe_o), w(s_dv, state["acc_ready"][tt])], inc=s_dv.inc())
            for r in regs:
                pread(r, "dv", s_dv.n)
            state["acc_ready"][tt] = s_dv.n

    def emit_phase_c(ts):
        for tt in range(NT):
            sl = tt % 2
            row0 = ts * S + tt * 128
            q.op("sync", (lambda row0, sl: lambda e: e.dma_start(
                out=xcb[sl][:], in_=x_h[row0:row0 + 128, :]))(row0, sl),
                waits=[w(s_dv, state["xc_rec"][sl])] +
                      (list(barrier) if (ts == 0 and tt < 2) else []),
                inc=s_xc[sl].inc(16))
            xn_ = s_xc[sl].n
            q.op("sync", (lambda row0, sl: lambda e: e.dma_start(
                out=scb[sl][:], in_=s_h[row0:row0 + 128, :]))(row0, sl),
                waits=[w(s_dv, state["sc_rec"][sl])], inc=s_sc[sl].inc(16))
            sn_ = s_sc[sl].n
            q.op("scalar", (lambda tt: lambda e: e.activation(
                gsb[:], acc[:, tt * D:(tt + 1) * D], AF.Silu))(tt),
                waits=[w(s_dv, state["acc_ready"][tt]), w(s_dv, state["g_free"])],
                inc=s_ac.inc())
            ac_g = s_ac.n
            gw = []
            if cfg.debug:
                q.op("sync", (lambda row0: lambda e: e.dma_start(
                    out=go_h[row0:row0 + 128, :], in_=gsb[:]))(row0),
                    waits=[w(s_ac, ac_g)], inc=s_dbg.inc(16))
                gw = [(s_dbg.h, s_dbg.n)]
            q.op("vector", (lambda sl: lambda e: e.tensor_mul(gsb[:], gsb[:], scb[sl][:]))(sl),
                 waits=[w(s_ac, ac_g), (s_sc[sl].h, sn_)] + gw, inc=s_dv.inc())
            dv_gm = s_dv.n
            q.op("vector", (lambda sl: lambda e: e.tensor_add(
                osb[sl][:], gsb[:], xcb[sl][:]))(sl),
                waits=[(s_xc[sl].h, xn_), w(s_dv, dv_gm),
                       (s_out[sl].h, state["osb_rec"][sl]) if state["osb_rec"][sl] else None],
                inc=s_dv.inc())
            dv_o = s_dv.n
            state["sc_rec"][sl] = dv_o
            state["xc_rec"][sl] = dv_o
            state["g_free"] = dv_o
            q.op("sync", (lambda row0, sl: lambda e: e.dma_start(
                out=o_h[row0:row0 + 128, :], in_=osb[sl][:]))(row0, sl),
                waits=[w(s_dv, dv_o)], inc=s_out[sl].inc(16))
            state["osb_rec"][sl] = s_out[sl].n
        state["c_done"] = s_dv.n

    FGW = FG * 128
    DH = min(512, D)
    NH = D // DH
    sp = dict(tb_dv=[0, 0], tb_gx=[(0, 0), (0, 0)], tb_osc=[(0, 0), (0, 0)],
              xgf_pe=[0, 0], xgT_pe=[0, 0], yT_pe=[0, 0],
              w1s_pe=[0, 0], w1a_cn=[0] * 64, w1b_cn=[0] * 64,
              w2_pe=[0, 0], conv_dv=[0, 0], gx_n=[0, 0],
              silu_dv=0, osb_osc=[0, 0], first_tbl=True, first_gx=True,
              first_w=True, w2chunk=0, w2_n={}, tbn=[0, 0], xgT_dv=[0, 0],
              yT_dv=[0, 0], b1_n=[0, 0], b1_pe=[0, 0])

    def emit_w1_chunk(ts, g, fg):
        wsl = fg % 2
        icol = widx_i[:, (ts * G + g) * NFG + fg:(ts * G + g) * NFG + fg + 1]
        wext = []
        if sp["first_w"]:
            wext = list(barrier)
            sp["first_w"] = False
        q.op("gpsimd", (lambda wsl, icol: lambda e: e.indirect_dma_start(
            out=w1a_b[wsl][:], out_offset=None, in_=w1a_sw,
            in_offset=bass.IndirectOffsetOnAxis(ap=icol, axis=0),
            bounds_check=BND["w1"], oob_is_err=False))(wsl, icol),
            waits=wext + [w(s_pe, sp["w1s_pe"][wsl])], inc=s_w1a[wsl].inc(16))
        sp["w1a_cn"][fg] = s_w1a[wsl].n
        q.op("gpsimd", (lambda wsl, icol: lambda e: e.indirect_dma_start(
            out=w1b_b[wsl][:], out_offset=None, in_=w1b_sw,
            in_offset=bass.IndirectOffsetOnAxis(ap=icol, axis=0),
            bounds_check=BND["w1"], oob_is_err=False))(wsl, icol),
            inc=s_w1b[wsl].inc(16))
        sp["w1b_cn"][fg] = s_w1b[wsl].n

    def emit_w2_chunk(ts, g, half, fg):
        c = sp["w2chunk"]
        sl2 = c % 2
        sp["w2chunk"] += 1
        col = (ts * G + g) * NH * NFG + half * NFG + fg
        icol = w2x_i[:, col:col + 1]
        q.op("gpsimd", (lambda sl2, icol: lambda e: e.indirect_dma_start(
            out=w2_b[sl2][:], out_offset=None, in_=w2_sw,
            in_offset=bass.IndirectOffsetOnAxis(ap=icol, axis=0),
            bounds_check=BND["w2"], oob_is_err=False))(sl2, icol),
            waits=[w(s_pe, sp["w2_pe"][sl2])], inc=s_w2[sl2].inc(16))
        sp["w2_n"][(ts, g, half, fg)] = (sl2, s_w2[sl2].n)

    def emit_sp_tblconv(ts, g):
        gsl = g % 2
        waits = [w(s_dv, sp["tb_dv"][gsl]),
                 (s_gx[0].h, sp["tb_gx"][gsl][0]) if sp["tb_gx"][gsl][0] else None,
                 (s_gx[1].h, sp["tb_gx"][gsl][1]) if sp["tb_gx"][gsl][1] else None,
                 (s_osc[0].h, sp["tb_osc"][gsl][0]) if sp["tb_osc"][gsl][0] else None,
                 (s_osc[1].h, sp["tb_osc"][gsl][1]) if sp["tb_osc"][gsl][1] else None,
                 (s_tbl.h, TBL_DONE[ts])]
        if sp["first_tbl"]:
            waits += list(barrier)
            sp["first_tbl"] = False
        flat = tbl_h[ts].rearrange("a b -> (a b)")
        q.op("sync", (lambda flat, g, gsl: lambda e: e.dma_start(
            out=tb_sb[gsl][:],
            in_=flat[g * GRP * 4:(g + 1) * GRP * 4].rearrange("(p j) -> p j", p=128)))(
                flat, g, gsl),
            waits=waits, inc=s_tb2s[gsl].inc(16))
        sp["tbn"][gsl] = s_tb2s[gsl].n
        tb4 = tb_sb[gsl][:].rearrange("p (q f) -> p q f", f=4)
        q.op("vector", (lambda gsl, tb4: lambda e: e.tensor_copy(
            gi32[gsl][:], tb4[:, :, 0:1].squeeze(2)))(gsl, tb4),
            waits=[(s_tb2s[gsl].h, sp["tbn"][gsl])], inc=s_dv.inc())
        dvx = s_dv.n
        q.op("vector", (lambda gsl, tb4: lambda e: e.tensor_copy(
            si32[gsl][:], tb4[:, :, 1:2].squeeze(2)))(gsl, tb4),
            waits=[w(s_dv, dvx)], inc=s_dv.inc())
        sp["conv_dv"][gsl] = s_dv.n

    def emit_gx(gsl, ti):
        xs = ti % 2
        waits = [w(s_dv, sp["conv_dv"][gsl]), w(s_pe, sp["xgf_pe"][xs])]
        if sp["first_gx"]:
            waits += list(barrier)
            sp["first_gx"] = False
        q.op("gpsimd", (lambda gsl, ti, xs: lambda e: e.indirect_dma_start(
            out=xgf[xs][:], out_offset=None,
            in_=xn2n_h,
            in_offset=bass.IndirectOffsetOnAxis(ap=gi32[gsl][:, ti:ti + 1], axis=0),
            bounds_check=BND["gath"], oob_is_err=False))(gsl, ti, xs),
            waits=waits, inc=s_gx[xs].inc(16))
        sp["gx_n"][xs] = s_gx[xs].n

    def emit_sp_loads(ts, g):
        gsl = g % 2
        for ti in range(min(NGT, 2)):
            emit_gx(gsl, ti)
        sp["tb_gx"][gsl] = (s_gx[0].n, s_gx[1].n)
        bcol_i = b1x_i[:, ts * G + g:ts * G + g + 1]
        q.op("gpsimd", (lambda gsl, bcol_i: lambda e: e.indirect_dma_start(
            out=b1ag[gsl][:], out_offset=None, in_=b1ag_h,
            in_offset=bass.IndirectOffsetOnAxis(ap=bcol_i, axis=0),
            bounds_check=BND["b1"], oob_is_err=False))(gsl, bcol_i),
            waits=[w(s_pe, sp["b1_pe"][gsl])], inc=s_wqkv.inc(16))
        q.op("gpsimd", (lambda gsl, bcol_i: lambda e: e.indirect_dma_start(
            out=b1bg[gsl][:], out_offset=None, in_=b1bg_h,
            in_offset=bass.IndirectOffsetOnAxis(ap=bcol_i, axis=0),
            bounds_check=BND["b1"], oob_is_err=False))(gsl, bcol_i),
            inc=s_wqkv.inc(16))
        sp["b1_n"][gsl] = s_wqkv.n
        emit_w1_chunk(ts, g, 0)
        if NFG > 1:
            emit_w1_chunk(ts, g, 1)

    def emit_sp_trans(ts, g):
        gsl = g % 2
        for ti in range(NGT):
            xs = ti % 2
            par = ti % 2
            regs = ("ba0", "bb0") if par == 0 else ("ba1", "bb1")
            pbase = par * 1024
            for k in range(KD):
                q.op("tensor", (lambda xs, k, pbase: lambda e: e.transpose(
                    ps_big[:, pbase + k * 128: pbase + (k + 1) * 128],
                    xgf[xs][:, k * 128:(k + 1) * 128], ident_sb[:]))(xs, k, pbase),
                    waits=([(s_gx[xs].h, sp["gx_n"][xs])] + pwait(*regs) + [wcw])
                    if k == 0 else [],
                    inc=s_pe.inc() if k == KD - 1 else None)
            pe_t = s_pe.n
            sp["xgf_pe"][xs] = pe_t
            if ti + 2 < NGT:
                emit_gx(gsl, ti + 2)
            q.op("vector", (lambda gsl, ti, pbase: lambda e: e.tensor_copy(
                xgT[gsl][:].rearrange("p (k c) -> p k c", k=KD)[
                    :, :, ti * 128:(ti + 1) * 128],
                ps_big[:, pbase:pbase + KD * 128].rearrange(
                    "p (k c) -> p k c", k=KD)))(gsl, ti, pbase),
                waits=[w(s_pe, pe_t)] +
                      ([w(s_pe, sp["xgT_pe"][gsl])] if ti == 0 else []),
                inc=s_dv.inc())
            for r in regs:
                pread(r, "dv", s_dv.n)
        sp["xgT_dv"][gsl] = s_dv.n
        sp["tb_gx"][gsl] = (s_gx[0].n, s_gx[1].n)

    def emit_sp_fl(ts, g):
        gsl = g % 2
        ys = g % 2
        for fg in range(NFG):
            wsl = fg % 2
            w1a_n, w1b_n = sp["w1a_cn"][fg], sp["w1b_cn"][fg]
            for fl in range(FG):
                par = fl % 2
                ra, rb = "ba%d" % par, "bb%d" % par
                pa = ps_big[:, par * 1024: par * 1024 + GRP]
                pb = ps_big[:, par * 1024 + 512: par * 1024 + 512 + GRP]
                for k in range(KD):
                    waits = []
                    if k == 0:
                        waits = pwait(ra) + [(s_w1a[wsl].h, w1a_n)]
                        if fg == 0 and fl == 0:
                            waits += [w(s_dv, sp["xgT_dv"][gsl])]
                    q.op("tensor", (lambda fl, k, gsl, wsl, pa: lambda e: e.matmul(
                        pa, w1a_b[wsl][:, k * FGW + fl * 128: k * FGW + (fl + 1) * 128],
                        xgT[gsl][:, k * GRP:(k + 1) * GRP],
                        start=(k == 0), stop=(k == KD - 1)))(fl, k, gsl, wsl, pa),
                        waits=waits, inc=s_pe.inc() if k == KD - 1 else None)
                pe_a = s_pe.n
                for k in range(KD):
                    waits = []
                    if k == 0:
                        waits = pwait(rb) + [(s_w1b[wsl].h, w1b_n)]
                    q.op("tensor", (lambda fl, k, gsl, wsl, pb: lambda e: e.matmul(
                        pb, w1b_b[wsl][:, k * FGW + fl * 128: k * FGW + (fl + 1) * 128],
                        xgT[gsl][:, k * GRP:(k + 1) * GRP],
                        start=(k == 0), stop=(k == KD - 1)))(fl, k, gsl, wsl, pb),
                        waits=waits, inc=s_pe.inc() if k == KD - 1 else None)
                pe_b = s_pe.n
                bcol = fg * FG + fl
                q.op("scalar", (lambda gsl, bcol, pa: lambda e: e.activation(
                    silu_sb[:], pa, AF.Silu, bias=b1ag[gsl][:, bcol:bcol + 1]))(
                        gsl, bcol, pa),
                    waits=[w(s_pe, pe_a), w(s_dv, sp["silu_dv"]),
                           (s_wqkv.h, sp["b1_n"][gsl])], inc=s_ac.inc())
                pread(ra, "ac", s_ac.n)
                ac_s = s_ac.n
                q.op("vector", (lambda gsl, bcol, pb, ys: lambda e: e.scalar_tensor_tensor(
                    yT[ys][:, bcol * GRP:(bcol + 1) * GRP], pb,
                    b1bg[gsl][:, bcol:bcol + 1], silu_sb[:], ALU.add, ALU.mult))(
                        gsl, bcol, pb, ys),
                    waits=[w(s_pe, pe_b), w(s_ac, ac_s)] +
                          ([w(s_pe, sp["yT_pe"][ys])] if (fg == 0 and fl == 0) else []),
                    inc=s_dv.inc())
                pread(rb, "dv", s_dv.n)
                sp["silu_dv"] = s_dv.n
            sp["w1s_pe"][wsl] = s_pe.n
            if fg + 2 < NFG:
                emit_w1_chunk(ts, g, fg + 2)
        sp["yT_dv"][ys] = s_dv.n
        sp["xgT_pe"][gsl] = s_pe.n
        sp["b1_pe"][gsl] = s_pe.n

    def emit_sp_o(ts, g):
        gsl = g % 2
        ys = g % 2
        PO = (ps_o0, ps_o0, ps_o1, ps_o1)
        RG = ("o0h0", "o0h1", "o1h0", "o1h1")
        emit_w2_chunk(ts, g, 0, 0)
        if NFG > 1:
            emit_w2_chunk(ts, g, 0, 1)
        elif NH > 1:
            emit_w2_chunk(ts, g, 1, 0)
        ck = 0
        for half in range(NH):
            for fg in range(NFG):
                sl2, w2n = sp["w2_n"][(ts, g, half, fg)]
                for ti in range(NGT):
                    po = PO[ti][:, (ti % 2) * 512:(ti % 2) * 512 + DH]
                    for fl in range(FG):
                        waits = []
                        if fl == 0 and ti == 0:
                            waits = [(s_w2[sl2].h, w2n)]
                            if fg == 0:
                                waits = waits + [w(s_dv, sp["yT_dv"][ys])]
                        if fg == 0 and fl == 0:
                            waits = waits + pwait(RG[ti])
                        fidx = fg * FG + fl
                        q.op("tensor", (lambda po, fidx, ti, fl, ys, sl2: lambda e: e.matmul(
                            po, yT[ys][:, fidx * GRP + ti * 128: fidx * GRP + (ti + 1) * 128],
                            w2_b[sl2][:, fl * DH:(fl + 1) * DH],
                            start=(fidx == 0), stop=(fidx == NFL - 1)))(
                                po, fidx, ti, fl, ys, sl2),
                            waits=waits,
                            inc=s_pe.inc() if (ti == NGT - 1 and fl == FG - 1) else None)
                sp["w2_pe"][sl2] = s_pe.n
                ck += 1
                nxt = ck + 1
                if nxt < NH * NFG:
                    emit_w2_chunk(ts, g, nxt // NFG, nxt % NFG)
            pe_o = s_pe.n
            for ti in range(NGT):
                po = PO[ti][:, (ti % 2) * 512:(ti % 2) * 512 + DH]
                osl = ti % 2
                waits = [w(s_pe, pe_o)]
                if sp["osb_osc"][osl]:
                    waits.append((s_osc[osl].h, sp["osb_osc"][osl]))
                if state["osb_rec"][osl]:
                    waits.append((s_out[osl].h, state["osb_rec"][osl]))
                q.op("vector", (lambda po, gsl, ti, osl, half: lambda e: e.tensor_scalar_mul(
                    osb[osl][:, half * DH:(half + 1) * DH], po,
                    tb_sb[gsl][:, ti * 4 + 2:ti * 4 + 3]))(po, gsl, ti, osl, half),
                    waits=waits, inc=s_dv.inc())
                pread(RG[ti], "dv", s_dv.n)
                dv_e = s_dv.n
                q.op("gpsimd", (lambda gsl, ti, osl, half: lambda e: e.indirect_dma_start(
                    out=obuf_hh[half],
                    out_offset=bass.IndirectOffsetOnAxis(
                        ap=si32[gsl][:, ti:ti + 1], axis=0),
                    in_=osb[osl][:, half * DH:(half + 1) * DH], in_offset=None,
                    bounds_check=BND["scat"], oob_is_err=False))(gsl, ti, osl, half),
                    waits=[w(s_dv, dv_e)], inc=s_osc[osl].inc(16))
                sp["osb_osc"][osl] = s_osc[osl].n
        sp["tb_dv"][gsl] = s_dv.n
        sp["tb_osc"][gsl] = (s_osc[0].n, s_osc[1].n)
        sp["yT_pe"][ys] = s_pe.n

    def emit_phase_c_sp(ts):
        osc_tot = (s_osc[0].n, s_osc[1].n)
        for tt in range(NT):
            sl = tt % 2
            row0 = ts * S + tt * 128
            q.op("sync", (lambda row0, sl: lambda e: e.dma_start(
                out=xcb[sl][:], in_=x_h[row0:row0 + 128, :]))(row0, sl),
                waits=[w(s_dv, state["xc_rec"][sl])], inc=s_xc[sl].inc(16))
            xn_ = s_xc[sl].n
            q.op("sync", (lambda row0, sl: lambda e: e.dma_start(
                out=scb[sl][:], in_=s_h[row0:row0 + 128, :]))(row0, sl),
                waits=[w(s_dv, state["sc_rec"][sl])], inc=s_sc[sl].inc(16))
            sn_ = s_sc[sl].n
            for h in range(NH):
                q.op("sync", (lambda row0, h: lambda e: e.dma_start(
                    out=o1b[:, h * DH:(h + 1) * DH],
                    in_=obuf_hh[h][row0:row0 + 128, :]))(row0, h),
                    waits=([w(s_dv, state["o1_rec"]),
                            (s_osc[0].h, osc_tot[0]) if tt < 2 else None,
                            (s_osc[1].h, osc_tot[1]) if tt < 2 else None]
                           if h == 0 else []),
                    inc=s_xc[sl].inc(16))
                q.op("sync", (lambda row0, h: lambda e: e.dma_start(
                    out=o2b[:, h * DH:(h + 1) * DH],
                    in_=obuf_hh[h][NBS + row0:NBS + row0 + 128, :]))(row0, h),
                    inc=s_xc[sl].inc(16))
            on_ = s_xc[sl].n
            po = ps_o0 if tt % 2 == 0 else ps_o1
            regs = ("o0h0", "o0h1") if tt % 2 == 0 else ("o1h0", "o1h1")
            nh = (D + 511) // 512
            for hi, h in enumerate(range(0, D, 512)):
                hw = min(512, D - h)
                waits = []
                if hi == 0:
                    waits = pwait(*regs) + [w(s_dv, we_done[ts]), wcw]
                q.op("tensor", (lambda ts, tt, h, hw, po: lambda e: e.matmul(
                    po[:, h:h + hw],
                    weT_all[0:E, ts * S + tt * 128: ts * S + (tt + 1) * 128],
                    b2_sb[0:E, h:h + hw],
                    start=True, stop=True))(ts, tt, h, hw, po),
                    waits=waits, inc=s_pe.inc() if hi == nh - 1 else None)
            pe_b2 = s_pe.n
            q.op("vector", (lambda sl: lambda e: e.tensor_add(
                gsb[:], o1b[:], o2b[:]))(sl),
                waits=[(s_xc[sl].h, on_), w(s_dv, state["g_free"])], inc=s_dv.inc())
            dvx = s_dv.n
            state["o1_rec"] = dvx
            q.op("vector", (lambda po: lambda e: e.tensor_add(
                gsb[:], gsb[:], po[:, 0:D]))(po),
                waits=[w(s_dv, dvx), w(s_pe, pe_b2)], inc=s_dv.inc())
            dvx = s_dv.n
            for r in regs:
                pread(r, "dv", dvx)
            q.op("scalar", lambda e: e.activation(gsb[:], gsb[:], AF.Silu),
                 waits=[w(s_dv, dvx)], inc=s_ac.inc())
            ac_g = s_ac.n
            q.op("vector", (lambda sl: lambda e: e.tensor_mul(gsb[:], gsb[:], scb[sl][:]))(sl),
                 waits=[w(s_ac, ac_g), (s_sc[sl].h, sn_)], inc=s_dv.inc())
            dv_gm = s_dv.n
            q.op("vector", (lambda sl: lambda e: e.tensor_add(
                osb[sl][:], gsb[:], xcb[sl][:]))(sl),
                waits=[(s_xc[sl].h, on_), w(s_dv, dv_gm),
                       (s_out[sl].h, state["osb_rec"][sl]) if state["osb_rec"][sl] else None,
                       (s_osc[sl].h, sp["osb_osc"][sl]) if sp["osb_osc"][sl] else None],
                inc=s_dv.inc())
            dv_o = s_dv.n
            sp["osb_osc"][sl] = 0
            state["sc_rec"][sl] = dv_o
            state["xc_rec"][sl] = dv_o
            state["g_free"] = dv_o
            q.op("sync", (lambda row0, sl: lambda e: e.dma_start(
                out=o_h[row0:row0 + 128, :], in_=osb[sl][:]))(row0, sl),
                waits=[w(s_dv, dv_o)], inc=s_out[sl].inc(16))
            state["osb_rec"][sl] = s_out[sl].n
        state["c_done"] = s_dv.n

    for ts in range(NB if not cfg.only_a else 0):
        if cfg.sparse:
            if ts == 0:
                q.op("gpsimd", _setup_bounds)
            NGL = G if cfg.ng_limit is None else min(G, cfg.ng_limit)
            for g in range(NGL):
                if g == 0:
                    emit_sp_tblconv(ts, 0)
                emit_sp_loads(ts, g)
                emit_sp_trans(ts, g)
                emit_sp_fl(ts, g)
                if g > 0:
                    emit_sp_o(ts, g - 1)
                if g + 1 < NGL:
                    emit_sp_tblconv(ts, g + 1)
            emit_sp_o(ts, NGL - 1)
            if not cfg.skip_c:
                emit_phase_c_sp(ts)
            continue
        if ts == 0:
            waits0 = list(barrier)
        else:
            waits0 = [w(s_pe, state["fl_last_pe"])]
        for k in range(KD):
            q.op("sync", (lambda ts, k: lambda e: e.dma_start(
                out=xn2B[:, k * S:(k + 1) * S],
                in_=xn2s_h[k, :, ts * S:(ts + 1) * S]))(ts, k),
                waits=waits0 if k == 0 else [], inc=s_xn2r.inc(16))
        xn2_n = s_xn2r.n

        for tt in range(NT):
            po = ps_o0 if tt % 2 == 0 else ps_o1
            regs = ("o0h0", "o0h1") if tt % 2 == 0 else ("o1h0", "o1h1")
            nh = (D + 511) // 512
            for hi, h in enumerate(range(0, D, 512)):
                hw = min(512, D - h)
                waits = []
                if hi == 0:
                    waits = pwait(*regs) + [w(s_dv, we_done[ts]), wcw]
                    if ts == 0 and tt == 0:
                        waits = waits + list(barrier)
                q.op("tensor", (lambda ts, tt, h, hw, po: lambda e: e.matmul(
                    po[:, h:h + hw],
                    weT_all[0:E, ts * S + tt * 128: ts * S + (tt + 1) * 128],
                    b2_sb[0:E, h:h + hw],
                    start=True, stop=True))(ts, tt, h, hw, po),
                    waits=waits, inc=s_pe.inc() if hi == nh - 1 else None)
            pe_b2 = s_pe.n
            q.op("vector", (lambda tt, po: lambda e: e.tensor_copy(
                acc[:, tt * D:(tt + 1) * D], po[:, 0:D]))(tt, po),
                waits=[w(s_pe, pe_b2), w(s_dv, state["c_done"])], inc=s_dv.inc())
            for r in regs:
                pread(r, "dv", s_dv.n)
            state["acc_ready"][tt] = s_dv.n

        pending = None
        for e_i in range(E):
            for fg in range(NFG):
                wslot, w1a_n, w1b_n, w2_n = emit_w_loads(ts, e_i, fg)
                for tb in range(NTB):
                    unit = (e_i * NFG + fg) * NTB + tb
                    emit_fl(ts, e_i, fg, tb, wslot, w1a_n, w1b_n, xn2_n, unit)
                    if pending is not None:
                        emit_o(*pending)
                    pending = (ts, e_i, fg, tb, wslot, w2_n, unit)
        emit_o(*pending)

        emit_phase_c(ts)

    if cfg.debug:
        q.op("sync", lambda e: e.dma_start(out=weo_h[:], in_=we_all[:]),
             waits=[(s_dv.h, s_dv.n)], inc=s_dbg.inc(16))
        q.op("sync", lambda e: e.wait_ge(s_dbg.h, s_dbg.n))

    if not cfg.only_a:
        q.op("sync", lambda e: e.wait_ge(s_out[0].h, s_out[0].n),
             waits=[(s_out[1].h, s_out[1].n)])
    else:
        q.op("sync", lambda e: e.wait_ge(s_spx.h, s_spx.n),
             waits=[(s_se[0].h, s_se[0].n), (s_se[1].h, s_se[1].n),
                    (s_dv.h, s_dv.n), (s_ac.h, s_ac.n), (s_pe.h, s_pe.n)] +
                   ([(s_dbg.h, s_dbg.n)] if cfg.debug else []))

    q.run_block(nc)
    stB.close()
    st.close()
    return nc


_CFG = Cfg()
_NC_CACHE = {}


def _get_nc(cfg):
    key = id(cfg)
    if key not in _NC_CACHE:
        _NC_CACHE[key] = build_program(cfg)
    return _NC_CACHE[key]


def make_in_maps(cfg, n_cores, x, wq, wk, wvd, wvu, lnw, wg, bg, w1a, b1a, w1b, b1b, w2, b2):
    B, S, D = x.shape
    per = B // n_cores
    ident = np.eye(128, dtype=np.float32)
    amask = np.triu(np.ones((128, 128), dtype=np.uint8), 1)
    f32 = lambda a: np.ascontiguousarray(np.asarray(a, dtype=np.float32))

    def btr(bv):
        bv = f32(bv)
        E_, F_ = bv.shape
        return np.ascontiguousarray(
            bv.reshape(E_, F_ // 128, 128).transpose(2, 0, 1).reshape(128, E_ * (F_ // 128)))

    def btg(bv):
        bv = f32(bv)
        E_, F_ = bv.shape
        return np.ascontiguousarray(
            bv.reshape(E_, F_ // 128, 128).transpose(0, 2, 1).reshape(E_ * 128, F_ // 128))

    shared = dict(wq=f32(wq), wk=f32(wk), wvd=f32(wvd), wvu=f32(wvu), lnw=f32(lnw),
                  wg=f32(wg), bg=f32(bg), w1a=f32(w1a), b1a=btr(b1a), w1b=f32(w1b),
                  b1b=btr(b1b), w2=f32(w2), b2=f32(b2), ident=ident, amask=amask)
    if cfg.sparse:
        shared["b1agh"] = btg(b1a)
        shared["b1bgh"] = btg(b1b)
    in_maps = []
    for c in range(n_cores):
        m = dict(shared)
        m["x"] = np.ascontiguousarray(x[c * per:(c + 1) * per].reshape(per * S, D))
        in_maps.append(m)
    return in_maps


def kernel(L0_attn_inputs, L0_attn_Wq, L0_attn_Wk, L0_attn_Wvdown, L0_attn_Wvup,
           L0_ffn_ln_w, L0_ffn_Wg, L0_ffn_bg, L0_ffn_W1a, L0_ffn_b1a,
           L0_ffn_W1b, L0_ffn_b1b, L0_ffn_W2, L0_ffn_b2):
    cfg = _CFG
    n_cores = 8
    x = np.ascontiguousarray(np.asarray(L0_attn_inputs, dtype=np.float32))
    B, S, D = x.shape
    per = B // n_cores
    nc = _get_nc(cfg)
    in_maps = make_in_maps(cfg, n_cores, x, L0_attn_Wq, L0_attn_Wk, L0_attn_Wvdown,
                           L0_attn_Wvup, L0_ffn_ln_w, L0_ffn_Wg, L0_ffn_bg,
                           L0_ffn_W1a, L0_ffn_b1a, L0_ffn_W1b, L0_ffn_b1b,
                           L0_ffn_W2, L0_ffn_b2)
    res = run_bass_kernel_spmd(nc, in_maps, core_ids=list(range(n_cores)))
    out = np.concatenate([np.asarray(r["o"]).reshape(per, S, D) for r in res.results], axis=0)
    return out.astype(np.float32)


# version 05
# speedup vs baseline: 1.3537x; 1.0462x over previous
import numpy as np
from contextlib import ExitStack

import concourse.bass as bass
import concourse.mybir as mybir
from concourse.bass_utils import run_bass_kernel_spmd

F32 = mybir.dt.float32
BF16 = mybir.dt.bfloat16
U8 = mybir.dt.uint8
AF = mybir.ActivationFunctionType
ALU = mybir.AluOpType
AX = mybir.AxisListType

ENGS = ("tensor", "vector", "scalar", "gpsimd", "sync")


class Cfg:
    def __init__(self, NB=4, S=2048, D=1024, DQ=256, E=8, F=4096, FG=4, TBW=512,
                 GRP=256):
        self.NB, self.S, self.D, self.DQ, self.E, self.F = NB, S, D, DQ, E, F
        self.KD = D // 128
        self.KQ = DQ // 128
        self.NT = S // 128
        self.NFL = F // 128
        self.FG = FG
        self.NFG = self.NFL // FG
        self.TBW = TBW
        self.NTB = S // TBW
        self.TPB = TBW // 128
        self.eps_ln = 1e-5
        self.eps_rms = float(np.finfo(np.float32).eps)
        self.scale = 1.0 / float(np.sqrt(DQ))
        self.debug = False
        self.only_a = False
        self.only_ln = False
        self.sparse = True
        self.ng_limit = None
        self.skip_c = False
        self.GRP = GRP
        self.NGT = self.GRP // 128
        self.G = E + (2 * S) // self.GRP
        self.LG = int(np.log2(self.GRP))


class Sem:
    def __init__(self, h):
        self.h = h
        self.n = 0

    def inc(self, amt=1):
        self.n += amt
        return (self.h, amt)


class Q:
    def __init__(self):
        self.ops = {e: [] for e in ENGS}

    def op(self, eng, fn, waits=(), inc=None):
        self.ops[eng].append((fn, [x for x in waits if x is not None], inc))

    def run_block(self, nc):
        with nc.Block() as block:
            for eng in ENGS:
                ops = self.ops[eng]
                if not ops:
                    continue

                def make(ops):
                    def prog(e):
                        for fn, waits, inc in ops:
                            for (sem, val) in waits:
                                e.wait_ge(sem, val)
                            ins = fn(e)
                            if inc is not None:
                                ins.then_inc(inc[0], inc[1])
                    return prog

                getattr(block, eng)(make(ops))
        self.ops = {e: [] for e in ENGS}


def build_program(cfg: Cfg):
    nc = bass.Bass("TRN2", target_bir_lowering=False)
    NB, S, D, DQ, E, F = cfg.NB, cfg.S, cfg.D, cfg.DQ, cfg.E, cfg.F
    KD, KQ, NT, NFL, FG, NFG = cfg.KD, cfg.KQ, cfg.NT, cfg.NFL, cfg.FG, cfg.NFG
    TBW, NTB, TPB = cfg.TBW, cfg.NTB, cfg.TPB

    x_h = nc.dram_tensor("x", [NB * S, D], F32, kind="ExternalInput").ap()
    wq_h = nc.dram_tensor("wq", [D, DQ], F32, kind="ExternalInput").ap()
    wk_h = nc.dram_tensor("wk", [D, DQ], F32, kind="ExternalInput").ap()
    wvd_h = nc.dram_tensor("wvd", [D, DQ], F32, kind="ExternalInput").ap()
    wvu_h = nc.dram_tensor("wvu", [DQ, D], F32, kind="ExternalInput").ap()
    lnw_h = nc.dram_tensor("lnw", [D], F32, kind="ExternalInput").ap()
    wg_h = nc.dram_tensor("wg", [D, E], F32, kind="ExternalInput").ap()
    bg_h = nc.dram_tensor("bg", [E], F32, kind="ExternalInput").ap()
    w1a_h = nc.dram_tensor("w1a", [E, D, F], F32, kind="ExternalInput").ap()
    b1a_h = nc.dram_tensor("b1a", [128, E * NFL], F32, kind="ExternalInput").ap()
    w1b_h = nc.dram_tensor("w1b", [E, D, F], F32, kind="ExternalInput").ap()
    b1b_h = nc.dram_tensor("b1b", [128, E * NFL], F32, kind="ExternalInput").ap()
    w2_h = nc.dram_tensor("w2", [E, F, D], F32, kind="ExternalInput").ap()
    b2_h = nc.dram_tensor("b2", [E, D], F32, kind="ExternalInput").ap()
    ident_h = nc.dram_tensor("ident", [128, 128], F32, kind="ExternalInput").ap()
    amask_h = nc.dram_tensor("amask", [128, 128], U8, kind="ExternalInput").ap()
    o_h = nc.dram_tensor("o", [NB * S, D], F32, kind="ExternalOutput").ap()
    dbgk = "ExternalOutput" if cfg.debug else "Internal"
    xn2s_h = nc.dram_tensor("xn2s", [KD, 128, NB * S], BF16, kind=dbgk).ap()
    s_h = nc.dram_tensor("sscr", [NB * S, D], F32, kind=dbgk).ap()
    GRP, NGT, G = cfg.GRP, cfg.NGT, cfg.G
    NBS = NB * S
    if cfg.sparse:
        xn2n_h = nc.dram_tensor("xn2n", [NBS, D], F32, kind="Internal").ap()
        rt_h = nc.dram_tensor("rt", [NB, 4 * NT, 128], F32, kind="Internal").ap()
        tbl_h = [nc.dram_tensor("tbl%d" % i, [G * GRP, 4], F32, kind="Internal").ap()
                 for i in range(NB)]
        FGWc = FG * 128
        DHc = min(512, D)
        NHc = D // DHc
        obuf_hh = [nc.dram_tensor("obuf%d" % h, [2 * NBS, DHc], F32,
                                  kind="Internal").ap() for h in range(NHc)]
        w1a_sw = nc.dram_tensor("w1asw", [E * NFG * 128, KD * FGWc], BF16,
                                kind="Internal").ap()
        w1b_sw = nc.dram_tensor("w1bsw", [E * NFG * 128, KD * FGWc], BF16,
                                kind="Internal").ap()
        w2_sw = nc.dram_tensor("w2sw", [E * NHc * NFG * 128, FG * DHc], BF16,
                               kind="Internal").ap()
        b1ag_h = nc.dram_tensor("b1agh", [E * 128, NFL], F32, kind="ExternalInput").ap()
        b1bg_h = nc.dram_tensor("b1bgh", [E * 128, NFL], F32, kind="ExternalInput").ap()
    if cfg.debug:
        weo_h = nc.dram_tensor("weo", [128, NB * NT * E], F32, kind="ExternalOutput").ap()
        go_h = nc.dram_tensor("go", [NB * S, D], F32, kind="ExternalOutput").ap()
        ao_h = nc.dram_tensor("ao", [NB * S, S], F32, kind="ExternalOutput").ap()
        xnso_h = nc.dram_tensor("xnso", [KD, 128, NB * S], BF16, kind="ExternalOutput").ap()
        colso_h = nc.dram_tensor("colso", [NB * NT, 128, 8], F32, kind="ExternalOutput").ap()
        lgo_h = nc.dram_tensor("lgo", [NB, 128, NT * E], F32, kind="ExternalOutput").ap()
        vno_h = nc.dram_tensor("vno", [NB, 128, S // 128 * D], BF16, kind="ExternalOutput").ap()
        qo_h = nc.dram_tensor("qo", [NB, 128, (DQ // 128) * S], BF16, kind="ExternalOutput").ap()
        ko_h = nc.dram_tensor("ko", [NB, 128, (DQ // 128) * S], BF16, kind="ExternalOutput").ap()

    st = ExitStack()

    def sbuf(name, shape, dt):
        return st.enter_context(nc.sbuf_tensor(name, shape, dt))

    def psum(name, shape, dt=F32):
        return st.enter_context(nc.psum_tensor(name, shape, dt))

    def sem(name):
        return Sem(st.enter_context(nc.semaphore(name)))

    we_all = sbuf("we_all", [128, NB * NT * E], F32)
    weT_all = sbuf("weT_all", [E, NB * S], BF16)
    if not cfg.sparse:
        b1a_sb = sbuf("b1a_sb", [128, E * NFL], F32)
        b1b_sb = sbuf("b1b_sb", [128, E * NFL], F32)
    b2_sb = sbuf("b2_sb", [E, D], BF16)
    ident_sb = sbuf("ident_sb", [128, 128], F32)
    amask_sb = sbuf("amask_sb", [128, 128], U8)
    neg_sb = sbuf("neg_sb", [128, 128], F32)
    cols = sbuf("colsA", [128, 32], F32)
    epsln_c = sbuf("epsln_c", [128, 1], F32)
    epsrms_c = sbuf("epsrms_c", [128, 1], F32)
    if cfg.sparse:
        I32 = mybir.dt.int32
        SEG = 2 * S // 128
        sent_sb = sbuf("sent_sb", [128, 256], F32)
        pcv = [sbuf("pcv0", [128, 2048], BF16), sbuf("pcv1", [128, 2048], BF16)]
        egrp_i = sbuf("egrp_i", [1, NB * G], I32)
        e_iota = sbuf("e_iota", [128, E], F32)
        e12g = sbuf("e12g", [128, SEG], F32)
        w12g = sbuf("w12g", [128, SEG], F32)
        mskg = sbuf("mskg", [128, SEG], F32)
        scng = sbuf("scng", [128, SEG], F32)
        zrog = sbuf("zrog", [128, SEG], F32)
        slag = sbuf("slag", [128, SEG], F32)
        tmpg = sbuf("tmpg", [128, SEG], F32)
        ubase = sbuf("ubase", [128, SEG], F32)
        rankc = sbuf("rankc", [128, 1], F32)
        rankcS = sbuf("rankcS", [128, 1], F32)
        rankcN = sbuf("rankcN", [128, 1], F32)
        pcol = sbuf("pcol", [128, 1], F32)
        ebase = sbuf("ebase", [128, G], F32)
        eb2 = sbuf("eb2", [128, G], F32)
        widx_i = sbuf("widx_i", [128, NB * G * NFG], I32)
        w2x_i = sbuf("w2x_i", [128, NB * G * (D // min(512, D)) * NFG], I32)
        b1x_i = sbuf("b1x_i", [128, NB * G], I32)
        basec = sbuf("basec", [128, E], F32)
        etmp = sbuf("etmp", [1, G], F32)
        ones_r = sbuf("ones_r", [1, 128], F32)
        segoff = sbuf("segoff", [128, 1], F32)
        rowa = sbuf("rowa", [1, 128], F32)
        rowb = sbuf("rowb", [1, 128], F32)
        rzro = sbuf("rzro", [1, 128], F32)
        crow = sbuf("crow", [1, 64], F32)
        cri = sbuf("cri", [1, 16], I32)
        grow = sbuf("grow", [1, G], F32)
        egf = sbuf("egf", [1, G], F32)
        sloti = sbuf("sloti", [128, SEG], I32)
        pdg = sbuf("pdg", [128, SEG * 4], F32)

    ps_big = psum("ps_big", [128, 2048])
    ps_o0 = psum("ps_o0", [128, 1024])
    ps_o1 = psum("ps_o1", [128, 1024])

    s_pe = sem("s_pe")
    s_dv = sem("s_dv")
    s_ac = sem("s_ac")
    s_wc = sem("s_wc")
    s_x = [sem("s_x0"), sem("s_x1")]
    s_se = [sem("s_se0"), sem("s_se1")]
    s_spx = sem("s_spx")
    s_w1a = [sem("s_w1a0"), sem("s_w1a1")]
    s_w1b = [sem("s_w1b0"), sem("s_w1b1")]
    s_w2 = [sem("s_w20"), sem("s_w21")]
    s_xn2r = sem("s_xn2r")
    s_tb2s = [sem("s_tb2a"), sem("s_tb2b")]
    s_xc = [sem("s_xc0"), sem("s_xc1")]
    s_sc = [sem("s_sc0"), sem("s_sc1")]
    s_out = [sem("s_out0"), sem("s_out1")]
    s_dbg = sem("s_dbg")
    s_wqkv = sem("s_wqkv")
    s_ms = sem("s_ms")
    s_spn = [sem("s_spn0"), sem("s_spn1")]
    s_tbl = sem("s_tbl")
    s_pre = sem("s_pre")
    s_rt = sem("s_rt")
    s_gx = [sem("s_gx0"), sem("s_gx1")]
    s_tb2 = sem("s_tb2")
    s_osc = [sem("s_osc0"), sem("s_osc1")]
    s_fill = sem("s_fill")
    s_pri = [sem("s_pri0"), sem("s_pri1")]
    s_pro = [sem("s_pro0"), sem("s_pro1")]

    q = Q()
    _regc = [0]

    def _rname(p):
        _regc[0] += 1
        return "%s%d" % (p, _regc[0])

    def w(s, val):
        return (s.h, val) if val > 0 else None

    psg = {r: {"dv": 0, "ac": 0} for r in
           ("bigA", "ba0", "bb0", "ba1", "bb1", "o0h0", "o0h1", "o1h0", "o1h1")}

    def pwait(*regions):
        rs = set(regions)
        quarters = {"ba0", "bb0", "ba1", "bb1"}
        if "bigA" in rs:
            rs |= quarters
        elif rs & quarters:
            rs.add("bigA")
        dv = max(psg[r]["dv"] for r in rs)
        ac = max(psg[r]["ac"] for r in rs)
        return [w(s_dv, dv), w(s_ac, ac)]

    def pread(region, kind, cnt):
        psg[region][kind] = max(psg[region][kind], cnt)

    stA = ExitStack()

    def sbufA(name, shape, dt):
        return stA.enter_context(nc.sbuf_tensor(name, shape, dt))

    assert KD * DQ == KQ * D
    wqkv_sb = sbufA("wqkv_sb", [128, KD * DQ], BF16)
    wg_sb = sbufA("wg_sb", [128, KD * E], BF16)
    lnw_bc = sbufA("lnw_bc", [128, D], F32)
    bg_bc = sbufA("bg_bc", [128, E], F32)
    xin = [sbufA("xin0", [128, D], F32), sbufA("xin1", [128, D], F32)]
    sqs = sbufA("sqs", [128, D], BF16)
    xnf = [sbufA("xnf0", [128, D], F32), sbufA("xnf1", [128, D], F32)]
    xn2f = [sbufA("xn2f0", [128, D], F32), sbufA("xn2f1", [128, D], F32)]
    xnT = sbufA("xnT", [128, KD * S], BF16)
    SW2 = 256 if cfg.sparse else S
    xn2T = sbufA("xn2T_A", [128, KD * SW2], BF16)
    logits = sbufA("logits", [128, NT * E], F32)
    glt = sbufA("glt", [128, 4 * NT], F32)
    geq = sbufA("geq", [128, 2 * NT * E], F32)
    qT_sb = sbufA("qT", [128, KQ * S], BF16)
    kT_sb = sbufA("kT", [128, KQ * S], BF16)
    vdT_sb = sbufA("vdT", [128, KQ * S], BF16)
    vnat = sbufA("vnat", [128, NT * D], BF16)
    expb = sbufA("expb", [128, S], F32)
    attnT = sbufA("attnT", [128, S], BF16)
    sev = [sbufA("sev0", [128, D], F32), sbufA("sev1", [128, D], F32)]
    if cfg.sparse:
        gsc = sbufA("gsc", [128, NT * E], F32)
        gidx = sbufA("gidx", [128, 2 * NT], F32)
        rtn = sbufA("rtn", [128, 4 * NT], F32)
        rtv = sbufA("rtv", [4 * NT, 128], F32)

    def ld3(dst, src, kk):
        return lambda g: g.dma_start(
            out=dst[:].rearrange("p (k m) -> p k m", k=kk),
            in_=src.rearrange("(k p) m -> p k m", p=128))

    q.op("gpsimd", ld3(wg_sb, wg_h, KD), inc=s_wc.inc(16))
    q.op("gpsimd", lambda g: g.dma_start(out=lnw_bc[:], in_=lnw_h[None, :].broadcast_to([128, D])),
         inc=s_wc.inc(16))
    q.op("gpsimd", lambda g: g.dma_start(out=bg_bc[:], in_=bg_h[None, :].broadcast_to([128, E])),
         inc=s_wc.inc(16))
    if not cfg.sparse:
        q.op("gpsimd", lambda g: g.dma_start(out=b1a_sb[:], in_=b1a_h[:]), inc=s_wc.inc(16))
        q.op("gpsimd", lambda g: g.dma_start(out=b1b_sb[:], in_=b1b_h[:]), inc=s_wc.inc(16))
    q.op("gpsimd", lambda g: g.dma_start(out=b2_sb[:], in_=b2_h[:]), inc=s_wc.inc(16))
    q.op("gpsimd", lambda g: g.dma_start(out=ident_sb[:], in_=ident_h[:]), inc=s_wc.inc(16))
    q.op("gpsimd", lambda g: g.dma_start(out=amask_sb[:], in_=amask_h[:]), inc=s_wc.inc(16))
    q.op("gpsimd", lambda g: g.memset(neg_sb[:], -1e30), inc=s_ms.inc(1))
    q.op("gpsimd", lambda g: g.memset(epsln_c[:], float(cfg.eps_ln)), inc=s_ms.inc(1))
    q.op("gpsimd", lambda g: g.memset(epsrms_c[:], float(cfg.eps_rms)), inc=s_ms.inc(1))
    if cfg.sparse:
        q.op("gpsimd", lambda g: g.memset(sent_sb[:], float(2 * NBS)), inc=s_ms.inc(1))
        q.op("gpsimd", lambda g: g.iota(ubase[:], [[1, SEG]], channel_multiplier=SEG,
                                        allow_small_or_imprecise_dtypes=True), inc=s_ms.inc(1))
        q.op("gpsimd", lambda g: g.iota(e_iota[:], [[1, E]], channel_multiplier=0,
                                        allow_small_or_imprecise_dtypes=True), inc=s_ms.inc(1))
        q.op("gpsimd", lambda g: g.iota(rankc[:], [[0, 1]], channel_multiplier=1,
                                        allow_small_or_imprecise_dtypes=True), inc=s_ms.inc(1))
        q.op("gpsimd", lambda g: g.iota(pcol[:], [[0, 1]], channel_multiplier=1,
                                        allow_small_or_imprecise_dtypes=True), inc=s_ms.inc(1))
        q.op("gpsimd", lambda g: g.iota(grow[:], [[1, G]], channel_multiplier=0,
                                        allow_small_or_imprecise_dtypes=True), inc=s_ms.inc(1))
        q.op("gpsimd", lambda g: g.memset(zrog[:], 0.0), inc=s_ms.inc(1))
        q.op("gpsimd", lambda g: g.memset(rzro[:], 0.0), inc=s_ms.inc(1))
        q.op("gpsimd", lambda g: g.memset(pdg[:], 0.0), inc=s_ms.inc(1))
        q.op("gpsimd", lambda g: g.memset(ones_r[:], 1.0), inc=s_ms.inc(1))
        BND = {}
        _bnd_gen = [0]

        def _setup_bounds(e):
            vals = dict(tbl=G * GRP - 1, gath=NBS - 1, scat=2 * NBS - 1,
                        w1=E * NFG * 128 - 1,
                        w2=E * NHc * NFG * 128 - 1, b1=E * 128 - 1)
            gen = _bnd_gen[0]
            _bnd_gen[0] += 1
            ins = None
            for kk, vv in vals.items():
                r = e.register("bnd%d_%s" % (gen, kk)).__enter__()
                ins = e.reg_mov(r, vv)
                BND[kk] = e.snap(r)
            return ins
        q.op("gpsimd", _setup_bounds)
        ms_init = s_ms.n
        q.op("vector", lambda e: e.tensor_scalar(
            rankc[:], rankc[:], 63.5, 0.0, ALU.is_gt, ALU.add),
            waits=[(s_ms.h, ms_init)], inc=s_dv.inc())
        dvx = s_dv.n
        q.op("vector", lambda e: e.tensor_scalar_mul(rankcS[:], rankc[:], float(S)),
             waits=[w(s_dv, dvx)], inc=s_dv.inc())
        dvx = s_dv.n
        q.op("vector", lambda e: e.tensor_scalar_mul(rankcN[:], rankc[:], float(NBS)),
             waits=[w(s_dv, dvx)], inc=s_dv.inc())
        dvx = s_dv.n
        q.op("vector", lambda e: e.tensor_scalar_mul(grow[:], grow[:], float(GRP)),
             waits=[w(s_dv, dvx)], inc=s_dv.inc())
        RT_INIT_DV = s_dv.n
        for b0 in range(NB):
            nfill = (G * GRP * 4 + (128 * 256) - 1) // (128 * 256)
            flat = tbl_h[b0].rearrange("a b -> (a b)")
            for fi in range(nfill):
                lo = fi * 128 * 256
                hi = min((fi + 1) * 128 * 256, G * GRP * 4)
                q.op("sync", (lambda flat, lo, hi: lambda e: e.dma_start(
                    out=flat[lo:hi].rearrange("(p j) -> p j", p=128),
                    in_=sent_sb[:, 0:(hi - lo) // 128]))(flat, lo, hi),
                    waits=[(s_ms.h, ms_init)], inc=s_fill.inc(16))
        TBL_FILL = s_fill.n
        pre_jobs = []
        NKH = 2 if KD % 2 == 0 and KD * FGWc > 2048 else 1
        KH = KD // NKH
        for (srcw, dstw) in ((w1a_h, w1a_sw), (w1b_h, w1b_sw)):
            for ei in range(E):
                for fg in range(NFG):
                    for kh in range(NKH):
                        pre_jobs.append((
                            srcw[ei, kh * KH * 128:(kh + 1) * KH * 128,
                                 fg * FGWc:(fg + 1) * FGWc].rearrange(
                                "(k p) f -> p k f", p=128),
                            dstw[(ei * NFG + fg) * 128:(ei * NFG + fg + 1) * 128,
                                 kh * KH * FGWc:(kh + 1) * KH * FGWc],
                            KH * FGWc))
        for ei in range(E):
            for half in range(NHc):
                for fg in range(NFG):
                    row = ((ei * NHc + half) * NFG + fg) * 128
                    pre_jobs.append((
                        w2_h[ei, fg * FGWc:(fg + 1) * FGWc,
                             half * DHc:(half + 1) * DHc].rearrange(
                            "(fl p) d -> p fl d", p=128),
                        w2_sw[row:row + 128, :],
                        FG * DHc))
        pre_state = dict(free=[0, 0], n=[0, 0], i=0)

        def emit_pre(count):
            for _ in range(count):
                if pre_state["i"] >= len(pre_jobs):
                    return
                srcap, dstap, ncols = pre_jobs[pre_state["i"]]
                sl = pre_state["i"] % 2
                pre_state["i"] += 1
                q.op("gpsimd", (lambda srcap, ncols, sl: lambda e: e.dma_start(
                    out=pcv[sl][:, 0:ncols].rearrange(
                        "p (a b) -> p a b", a=srcap.shape[1]),
                    in_=srcap))(srcap, ncols, sl),
                    waits=[w(s_pro[sl], pre_state["free"][sl])], inc=s_pri[sl].inc(16))
                pre_state["n"][sl] = s_pri[sl].n
                q.op("sync", (lambda dstap, ncols, sl: lambda e: e.dma_start(
                    out=dstap, in_=pcv[sl][:, 0:ncols]))(dstap, ncols, sl),
                    waits=[(s_pri[sl].h, pre_state["n"][sl])], inc=s_pro[sl].inc(16))
                pre_state["free"][sl] = s_pro[sl].n
        PRE_TOTAL = None
    WC = s_wc.n
    wcw = (s_wc.h, WC)
    wcm = (s_ms.h, s_ms.n)

    dbg_cols = [0, 0]
    dbg_xnt = 0
    x_free = [0, 0]
    xnf_free = [0, 0]
    xn2f_free = [0, 0]
    xn2Tslot_free = [0, 0]
    sev_rec = [0, 0]
    we_done = [0] * NB
    smm_last = [0] * NB
    wqkv_carry = [0]

    TBL_DONE = [0] * NB
    EGRP_DV = [0] * NB
    SPN_END = [0, 0]

    def emit_routing(b):
        dvc = [s_dv.n]

        def vq(fn, extra=()):
            q.op("vector", fn, waits=[w(s_dv, dvc[0])] + [x for x in extra if x], inc=s_dv.inc())
            dvc[0] = s_dv.n

        vq(lambda e: e.tensor_copy(rtn[:, 0:2 * NT], gidx[:]), extra=[w(s_dv, dv_gidx)])
        vq(lambda e: e.tensor_copy(rtn[:, 2 * NT:3 * NT], glt[:, 2 * NT:3 * NT]),
           extra=[w(s_dv, dv_we)])
        vq(lambda e: e.tensor_copy(rtn[:, 3 * NT:4 * NT], glt[:, 3 * NT:4 * NT]))
        rdy = dvc[0]
        q.op("tensor", lambda e: e.transpose(
            ps_o0[0:4 * NT, 0:128], rtn[:], ident_sb[:]),
            waits=[w(s_dv, rdy), wcw] + pwait("o0h0"), inc=s_pe.inc())
        pe_rt = s_pe.n
        vq(lambda e: e.tensor_copy(rtv[:], ps_o0[0:4 * NT, 0:128]),
           extra=[w(s_pe, pe_rt)])
        pread("o0h0", "dv", s_dv.n)
        q.op("sync", (lambda b: lambda e: e.dma_start(
            out=rt_h[b], in_=rtv[:]))(b), waits=[w(s_dv, dvc[0])], inc=s_rt.inc(16))
        rt_st = s_rt.n
        flat = rt_h[b].rearrange("a b -> (a b)")
        q.op("sync", lambda e: e.dma_start(
            out=e12g[:], in_=flat[0:2 * S].rearrange("(p j) -> p j", p=128)),
            waits=[(s_rt.h, rt_st)], inc=s_rt.inc(16))
        q.op("sync", lambda e: e.dma_start(
            out=w12g[:], in_=flat[2 * S:4 * S].rearrange("(p j) -> p j", p=128)),
            inc=s_rt.inc(16))
        rt_ld = s_rt.n

        vq(lambda e: e.memset(slag[:], 0.0), extra=[(s_rt.h, rt_ld), (s_ms.h, s_ms.n),
                                                    w(s_dv, RT_INIT_DV)])
        for ei in range(E):
            vq((lambda ei: lambda e: e.tensor_single_scalar(
                mskg[:], e12g[:], float(ei), ALU.is_equal))(ei))
            vq(lambda e: e.tensor_tensor_scan(
                scng[:], mskg[:], zrog[:], 0.0, ALU.add, ALU.add))
            rdy = dvc[0]
            q.op("tensor", lambda e: e.transpose(
                ps_o0[0:1, 0:128], scng[:, SEG - 1:SEG], ident_sb[:]),
                waits=[w(s_dv, rdy)] + pwait("o0h0"), inc=s_pe.inc())
            pe_t = s_pe.n
            vq(lambda e: e.tensor_copy(rowa[:], ps_o0[0:1, 0:128]),
               extra=[w(s_pe, pe_t)])
            pread("o0h0", "dv", s_dv.n)
            vq(lambda e: e.tensor_tensor_scan(
                rowb[:], rowa[:], rzro[:], 0.0, ALU.add, ALU.add))
            vq((lambda ei: lambda e: e.tensor_copy(
                crow[0:1, ei:ei + 1], rowb[0:1, 127:128]))(ei))
            vq(lambda e: e.tensor_sub(rowa[:], rowb[:], rowa[:]))
            rdy = dvc[0]
            q.op("tensor", lambda e: e.transpose(
                ps_o0[:, 128:129], rowa[:], ident_sb[0:1, 0:1]),
                waits=[w(s_dv, rdy)] + pwait("o0h0"), inc=s_pe.inc())
            pe_t = s_pe.n
            vq(lambda e: e.tensor_copy(segoff[:], ps_o0[:, 128:129]),
               extra=[w(s_pe, pe_t)])
            pread("o0h0", "dv", s_dv.n)
            vq(lambda e: e.tensor_scalar(
                tmpg[:], scng[:], segoff[:], -1.0, ALU.add, ALU.add))
            vq(lambda e: e.tensor_mul(tmpg[:], tmpg[:], mskg[:]))
            vq(lambda e: e.tensor_add(slag[:], slag[:], tmpg[:]))

        vq(lambda e: e.tensor_copy(cri[0:1, 0:E], crow[0:1, 0:E]))
        vq(lambda e: e.tensor_scalar_add(cri[0:1, 0:E], cri[0:1, 0:E], GRP - 1))
        vq(lambda e: e.tensor_single_scalar(
            cri[0:1, 0:E], cri[0:1, 0:E], cfg.LG, ALU.arith_shift_right))
        vq(lambda e: e.tensor_single_scalar(
            cri[0:1, 0:E], cri[0:1, 0:E], cfg.LG, ALU.arith_shift_left))
        vq(lambda e: e.tensor_copy(crow[0:1, 8:8 + E], cri[0:1, 0:E]))
        vq(lambda e: e.tensor_tensor_scan(
            crow[0:1, 16:16 + E], crow[0:1, 8:8 + E], rzro[0:1, 0:E], 0.0,
            ALU.add, ALU.add))
        vq(lambda e: e.tensor_sub(crow[0:1, 24:24 + E], crow[0:1, 16:16 + E],
                                  crow[0:1, 8:8 + E]))
        rdy = dvc[0]
        q.op("tensor", lambda e: e.matmul(
            ps_o0[:, 0:E], ones_r[:], crow[0:1, 24:24 + E], start=True, stop=True),
            waits=[w(s_dv, rdy), (s_ms.h, s_ms.n)] + pwait("o0h0"), inc=s_pe.inc())
        pe_t = s_pe.n
        vq(lambda e: e.tensor_copy(basec[:, 0:E], ps_o0[:, 0:E]), extra=[w(s_pe, pe_t)])
        pread("o0h0", "dv", s_dv.n)

        vq(lambda e: e.memset(egf[:], -1.0))
        for ei in range(E):
            vq((lambda ei: lambda e: e.tensor_single_scalar(
                etmp[:], grow[:], crow[0:1, 24 + ei:25 + ei], ALU.is_ge))(ei))
            vq(lambda e: e.tensor_add(egf[:], egf[:], etmp[:]))
        vq((lambda b: lambda e: e.tensor_copy(
            egrp_i[0:1, b * G:(b + 1) * G], egf[:]))(b))
        rdy = dvc[0]
        q.op("tensor", lambda e: e.matmul(
            ps_o0[:, 256:256 + G], ones_r[:], egf[:], start=True, stop=True),
            waits=[w(s_dv, rdy), (s_ms.h, s_ms.n)] + pwait("o0h0"), inc=s_pe.inc())
        pe_t = s_pe.n
        vq(lambda e: e.tensor_copy(ebase[:], ps_o0[:, 256:256 + G]),
           extra=[w(s_pe, pe_t)])
        pread("o0h0", "dv", s_dv.n)
        NHl = D // min(512, D)
        DHl = min(512, D)
        vq(lambda e: e.tensor_scalar(eb2[:], ebase[:], 128.0, pcol[:],
                                     ALU.mult, ALU.add))
        vq((lambda b: lambda e: e.tensor_copy(b1x_i[:, b * G:(b + 1) * G], eb2[:]))(b))
        vq(lambda e: e.tensor_scalar(eb2[:], ebase[:], float(NFG * 128), pcol[:],
                                     ALU.mult, ALU.add))
        wv = widx_i[:, b * G * NFG:(b + 1) * G * NFG].rearrange(
            "p (g c) -> p g c", c=NFG)
        for fg in range(NFG):
            vq((lambda fg, wv: lambda e: e.tensor_scalar_add(
                wv[:, :, fg:fg + 1].squeeze(2), eb2[:], float(fg * 128)))(fg, wv))
        vq(lambda e: e.tensor_scalar(eb2[:], ebase[:], float(NHl * NFG * 128), pcol[:],
                                     ALU.mult, ALU.add))
        w2v = w2x_i[:, b * G * NHl * NFG:(b + 1) * G * NHl * NFG].rearrange(
            "p (g c) -> p g c", c=NHl * NFG)
        for c in range(NHl * NFG):
            vq((lambda c, w2v: lambda e: e.tensor_scalar_add(
                w2v[:, :, c:c + 1].squeeze(2), eb2[:], float(c * 128)))(c, w2v))
        EGRP_DV[b] = dvc[0]

        for ei in range(E):
            vq((lambda ei: lambda e: e.tensor_single_scalar(
                mskg[:], e12g[:], float(ei), ALU.is_equal))(ei))
            vq((lambda ei: lambda e: e.tensor_scalar_mul(
                tmpg[:], mskg[:], basec[:, ei:ei + 1]))(ei))
            vq(lambda e: e.tensor_add(slag[:], slag[:], tmpg[:]))
        vq(lambda e: e.tensor_copy(sloti[:], slag[:]),
           extra=[(s_tbl.h, TBL_DONE[b - 1]) if b > 0 and TBL_DONE[b - 1] else None])

        vq((lambda b: lambda e: e.tensor_scalar(
            tmpg[:], ubase[:], rankcS[:], float(b * S), ALU.subtract, ALU.add))(b))
        pdg4 = pdg[:].rearrange("p (j f) -> p j f", f=4)
        vq(lambda e: e.tensor_copy(pdg4[:, :, 0:1].squeeze(2), tmpg[:]))
        vq(lambda e: e.tensor_scalar(
            pdg4[:, :, 1:2].squeeze(2), tmpg[:], rankcN[:], 0.0, ALU.add, ALU.add))
        vq(lambda e: e.tensor_copy(pdg4[:, :, 2:3].squeeze(2), w12g[:]))
        rdy = dvc[0]
        for j in range(SEG):
            q.op("gpsimd", (lambda b, j: lambda e: e.indirect_dma_start(
                out=tbl_h[b],
                out_offset=bass.IndirectOffsetOnAxis(ap=sloti[:, j:j + 1], axis=0),
                in_=pdg[:, j * 4:(j + 1) * 4], in_offset=None,
                bounds_check=BND["tbl"], oob_is_err=False))(b, j),
                waits=[(s_fill.h, TBL_FILL), w(s_dv, rdy)] if j == 0 else [],
                inc=s_tbl.inc(16))
        TBL_DONE[b] = s_tbl.n

    for b in range(NB):
        logit_guard = 0

        for tt in range(NT):
            sl = tt % 2
            par = tt % 2
            row0 = b * S + tt * 128
            cb = 16 * par
            q.op("sync", (lambda r, sl: lambda e: e.dma_start(
                out=xin[sl][:], in_=x_h[r:r + 128, :]))(r=row0, sl=sl),
                waits=[w(s_dv, x_free[sl])], inc=s_x[sl].inc(16))
            xw = [(s_x[sl].h, s_x[sl].n)]
            q.op("scalar", (lambda sl, cb: lambda e: e.activation(
                sqs[:], xin[sl][:], AF.Square, accum_out=cols[:, cb + 1:cb + 2]))(sl, cb),
                waits=xw + [w(s_dv, x_free[sl])] +
                ([w(s_dbg, dbg_cols[par])] if cfg.debug else []), inc=s_ac.inc())
            ac_sq = s_ac.n
            q.op("vector", (lambda sl, cb: lambda e: e.reduce_sum(
                cols[:, cb:cb + 1], xin[sl][:], axis=AX.X))(sl, cb),
                waits=xw + ([w(s_dbg, dbg_cols[par])] if cfg.debug else []),
                inc=s_dv.inc())
            dv_rs = s_dv.n
            q.op("vector", (lambda cb: lambda e: e.tensor_scalar_mul(
                cols[:, cb + 2:cb + 3], cols[:, cb:cb + 1], 1.0 / D))(cb),
                waits=[w(s_dv, dv_rs)], inc=s_dv.inc())
            dv_mean = s_dv.n
            q.op("vector", (lambda cb: lambda e: e.tensor_scalar_mul(
                cols[:, cb + 5:cb + 6], cols[:, cb + 1:cb + 2], 1.0 / D))(cb),
                waits=[w(s_ac, ac_sq)], inc=s_dv.inc())
            dv_ex2 = s_dv.n
            q.op("vector", (lambda cb: lambda e: e.scalar_tensor_tensor(
                cols[:, cb + 6:cb + 7], cols[:, cb + 2:cb + 3], cols[:, cb + 2:cb + 3],
                cols[:, cb + 5:cb + 6], ALU.mult, ALU.subtract))(cb),
                waits=[w(s_dv, dv_mean), w(s_dv, dv_ex2)], inc=s_dv.inc())
            dv_nv = s_dv.n
            q.op("scalar", (lambda cb: lambda e: e.activation(
                cols[:, cb + 3:cb + 4], cols[:, cb + 6:cb + 7], AF.Sqrt,
                scale=-1.0, bias=epsln_c[:]))(cb),
                waits=[w(s_dv, dv_nv), wcw, wcm], inc=s_ac.inc())
            ac_sd = s_ac.n
            q.op("vector", (lambda cb: lambda e: e.reciprocal(
                cols[:, cb + 3:cb + 4], cols[:, cb + 3:cb + 4]))(cb),
                waits=[w(s_ac, ac_sd)], inc=s_dv.inc())
            dv_rstd = s_dv.n
            q.op("vector", (lambda sl, par, cb: lambda e: e.tensor_scalar(
                xnf[par][:], xin[sl][:], cols[:, cb + 2:cb + 3], cols[:, cb + 3:cb + 4],
                ALU.subtract, ALU.mult))(sl, par, cb),
                waits=[w(s_pe, xnf_free[par]), w(s_dv, dv_rstd)], inc=s_dv.inc())
            dv_xn = s_dv.n
            x_free[sl] = dv_xn
            q.op("scalar", (lambda par, cb: lambda e: e.activation(
                sqs[:], xnf[par][:], AF.Square, accum_out=cols[:, cb + 7:cb + 8]))(par, cb),
                waits=[w(s_dv, dv_xn)], inc=s_ac.inc())
            ac_s2 = s_ac.n
            q.op("scalar", (lambda cb: lambda e: e.activation(
                cols[:, cb + 4:cb + 5], cols[:, cb + 7:cb + 8], AF.Sqrt,
                scale=1.0 / D, bias=epsrms_c[:]))(cb),
                waits=[w(s_ac, ac_s2)], inc=s_ac.inc())
            ac_sd2 = s_ac.n
            q.op("vector", (lambda cb: lambda e: e.reciprocal(
                cols[:, cb + 4:cb + 5], cols[:, cb + 4:cb + 5]))(cb),
                waits=[w(s_ac, ac_sd2)], inc=s_dv.inc())
            dv_rstd2 = s_dv.n
            q.op("vector", (lambda par, cb: lambda e: e.scalar_tensor_tensor(
                xn2f[par][:], xnf[par][:], cols[:, cb + 4:cb + 5], lnw_bc[:],
                ALU.mult, ALU.mult))(par, cb),
                waits=[wcw, wcm, w(s_pe, xn2f_free[par]), w(s_dv, dv_rstd2)] +
                      ([(s_spn[par].h, s_spn[par].n)] if cfg.sparse else []),
                inc=s_dv.inc())
            dv_xn2 = s_dv.n
            if cfg.sparse:
                q.op("sync", (lambda par, row0: lambda e: e.dma_start(
                    out=xn2n_h[row0:row0 + 128, :], in_=xn2f[par][:]))(par, row0),
                    waits=[w(s_dv, dv_xn2)], inc=s_spn[par].inc(16))

            if cfg.debug:
                q.op("sync", (lambda b, tt, cb: lambda e: e.dma_start(
                    out=colso_h[b * NT + tt], in_=cols[:, cb:cb + 8]))(b, tt, cb),
                    waits=[w(s_dv, dv_xn2)], inc=s_dbg.inc(16))
                dbg_cols[par] = s_dbg.n

            for which, srcf, dstT, dvready in ((0, xnf[par], xnT, dv_xn),
                                               (1, xn2f[par], xn2T, dv_xn2)):
                for c0 in range(0, KD, 4):
                    half = (c0 // 4) % 2
                    reg = "o0h%d" % half
                    n = min(4, KD - c0)
                    for j in range(n):
                        q.op("tensor", (lambda srcf, c0, j, half: lambda e: e.transpose(
                            ps_o0[:, half * 512 + j * 128: half * 512 + (j + 1) * 128],
                            srcf[:, (c0 + j) * 128: (c0 + j + 1) * 128], ident_sb[:]))(
                                srcf, c0, j, half),
                            waits=([w(s_dv, dvready), wcw, wcm] + pwait(reg) +
                                   ([w(s_dbg, dbg_xnt)] if cfg.debug else [])) if j == 0 else [],
                            inc=s_pe.inc() if j == n - 1 else None)
                    pe_tr = s_pe.n
                    extra = []
                    if which == 1 and c0 == 0 and not cfg.sparse:
                        extra = [w(s_spx, 16 * KD * b)]
                    if which == 1 and c0 == 0 and cfg.sparse:
                        extra = [w(s_pe, xn2Tslot_free[par])]
                    ttc = (tt % 2) if (which == 1 and cfg.sparse) else tt
                    q.op("vector", (lambda dstT, c0, n, half, ttc: lambda e: e.tensor_copy(
                        dstT[:].rearrange("p (k s) -> p k s", k=KD)[
                            :, c0:c0 + n, ttc * 128:(ttc + 1) * 128],
                        ps_o0[:, half * 512: half * 512 + n * 128].rearrange(
                            "p (k s) -> p k s", k=n)))(dstT, c0, n, half, ttc),
                        waits=[w(s_pe, pe_tr)] + extra, inc=s_dv.inc())
                    pread(reg, "dv", s_dv.n)
                if which == 0:
                    xnf_free[par] = pe_tr
                    xnT_done = s_dv.n
                else:
                    xn2f_free[par] = pe_tr
                    xn2T_done = s_dv.n

            x2b = (tt % 2) * 128 if cfg.sparse else tt * 128
            for k in range(KD):
                q.op("tensor", (lambda k, x2b: lambda e: e.matmul(
                    ps_o1[:, 0:E],
                    xn2T[:, k * SW2 + x2b: k * SW2 + x2b + 128],
                    wg_sb[:, k * E:(k + 1) * E],
                    start=(k == 0), stop=(k == KD - 1)))(k, x2b),
                    waits=([w(s_dv, xn2T_done)] + pwait("o1h0")) if k == 0 else [],
                    inc=s_pe.inc() if k == KD - 1 else None)
            pe_lg = s_pe.n
            xn2Tslot_free[tt % 2] = pe_lg
            q.op("vector", (lambda tt: lambda e: e.tensor_add(
                logits[:, tt * E:(tt + 1) * E], ps_o1[:, 0:E], bg_bc[:]))(tt),
                waits=[w(s_pe, pe_lg), wcw, wcm], inc=s_dv.inc())
            pread("o1h0", "dv", s_dv.n)
            lg_done = s_dv.n

        if cfg.debug:
            q.op("sync", (lambda b: lambda e: e.dma_start(
                out=lgo_h[b], in_=logits[:]))(b),
                waits=[w(s_dv, s_dv.n)], inc=s_dbg.inc(16))
            dbg_lg = s_dbg.n
        lg3 = logits[:].rearrange("p (t e) -> p t e", e=E)
        m1 = glt[:, 0 * NT:1 * NT]
        m2 = glt[:, 1 * NT:2 * NT]
        w1c = glt[:, 2 * NT:3 * NT]
        w2c = glt[:, 3 * NT:4 * NT]
        eq1 = geq[:, 0:NT * E]
        eq2 = geq[:, NT * E:2 * NT * E]
        eq13 = eq1.rearrange("p (t e) -> p t e", e=E)
        eq23 = eq2.rearrange("p (t e) -> p t e", e=E)
        q.op("vector", lambda e: e.reduce_max(m1, lg3, axis=AX.X),
             waits=[w(s_dv, lg_done)] + ([w(s_dbg, dbg_lg)] if cfg.debug else []),
             inc=s_dv.inc())
        dv_m1 = s_dv.n
        q.op("vector", lambda e: e.tensor_tensor(
            eq13, lg3, m1.unsqueeze(2).broadcast_to([128, NT, E]), ALU.is_equal),
            waits=[w(s_dv, dv_m1)], inc=s_dv.inc())
        dv_eq1 = s_dv.n
        q.op("vector", lambda e: e.scalar_tensor_tensor(
            eq2, eq1, -1e30, logits[:], ALU.mult, ALU.add),
            waits=[w(s_dv, dv_eq1)], inc=s_dv.inc())
        dv_msk = s_dv.n
        q.op("vector", lambda e: e.reduce_max(m2, eq23, axis=AX.X),
             waits=[w(s_dv, dv_msk)], inc=s_dv.inc())
        dv_m2 = s_dv.n
        q.op("vector", lambda e: e.tensor_sub(w1c, m1, m2), waits=[w(s_dv, dv_m2)],
             inc=s_dv.inc())
        dv_w1 = s_dv.n
        q.op("scalar", lambda e: e.activation(w1c, w1c, AF.Sigmoid),
             waits=[w(s_dv, dv_w1)], inc=s_ac.inc())
        ac_w1 = s_ac.n
        q.op("vector", lambda e: e.tensor_tensor(
            eq23, eq23, m2.unsqueeze(2).broadcast_to([128, NT, E]), ALU.is_equal),
            waits=[w(s_ac, ac_w1)], inc=s_dv.inc())
        dv_eq2 = s_dv.n
        if cfg.sparse:
            eb = e_iota[:].unsqueeze(1).broadcast_to([128, NT, E])
            gsc3 = gsc[:].rearrange("p (t e) -> p t e", e=E)
            q.op("vector", lambda e: e.tensor_tensor(gsc3, eq13, eb, ALU.mult),
                 waits=[w(s_dv, dv_eq2), (s_ms.h, s_ms.n)], inc=s_dv.inc())
            dvtmp = s_dv.n
            q.op("vector", lambda e: e.reduce_sum(gidx[:, 0:NT], gsc3, axis=AX.X),
                 waits=[w(s_dv, dvtmp)], inc=s_dv.inc())
            dvtmp = s_dv.n
            q.op("vector", lambda e: e.tensor_tensor(gsc3, eq23, eb, ALU.mult),
                 waits=[w(s_dv, dvtmp)], inc=s_dv.inc())
            dvtmp = s_dv.n
            q.op("vector", lambda e: e.reduce_sum(gidx[:, NT:2 * NT], gsc3, axis=AX.X),
                 waits=[w(s_dv, dvtmp)], inc=s_dv.inc())
            dv_gidx = s_dv.n
        q.op("vector", lambda e: e.tensor_scalar(w2c, w1c, -1.0, 1.0, ALU.mult, ALU.add),
             inc=s_dv.inc())
        dv_w2 = s_dv.n
        q.op("vector", lambda e: e.tensor_tensor(
            eq13, eq13, w1c.unsqueeze(2).broadcast_to([128, NT, E]), ALU.mult),
            waits=[w(s_dv, dv_eq2)], inc=s_dv.inc())
        dv_p1 = s_dv.n
        q.op("vector", lambda e: e.tensor_tensor(
            eq23, eq23, w2c.unsqueeze(2).broadcast_to([128, NT, E]), ALU.mult),
            waits=[w(s_dv, dv_w2)], inc=s_dv.inc())
        dv_p2 = s_dv.n
        q.op("vector", (lambda b: lambda e: e.tensor_add(
            we_all[:, b * NT * E:(b + 1) * NT * E], eq1, eq2))(b),
            waits=[w(s_dv, dv_p1), w(s_dv, dv_p2)], inc=s_dv.inc())
        dv_we = s_dv.n

        for c0 in range(0, NT, 4):
            half = (c0 // 4) % 2
            reg = "o0h%d" % half
            n = min(4, NT - c0)
            for j in range(n):
                q.op("tensor", (lambda b, c0, j, half: lambda e: e.transpose(
                    ps_o0[0:E, half * 512 + j * 128: half * 512 + (j + 1) * 128],
                    we_all[:, (b * NT + c0 + j) * E:(b * NT + c0 + j + 1) * E],
                    ident_sb[:]))(b, c0, j, half),
                    waits=([w(s_dv, dv_we)] + pwait(reg)) if j == 0 else [],
                    inc=s_pe.inc() if j == n - 1 else None)
            pe_wt = s_pe.n
            q.op("vector", (lambda b, c0, n, half: lambda e: e.tensor_copy(
                weT_all[0:E, b * S + c0 * 128: b * S + (c0 + n) * 128],
                ps_o0[0:E, half * 512: half * 512 + n * 128]))(b, c0, n, half),
                waits=[w(s_pe, pe_wt)], inc=s_dv.inc())
            pread(reg, "dv", s_dv.n)
        we_done[b] = s_dv.n

        if cfg.sparse:
            emit_routing(b)
            emit_pre((len(pre_jobs) + NB - 1) // NB)

        if cfg.debug:
            for k in range(KD):
                q.op("sync", (lambda b, k: lambda e: e.dma_start(
                    out=xnso_h[k, :, b * S:(b + 1) * S],
                    in_=xnT[:, k * S:(k + 1) * S]))(b, k),
                    waits=[w(s_dv, xnT_done)] if k == 0 else [], inc=s_dbg.inc(16))
            dbg_xnt = s_dbg.n
        if not cfg.sparse:
            for k in range(KD):
                q.op("sync", (lambda b, k: lambda e: e.dma_start(
                    out=xn2s_h[k, :, b * S:(b + 1) * S],
                    in_=xn2T[:, k * S:(k + 1) * S]))(b, k),
                    waits=[w(s_dv, xn2T_done)] if k == 0 else [], inc=s_spx.inc(16))

        wqkv_last_pe = wqkv_carry[0]
        if not cfg.only_ln:
            for (wsrc, kk, dstT) in ((wq_h, KD, qT_sb), (wk_h, KD, kT_sb), (wvd_h, KD, vdT_sb)):
                q.op("gpsimd", (lambda wsrc, kk: ld3(wqkv_sb, wsrc, kk))(wsrc, kk),
                     waits=[w(s_pe, wqkv_last_pe)], inc=s_wqkv.inc(16))
                wq_n = s_wqkv.n
                for m in range(KQ):
                    for c in range(0, S, 512):
                        cw = min(512, S - c)
                        half = (c // 512) % 2
                        reg = "o1h%d" % half
                        for k in range(KD):
                            q.op("tensor", (lambda m, c, cw, k, half: lambda e: e.matmul(
                                ps_o1[:, half * 512: half * 512 + cw],
                                wqkv_sb[:, k * DQ + m * 128: k * DQ + (m + 1) * 128],
                                xnT[:, k * S + c: k * S + c + cw],
                                start=(k == 0), stop=(k == KD - 1)))(m, c, cw, k, half),
                                waits=([w(s_dv, xnT_done), (s_wqkv.h, wq_n)] + pwait(reg))
                                if k == 0 else [],
                                inc=s_pe.inc() if k == KD - 1 else None)
                        pe_g = s_pe.n
                        q.op("vector", (lambda dstT, m, c, cw, half: lambda e: e.tensor_copy(
                            dstT[:, m * S + c: m * S + c + cw],
                            ps_o1[:, half * 512: half * 512 + cw]))(dstT, m, c, cw, half),
                            waits=[w(s_pe, pe_g)], inc=s_dv.inc())
                        pread(reg, "dv", s_dv.n)
                wqkv_last_pe = s_pe.n
        qkv_done = s_dv.n

        if not cfg.only_ln:
            q.op("gpsimd", ld3(wqkv_sb, wvu_h, KQ),
                 waits=[w(s_pe, wqkv_last_pe)], inc=s_wqkv.inc(16))
            wvu_n = s_wqkv.n
        for tt in range(0 if cfg.only_ln else NT):
            regs = ("o0h0", "o0h1") if tt % 2 == 0 else ("o1h0", "o1h1")
            pst = ps_o0 if tt % 2 == 0 else ps_o1
            nmm = 0
            tot = ((D + 511) // 512) * KQ
            for h in range(0, D, 512):
                hw = min(512, D - h)
                for kq in range(KQ):
                    nmm += 1
                    q.op("tensor", (lambda tt, h, hw, kq, pst: lambda e: e.matmul(
                        pst[:, h:h + hw],
                        vdT_sb[:, kq * S + tt * 128: kq * S + (tt + 1) * 128],
                        wqkv_sb[:, kq * D + h: kq * D + h + hw],
                        start=(kq == 0), stop=(kq == KQ - 1)))(tt, h, hw, kq, pst),
                        waits=(pwait(*regs) + [w(s_dv, qkv_done), (s_wqkv.h, wvu_n)])
                        if nmm == 1 else [],
                        inc=s_pe.inc() if nmm == tot else None)
            pe_v = s_pe.n
            q.op("vector", (lambda tt, pst: lambda e: e.tensor_copy(
                vnat[:, tt * D:(tt + 1) * D], pst[:, 0:D]))(tt, pst),
                waits=[w(s_pe, pe_v)], inc=s_dv.inc())
            for r in regs:
                pread(r, "dv", s_dv.n)
        if not cfg.only_ln:
            wqkv_carry[0] = s_pe.n

        expb_free = 0

        def emit_scores(tt):
            ncols = (tt + 1) * 128
            nblk = (ncols + 511) // 512
            for cbk in range(nblk):
                cw = min(512, ncols - cbk * 512)
                for kq in range(KQ):
                    q.op("tensor", (lambda tt, cbk, cw, kq: lambda e: e.matmul(
                        ps_big[:, cbk * 512: cbk * 512 + cw],
                        qT_sb[:, kq * S + tt * 128: kq * S + (tt + 1) * 128],
                        kT_sb[:, kq * S + cbk * 512: kq * S + cbk * 512 + cw],
                        start=(kq == 0), stop=(kq == KQ - 1)))(tt, cbk, cw, kq),
                        waits=pwait("bigA") if (cbk == 0 and kq == 0) else [],
                        inc=s_pe.inc() if (cbk == nblk - 1 and kq == KQ - 1) else None)
            return s_pe.n

        def emit_softmax(tt, pe_sc):
            ncols = (tt + 1) * 128
            q.op("vector", (lambda tt: lambda e: e.copy_predicated(
                ps_big[:, tt * 128:(tt + 1) * 128], amask_sb[:], neg_sb[:]))(tt),
                waits=[w(s_pe, pe_sc), wcw, wcm], inc=s_dv.inc())
            dv_mask = s_dv.n
            pread("bigA", "dv", dv_mask)
            q.op("vector", (lambda ncols: lambda e: e.reduce_max(
                cols[:, 16:17], ps_big[:, 0:ncols], axis=AX.X, negate=True))(ncols),
                waits=[w(s_dv, dv_mask)], inc=s_dv.inc())
            dv_mx = s_dv.n
            pread("bigA", "dv", dv_mx)
            q.op("scalar", lambda e: e.activation(
                cols[:, 17:18], cols[:, 16:17], AF.Copy, scale=float(cfg.scale)),
                waits=[w(s_dv, dv_mx)], inc=s_ac.inc())
            ac_cp = s_ac.n
            q.op("scalar", (lambda ncols: lambda e: e.activation(
                expb[:, 0:ncols], ps_big[:, 0:ncols], AF.Exp,
                bias=cols[:, 17:18], scale=float(cfg.scale),
                accum_out=cols[:, 18:19]))(ncols),
                waits=[w(s_pe, expb_free), w(s_ac, ac_cp)] +
                      ([(s_dbg.h, s_dbg.n)] if cfg.debug else []), inc=s_ac.inc())
            ac_exp = s_ac.n
            pread("bigA", "ac", ac_exp)
            q.op("vector", lambda e: e.reciprocal(cols[:, 19:20], cols[:, 18:19]),
                 waits=[w(s_ac, ac_exp)], inc=s_dv.inc())
            dv_rc = s_dv.n
            q.op("vector", (lambda ncols: lambda e: e.tensor_scalar_mul(
                expb[:, 0:ncols], expb[:, 0:ncols], cols[:, 19:20]))(ncols),
                waits=[w(s_dv, dv_rc)], inc=s_dv.inc())
            return s_dv.n

        def emit_trans_smm(tt, dv_attn):
            nonlocal expb_free
            n_tk = tt + 1
            if cfg.debug:
                rr = b * S + tt * 128
                q.op("sync", (lambda rr, n_tk: lambda e: e.dma_start(
                    out=ao_h[rr:rr + 128, 0:n_tk * 128], in_=expb[:, 0:n_tk * 128]))(rr, n_tk),
                    waits=[w(s_dv, dv_attn)], inc=s_dbg.inc(16))
            for c0 in range(0, n_tk, 4):
                half = (c0 // 4) % 2
                reg = "o0h%d" % half
                n = min(4, n_tk - c0)
                for j in range(n):
                    q.op("tensor", (lambda c0, j, half: lambda e: e.transpose(
                        ps_o0[:, half * 512 + j * 128: half * 512 + (j + 1) * 128],
                        expb[:, (c0 + j) * 128: (c0 + j + 1) * 128], ident_sb[:]))(c0, j, half),
                        waits=([w(s_dv, dv_attn)] + pwait(reg)) if j == 0 else [],
                        inc=s_pe.inc() if j == n - 1 else None)
                pe_tr = s_pe.n
                q.op("vector", (lambda c0, n, half: lambda e: e.tensor_copy(
                    attnT[:, c0 * 128:(c0 + n) * 128],
                    ps_o0[:, half * 512: half * 512 + n * 128]))(c0, n, half),
                    waits=[w(s_pe, pe_tr)], inc=s_dv.inc())
                pread(reg, "dv", s_dv.n)
            expb_free = s_pe.n
            attnT_done = s_dv.n
            nmm = 0
            tot = n_tk * ((D + 511) // 512)
            for tk in range(n_tk):
                for h in range(0, D, 512):
                    hw = min(512, D - h)
                    nmm += 1
                    q.op("tensor", (lambda tk, h, hw, n_tk: lambda e: e.matmul(
                        ps_o1[:, h:h + hw],
                        attnT[:, tk * 128:(tk + 1) * 128],
                        vnat[:, tk * D + h: tk * D + h + hw],
                        start=(tk == 0), stop=(tk == n_tk - 1)))(tk, h, hw, n_tk),
                        waits=([w(s_dv, attnT_done)] + pwait("o1h0", "o1h1"))
                        if nmm == 1 else [],
                        inc=s_pe.inc() if nmm == tot else None)
            pe_s = s_pe.n
            sl = tt % 2
            q.op("vector", (lambda sl: lambda e: e.tensor_copy(
                sev[sl][:], ps_o1[:, 0:D]))(sl),
                waits=[w(s_pe, pe_s), (s_se[sl].h, sev_rec[sl]) if sev_rec[sl] else None],
                inc=s_dv.inc())
            pread("o1h0", "dv", s_dv.n)
            pread("o1h1", "dv", s_dv.n)
            dv_sev = s_dv.n
            row0 = b * S + tt * 128
            q.op("sync", (lambda sl, row0: lambda e: e.dma_start(
                out=s_h[row0:row0 + 128, :], in_=sev[sl][:]))(sl, row0),
                waits=[w(s_dv, dv_sev)], inc=s_se[sl].inc(16))
            sev_rec[sl] = s_se[sl].n
            return pe_s

        prev = None
        for tt in range(0 if cfg.only_ln else NT):
            pe_sc = emit_scores(tt)
            if prev is not None:
                smm_last[b] = emit_trans_smm(prev[0], prev[1])
            dv_attn = emit_softmax(tt, pe_sc)
            prev = (tt, dv_attn)
        if prev is not None:
            smm_last[b] = emit_trans_smm(prev[0], prev[1])
        if cfg.debug:
            vnat_dv = s_dv.n
            q.op("sync", (lambda b: lambda e: e.dma_start(
                out=vno_h[b], in_=vnat[:]))(b), waits=[w(s_dv, vnat_dv)], inc=s_dbg.inc(16))
            q.op("sync", (lambda b: lambda e: e.dma_start(
                out=qo_h[b], in_=qT_sb[:]))(b), inc=s_dbg.inc(16))
            q.op("sync", (lambda b: lambda e: e.dma_start(
                out=ko_h[b], in_=kT_sb[:]))(b), inc=s_dbg.inc(16))

    PE_END_A = s_pe.n
    DV_END_A = s_dv.n
    AC_END_A = s_ac.n
    SE_END = (s_se[0].n, s_se[1].n)

    q.run_block(nc)
    stA.close()

    stB = ExitStack()

    def sbufB(name, shape, dt):
        return stB.enter_context(nc.sbuf_tensor(name, shape, dt))

    w1a_b = [sbufB("w1a_b0", [128, KD * FG * 128], BF16),
             sbufB("w1a_b1", [128, KD * FG * 128], BF16)]
    w1b_b = [sbufB("w1b_b0", [128, KD * FG * 128], BF16),
             sbufB("w1b_b1", [128, KD * FG * 128], BF16)]
    xcb = [sbufB("xc0", [128, D], F32), sbufB("xc1", [128, D], F32)]
    scb = [sbufB("scb0", [128, D], F32), sbufB("scb1", [128, D], F32)]
    gsb = sbufB("gsb", [128, D], F32)
    osb = [sbufB("osb0", [128, D], F32), sbufB("osb1", [128, D], F32)]
    if not cfg.sparse:
        xn2B = sbufB("xn2B", [128, KD * S], BF16)
        acc = sbufB("acc", [128, NT * D], F32)
        w2_b = [sbufB("w2_b0", [128, FG * D], BF16),
                sbufB("w2_b1", [128, FG * D], BF16)]
        yT = [sbufB("yT0", [128, FG * TBW], BF16), sbufB("yT1", [128, FG * TBW], BF16)]
        silu_sb = sbufB("silu_sb", [128, TBW], BF16)
    else:
        DHp = min(512, D)
        w2_b = [sbufB("w2_b0", [128, FG * DHp], BF16),
                sbufB("w2_b1", [128, FG * DHp], BF16)]
        yT = [sbufB("yT0", [128, NFL * GRP], BF16), sbufB("yT1", [128, NFL * GRP], BF16)]
        silu_sb = sbufB("silu_sb", [128, GRP], BF16)
        xgT = [sbufB("xgT0", [128, KD * GRP], BF16), sbufB("xgT1", [128, KD * GRP], BF16)]
        xgf = [sbufB("xgf0", [128, D], F32), sbufB("xgf1", [128, D], F32)]
        tb_sb = [sbufB("tb0", [128, NGT * 4], F32), sbufB("tb1", [128, NGT * 4], F32)]
        gi32 = [sbufB("gi0", [128, NGT], mybir.dt.int32),
                sbufB("gi1", [128, NGT], mybir.dt.int32)]
        si32 = [sbufB("si0", [128, NGT], mybir.dt.int32),
                sbufB("si1", [128, NGT], mybir.dt.int32)]
        b1ag = [sbufB("b1ag0", [128, NFL], F32), sbufB("b1ag1", [128, NFL], F32)]
        b1bg = [sbufB("b1bg0", [128, NFL], F32), sbufB("b1bg1", [128, NFL], F32)]
        o1b = sbufB("o1b", [128, D], F32)
        o2b = sbufB("o2b", [128, D], F32)


    barrier = [(s_pe.h, PE_END_A), (s_dv.h, DV_END_A), (s_ac.h, AC_END_A),
               (s_spx.h, 16 * KD * NB) if not cfg.sparse else None,
               (s_se[0].h, SE_END[0]) if SE_END[0] else None,
               (s_se[1].h, SE_END[1]) if SE_END[1] else None]
    barrier = [x for x in barrier if x is not None]
    if cfg.sparse:
        barrier = barrier + [(s_spn[0].h, s_spn[0].n) if s_spn[0].n else None,
                             (s_spn[1].h, s_spn[1].n) if s_spn[1].n else None,
                             (s_pro[0].h, s_pro[0].n), (s_pro[1].h, s_pro[1].n)]
        barrier = [x for x in barrier if x is not None]

    state = dict(w1_free=[0, 0], w2_free=[0, 0], yT_free=[0, 0], yT_ready=[0, 0],
                 silu_free=0, fl_last_pe=0, c_done=0, g_free=0,
                 xc_rec=[0, 0], sc_rec=[0, 0], osb_rec=[0, 0], o1_rec=0,
                 acc_ready=[0] * NT, first=True)

    def emit_w_loads(ts, e_i, fg):
        gidx = (ts * E + e_i) * NFG + fg
        slot = gidx % 2
        extra = list(barrier) if state["first"] else []
        state["first"] = False
        q.op("gpsimd", (lambda e_i, fg, slot: lambda e: e.dma_start(
            out=w1a_b[slot][:].rearrange("p (k f) -> p k f", k=KD),
            in_=w1a_h[e_i, :, fg * FG * 128:(fg + 1) * FG * 128].rearrange(
                "(k p) f -> p k f", p=128)))(e_i, fg, slot),
            waits=extra + [w(s_pe, state["w1_free"][slot])], inc=s_w1a[slot].inc(16))
        q.op("gpsimd", (lambda e_i, fg, slot: lambda e: e.dma_start(
            out=w1b_b[slot][:].rearrange("p (k f) -> p k f", k=KD),
            in_=w1b_h[e_i, :, fg * FG * 128:(fg + 1) * FG * 128].rearrange(
                "(k p) f -> p k f", p=128)))(e_i, fg, slot),
            inc=s_w1b[slot].inc(16))
        q.op("gpsimd", (lambda e_i, fg, slot: lambda e: e.dma_start(
            out=w2_b[slot][:].rearrange("p (f d) -> p f d", f=FG),
            in_=w2_h[e_i, fg * FG * 128:(fg + 1) * FG * 128, :].rearrange(
                "(f p) d -> p f d", p=128)))(e_i, fg, slot),
            waits=[w(s_pe, state["w2_free"][slot])], inc=s_w2[slot].inc(16))
        return slot, s_w1a[slot].n, s_w1b[slot].n, s_w2[slot].n

    def emit_fl(ts, e_i, fg, tb, wslot, w1a_n, w1b_n, xn2_n, unit):
        ys = unit % 2
        for fl in range(FG):
            par = fl % 2
            ra, rb = "ba%d" % par, "bb%d" % par
            pa = ps_big[:, par * 1024: par * 1024 + TBW]
            pb = ps_big[:, par * 1024 + 512: par * 1024 + 512 + TBW]
            for k in range(KD):
                waits = []
                if k == 0:
                    waits = pwait(ra) + [(s_w1a[wslot].h, w1a_n), (s_xn2r.h, xn2_n)]
                q.op("tensor", (lambda fl, k, tb, wslot, pa: lambda e: e.matmul(
                    pa, w1a_b[wslot][:, k * FG * 128 + fl * 128: k * FG * 128 + (fl + 1) * 128],
                    xn2B[:, k * S + tb * TBW: k * S + (tb + 1) * TBW],
                    start=(k == 0), stop=(k == KD - 1)))(fl, k, tb, wslot, pa),
                    waits=waits, inc=s_pe.inc() if k == KD - 1 else None)
            pe_a = s_pe.n
            for k in range(KD):
                waits = []
                if k == 0:
                    waits = pwait(rb) + [(s_w1b[wslot].h, w1b_n)]
                q.op("tensor", (lambda fl, k, tb, wslot, pb: lambda e: e.matmul(
                    pb, w1b_b[wslot][:, k * FG * 128 + fl * 128: k * FG * 128 + (fl + 1) * 128],
                    xn2B[:, k * S + tb * TBW: k * S + (tb + 1) * TBW],
                    start=(k == 0), stop=(k == KD - 1)))(fl, k, tb, wslot, pb),
                    waits=waits, inc=s_pe.inc() if k == KD - 1 else None)
            pe_b = s_pe.n
            bcol = e_i * NFL + fg * FG + fl
            q.op("scalar", (lambda bcol, pa: lambda e: e.activation(
                silu_sb[:], pa, AF.Silu, bias=b1a_sb[:, bcol:bcol + 1]))(bcol, pa),
                waits=[w(s_pe, pe_a), w(s_dv, state["silu_free"]), wcw], inc=s_ac.inc())
            pread(ra, "ac", s_ac.n)
            ac_silu = s_ac.n
            q.op("vector", (lambda bcol, pb, ys, fl: lambda e: e.scalar_tensor_tensor(
                yT[ys][:, fl * TBW:(fl + 1) * TBW], pb, b1b_sb[:, bcol:bcol + 1],
                silu_sb[:], ALU.add, ALU.mult))(bcol, pb, ys, fl),
                waits=[w(s_pe, pe_b), w(s_ac, ac_silu), wcw] +
                      ([w(s_pe, state["yT_free"][ys])] if fl == 0 else []),
                inc=s_dv.inc())
            pread(rb, "dv", s_dv.n)
            state["silu_free"] = s_dv.n
        state["yT_ready"][ys] = s_dv.n
        state["fl_last_pe"] = s_pe.n
        state["w1_free"][wslot] = s_pe.n

    def emit_o(ts, e_i, fg, tb, wslot, w2_n, unit):
        ys = unit % 2
        for ttl in range(TPB):
            tt = tb * TPB + ttl
            po = ps_o0 if ttl % 2 == 0 else ps_o1
            regs = ("o0h0", "o0h1") if ttl % 2 == 0 else ("o1h0", "o1h1")
            nmm = 0
            tot = FG * ((D + 511) // 512)
            for fl in range(FG):
                for h in range(0, D, 512):
                    hw = min(512, D - h)
                    nmm += 1
                    waits = []
                    if nmm == 1:
                        waits = pwait(*regs) + [w(s_dv, state["yT_ready"][ys]),
                                                (s_w2[wslot].h, w2_n)]
                    q.op("tensor", (lambda fl, h, hw, ttl, ys, wslot, po: lambda e: e.matmul(
                        po[:, h:h + hw],
                        yT[ys][:, fl * TBW + ttl * 128: fl * TBW + (ttl + 1) * 128],
                        w2_b[wslot][:, fl * D + h: fl * D + h + hw],
                        start=(fl == 0), stop=(fl == FG - 1)))(fl, h, hw, ttl, ys, wslot, po),
                        waits=waits,
                        inc=s_pe.inc() if nmm == tot else None)
            pe_o = s_pe.n
            state["w2_free"][wslot] = pe_o
            state["yT_free"][ys] = pe_o
            q.op("vector", (lambda ts, e_i, tt, po: lambda e: e.scalar_tensor_tensor(
                acc[:, tt * D:(tt + 1) * D], po[:, 0:D],
                we_all[:, (ts * NT + tt) * E + e_i:(ts * NT + tt) * E + e_i + 1],
                acc[:, tt * D:(tt + 1) * D], ALU.mult, ALU.add))(ts, e_i, tt, po),
                waits=[w(s_pe, pe_o), w(s_dv, state["acc_ready"][tt])], inc=s_dv.inc())
            for r in regs:
                pread(r, "dv", s_dv.n)
            state["acc_ready"][tt] = s_dv.n

    def emit_phase_c(ts):
        for tt in range(NT):
            sl = tt % 2
            row0 = ts * S + tt * 128
            q.op("sync", (lambda row0, sl: lambda e: e.dma_start(
                out=xcb[sl][:], in_=x_h[row0:row0 + 128, :]))(row0, sl),
                waits=[w(s_dv, state["xc_rec"][sl])] +
                      (list(barrier) if (ts == 0 and tt < 2) else []),
                inc=s_xc[sl].inc(16))
            xn_ = s_xc[sl].n
            q.op("sync", (lambda row0, sl: lambda e: e.dma_start(
                out=scb[sl][:], in_=s_h[row0:row0 + 128, :]))(row0, sl),
                waits=[w(s_dv, state["sc_rec"][sl])], inc=s_sc[sl].inc(16))
            sn_ = s_sc[sl].n
            q.op("scalar", (lambda tt: lambda e: e.activation(
                gsb[:], acc[:, tt * D:(tt + 1) * D], AF.Silu))(tt),
                waits=[w(s_dv, state["acc_ready"][tt]), w(s_dv, state["g_free"])],
                inc=s_ac.inc())
            ac_g = s_ac.n
            gw = []
            if cfg.debug:
                q.op("sync", (lambda row0: lambda e: e.dma_start(
                    out=go_h[row0:row0 + 128, :], in_=gsb[:]))(row0),
                    waits=[w(s_ac, ac_g)], inc=s_dbg.inc(16))
                gw = [(s_dbg.h, s_dbg.n)]
            q.op("vector", (lambda sl: lambda e: e.tensor_mul(gsb[:], gsb[:], scb[sl][:]))(sl),
                 waits=[w(s_ac, ac_g), (s_sc[sl].h, sn_)] + gw, inc=s_dv.inc())
            dv_gm = s_dv.n
            q.op("vector", (lambda sl: lambda e: e.tensor_add(
                osb[sl][:], gsb[:], xcb[sl][:]))(sl),
                waits=[(s_xc[sl].h, xn_), w(s_dv, dv_gm),
                       (s_out[sl].h, state["osb_rec"][sl]) if state["osb_rec"][sl] else None],
                inc=s_dv.inc())
            dv_o = s_dv.n
            state["sc_rec"][sl] = dv_o
            state["xc_rec"][sl] = dv_o
            state["g_free"] = dv_o
            q.op("sync", (lambda row0, sl: lambda e: e.dma_start(
                out=o_h[row0:row0 + 128, :], in_=osb[sl][:]))(row0, sl),
                waits=[w(s_dv, dv_o)], inc=s_out[sl].inc(16))
            state["osb_rec"][sl] = s_out[sl].n
        state["c_done"] = s_dv.n

    FGW = FG * 128
    DH = min(512, D)
    NH = D // DH
    sp = dict(tb_dv=[0, 0], tb_gx=[(0, 0), (0, 0)], tb_osc=[(0, 0), (0, 0)],
              xgf_pe=[0, 0], xgT_pe=[0, 0], yT_pe=[0, 0],
              w1s_pe=[0, 0], w1a_cn=[0] * 64, w1b_cn=[0] * 64,
              w2_pe=[0, 0], conv_dv=[0, 0], gx_n=[0, 0],
              silu_dv=0, osb_osc=[0, 0], first_tbl=True, first_gx=True,
              first_w=True, w2chunk=0, w2_n={}, tbn=[0, 0], xgT_dv=[0, 0],
              yT_dv=[0, 0], b1_n=[0, 0], b1_pe=[0, 0])

    def emit_w1_chunk(ts, g, fg):
        wsl = fg % 2
        icol = widx_i[:, (ts * G + g) * NFG + fg:(ts * G + g) * NFG + fg + 1]
        wext = []
        if sp["first_w"]:
            wext = list(barrier)
            sp["first_w"] = False
        q.op("gpsimd", (lambda wsl, icol: lambda e: e.indirect_dma_start(
            out=w1a_b[wsl][:], out_offset=None, in_=w1a_sw,
            in_offset=bass.IndirectOffsetOnAxis(ap=icol, axis=0),
            bounds_check=BND["w1"], oob_is_err=False))(wsl, icol),
            waits=wext + [w(s_pe, sp["w1s_pe"][wsl])], inc=s_w1a[wsl].inc(16))
        sp["w1a_cn"][fg] = s_w1a[wsl].n
        q.op("gpsimd", (lambda wsl, icol: lambda e: e.indirect_dma_start(
            out=w1b_b[wsl][:], out_offset=None, in_=w1b_sw,
            in_offset=bass.IndirectOffsetOnAxis(ap=icol, axis=0),
            bounds_check=BND["w1"], oob_is_err=False))(wsl, icol),
            inc=s_w1b[wsl].inc(16))
        sp["w1b_cn"][fg] = s_w1b[wsl].n

    def emit_w2_chunk(ts, g, half, fg):
        c = sp["w2chunk"]
        sl2 = c % 2
        sp["w2chunk"] += 1
        col = (ts * G + g) * NH * NFG + half * NFG + fg
        icol = w2x_i[:, col:col + 1]
        q.op("gpsimd", (lambda sl2, icol: lambda e: e.indirect_dma_start(
            out=w2_b[sl2][:], out_offset=None, in_=w2_sw,
            in_offset=bass.IndirectOffsetOnAxis(ap=icol, axis=0),
            bounds_check=BND["w2"], oob_is_err=False))(sl2, icol),
            waits=[w(s_pe, sp["w2_pe"][sl2])], inc=s_w2[sl2].inc(16))
        sp["w2_n"][(ts, g, half, fg)] = (sl2, s_w2[sl2].n)

    def emit_sp_tblconv(ts, g):
        gsl = g % 2
        waits = [w(s_dv, sp["tb_dv"][gsl]),
                 (s_gx[0].h, sp["tb_gx"][gsl][0]) if sp["tb_gx"][gsl][0] else None,
                 (s_gx[1].h, sp["tb_gx"][gsl][1]) if sp["tb_gx"][gsl][1] else None,
                 (s_osc[0].h, sp["tb_osc"][gsl][0]) if sp["tb_osc"][gsl][0] else None,
                 (s_osc[1].h, sp["tb_osc"][gsl][1]) if sp["tb_osc"][gsl][1] else None,
                 (s_tbl.h, TBL_DONE[ts])]
        if sp["first_tbl"]:
            waits += list(barrier)
            sp["first_tbl"] = False
        flat = tbl_h[ts].rearrange("a b -> (a b)")
        q.op("sync", (lambda flat, g, gsl: lambda e: e.dma_start(
            out=tb_sb[gsl][:],
            in_=flat[g * GRP * 4:(g + 1) * GRP * 4].rearrange("(p j) -> p j", p=128)))(
                flat, g, gsl),
            waits=waits, inc=s_tb2s[gsl].inc(16))
        sp["tbn"][gsl] = s_tb2s[gsl].n
        tb4 = tb_sb[gsl][:].rearrange("p (q f) -> p q f", f=4)
        q.op("vector", (lambda gsl, tb4: lambda e: e.tensor_copy(
            gi32[gsl][:], tb4[:, :, 0:1].squeeze(2)))(gsl, tb4),
            waits=[(s_tb2s[gsl].h, sp["tbn"][gsl])], inc=s_dv.inc())
        dvx = s_dv.n
        q.op("vector", (lambda gsl, tb4: lambda e: e.tensor_copy(
            si32[gsl][:], tb4[:, :, 1:2].squeeze(2)))(gsl, tb4),
            waits=[w(s_dv, dvx)], inc=s_dv.inc())
        sp["conv_dv"][gsl] = s_dv.n

    def emit_gx(gsl, ti):
        xs = ti % 2
        waits = [w(s_dv, sp["conv_dv"][gsl]), w(s_pe, sp["xgf_pe"][xs])]
        if sp["first_gx"]:
            waits += list(barrier)
            sp["first_gx"] = False
        q.op("gpsimd", (lambda gsl, ti, xs: lambda e: e.indirect_dma_start(
            out=xgf[xs][:], out_offset=None,
            in_=xn2n_h,
            in_offset=bass.IndirectOffsetOnAxis(ap=gi32[gsl][:, ti:ti + 1], axis=0),
            bounds_check=BND["gath"], oob_is_err=False))(gsl, ti, xs),
            waits=waits, inc=s_gx[xs].inc(16))
        sp["gx_n"][xs] = s_gx[xs].n

    def emit_sp_loads(ts, g):
        gsl = g % 2
        for ti in range(min(NGT, 2)):
            emit_gx(gsl, ti)
        sp["tb_gx"][gsl] = (s_gx[0].n, s_gx[1].n)
        bcol_i = b1x_i[:, ts * G + g:ts * G + g + 1]
        q.op("gpsimd", (lambda gsl, bcol_i: lambda e: e.indirect_dma_start(
            out=b1ag[gsl][:], out_offset=None, in_=b1ag_h,
            in_offset=bass.IndirectOffsetOnAxis(ap=bcol_i, axis=0),
            bounds_check=BND["b1"], oob_is_err=False))(gsl, bcol_i),
            waits=[w(s_pe, sp["b1_pe"][gsl])], inc=s_wqkv.inc(16))
        q.op("gpsimd", (lambda gsl, bcol_i: lambda e: e.indirect_dma_start(
            out=b1bg[gsl][:], out_offset=None, in_=b1bg_h,
            in_offset=bass.IndirectOffsetOnAxis(ap=bcol_i, axis=0),
            bounds_check=BND["b1"], oob_is_err=False))(gsl, bcol_i),
            inc=s_wqkv.inc(16))
        sp["b1_n"][gsl] = s_wqkv.n
        emit_w1_chunk(ts, g, 0)
        if NFG > 1:
            emit_w1_chunk(ts, g, 1)

    def emit_sp_trans(ts, g):
        gsl = g % 2
        for ti in range(NGT):
            xs = ti % 2
            par = ti % 2
            regs = ("ba0", "bb0") if par == 0 else ("ba1", "bb1")
            pbase = par * 1024
            for k in range(KD):
                q.op("tensor", (lambda xs, k, pbase: lambda e: e.transpose(
                    ps_big[:, pbase + k * 128: pbase + (k + 1) * 128],
                    xgf[xs][:, k * 128:(k + 1) * 128], ident_sb[:]))(xs, k, pbase),
                    waits=([(s_gx[xs].h, sp["gx_n"][xs])] + pwait(*regs) + [wcw])
                    if k == 0 else [],
                    inc=s_pe.inc() if k == KD - 1 else None)
            pe_t = s_pe.n
            sp["xgf_pe"][xs] = pe_t
            if ti + 2 < NGT:
                emit_gx(gsl, ti + 2)
            q.op("vector", (lambda gsl, ti, pbase: lambda e: e.tensor_copy(
                xgT[gsl][:].rearrange("p (k c) -> p k c", k=KD)[
                    :, :, ti * 128:(ti + 1) * 128],
                ps_big[:, pbase:pbase + KD * 128].rearrange(
                    "p (k c) -> p k c", k=KD)))(gsl, ti, pbase),
                waits=[w(s_pe, pe_t)] +
                      ([w(s_pe, sp["xgT_pe"][gsl])] if ti == 0 else []),
                inc=s_dv.inc())
            for r in regs:
                pread(r, "dv", s_dv.n)
        sp["xgT_dv"][gsl] = s_dv.n
        sp["tb_gx"][gsl] = (s_gx[0].n, s_gx[1].n)

    def emit_sp_fl(ts, g):
        gsl = g % 2
        ys = g % 2
        for fg in range(NFG):
            wsl = fg % 2
            w1a_n, w1b_n = sp["w1a_cn"][fg], sp["w1b_cn"][fg]
            for fl in range(FG):
                par = fl % 2
                ra, rb = "ba%d" % par, "bb%d" % par
                pa = ps_big[:, par * 1024: par * 1024 + GRP]
                pb = ps_big[:, par * 1024 + 512: par * 1024 + 512 + GRP]
                for k in range(KD):
                    waits = []
                    if k == 0:
                        waits = pwait(ra) + [(s_w1a[wsl].h, w1a_n)]
                        if fg == 0 and fl == 0:
                            waits += [w(s_dv, sp["xgT_dv"][gsl])]
                    q.op("tensor", (lambda fl, k, gsl, wsl, pa: lambda e: e.matmul(
                        pa, w1a_b[wsl][:, k * FGW + fl * 128: k * FGW + (fl + 1) * 128],
                        xgT[gsl][:, k * GRP:(k + 1) * GRP],
                        start=(k == 0), stop=(k == KD - 1)))(fl, k, gsl, wsl, pa),
                        waits=waits, inc=s_pe.inc() if k == KD - 1 else None)
                pe_a = s_pe.n
                for k in range(KD):
                    waits = []
                    if k == 0:
                        waits = pwait(rb) + [(s_w1b[wsl].h, w1b_n)]
                    q.op("tensor", (lambda fl, k, gsl, wsl, pb: lambda e: e.matmul(
                        pb, w1b_b[wsl][:, k * FGW + fl * 128: k * FGW + (fl + 1) * 128],
                        xgT[gsl][:, k * GRP:(k + 1) * GRP],
                        start=(k == 0), stop=(k == KD - 1)))(fl, k, gsl, wsl, pb),
                        waits=waits, inc=s_pe.inc() if k == KD - 1 else None)
                pe_b = s_pe.n
                bcol = fg * FG + fl
                q.op("scalar", (lambda gsl, bcol, pa: lambda e: e.activation(
                    silu_sb[:], pa, AF.Silu, bias=b1ag[gsl][:, bcol:bcol + 1]))(
                        gsl, bcol, pa),
                    waits=[w(s_pe, pe_a), w(s_dv, sp["silu_dv"]),
                           (s_wqkv.h, sp["b1_n"][gsl])], inc=s_ac.inc())
                pread(ra, "ac", s_ac.n)
                ac_s = s_ac.n
                q.op("vector", (lambda gsl, bcol, pb, ys: lambda e: e.scalar_tensor_tensor(
                    yT[ys][:, bcol * GRP:(bcol + 1) * GRP], pb,
                    b1bg[gsl][:, bcol:bcol + 1], silu_sb[:], ALU.add, ALU.mult))(
                        gsl, bcol, pb, ys),
                    waits=[w(s_pe, pe_b), w(s_ac, ac_s)] +
                          ([w(s_pe, sp["yT_pe"][ys])] if (fg == 0 and fl == 0) else []),
                    inc=s_dv.inc())
                pread(rb, "dv", s_dv.n)
                sp["silu_dv"] = s_dv.n
            sp["w1s_pe"][wsl] = s_pe.n
            if fg + 2 < NFG:
                emit_w1_chunk(ts, g, fg + 2)
        sp["yT_dv"][ys] = s_dv.n
        sp["xgT_pe"][gsl] = s_pe.n
        sp["b1_pe"][gsl] = s_pe.n

    def emit_sp_o(ts, g):
        gsl = g % 2
        ys = g % 2
        PO = (ps_o0, ps_o0, ps_o1, ps_o1)
        RG = ("o0h0", "o0h1", "o1h0", "o1h1")
        emit_w2_chunk(ts, g, 0, 0)
        if NFG > 1:
            emit_w2_chunk(ts, g, 0, 1)
        elif NH > 1:
            emit_w2_chunk(ts, g, 1, 0)
        ck = 0
        for half in range(NH):
            for fg in range(NFG):
                sl2, w2n = sp["w2_n"][(ts, g, half, fg)]
                for ti in range(NGT):
                    po = PO[ti][:, (ti % 2) * 512:(ti % 2) * 512 + DH]
                    for fl in range(FG):
                        waits = []
                        if fl == 0 and ti == 0:
                            waits = [(s_w2[sl2].h, w2n)]
                            if fg == 0:
                                waits = waits + [w(s_dv, sp["yT_dv"][ys])]
                        if fg == 0 and fl == 0:
                            waits = waits + pwait(RG[ti])
                        fidx = fg * FG + fl
                        q.op("tensor", (lambda po, fidx, ti, fl, ys, sl2: lambda e: e.matmul(
                            po, yT[ys][:, fidx * GRP + ti * 128: fidx * GRP + (ti + 1) * 128],
                            w2_b[sl2][:, fl * DH:(fl + 1) * DH],
                            start=(fidx == 0), stop=(fidx == NFL - 1)))(
                                po, fidx, ti, fl, ys, sl2),
                            waits=waits,
                            inc=s_pe.inc() if (ti == NGT - 1 and fl == FG - 1) else None)
                sp["w2_pe"][sl2] = s_pe.n
                ck += 1
                nxt = ck + 1
                if nxt < NH * NFG:
                    emit_w2_chunk(ts, g, nxt // NFG, nxt % NFG)
            pe_o = s_pe.n
            for ti in range(NGT):
                po = PO[ti][:, (ti % 2) * 512:(ti % 2) * 512 + DH]
                osl = ti % 2
                waits = [w(s_pe, pe_o)]
                if sp["osb_osc"][osl]:
                    waits.append((s_osc[osl].h, sp["osb_osc"][osl]))
                if state["osb_rec"][osl]:
                    waits.append((s_out[osl].h, state["osb_rec"][osl]))
                q.op("vector", (lambda po, gsl, ti, osl, half: lambda e: e.tensor_scalar_mul(
                    osb[osl][:, half * DH:(half + 1) * DH], po,
                    tb_sb[gsl][:, ti * 4 + 2:ti * 4 + 3]))(po, gsl, ti, osl, half),
                    waits=waits, inc=s_dv.inc())
                pread(RG[ti], "dv", s_dv.n)
                dv_e = s_dv.n
                q.op("gpsimd", (lambda gsl, ti, osl, half: lambda e: e.indirect_dma_start(
                    out=obuf_hh[half],
                    out_offset=bass.IndirectOffsetOnAxis(
                        ap=si32[gsl][:, ti:ti + 1], axis=0),
                    in_=osb[osl][:, half * DH:(half + 1) * DH], in_offset=None,
                    bounds_check=BND["scat"], oob_is_err=False))(gsl, ti, osl, half),
                    waits=[w(s_dv, dv_e)], inc=s_osc[osl].inc(16))
                sp["osb_osc"][osl] = s_osc[osl].n
        sp["tb_dv"][gsl] = s_dv.n
        sp["tb_osc"][gsl] = (s_osc[0].n, s_osc[1].n)
        sp["yT_pe"][ys] = s_pe.n

    def emit_phase_c_sp(ts):
        osc_tot = (s_osc[0].n, s_osc[1].n)
        for tt in range(NT):
            sl = tt % 2
            row0 = ts * S + tt * 128
            q.op("sync", (lambda row0, sl: lambda e: e.dma_start(
                out=xcb[sl][:], in_=x_h[row0:row0 + 128, :]))(row0, sl),
                waits=[w(s_dv, state["xc_rec"][sl])], inc=s_xc[sl].inc(16))
            xn_ = s_xc[sl].n
            q.op("sync", (lambda row0, sl: lambda e: e.dma_start(
                out=scb[sl][:], in_=s_h[row0:row0 + 128, :]))(row0, sl),
                waits=[w(s_dv, state["sc_rec"][sl])], inc=s_sc[sl].inc(16))
            sn_ = s_sc[sl].n
            for h in range(NH):
                q.op("sync", (lambda row0, h: lambda e: e.dma_start(
                    out=o1b[:, h * DH:(h + 1) * DH],
                    in_=obuf_hh[h][row0:row0 + 128, :]))(row0, h),
                    waits=([w(s_dv, state["o1_rec"]),
                            (s_osc[0].h, osc_tot[0]) if tt < 2 else None,
                            (s_osc[1].h, osc_tot[1]) if tt < 2 else None]
                           if h == 0 else []),
                    inc=s_xc[sl].inc(16))
                q.op("sync", (lambda row0, h: lambda e: e.dma_start(
                    out=o2b[:, h * DH:(h + 1) * DH],
                    in_=obuf_hh[h][NBS + row0:NBS + row0 + 128, :]))(row0, h),
                    inc=s_xc[sl].inc(16))
            on_ = s_xc[sl].n
            po = ps_o0 if tt % 2 == 0 else ps_o1
            regs = ("o0h0", "o0h1") if tt % 2 == 0 else ("o1h0", "o1h1")
            nh = (D + 511) // 512
            for hi, h in enumerate(range(0, D, 512)):
                hw = min(512, D - h)
                waits = []
                if hi == 0:
                    waits = pwait(*regs) + [w(s_dv, we_done[ts]), wcw]
                q.op("tensor", (lambda ts, tt, h, hw, po: lambda e: e.matmul(
                    po[:, h:h + hw],
                    weT_all[0:E, ts * S + tt * 128: ts * S + (tt + 1) * 128],
                    b2_sb[0:E, h:h + hw],
                    start=True, stop=True))(ts, tt, h, hw, po),
                    waits=waits, inc=s_pe.inc() if hi == nh - 1 else None)
            pe_b2 = s_pe.n
            q.op("vector", (lambda sl: lambda e: e.tensor_add(
                gsb[:], o1b[:], o2b[:]))(sl),
                waits=[(s_xc[sl].h, on_), w(s_dv, state["g_free"])], inc=s_dv.inc())
            dvx = s_dv.n
            state["o1_rec"] = dvx
            q.op("vector", (lambda po: lambda e: e.tensor_add(
                gsb[:], gsb[:], po[:, 0:D]))(po),
                waits=[w(s_dv, dvx), w(s_pe, pe_b2)], inc=s_dv.inc())
            dvx = s_dv.n
            for r in regs:
                pread(r, "dv", dvx)
            q.op("scalar", lambda e: e.activation(gsb[:], gsb[:], AF.Silu),
                 waits=[w(s_dv, dvx)], inc=s_ac.inc())
            ac_g = s_ac.n
            q.op("vector", (lambda sl: lambda e: e.tensor_mul(gsb[:], gsb[:], scb[sl][:]))(sl),
                 waits=[w(s_ac, ac_g), (s_sc[sl].h, sn_)], inc=s_dv.inc())
            dv_gm = s_dv.n
            q.op("vector", (lambda sl: lambda e: e.tensor_add(
                osb[sl][:], gsb[:], xcb[sl][:]))(sl),
                waits=[(s_xc[sl].h, on_), w(s_dv, dv_gm),
                       (s_out[sl].h, state["osb_rec"][sl]) if state["osb_rec"][sl] else None,
                       (s_osc[sl].h, sp["osb_osc"][sl]) if sp["osb_osc"][sl] else None],
                inc=s_dv.inc())
            dv_o = s_dv.n
            sp["osb_osc"][sl] = 0
            state["sc_rec"][sl] = dv_o
            state["xc_rec"][sl] = dv_o
            state["g_free"] = dv_o
            q.op("sync", (lambda row0, sl: lambda e: e.dma_start(
                out=o_h[row0:row0 + 128, :], in_=osb[sl][:]))(row0, sl),
                waits=[w(s_dv, dv_o)], inc=s_out[sl].inc(16))
            state["osb_rec"][sl] = s_out[sl].n
        state["c_done"] = s_dv.n

    for ts in range(NB if not cfg.only_a else 0):
        if cfg.sparse:
            if ts == 0:
                q.op("gpsimd", _setup_bounds)
            NGL = G if cfg.ng_limit is None else min(G, cfg.ng_limit)
            for g in range(NGL):
                if g == 0:
                    emit_sp_tblconv(ts, 0)
                emit_sp_loads(ts, g)
                emit_sp_trans(ts, g)
                emit_sp_fl(ts, g)
                if g > 0:
                    emit_sp_o(ts, g - 1)
                if g + 1 < NGL:
                    emit_sp_tblconv(ts, g + 1)
            emit_sp_o(ts, NGL - 1)
            if not cfg.skip_c:
                emit_phase_c_sp(ts)
            continue
        if ts == 0:
            waits0 = list(barrier)
        else:
            waits0 = [w(s_pe, state["fl_last_pe"])]
        for k in range(KD):
            q.op("sync", (lambda ts, k: lambda e: e.dma_start(
                out=xn2B[:, k * S:(k + 1) * S],
                in_=xn2s_h[k, :, ts * S:(ts + 1) * S]))(ts, k),
                waits=waits0 if k == 0 else [], inc=s_xn2r.inc(16))
        xn2_n = s_xn2r.n

        for tt in range(NT):
            po = ps_o0 if tt % 2 == 0 else ps_o1
            regs = ("o0h0", "o0h1") if tt % 2 == 0 else ("o1h0", "o1h1")
            nh = (D + 511) // 512
            for hi, h in enumerate(range(0, D, 512)):
                hw = min(512, D - h)
                waits = []
                if hi == 0:
                    waits = pwait(*regs) + [w(s_dv, we_done[ts]), wcw]
                    if ts == 0 and tt == 0:
                        waits = waits + list(barrier)
                q.op("tensor", (lambda ts, tt, h, hw, po: lambda e: e.matmul(
                    po[:, h:h + hw],
                    weT_all[0:E, ts * S + tt * 128: ts * S + (tt + 1) * 128],
                    b2_sb[0:E, h:h + hw],
                    start=True, stop=True))(ts, tt, h, hw, po),
                    waits=waits, inc=s_pe.inc() if hi == nh - 1 else None)
            pe_b2 = s_pe.n
            q.op("vector", (lambda tt, po: lambda e: e.tensor_copy(
                acc[:, tt * D:(tt + 1) * D], po[:, 0:D]))(tt, po),
                waits=[w(s_pe, pe_b2), w(s_dv, state["c_done"])], inc=s_dv.inc())
            for r in regs:
                pread(r, "dv", s_dv.n)
            state["acc_ready"][tt] = s_dv.n

        pending = None
        for e_i in range(E):
            for fg in range(NFG):
                wslot, w1a_n, w1b_n, w2_n = emit_w_loads(ts, e_i, fg)
                for tb in range(NTB):
                    unit = (e_i * NFG + fg) * NTB + tb
                    emit_fl(ts, e_i, fg, tb, wslot, w1a_n, w1b_n, xn2_n, unit)
                    if pending is not None:
                        emit_o(*pending)
                    pending = (ts, e_i, fg, tb, wslot, w2_n, unit)
        emit_o(*pending)

        emit_phase_c(ts)

    if cfg.debug:
        q.op("sync", lambda e: e.dma_start(out=weo_h[:], in_=we_all[:]),
             waits=[(s_dv.h, s_dv.n)], inc=s_dbg.inc(16))
        q.op("sync", lambda e: e.wait_ge(s_dbg.h, s_dbg.n))

    if not cfg.only_a:
        q.op("sync", lambda e: e.wait_ge(s_out[0].h, s_out[0].n),
             waits=[(s_out[1].h, s_out[1].n)])
    else:
        q.op("sync", lambda e: e.wait_ge(s_spx.h, s_spx.n),
             waits=[(s_se[0].h, s_se[0].n), (s_se[1].h, s_se[1].n),
                    (s_dv.h, s_dv.n), (s_ac.h, s_ac.n), (s_pe.h, s_pe.n)] +
                   ([(s_dbg.h, s_dbg.n)] if cfg.debug else []))

    q.run_block(nc)
    stB.close()
    st.close()
    return nc


_CFG = Cfg()
_NC_CACHE = {}


def _get_nc(cfg):
    key = id(cfg)
    if key not in _NC_CACHE:
        _NC_CACHE[key] = build_program(cfg)
    return _NC_CACHE[key]


def make_in_maps(cfg, n_cores, x, wq, wk, wvd, wvu, lnw, wg, bg, w1a, b1a, w1b, b1b, w2, b2):
    B, S, D = x.shape
    per = B // n_cores
    ident = np.eye(128, dtype=np.float32)
    amask = np.triu(np.ones((128, 128), dtype=np.uint8), 1)
    f32 = lambda a: np.ascontiguousarray(np.asarray(a, dtype=np.float32))

    def btr(bv):
        bv = f32(bv)
        E_, F_ = bv.shape
        return np.ascontiguousarray(
            bv.reshape(E_, F_ // 128, 128).transpose(2, 0, 1).reshape(128, E_ * (F_ // 128)))

    def btg(bv):
        bv = f32(bv)
        E_, F_ = bv.shape
        return np.ascontiguousarray(
            bv.reshape(E_, F_ // 128, 128).transpose(0, 2, 1).reshape(E_ * 128, F_ // 128))

    shared = dict(wq=f32(wq), wk=f32(wk), wvd=f32(wvd), wvu=f32(wvu), lnw=f32(lnw),
                  wg=f32(wg), bg=f32(bg), w1a=f32(w1a), b1a=btr(b1a), w1b=f32(w1b),
                  b1b=btr(b1b), w2=f32(w2), b2=f32(b2), ident=ident, amask=amask)
    if cfg.sparse:
        shared["b1agh"] = btg(b1a)
        shared["b1bgh"] = btg(b1b)
    in_maps = []
    for c in range(n_cores):
        m = dict(shared)
        m["x"] = np.ascontiguousarray(x[c * per:(c + 1) * per].reshape(per * S, D))
        in_maps.append(m)
    return in_maps


def kernel(L0_attn_inputs, L0_attn_Wq, L0_attn_Wk, L0_attn_Wvdown, L0_attn_Wvup,
           L0_ffn_ln_w, L0_ffn_Wg, L0_ffn_bg, L0_ffn_W1a, L0_ffn_b1a,
           L0_ffn_W1b, L0_ffn_b1b, L0_ffn_W2, L0_ffn_b2):
    cfg = _CFG
    n_cores = 8
    x = np.ascontiguousarray(np.asarray(L0_attn_inputs, dtype=np.float32))
    B, S, D = x.shape
    per = B // n_cores
    nc = _get_nc(cfg)
    in_maps = make_in_maps(cfg, n_cores, x, L0_attn_Wq, L0_attn_Wk, L0_attn_Wvdown,
                           L0_attn_Wvup, L0_ffn_ln_w, L0_ffn_Wg, L0_ffn_bg,
                           L0_ffn_W1a, L0_ffn_b1a, L0_ffn_W1b, L0_ffn_b1b,
                           L0_ffn_W2, L0_ffn_b2)
    res = run_bass_kernel_spmd(nc, in_maps, core_ids=list(range(n_cores)))
    out = np.concatenate([np.asarray(r["o"]).reshape(per, S, D) for r in res.results], axis=0)
    return out.astype(np.float32)


# version 06
# speedup vs baseline: 1.8082x; 1.3357x over previous
import numpy as np
from contextlib import ExitStack

import concourse.bass as bass
import concourse.mybir as mybir
from concourse.bass_utils import run_bass_kernel_spmd

F32 = mybir.dt.float32
BF16 = mybir.dt.bfloat16
U8 = mybir.dt.uint8
AF = mybir.ActivationFunctionType
ALU = mybir.AluOpType
AX = mybir.AxisListType

ENGS = ("tensor", "vector", "scalar", "gpsimd", "sync")


class Cfg:
    def __init__(self, NB=4, S=2048, D=1024, DQ=256, E=8, F=4096, FG=4, TBW=512,
                 GRP=512):
        self.NB, self.S, self.D, self.DQ, self.E, self.F = NB, S, D, DQ, E, F
        self.KD = D // 128
        self.KQ = DQ // 128
        self.NT = S // 128
        self.NFL = F // 128
        self.FG = FG
        self.NFG = self.NFL // FG
        self.TBW = TBW
        self.NTB = S // TBW
        self.TPB = TBW // 128
        self.eps_ln = 1e-5
        self.eps_rms = float(np.finfo(np.float32).eps)
        self.scale = 1.0 / float(np.sqrt(DQ))
        self.debug = False
        self.only_a = False
        self.only_ln = False
        self.sparse = True
        self.ng_limit = None
        self.skip_c = False
        self.GRP = GRP
        self.NGT = self.GRP // 128
        self.G = E + (2 * NB * S) // self.GRP
        self.LG = int(np.log2(self.GRP))


class Sem:
    def __init__(self, h):
        self.h = h
        self.n = 0

    def inc(self, amt=1):
        self.n += amt
        return (self.h, amt)


class Q:
    def __init__(self):
        self.ops = {e: [] for e in ENGS}

    def op(self, eng, fn, waits=(), inc=None):
        self.ops[eng].append((fn, [x for x in waits if x is not None], inc))

    def run_block(self, nc):
        with nc.Block() as block:
            for eng in ENGS:
                ops = self.ops[eng]
                if not ops:
                    continue

                def make(ops):
                    def prog(e):
                        for fn, waits, inc in ops:
                            for (sem, val) in waits:
                                e.wait_ge(sem, val)
                            ins = fn(e)
                            if inc is not None:
                                ins.then_inc(inc[0], inc[1])
                    return prog

                getattr(block, eng)(make(ops))
        self.ops = {e: [] for e in ENGS}


def build_program(cfg: Cfg):
    nc = bass.Bass("TRN2", target_bir_lowering=False)
    NB, S, D, DQ, E, F = cfg.NB, cfg.S, cfg.D, cfg.DQ, cfg.E, cfg.F
    KD, KQ, NT, NFL, FG, NFG = cfg.KD, cfg.KQ, cfg.NT, cfg.NFL, cfg.FG, cfg.NFG
    TBW, NTB, TPB = cfg.TBW, cfg.NTB, cfg.TPB

    x_h = nc.dram_tensor("x", [NB * S, D], F32, kind="ExternalInput").ap()
    wq_h = nc.dram_tensor("wq", [D, DQ], F32, kind="ExternalInput").ap()
    wk_h = nc.dram_tensor("wk", [D, DQ], F32, kind="ExternalInput").ap()
    wvd_h = nc.dram_tensor("wvd", [D, DQ], F32, kind="ExternalInput").ap()
    wvu_h = nc.dram_tensor("wvu", [DQ, D], F32, kind="ExternalInput").ap()
    lnw_h = nc.dram_tensor("lnw", [D], F32, kind="ExternalInput").ap()
    wg_h = nc.dram_tensor("wg", [D, E], F32, kind="ExternalInput").ap()
    bg_h = nc.dram_tensor("bg", [E], F32, kind="ExternalInput").ap()
    w1a_h = nc.dram_tensor("w1a", [E, D, F], F32, kind="ExternalInput").ap()
    b1a_h = nc.dram_tensor("b1a", [128, E * NFL], F32, kind="ExternalInput").ap()
    w1b_h = nc.dram_tensor("w1b", [E, D, F], F32, kind="ExternalInput").ap()
    b1b_h = nc.dram_tensor("b1b", [128, E * NFL], F32, kind="ExternalInput").ap()
    w2_h = nc.dram_tensor("w2", [E, F, D], F32, kind="ExternalInput").ap()
    b2_h = nc.dram_tensor("b2", [E, D], F32, kind="ExternalInput").ap()
    ident_h = nc.dram_tensor("ident", [128, 128], F32, kind="ExternalInput").ap()
    amask_h = nc.dram_tensor("amask", [128, 128], U8, kind="ExternalInput").ap()
    o_h = nc.dram_tensor("o", [NB * S, D], F32, kind="ExternalOutput").ap()
    dbgk = "ExternalOutput" if cfg.debug else "Internal"
    xn2s_h = nc.dram_tensor("xn2s", [KD, 128, NB * S], BF16, kind=dbgk).ap()
    s_h = nc.dram_tensor("sscr", [NB * S, D], F32, kind=dbgk).ap()
    GRP, NGT, G = cfg.GRP, cfg.NGT, cfg.G
    NBS = NB * S
    if cfg.sparse:
        xn2n_h = nc.dram_tensor("xn2n", [NBS, D], F32, kind="Internal").ap()
        rt_h = nc.dram_tensor("rt", [NB, 4 * NT, 128], F32, kind="Internal").ap()
        tbl_h = nc.dram_tensor("tbl", [G * GRP, 4], F32, kind="Internal").ap()
        FGWc = FG * 128
        DHc = min(512, D)
        NHc = D // DHc
        obuf_hh = [nc.dram_tensor("obuf%d" % h, [2 * NBS, DHc], F32,
                                  kind="Internal").ap() for h in range(NHc)]
        w1a_sw = nc.dram_tensor("w1asw", [E * NFG * 128, KD * FGWc], BF16,
                                kind="Internal").ap()
        w1b_sw = nc.dram_tensor("w1bsw", [E * NFG * 128, KD * FGWc], BF16,
                                kind="Internal").ap()
        w2_sw = nc.dram_tensor("w2sw", [E * NHc * NFG * 128, FG * DHc], BF16,
                               kind="Internal").ap()
        b1ag_h = nc.dram_tensor("b1agh", [E * 128, NFL], F32, kind="ExternalInput").ap()
        b1bg_h = nc.dram_tensor("b1bgh", [E * 128, NFL], F32, kind="ExternalInput").ap()
    if cfg.debug:
        weo_h = nc.dram_tensor("weo", [128, NB * NT * E], F32, kind="ExternalOutput").ap()
        go_h = nc.dram_tensor("go", [NB * S, D], F32, kind="ExternalOutput").ap()
        ao_h = nc.dram_tensor("ao", [NB * S, S], F32, kind="ExternalOutput").ap()
        xnso_h = nc.dram_tensor("xnso", [KD, 128, NB * S], BF16, kind="ExternalOutput").ap()
        colso_h = nc.dram_tensor("colso", [NB * NT, 128, 8], F32, kind="ExternalOutput").ap()
        lgo_h = nc.dram_tensor("lgo", [NB, 128, NT * E], F32, kind="ExternalOutput").ap()
        vno_h = nc.dram_tensor("vno", [NB, 128, S // 128 * D], BF16, kind="ExternalOutput").ap()
        qo_h = nc.dram_tensor("qo", [NB, 128, (DQ // 128) * S], BF16, kind="ExternalOutput").ap()
        ko_h = nc.dram_tensor("ko", [NB, 128, (DQ // 128) * S], BF16, kind="ExternalOutput").ap()

    st = ExitStack()

    def sbuf(name, shape, dt):
        return st.enter_context(nc.sbuf_tensor(name, shape, dt))

    def psum(name, shape, dt=F32):
        return st.enter_context(nc.psum_tensor(name, shape, dt))

    def sem(name):
        return Sem(st.enter_context(nc.semaphore(name)))

    we_all = sbuf("we_all", [128, NB * NT * E], F32)
    weT_all = sbuf("weT_all", [E, NB * S], BF16)
    if not cfg.sparse:
        b1a_sb = sbuf("b1a_sb", [128, E * NFL], F32)
        b1b_sb = sbuf("b1b_sb", [128, E * NFL], F32)
    b2_sb = sbuf("b2_sb", [E, D], BF16)
    ident_sb = sbuf("ident_sb", [128, 128], F32)
    amask_sb = sbuf("amask_sb", [128, 128], U8)
    neg_sb = sbuf("neg_sb", [128, 128], F32)
    cols = sbuf("colsA", [128, 32], F32)
    epsln_c = sbuf("epsln_c", [128, 1], F32)
    epsrms_c = sbuf("epsrms_c", [128, 1], F32)
    if cfg.sparse:
        I32 = mybir.dt.int32
        SEGB = 2 * S // 128
        SEG = NB * SEGB
        sent_sb = sbuf("sent_sb", [128, 256], F32)
        pcv = [sbuf("pcv0", [128, 2048], BF16), sbuf("pcv1", [128, 2048], BF16)]
        egrp_i = sbuf("egrp_i", [1, G], I32)
        e_iota = sbuf("e_iota", [128, E], F32)
        rankc = sbuf("rankc", [128, 1], F32)
        rankcS = sbuf("rankcS", [128, 1], F32)
        rankcN = sbuf("rankcN", [128, 1], F32)
        pcol = sbuf("pcol", [128, 1], F32)
        ebase = sbuf("ebase", [128, G], F32)
        eb2 = sbuf("eb2", [128, G], F32)
        widx_i = sbuf("widx_i", [128, G * NFG], I32)
        w2x_i = sbuf("w2x_i", [128, G * (D // min(512, D)) * NFG], I32)
        b1x_i = sbuf("b1x_i", [128, G], I32)
        basec = sbuf("basec", [128, E], F32)
        etmp = sbuf("etmp", [1, G], F32)
        ones_r = sbuf("ones_r", [1, 128], F32)
        segoff = sbuf("segoff", [128, 1], F32)
        rowa = sbuf("rowa", [1, 128], F32)
        rowb = sbuf("rowb", [1, 128], F32)
        rzro = sbuf("rzro", [1, 128], F32)
        crow = sbuf("crow", [1, 64], F32)
        cri = sbuf("cri", [1, 16], I32)
        grow = sbuf("grow", [1, G], F32)
        egf = sbuf("egf", [1, G], F32)

    ps_big = psum("ps_big", [128, 2048])
    ps_o0 = psum("ps_o0", [128, 1024])
    ps_o1 = psum("ps_o1", [128, 1024])

    s_pe = sem("s_pe")
    s_dv = sem("s_dv")
    s_ac = sem("s_ac")
    s_wc = sem("s_wc")
    s_x = [sem("s_x0"), sem("s_x1")]
    s_se = [sem("s_se0"), sem("s_se1")]
    s_spx = sem("s_spx")
    s_w1a = [sem("s_w1a0"), sem("s_w1a1")]
    s_w1b = [sem("s_w1b0"), sem("s_w1b1")]
    s_w2 = [sem("s_w20"), sem("s_w21")]
    s_xn2r = sem("s_xn2r")
    s_tb2s = [sem("s_tb2a"), sem("s_tb2b")]
    s_xc = [sem("s_xc0"), sem("s_xc1")]
    s_sc = [sem("s_sc0"), sem("s_sc1")]
    s_out = [sem("s_out0"), sem("s_out1")]
    s_dbg = sem("s_dbg")
    s_wqkv = sem("s_wqkv")
    s_ms = sem("s_ms")
    s_spn = [sem("s_spn0"), sem("s_spn1")]
    s_tbl = sem("s_tbl")
    s_pre = sem("s_pre")
    s_rt = sem("s_rt")
    s_gx = [sem("s_gx0"), sem("s_gx1")]
    s_tb2 = sem("s_tb2")
    s_osc = [sem("s_osc0"), sem("s_osc1")]
    s_fill = sem("s_fill")
    s_pri = [sem("s_pri0"), sem("s_pri1")]
    s_pro = [sem("s_pro0"), sem("s_pro1")]

    q = Q()
    _regc = [0]

    def _rname(p):
        _regc[0] += 1
        return "%s%d" % (p, _regc[0])

    def w(s, val):
        return (s.h, val) if val > 0 else None

    psg = {r: {"dv": 0, "ac": 0} for r in
           ("bigA", "ba0", "bb0", "ba1", "bb1", "o0h0", "o0h1", "o1h0", "o1h1")}

    def pwait(*regions):
        rs = set(regions)
        quarters = {"ba0", "bb0", "ba1", "bb1"}
        if "bigA" in rs:
            rs |= quarters
        elif rs & quarters:
            rs.add("bigA")
        dv = max(psg[r]["dv"] for r in rs)
        ac = max(psg[r]["ac"] for r in rs)
        return [w(s_dv, dv), w(s_ac, ac)]

    def pread(region, kind, cnt):
        psg[region][kind] = max(psg[region][kind], cnt)

    stA = ExitStack()

    def sbufA(name, shape, dt):
        return stA.enter_context(nc.sbuf_tensor(name, shape, dt))

    assert KD * DQ == KQ * D
    wqkv_sb = sbufA("wqkv_sb", [128, KD * DQ], BF16)
    wg_sb = sbufA("wg_sb", [128, KD * E], BF16)
    lnw_bc = sbufA("lnw_bc", [128, D], F32)
    bg_bc = sbufA("bg_bc", [128, E], F32)
    xin = [sbufA("xin0", [128, D], F32), sbufA("xin1", [128, D], F32)]
    sqs = sbufA("sqs", [128, D], BF16)
    xnf = [sbufA("xnf0", [128, D], F32), sbufA("xnf1", [128, D], F32)]
    xn2f = [sbufA("xn2f0", [128, D], F32), sbufA("xn2f1", [128, D], F32)]
    xnT = sbufA("xnT", [128, KD * S], BF16)
    SW2 = 256 if cfg.sparse else S
    xn2T = sbufA("xn2T_A", [128, KD * SW2], BF16)
    logits = sbufA("logits", [128, NT * E], F32)
    glt = sbufA("glt", [128, 4 * NT], F32)
    geq = sbufA("geq", [128, 2 * NT * E], F32)
    qT_sb = sbufA("qT", [128, KQ * S], BF16)
    kT_sb = sbufA("kT", [128, KQ * S], BF16)
    vdT_sb = sbufA("vdT", [128, KQ * S], BF16)
    vnat = sbufA("vnat", [128, NT * D], BF16)
    expb = sbufA("expb", [128, S], F32)
    attnT = sbufA("attnT", [128, S], BF16)
    sev = [sbufA("sev0", [128, D], F32), sbufA("sev1", [128, D], F32)]
    if cfg.sparse:
        gsc = sbufA("gsc", [128, NT * E], F32)
        gidx = sbufA("gidx", [128, 2 * NT], F32)
        rtn = sbufA("rtn", [128, 4 * NT], F32)
        rtv = sbufA("rtv", [4 * NT, 128], F32)
        e12g = sbufA("e12g", [128, SEG], F32)
        w12g = sbufA("w12g", [128, SEG], F32)
        mskg = sbufA("mskg", [128, SEG], F32)
        scng = sbufA("scng", [128, SEG], F32)
        zrog = sbufA("zrog", [128, SEG], F32)
        slag = sbufA("slag", [128, SEG], F32)
        tmpg = sbufA("tmpg", [128, SEG], F32)
        ubase = sbufA("ubase", [128, SEG], F32)
        sloti = sbufA("sloti", [128, SEG], I32)
        pdg = sbufA("pdg", [128, SEG * 4], F32)

    def ld3(dst, src, kk):
        return lambda g: g.dma_start(
            out=dst[:].rearrange("p (k m) -> p k m", k=kk),
            in_=src.rearrange("(k p) m -> p k m", p=128))

    q.op("gpsimd", ld3(wg_sb, wg_h, KD), inc=s_wc.inc(16))
    q.op("gpsimd", lambda g: g.dma_start(out=lnw_bc[:], in_=lnw_h[None, :].broadcast_to([128, D])),
         inc=s_wc.inc(16))
    q.op("gpsimd", lambda g: g.dma_start(out=bg_bc[:], in_=bg_h[None, :].broadcast_to([128, E])),
         inc=s_wc.inc(16))
    if not cfg.sparse:
        q.op("gpsimd", lambda g: g.dma_start(out=b1a_sb[:], in_=b1a_h[:]), inc=s_wc.inc(16))
        q.op("gpsimd", lambda g: g.dma_start(out=b1b_sb[:], in_=b1b_h[:]), inc=s_wc.inc(16))
    q.op("gpsimd", lambda g: g.dma_start(out=b2_sb[:], in_=b2_h[:]), inc=s_wc.inc(16))
    q.op("gpsimd", lambda g: g.dma_start(out=ident_sb[:], in_=ident_h[:]), inc=s_wc.inc(16))
    q.op("gpsimd", lambda g: g.dma_start(out=amask_sb[:], in_=amask_h[:]), inc=s_wc.inc(16))
    q.op("gpsimd", lambda g: g.memset(neg_sb[:], -1e30), inc=s_ms.inc(1))
    q.op("gpsimd", lambda g: g.memset(epsln_c[:], float(cfg.eps_ln)), inc=s_ms.inc(1))
    q.op("gpsimd", lambda g: g.memset(epsrms_c[:], float(cfg.eps_rms)), inc=s_ms.inc(1))
    if cfg.sparse:
        q.op("gpsimd", lambda g: g.memset(sent_sb[:], float(2 * NBS)), inc=s_ms.inc(1))
        q.op("gpsimd", lambda g: g.iota(ubase[:], [[S, NB], [1, SEGB]],
                                        channel_multiplier=SEGB,
                                        allow_small_or_imprecise_dtypes=True), inc=s_ms.inc(1))
        q.op("gpsimd", lambda g: g.iota(e_iota[:], [[1, E]], channel_multiplier=0,
                                        allow_small_or_imprecise_dtypes=True), inc=s_ms.inc(1))
        q.op("gpsimd", lambda g: g.iota(rankc[:], [[0, 1]], channel_multiplier=1,
                                        allow_small_or_imprecise_dtypes=True), inc=s_ms.inc(1))
        q.op("gpsimd", lambda g: g.iota(pcol[:], [[0, 1]], channel_multiplier=1,
                                        allow_small_or_imprecise_dtypes=True), inc=s_ms.inc(1))
        q.op("gpsimd", lambda g: g.iota(grow[:], [[1, G]], channel_multiplier=0,
                                        allow_small_or_imprecise_dtypes=True), inc=s_ms.inc(1))
        q.op("gpsimd", lambda g: g.memset(zrog[:], 0.0), inc=s_ms.inc(1))
        q.op("gpsimd", lambda g: g.memset(rzro[:], 0.0), inc=s_ms.inc(1))
        q.op("gpsimd", lambda g: g.memset(pdg[:], 0.0), inc=s_ms.inc(1))
        q.op("gpsimd", lambda g: g.memset(ones_r[:], 1.0), inc=s_ms.inc(1))
        BND = {}
        _bnd_gen = [0]

        def _setup_bounds(e):
            vals = dict(tbl=G * GRP - 1, gath=NBS - 1, scat=2 * NBS - 1,
                        w1=E * NFG * 128 - 1,
                        w2=E * NHc * NFG * 128 - 1, b1=E * 128 - 1)
            gen = _bnd_gen[0]
            _bnd_gen[0] += 1
            ins = None
            for kk, vv in vals.items():
                r = e.register("bnd%d_%s" % (gen, kk)).__enter__()
                ins = e.reg_mov(r, vv)
                BND[kk] = e.snap(r)
            return ins
        q.op("gpsimd", _setup_bounds)
        ms_init = s_ms.n
        q.op("vector", lambda e: e.tensor_scalar(
            rankc[:], rankc[:], 63.5, 0.0, ALU.is_gt, ALU.add),
            waits=[(s_ms.h, ms_init)], inc=s_dv.inc())
        dvx = s_dv.n
        q.op("vector", lambda e: e.tensor_scalar_mul(rankcS[:], rankc[:], float(S)),
             waits=[w(s_dv, dvx)], inc=s_dv.inc())
        dvx = s_dv.n
        q.op("vector", lambda e: e.tensor_scalar_mul(rankcN[:], rankc[:], float(NBS)),
             waits=[w(s_dv, dvx)], inc=s_dv.inc())
        dvx = s_dv.n
        q.op("vector", lambda e: e.tensor_scalar_mul(grow[:], grow[:], float(GRP)),
             waits=[w(s_dv, dvx)], inc=s_dv.inc())
        RT_INIT_DV = s_dv.n
        nfill = (G * GRP * 4 + (128 * 256) - 1) // (128 * 256)
        flat = tbl_h.rearrange("a b -> (a b)")
        for fi in range(nfill):
            lo = fi * 128 * 256
            hi = min((fi + 1) * 128 * 256, G * GRP * 4)
            q.op("sync", (lambda flat, lo, hi: lambda e: e.dma_start(
                out=flat[lo:hi].rearrange("(p j) -> p j", p=128),
                in_=sent_sb[:, 0:(hi - lo) // 128]))(flat, lo, hi),
                waits=[(s_ms.h, ms_init)], inc=s_fill.inc(16))
        TBL_FILL = s_fill.n
        pre_jobs = []
        NKH = 2 if KD % 2 == 0 and KD * FGWc > 2048 else 1
        KH = KD // NKH
        for (srcw, dstw) in ((w1a_h, w1a_sw), (w1b_h, w1b_sw)):
            for ei in range(E):
                for fg in range(NFG):
                    for kh in range(NKH):
                        pre_jobs.append((
                            srcw[ei, kh * KH * 128:(kh + 1) * KH * 128,
                                 fg * FGWc:(fg + 1) * FGWc].rearrange(
                                "(k p) f -> p k f", p=128),
                            dstw[(ei * NFG + fg) * 128:(ei * NFG + fg + 1) * 128,
                                 kh * KH * FGWc:(kh + 1) * KH * FGWc],
                            KH * FGWc))
        for ei in range(E):
            for half in range(NHc):
                for fg in range(NFG):
                    row = ((ei * NHc + half) * NFG + fg) * 128
                    pre_jobs.append((
                        w2_h[ei, fg * FGWc:(fg + 1) * FGWc,
                             half * DHc:(half + 1) * DHc].rearrange(
                            "(fl p) d -> p fl d", p=128),
                        w2_sw[row:row + 128, :],
                        FG * DHc))
        pre_state = dict(free=[0, 0], n=[0, 0], i=0)

        def emit_pre(count):
            for _ in range(count):
                if pre_state["i"] >= len(pre_jobs):
                    return
                srcap, dstap, ncols = pre_jobs[pre_state["i"]]
                sl = pre_state["i"] % 2
                pre_state["i"] += 1
                q.op("gpsimd", (lambda srcap, ncols, sl: lambda e: e.dma_start(
                    out=pcv[sl][:, 0:ncols].rearrange(
                        "p (a b) -> p a b", a=srcap.shape[1]),
                    in_=srcap))(srcap, ncols, sl),
                    waits=[w(s_pro[sl], pre_state["free"][sl])], inc=s_pri[sl].inc(16))
                pre_state["n"][sl] = s_pri[sl].n
                q.op("sync", (lambda dstap, ncols, sl: lambda e: e.dma_start(
                    out=dstap, in_=pcv[sl][:, 0:ncols]))(dstap, ncols, sl),
                    waits=[(s_pri[sl].h, pre_state["n"][sl])], inc=s_pro[sl].inc(16))
                pre_state["free"][sl] = s_pro[sl].n
        PRE_TOTAL = None
    WC = s_wc.n
    wcw = (s_wc.h, WC)
    wcm = (s_ms.h, s_ms.n)

    dbg_cols = [0, 0]
    dbg_xnt = 0
    x_free = [0, 0]
    xnf_free = [0, 0]
    xn2f_free = [0, 0]
    xn2Tslot_free = [0, 0]
    sev_rec = [0, 0]
    we_done = [0] * NB
    smm_last = [0] * NB
    wqkv_carry = [0]

    TBL_DONE = [0]
    RT_SPILL_N = [0]
    SPN_END = [0, 0]

    def emit_rt_spill(b):
        dvc = [s_dv.n]

        def vq(fn, extra=()):
            q.op("vector", fn, waits=[w(s_dv, dvc[0])] + [x for x in extra if x], inc=s_dv.inc())
            dvc[0] = s_dv.n

        vq(lambda e: e.tensor_copy(rtn[:, 0:2 * NT], gidx[:]), extra=[w(s_dv, dv_gidx)])
        vq(lambda e: e.tensor_copy(rtn[:, 2 * NT:3 * NT], glt[:, 2 * NT:3 * NT]),
           extra=[w(s_dv, dv_we)])
        vq(lambda e: e.tensor_copy(rtn[:, 3 * NT:4 * NT], glt[:, 3 * NT:4 * NT]))
        rdy = dvc[0]
        q.op("tensor", lambda e: e.transpose(
            ps_o0[0:4 * NT, 0:128], rtn[:], ident_sb[:]),
            waits=[w(s_dv, rdy), wcw] + pwait("o0h0"), inc=s_pe.inc())
        pe_rt = s_pe.n
        vq(lambda e: e.tensor_copy(rtv[:], ps_o0[0:4 * NT, 0:128]),
           extra=[w(s_pe, pe_rt),
                  (s_rt.h, RT_SPILL_N[0]) if RT_SPILL_N[0] else None])
        pread("o0h0", "dv", s_dv.n)
        q.op("sync", (lambda b: lambda e: e.dma_start(
            out=rt_h[b], in_=rtv[:]))(b), waits=[w(s_dv, dvc[0])], inc=s_rt.inc(16))
        RT_SPILL_N[0] = s_rt.n

    def emit_routing():
        dvc = [s_dv.n]

        def vq(fn, extra=()):
            q.op("vector", fn, waits=[w(s_dv, dvc[0])] + [x for x in extra if x], inc=s_dv.inc())
            dvc[0] = s_dv.n

        for b in range(NB):
            flat = rt_h[b].rearrange("a b -> (a b)")
            q.op("sync", (lambda flat, b: lambda e: e.dma_start(
                out=e12g[:, b * SEGB:(b + 1) * SEGB],
                in_=flat[0:2 * S].rearrange("(p j) -> p j", p=128)))(flat, b),
                waits=[(s_rt.h, RT_SPILL_N[0])] if b == 0 else [], inc=s_rt.inc(16))
            q.op("sync", (lambda flat, b: lambda e: e.dma_start(
                out=w12g[:, b * SEGB:(b + 1) * SEGB],
                in_=flat[2 * S:4 * S].rearrange("(p j) -> p j", p=128)))(flat, b),
                inc=s_rt.inc(16))
        rt_ld = s_rt.n

        pdg4 = pdg[:].rearrange("p (j f) -> p j f", f=4)
        vq(lambda e: e.tensor_scalar(
            tmpg[:], ubase[:], rankcS[:], 0.0, ALU.subtract, ALU.add),
            extra=[(s_rt.h, rt_ld), (s_ms.h, s_ms.n), w(s_dv, RT_INIT_DV)])
        vq(lambda e: e.tensor_copy(pdg4[:, :, 0:1].squeeze(2), tmpg[:]))
        vq(lambda e: e.tensor_scalar(
            pdg4[:, :, 1:2].squeeze(2), tmpg[:], rankcN[:], 0.0, ALU.add, ALU.add))
        vq(lambda e: e.tensor_copy(pdg4[:, :, 2:3].squeeze(2), w12g[:]))

        vq(lambda e: e.memset(slag[:], 0.0))
        for ei in range(E):
            vq((lambda ei: lambda e: e.tensor_single_scalar(
                mskg[:], e12g[:], float(ei), ALU.is_equal))(ei))
            vq(lambda e: e.tensor_tensor_scan(
                scng[:], mskg[:], zrog[:], 0.0, ALU.add, ALU.add))
            rdy = dvc[0]
            q.op("tensor", lambda e: e.transpose(
                ps_o0[0:1, 0:128], scng[:, SEG - 1:SEG], ident_sb[:]),
                waits=[w(s_dv, rdy)] + pwait("o0h0"), inc=s_pe.inc())
            pe_t = s_pe.n
            vq(lambda e: e.tensor_copy(rowa[:], ps_o0[0:1, 0:128]),
               extra=[w(s_pe, pe_t)])
            pread("o0h0", "dv", s_dv.n)
            vq(lambda e: e.tensor_tensor_scan(
                rowb[:], rowa[:], rzro[:], 0.0, ALU.add, ALU.add))
            vq((lambda ei: lambda e: e.tensor_copy(
                crow[0:1, ei:ei + 1], rowb[0:1, 127:128]))(ei))
            vq(lambda e: e.tensor_sub(rowa[:], rowb[:], rowa[:]))
            rdy = dvc[0]
            q.op("tensor", lambda e: e.transpose(
                ps_o0[:, 128:129], rowa[:], ident_sb[0:1, 0:1]),
                waits=[w(s_dv, rdy)] + pwait("o0h0"), inc=s_pe.inc())
            pe_t = s_pe.n
            vq(lambda e: e.tensor_copy(segoff[:], ps_o0[:, 128:129]),
               extra=[w(s_pe, pe_t)])
            pread("o0h0", "dv", s_dv.n)
            vq(lambda e: e.tensor_scalar(
                tmpg[:], scng[:], segoff[:], -1.0, ALU.add, ALU.add))
            vq(lambda e: e.tensor_mul(tmpg[:], tmpg[:], mskg[:]))
            vq(lambda e: e.tensor_add(slag[:], slag[:], tmpg[:]))

        vq(lambda e: e.tensor_copy(cri[0:1, 0:E], crow[0:1, 0:E]))
        vq(lambda e: e.tensor_scalar_add(cri[0:1, 0:E], cri[0:1, 0:E], GRP - 1))
        vq(lambda e: e.tensor_single_scalar(
            cri[0:1, 0:E], cri[0:1, 0:E], cfg.LG, ALU.arith_shift_right))
        vq(lambda e: e.tensor_single_scalar(
            cri[0:1, 0:E], cri[0:1, 0:E], cfg.LG, ALU.arith_shift_left))
        vq(lambda e: e.tensor_copy(crow[0:1, 8:8 + E], cri[0:1, 0:E]))
        vq(lambda e: e.tensor_tensor_scan(
            crow[0:1, 16:16 + E], crow[0:1, 8:8 + E], rzro[0:1, 0:E], 0.0,
            ALU.add, ALU.add))
        vq(lambda e: e.tensor_sub(crow[0:1, 24:24 + E], crow[0:1, 16:16 + E],
                                  crow[0:1, 8:8 + E]))
        rdy = dvc[0]
        q.op("tensor", lambda e: e.matmul(
            ps_o0[:, 0:E], ones_r[:], crow[0:1, 24:24 + E], start=True, stop=True),
            waits=[w(s_dv, rdy), (s_ms.h, s_ms.n)] + pwait("o0h0"), inc=s_pe.inc())
        pe_t = s_pe.n
        vq(lambda e: e.tensor_copy(basec[:, 0:E], ps_o0[:, 0:E]), extra=[w(s_pe, pe_t)])
        pread("o0h0", "dv", s_dv.n)

        vq(lambda e: e.memset(egf[:], -1.0))
        for ei in range(E):
            vq((lambda ei: lambda e: e.tensor_single_scalar(
                etmp[:], grow[:], crow[0:1, 24 + ei:25 + ei], ALU.is_ge))(ei))
            vq(lambda e: e.tensor_add(egf[:], egf[:], etmp[:]))
        vq(lambda e: e.tensor_copy(egrp_i[0:1, 0:G], egf[:]))
        rdy = dvc[0]
        q.op("tensor", lambda e: e.matmul(
            ps_o0[:, 256:256 + G], ones_r[:], egf[:], start=True, stop=True),
            waits=[w(s_dv, rdy), (s_ms.h, s_ms.n)] + pwait("o0h0"), inc=s_pe.inc())
        pe_t = s_pe.n
        vq(lambda e: e.tensor_copy(ebase[:], ps_o0[:, 256:256 + G]),
           extra=[w(s_pe, pe_t)])
        pread("o0h0", "dv", s_dv.n)
        NHl = D // min(512, D)
        DHl = min(512, D)
        vq(lambda e: e.tensor_scalar(eb2[:], ebase[:], 128.0, pcol[:],
                                     ALU.mult, ALU.add))
        vq(lambda e: e.tensor_copy(b1x_i[:, 0:G], eb2[:]))
        vq(lambda e: e.tensor_scalar(eb2[:], ebase[:], float(NFG * 128), pcol[:],
                                     ALU.mult, ALU.add))
        wv = widx_i[:, 0:G * NFG].rearrange("p (g c) -> p g c", c=NFG)
        for fg in range(NFG):
            vq((lambda fg, wv: lambda e: e.tensor_scalar_add(
                wv[:, :, fg:fg + 1].squeeze(2), eb2[:], float(fg * 128)))(fg, wv))
        vq(lambda e: e.tensor_scalar(eb2[:], ebase[:], float(NHl * NFG * 128), pcol[:],
                                     ALU.mult, ALU.add))
        w2v = w2x_i[:, 0:G * NHl * NFG].rearrange("p (g c) -> p g c", c=NHl * NFG)
        for c in range(NHl * NFG):
            vq((lambda c, w2v: lambda e: e.tensor_scalar_add(
                w2v[:, :, c:c + 1].squeeze(2), eb2[:], float(c * 128)))(c, w2v))
        EGRP_DV = dvc[0]

        for ei in range(E):
            vq((lambda ei: lambda e: e.tensor_single_scalar(
                mskg[:], e12g[:], float(ei), ALU.is_equal))(ei))
            vq((lambda ei: lambda e: e.tensor_scalar_mul(
                tmpg[:], mskg[:], basec[:, ei:ei + 1]))(ei))
            vq(lambda e: e.tensor_add(slag[:], slag[:], tmpg[:]))
        vq(lambda e: e.tensor_copy(sloti[:], slag[:]))
        rdy = dvc[0]
        for j in range(SEG):
            q.op("gpsimd", (lambda j: lambda e: e.indirect_dma_start(
                out=tbl_h,
                out_offset=bass.IndirectOffsetOnAxis(ap=sloti[:, j:j + 1], axis=0),
                in_=pdg[:, j * 4:(j + 1) * 4], in_offset=None,
                bounds_check=BND["tbl"], oob_is_err=False))(j),
                waits=[(s_fill.h, TBL_FILL), w(s_dv, rdy)] if j == 0 else [],
                inc=s_tbl.inc(16))
        TBL_DONE[0] = s_tbl.n

    for b in range(NB):
        logit_guard = 0

        for tt in range(NT):
            sl = tt % 2
            par = tt % 2
            row0 = b * S + tt * 128
            cb = 16 * par
            q.op("sync", (lambda r, sl: lambda e: e.dma_start(
                out=xin[sl][:], in_=x_h[r:r + 128, :]))(r=row0, sl=sl),
                waits=[w(s_dv, x_free[sl])], inc=s_x[sl].inc(16))
            xw = [(s_x[sl].h, s_x[sl].n)]
            q.op("scalar", (lambda sl, cb: lambda e: e.activation(
                sqs[:], xin[sl][:], AF.Square, accum_out=cols[:, cb + 1:cb + 2]))(sl, cb),
                waits=xw + [w(s_dv, x_free[sl])] +
                ([w(s_dbg, dbg_cols[par])] if cfg.debug else []), inc=s_ac.inc())
            ac_sq = s_ac.n
            q.op("vector", (lambda sl, cb: lambda e: e.reduce_sum(
                cols[:, cb:cb + 1], xin[sl][:], axis=AX.X))(sl, cb),
                waits=xw + ([w(s_dbg, dbg_cols[par])] if cfg.debug else []),
                inc=s_dv.inc())
            dv_rs = s_dv.n
            q.op("vector", (lambda cb: lambda e: e.tensor_scalar_mul(
                cols[:, cb + 2:cb + 3], cols[:, cb:cb + 1], 1.0 / D))(cb),
                waits=[w(s_dv, dv_rs)], inc=s_dv.inc())
            dv_mean = s_dv.n
            q.op("vector", (lambda cb: lambda e: e.tensor_scalar_mul(
                cols[:, cb + 5:cb + 6], cols[:, cb + 1:cb + 2], 1.0 / D))(cb),
                waits=[w(s_ac, ac_sq)], inc=s_dv.inc())
            dv_ex2 = s_dv.n
            q.op("vector", (lambda cb: lambda e: e.scalar_tensor_tensor(
                cols[:, cb + 6:cb + 7], cols[:, cb + 2:cb + 3], cols[:, cb + 2:cb + 3],
                cols[:, cb + 5:cb + 6], ALU.mult, ALU.subtract))(cb),
                waits=[w(s_dv, dv_mean), w(s_dv, dv_ex2)], inc=s_dv.inc())
            dv_nv = s_dv.n
            q.op("scalar", (lambda cb: lambda e: e.activation(
                cols[:, cb + 3:cb + 4], cols[:, cb + 6:cb + 7], AF.Sqrt,
                scale=-1.0, bias=epsln_c[:]))(cb),
                waits=[w(s_dv, dv_nv), wcw, wcm], inc=s_ac.inc())
            ac_sd = s_ac.n
            q.op("vector", (lambda cb: lambda e: e.reciprocal(
                cols[:, cb + 3:cb + 4], cols[:, cb + 3:cb + 4]))(cb),
                waits=[w(s_ac, ac_sd)], inc=s_dv.inc())
            dv_rstd = s_dv.n
            q.op("vector", (lambda sl, par, cb: lambda e: e.tensor_scalar(
                xnf[par][:], xin[sl][:], cols[:, cb + 2:cb + 3], cols[:, cb + 3:cb + 4],
                ALU.subtract, ALU.mult))(sl, par, cb),
                waits=[w(s_pe, xnf_free[par]), w(s_dv, dv_rstd)], inc=s_dv.inc())
            dv_xn = s_dv.n
            x_free[sl] = dv_xn
            q.op("scalar", (lambda par, cb: lambda e: e.activation(
                sqs[:], xnf[par][:], AF.Square, accum_out=cols[:, cb + 7:cb + 8]))(par, cb),
                waits=[w(s_dv, dv_xn)], inc=s_ac.inc())
            ac_s2 = s_ac.n
            q.op("scalar", (lambda cb: lambda e: e.activation(
                cols[:, cb + 4:cb + 5], cols[:, cb + 7:cb + 8], AF.Sqrt,
                scale=1.0 / D, bias=epsrms_c[:]))(cb),
                waits=[w(s_ac, ac_s2)], inc=s_ac.inc())
            ac_sd2 = s_ac.n
            q.op("vector", (lambda cb: lambda e: e.reciprocal(
                cols[:, cb + 4:cb + 5], cols[:, cb + 4:cb + 5]))(cb),
                waits=[w(s_ac, ac_sd2)], inc=s_dv.inc())
            dv_rstd2 = s_dv.n
            q.op("vector", (lambda par, cb: lambda e: e.scalar_tensor_tensor(
                xn2f[par][:], xnf[par][:], cols[:, cb + 4:cb + 5], lnw_bc[:],
                ALU.mult, ALU.mult))(par, cb),
                waits=[wcw, wcm, w(s_pe, xn2f_free[par]), w(s_dv, dv_rstd2)] +
                      ([(s_spn[par].h, s_spn[par].n)] if cfg.sparse else []),
                inc=s_dv.inc())
            dv_xn2 = s_dv.n
            if cfg.sparse:
                q.op("sync", (lambda par, row0: lambda e: e.dma_start(
                    out=xn2n_h[row0:row0 + 128, :], in_=xn2f[par][:]))(par, row0),
                    waits=[w(s_dv, dv_xn2)], inc=s_spn[par].inc(16))

            if cfg.debug:
                q.op("sync", (lambda b, tt, cb: lambda e: e.dma_start(
                    out=colso_h[b * NT + tt], in_=cols[:, cb:cb + 8]))(b, tt, cb),
                    waits=[w(s_dv, dv_xn2)], inc=s_dbg.inc(16))
                dbg_cols[par] = s_dbg.n

            for which, srcf, dstT, dvready in ((0, xnf[par], xnT, dv_xn),
                                               (1, xn2f[par], xn2T, dv_xn2)):
                for c0 in range(0, KD, 4):
                    half = (c0 // 4) % 2
                    reg = "o0h%d" % half
                    n = min(4, KD - c0)
                    for j in range(n):
                        q.op("tensor", (lambda srcf, c0, j, half: lambda e: e.transpose(
                            ps_o0[:, half * 512 + j * 128: half * 512 + (j + 1) * 128],
                            srcf[:, (c0 + j) * 128: (c0 + j + 1) * 128], ident_sb[:]))(
                                srcf, c0, j, half),
                            waits=([w(s_dv, dvready), wcw, wcm] + pwait(reg) +
                                   ([w(s_dbg, dbg_xnt)] if cfg.debug else [])) if j == 0 else [],
                            inc=s_pe.inc() if j == n - 1 else None)
                    pe_tr = s_pe.n
                    extra = []
                    if which == 1 and c0 == 0 and not cfg.sparse:
                        extra = [w(s_spx, 16 * KD * b)]
                    if which == 1 and c0 == 0 and cfg.sparse:
                        extra = [w(s_pe, xn2Tslot_free[par])]
                    ttc = (tt % 2) if (which == 1 and cfg.sparse) else tt
                    q.op("vector", (lambda dstT, c0, n, half, ttc: lambda e: e.tensor_copy(
                        dstT[:].rearrange("p (k s) -> p k s", k=KD)[
                            :, c0:c0 + n, ttc * 128:(ttc + 1) * 128],
                        ps_o0[:, half * 512: half * 512 + n * 128].rearrange(
                            "p (k s) -> p k s", k=n)))(dstT, c0, n, half, ttc),
                        waits=[w(s_pe, pe_tr)] + extra, inc=s_dv.inc())
                    pread(reg, "dv", s_dv.n)
                if which == 0:
                    xnf_free[par] = pe_tr
                    xnT_done = s_dv.n
                else:
                    xn2f_free[par] = pe_tr
                    xn2T_done = s_dv.n

            x2b = (tt % 2) * 128 if cfg.sparse else tt * 128
            for k in range(KD):
                q.op("tensor", (lambda k, x2b: lambda e: e.matmul(
                    ps_o1[:, 0:E],
                    xn2T[:, k * SW2 + x2b: k * SW2 + x2b + 128],
                    wg_sb[:, k * E:(k + 1) * E],
                    start=(k == 0), stop=(k == KD - 1)))(k, x2b),
                    waits=([w(s_dv, xn2T_done)] + pwait("o1h0")) if k == 0 else [],
                    inc=s_pe.inc() if k == KD - 1 else None)
            pe_lg = s_pe.n
            xn2Tslot_free[tt % 2] = pe_lg
            q.op("vector", (lambda tt: lambda e: e.tensor_add(
                logits[:, tt * E:(tt + 1) * E], ps_o1[:, 0:E], bg_bc[:]))(tt),
                waits=[w(s_pe, pe_lg), wcw, wcm], inc=s_dv.inc())
            pread("o1h0", "dv", s_dv.n)
            lg_done = s_dv.n

        if cfg.debug:
            q.op("sync", (lambda b: lambda e: e.dma_start(
                out=lgo_h[b], in_=logits[:]))(b),
                waits=[w(s_dv, s_dv.n)], inc=s_dbg.inc(16))
            dbg_lg = s_dbg.n
        lg3 = logits[:].rearrange("p (t e) -> p t e", e=E)
        m1 = glt[:, 0 * NT:1 * NT]
        m2 = glt[:, 1 * NT:2 * NT]
        w1c = glt[:, 2 * NT:3 * NT]
        w2c = glt[:, 3 * NT:4 * NT]
        eq1 = geq[:, 0:NT * E]
        eq2 = geq[:, NT * E:2 * NT * E]
        eq13 = eq1.rearrange("p (t e) -> p t e", e=E)
        eq23 = eq2.rearrange("p (t e) -> p t e", e=E)
        q.op("vector", lambda e: e.reduce_max(m1, lg3, axis=AX.X),
             waits=[w(s_dv, lg_done)] + ([w(s_dbg, dbg_lg)] if cfg.debug else []),
             inc=s_dv.inc())
        dv_m1 = s_dv.n
        q.op("vector", lambda e: e.tensor_tensor(
            eq13, lg3, m1.unsqueeze(2).broadcast_to([128, NT, E]), ALU.is_equal),
            waits=[w(s_dv, dv_m1)], inc=s_dv.inc())
        dv_eq1 = s_dv.n
        q.op("vector", lambda e: e.scalar_tensor_tensor(
            eq2, eq1, -1e30, logits[:], ALU.mult, ALU.add),
            waits=[w(s_dv, dv_eq1)], inc=s_dv.inc())
        dv_msk = s_dv.n
        q.op("vector", lambda e: e.reduce_max(m2, eq23, axis=AX.X),
             waits=[w(s_dv, dv_msk)], inc=s_dv.inc())
        dv_m2 = s_dv.n
        q.op("vector", lambda e: e.tensor_sub(w1c, m1, m2), waits=[w(s_dv, dv_m2)],
             inc=s_dv.inc())
        dv_w1 = s_dv.n
        q.op("scalar", lambda e: e.activation(w1c, w1c, AF.Sigmoid),
             waits=[w(s_dv, dv_w1)], inc=s_ac.inc())
        ac_w1 = s_ac.n
        q.op("vector", lambda e: e.tensor_tensor(
            eq23, eq23, m2.unsqueeze(2).broadcast_to([128, NT, E]), ALU.is_equal),
            waits=[w(s_ac, ac_w1)], inc=s_dv.inc())
        dv_eq2 = s_dv.n
        if cfg.sparse:
            eb = e_iota[:].unsqueeze(1).broadcast_to([128, NT, E])
            gsc3 = gsc[:].rearrange("p (t e) -> p t e", e=E)
            q.op("vector", lambda e: e.tensor_tensor(gsc3, eq13, eb, ALU.mult),
                 waits=[w(s_dv, dv_eq2), (s_ms.h, s_ms.n)], inc=s_dv.inc())
            dvtmp = s_dv.n
            q.op("vector", lambda e: e.reduce_sum(gidx[:, 0:NT], gsc3, axis=AX.X),
                 waits=[w(s_dv, dvtmp)], inc=s_dv.inc())
            dvtmp = s_dv.n
            q.op("vector", lambda e: e.tensor_tensor(gsc3, eq23, eb, ALU.mult),
                 waits=[w(s_dv, dvtmp)], inc=s_dv.inc())
            dvtmp = s_dv.n
            q.op("vector", lambda e: e.reduce_sum(gidx[:, NT:2 * NT], gsc3, axis=AX.X),
                 waits=[w(s_dv, dvtmp)], inc=s_dv.inc())
            dv_gidx = s_dv.n
        q.op("vector", lambda e: e.tensor_scalar(w2c, w1c, -1.0, 1.0, ALU.mult, ALU.add),
             inc=s_dv.inc())
        dv_w2 = s_dv.n
        q.op("vector", lambda e: e.tensor_tensor(
            eq13, eq13, w1c.unsqueeze(2).broadcast_to([128, NT, E]), ALU.mult),
            waits=[w(s_dv, dv_eq2)], inc=s_dv.inc())
        dv_p1 = s_dv.n
        q.op("vector", lambda e: e.tensor_tensor(
            eq23, eq23, w2c.unsqueeze(2).broadcast_to([128, NT, E]), ALU.mult),
            waits=[w(s_dv, dv_w2)], inc=s_dv.inc())
        dv_p2 = s_dv.n
        q.op("vector", (lambda b: lambda e: e.tensor_add(
            we_all[:, b * NT * E:(b + 1) * NT * E], eq1, eq2))(b),
            waits=[w(s_dv, dv_p1), w(s_dv, dv_p2)], inc=s_dv.inc())
        dv_we = s_dv.n

        for c0 in range(0, NT, 4):
            half = (c0 // 4) % 2
            reg = "o0h%d" % half
            n = min(4, NT - c0)
            for j in range(n):
                q.op("tensor", (lambda b, c0, j, half: lambda e: e.transpose(
                    ps_o0[0:E, half * 512 + j * 128: half * 512 + (j + 1) * 128],
                    we_all[:, (b * NT + c0 + j) * E:(b * NT + c0 + j + 1) * E],
                    ident_sb[:]))(b, c0, j, half),
                    waits=([w(s_dv, dv_we)] + pwait(reg)) if j == 0 else [],
                    inc=s_pe.inc() if j == n - 1 else None)
            pe_wt = s_pe.n
            q.op("vector", (lambda b, c0, n, half: lambda e: e.tensor_copy(
                weT_all[0:E, b * S + c0 * 128: b * S + (c0 + n) * 128],
                ps_o0[0:E, half * 512: half * 512 + n * 128]))(b, c0, n, half),
                waits=[w(s_pe, pe_wt)], inc=s_dv.inc())
            pread(reg, "dv", s_dv.n)
        we_done[b] = s_dv.n

        if cfg.sparse:
            emit_rt_spill(b)
            if b == NB - 1:
                emit_routing()
            emit_pre((len(pre_jobs) + NB - 1) // NB)

        if cfg.debug:
            for k in range(KD):
                q.op("sync", (lambda b, k: lambda e: e.dma_start(
                    out=xnso_h[k, :, b * S:(b + 1) * S],
                    in_=xnT[:, k * S:(k + 1) * S]))(b, k),
                    waits=[w(s_dv, xnT_done)] if k == 0 else [], inc=s_dbg.inc(16))
            dbg_xnt = s_dbg.n
        if not cfg.sparse:
            for k in range(KD):
                q.op("sync", (lambda b, k: lambda e: e.dma_start(
                    out=xn2s_h[k, :, b * S:(b + 1) * S],
                    in_=xn2T[:, k * S:(k + 1) * S]))(b, k),
                    waits=[w(s_dv, xn2T_done)] if k == 0 else [], inc=s_spx.inc(16))

        wqkv_last_pe = wqkv_carry[0]
        if not cfg.only_ln:
            for (wsrc, kk, dstT) in ((wq_h, KD, qT_sb), (wk_h, KD, kT_sb), (wvd_h, KD, vdT_sb)):
                q.op("gpsimd", (lambda wsrc, kk: ld3(wqkv_sb, wsrc, kk))(wsrc, kk),
                     waits=[w(s_pe, wqkv_last_pe)], inc=s_wqkv.inc(16))
                wq_n = s_wqkv.n
                for m in range(KQ):
                    for c in range(0, S, 512):
                        cw = min(512, S - c)
                        half = (c // 512) % 2
                        reg = "o1h%d" % half
                        for k in range(KD):
                            q.op("tensor", (lambda m, c, cw, k, half: lambda e: e.matmul(
                                ps_o1[:, half * 512: half * 512 + cw],
                                wqkv_sb[:, k * DQ + m * 128: k * DQ + (m + 1) * 128],
                                xnT[:, k * S + c: k * S + c + cw],
                                start=(k == 0), stop=(k == KD - 1)))(m, c, cw, k, half),
                                waits=([w(s_dv, xnT_done), (s_wqkv.h, wq_n)] + pwait(reg))
                                if k == 0 else [],
                                inc=s_pe.inc() if k == KD - 1 else None)
                        pe_g = s_pe.n
                        q.op("vector", (lambda dstT, m, c, cw, half: lambda e: e.tensor_copy(
                            dstT[:, m * S + c: m * S + c + cw],
                            ps_o1[:, half * 512: half * 512 + cw]))(dstT, m, c, cw, half),
                            waits=[w(s_pe, pe_g)], inc=s_dv.inc())
                        pread(reg, "dv", s_dv.n)
                wqkv_last_pe = s_pe.n
        qkv_done = s_dv.n

        if not cfg.only_ln:
            q.op("gpsimd", ld3(wqkv_sb, wvu_h, KQ),
                 waits=[w(s_pe, wqkv_last_pe)], inc=s_wqkv.inc(16))
            wvu_n = s_wqkv.n
        for tt in range(0 if cfg.only_ln else NT):
            regs = ("o0h0", "o0h1") if tt % 2 == 0 else ("o1h0", "o1h1")
            pst = ps_o0 if tt % 2 == 0 else ps_o1
            nmm = 0
            tot = ((D + 511) // 512) * KQ
            for h in range(0, D, 512):
                hw = min(512, D - h)
                for kq in range(KQ):
                    nmm += 1
                    q.op("tensor", (lambda tt, h, hw, kq, pst: lambda e: e.matmul(
                        pst[:, h:h + hw],
                        vdT_sb[:, kq * S + tt * 128: kq * S + (tt + 1) * 128],
                        wqkv_sb[:, kq * D + h: kq * D + h + hw],
                        start=(kq == 0), stop=(kq == KQ - 1)))(tt, h, hw, kq, pst),
                        waits=(pwait(*regs) + [w(s_dv, qkv_done), (s_wqkv.h, wvu_n)])
                        if nmm == 1 else [],
                        inc=s_pe.inc() if nmm == tot else None)
            pe_v = s_pe.n
            q.op("vector", (lambda tt, pst: lambda e: e.tensor_copy(
                vnat[:, tt * D:(tt + 1) * D], pst[:, 0:D]))(tt, pst),
                waits=[w(s_pe, pe_v)], inc=s_dv.inc())
            for r in regs:
                pread(r, "dv", s_dv.n)
        if not cfg.only_ln:
            wqkv_carry[0] = s_pe.n

        expb_free = 0

        def emit_scores(tt):
            ncols = (tt + 1) * 128
            nblk = (ncols + 511) // 512
            for cbk in range(nblk):
                cw = min(512, ncols - cbk * 512)
                for kq in range(KQ):
                    q.op("tensor", (lambda tt, cbk, cw, kq: lambda e: e.matmul(
                        ps_big[:, cbk * 512: cbk * 512 + cw],
                        qT_sb[:, kq * S + tt * 128: kq * S + (tt + 1) * 128],
                        kT_sb[:, kq * S + cbk * 512: kq * S + cbk * 512 + cw],
                        start=(kq == 0), stop=(kq == KQ - 1)))(tt, cbk, cw, kq),
                        waits=pwait("bigA") if (cbk == 0 and kq == 0) else [],
                        inc=s_pe.inc() if (cbk == nblk - 1 and kq == KQ - 1) else None)
            return s_pe.n

        def emit_softmax(tt, pe_sc):
            ncols = (tt + 1) * 128
            q.op("vector", (lambda tt: lambda e: e.copy_predicated(
                ps_big[:, tt * 128:(tt + 1) * 128], amask_sb[:], neg_sb[:]))(tt),
                waits=[w(s_pe, pe_sc), wcw, wcm], inc=s_dv.inc())
            dv_mask = s_dv.n
            pread("bigA", "dv", dv_mask)
            q.op("vector", (lambda ncols: lambda e: e.reduce_max(
                cols[:, 16:17], ps_big[:, 0:ncols], axis=AX.X, negate=True))(ncols),
                waits=[w(s_dv, dv_mask)], inc=s_dv.inc())
            dv_mx = s_dv.n
            pread("bigA", "dv", dv_mx)
            q.op("scalar", lambda e: e.activation(
                cols[:, 17:18], cols[:, 16:17], AF.Copy, scale=float(cfg.scale)),
                waits=[w(s_dv, dv_mx)], inc=s_ac.inc())
            ac_cp = s_ac.n
            q.op("scalar", (lambda ncols: lambda e: e.activation(
                expb[:, 0:ncols], ps_big[:, 0:ncols], AF.Exp,
                bias=cols[:, 17:18], scale=float(cfg.scale),
                accum_out=cols[:, 18:19]))(ncols),
                waits=[w(s_pe, expb_free), w(s_ac, ac_cp)] +
                      ([(s_dbg.h, s_dbg.n)] if cfg.debug else []), inc=s_ac.inc())
            ac_exp = s_ac.n
            pread("bigA", "ac", ac_exp)
            q.op("vector", lambda e: e.reciprocal(cols[:, 19:20], cols[:, 18:19]),
                 waits=[w(s_ac, ac_exp)], inc=s_dv.inc())
            dv_rc = s_dv.n
            q.op("vector", (lambda ncols: lambda e: e.tensor_scalar_mul(
                expb[:, 0:ncols], expb[:, 0:ncols], cols[:, 19:20]))(ncols),
                waits=[w(s_dv, dv_rc)], inc=s_dv.inc())
            return s_dv.n

        def emit_trans_smm(tt, dv_attn):
            nonlocal expb_free
            n_tk = tt + 1
            if cfg.debug:
                rr = b * S + tt * 128
                q.op("sync", (lambda rr, n_tk: lambda e: e.dma_start(
                    out=ao_h[rr:rr + 128, 0:n_tk * 128], in_=expb[:, 0:n_tk * 128]))(rr, n_tk),
                    waits=[w(s_dv, dv_attn)], inc=s_dbg.inc(16))
            for c0 in range(0, n_tk, 4):
                half = (c0 // 4) % 2
                reg = "o0h%d" % half
                n = min(4, n_tk - c0)
                for j in range(n):
                    q.op("tensor", (lambda c0, j, half: lambda e: e.transpose(
                        ps_o0[:, half * 512 + j * 128: half * 512 + (j + 1) * 128],
                        expb[:, (c0 + j) * 128: (c0 + j + 1) * 128], ident_sb[:]))(c0, j, half),
                        waits=([w(s_dv, dv_attn)] + pwait(reg)) if j == 0 else [],
                        inc=s_pe.inc() if j == n - 1 else None)
                pe_tr = s_pe.n
                q.op("vector", (lambda c0, n, half: lambda e: e.tensor_copy(
                    attnT[:, c0 * 128:(c0 + n) * 128],
                    ps_o0[:, half * 512: half * 512 + n * 128]))(c0, n, half),
                    waits=[w(s_pe, pe_tr)], inc=s_dv.inc())
                pread(reg, "dv", s_dv.n)
            expb_free = s_pe.n
            attnT_done = s_dv.n
            nmm = 0
            tot = n_tk * ((D + 511) // 512)
            for tk in range(n_tk):
                for h in range(0, D, 512):
                    hw = min(512, D - h)
                    nmm += 1
                    q.op("tensor", (lambda tk, h, hw, n_tk: lambda e: e.matmul(
                        ps_o1[:, h:h + hw],
                        attnT[:, tk * 128:(tk + 1) * 128],
                        vnat[:, tk * D + h: tk * D + h + hw],
                        start=(tk == 0), stop=(tk == n_tk - 1)))(tk, h, hw, n_tk),
                        waits=([w(s_dv, attnT_done)] + pwait("o1h0", "o1h1"))
                        if nmm == 1 else [],
                        inc=s_pe.inc() if nmm == tot else None)
            pe_s = s_pe.n
            sl = tt % 2
            q.op("vector", (lambda sl: lambda e: e.tensor_copy(
                sev[sl][:], ps_o1[:, 0:D]))(sl),
                waits=[w(s_pe, pe_s), (s_se[sl].h, sev_rec[sl]) if sev_rec[sl] else None],
                inc=s_dv.inc())
            pread("o1h0", "dv", s_dv.n)
            pread("o1h1", "dv", s_dv.n)
            dv_sev = s_dv.n
            row0 = b * S + tt * 128
            q.op("sync", (lambda sl, row0: lambda e: e.dma_start(
                out=s_h[row0:row0 + 128, :], in_=sev[sl][:]))(sl, row0),
                waits=[w(s_dv, dv_sev)], inc=s_se[sl].inc(16))
            sev_rec[sl] = s_se[sl].n
            return pe_s

        prev = None
        for tt in range(0 if cfg.only_ln else NT):
            pe_sc = emit_scores(tt)
            if prev is not None:
                smm_last[b] = emit_trans_smm(prev[0], prev[1])
            dv_attn = emit_softmax(tt, pe_sc)
            prev = (tt, dv_attn)
        if prev is not None:
            smm_last[b] = emit_trans_smm(prev[0], prev[1])
        if cfg.debug:
            vnat_dv = s_dv.n
            q.op("sync", (lambda b: lambda e: e.dma_start(
                out=vno_h[b], in_=vnat[:]))(b), waits=[w(s_dv, vnat_dv)], inc=s_dbg.inc(16))
            q.op("sync", (lambda b: lambda e: e.dma_start(
                out=qo_h[b], in_=qT_sb[:]))(b), inc=s_dbg.inc(16))
            q.op("sync", (lambda b: lambda e: e.dma_start(
                out=ko_h[b], in_=kT_sb[:]))(b), inc=s_dbg.inc(16))

    PE_END_A = s_pe.n
    DV_END_A = s_dv.n
    AC_END_A = s_ac.n
    SE_END = (s_se[0].n, s_se[1].n)

    q.run_block(nc)
    stA.close()

    stB = ExitStack()

    def sbufB(name, shape, dt):
        return stB.enter_context(nc.sbuf_tensor(name, shape, dt))

    w1a_b = [sbufB("w1a_b0", [128, KD * FG * 128], BF16),
             sbufB("w1a_b1", [128, KD * FG * 128], BF16)]
    w1b_b = [sbufB("w1b_b0", [128, KD * FG * 128], BF16),
             sbufB("w1b_b1", [128, KD * FG * 128], BF16)]
    xcb = [sbufB("xc0", [128, D], F32), sbufB("xc1", [128, D], F32)]
    scb = [sbufB("scb0", [128, D], F32), sbufB("scb1", [128, D], F32)]
    gsb = sbufB("gsb", [128, D], F32)
    osb = [sbufB("osb0", [128, D], F32), sbufB("osb1", [128, D], F32)]
    if not cfg.sparse:
        xn2B = sbufB("xn2B", [128, KD * S], BF16)
        acc = sbufB("acc", [128, NT * D], F32)
        w2_b = [sbufB("w2_b0", [128, FG * D], BF16),
                sbufB("w2_b1", [128, FG * D], BF16)]
        yT = [sbufB("yT0", [128, FG * TBW], BF16), sbufB("yT1", [128, FG * TBW], BF16)]
        silu_sb = sbufB("silu_sb", [128, TBW], BF16)
    else:
        DHp = min(512, D)
        w2_b = [sbufB("w2_b0", [128, FG * DHp], BF16),
                sbufB("w2_b1", [128, FG * DHp], BF16)]
        yT = [sbufB("yT0", [128, NFL * GRP], BF16), sbufB("yT1", [128, NFL * GRP], BF16)]
        silu_sb = sbufB("silu_sb", [128, GRP], BF16)
        xgT = [sbufB("xgT0", [128, KD * GRP], BF16), sbufB("xgT1", [128, KD * GRP], BF16)]
        xgf = [sbufB("xgf0", [128, D], F32), sbufB("xgf1", [128, D], F32)]
        tb_sb = [sbufB("tb0", [128, NGT * 4], F32), sbufB("tb1", [128, NGT * 4], F32)]
        gi32 = [sbufB("gi0", [128, NGT], mybir.dt.int32),
                sbufB("gi1", [128, NGT], mybir.dt.int32)]
        si32 = [sbufB("si0", [128, NGT], mybir.dt.int32),
                sbufB("si1", [128, NGT], mybir.dt.int32)]
        b1ag = [sbufB("b1ag0", [128, NFL], F32), sbufB("b1ag1", [128, NFL], F32)]
        b1bg = [sbufB("b1bg0", [128, NFL], F32), sbufB("b1bg1", [128, NFL], F32)]
        o1b = sbufB("o1b", [128, D], F32)
        o2b = sbufB("o2b", [128, D], F32)


    barrier = [(s_pe.h, PE_END_A), (s_dv.h, DV_END_A), (s_ac.h, AC_END_A),
               (s_spx.h, 16 * KD * NB) if not cfg.sparse else None,
               (s_se[0].h, SE_END[0]) if SE_END[0] else None,
               (s_se[1].h, SE_END[1]) if SE_END[1] else None]
    barrier = [x for x in barrier if x is not None]
    if cfg.sparse:
        barrier = barrier + [(s_spn[0].h, s_spn[0].n) if s_spn[0].n else None,
                             (s_spn[1].h, s_spn[1].n) if s_spn[1].n else None,
                             (s_pro[0].h, s_pro[0].n), (s_pro[1].h, s_pro[1].n),
                             (s_tbl.h, TBL_DONE[0]) if TBL_DONE[0] else None]
        barrier = [x for x in barrier if x is not None]

    state = dict(w1_free=[0, 0], w2_free=[0, 0], yT_free=[0, 0], yT_ready=[0, 0],
                 silu_free=0, fl_last_pe=0, c_done=0, g_free=0,
                 xc_rec=[0, 0], sc_rec=[0, 0], osb_rec=[0, 0], o1_rec=0,
                 acc_ready=[0] * NT, first=True)

    def emit_w_loads(ts, e_i, fg):
        gidx = (ts * E + e_i) * NFG + fg
        slot = gidx % 2
        extra = list(barrier) if state["first"] else []
        state["first"] = False
        q.op("gpsimd", (lambda e_i, fg, slot: lambda e: e.dma_start(
            out=w1a_b[slot][:].rearrange("p (k f) -> p k f", k=KD),
            in_=w1a_h[e_i, :, fg * FG * 128:(fg + 1) * FG * 128].rearrange(
                "(k p) f -> p k f", p=128)))(e_i, fg, slot),
            waits=extra + [w(s_pe, state["w1_free"][slot])], inc=s_w1a[slot].inc(16))
        q.op("gpsimd", (lambda e_i, fg, slot: lambda e: e.dma_start(
            out=w1b_b[slot][:].rearrange("p (k f) -> p k f", k=KD),
            in_=w1b_h[e_i, :, fg * FG * 128:(fg + 1) * FG * 128].rearrange(
                "(k p) f -> p k f", p=128)))(e_i, fg, slot),
            inc=s_w1b[slot].inc(16))
        q.op("gpsimd", (lambda e_i, fg, slot: lambda e: e.dma_start(
            out=w2_b[slot][:].rearrange("p (f d) -> p f d", f=FG),
            in_=w2_h[e_i, fg * FG * 128:(fg + 1) * FG * 128, :].rearrange(
                "(f p) d -> p f d", p=128)))(e_i, fg, slot),
            waits=[w(s_pe, state["w2_free"][slot])], inc=s_w2[slot].inc(16))
        return slot, s_w1a[slot].n, s_w1b[slot].n, s_w2[slot].n

    def emit_fl(ts, e_i, fg, tb, wslot, w1a_n, w1b_n, xn2_n, unit):
        ys = unit % 2
        for fl in range(FG):
            par = fl % 2
            ra, rb = "ba%d" % par, "bb%d" % par
            pa = ps_big[:, par * 1024: par * 1024 + TBW]
            pb = ps_big[:, par * 1024 + 512: par * 1024 + 512 + TBW]
            for k in range(KD):
                waits = []
                if k == 0:
                    waits = pwait(ra) + [(s_w1a[wslot].h, w1a_n), (s_xn2r.h, xn2_n)]
                q.op("tensor", (lambda fl, k, tb, wslot, pa: lambda e: e.matmul(
                    pa, w1a_b[wslot][:, k * FG * 128 + fl * 128: k * FG * 128 + (fl + 1) * 128],
                    xn2B[:, k * S + tb * TBW: k * S + (tb + 1) * TBW],
                    start=(k == 0), stop=(k == KD - 1)))(fl, k, tb, wslot, pa),
                    waits=waits, inc=s_pe.inc() if k == KD - 1 else None)
            pe_a = s_pe.n
            for k in range(KD):
                waits = []
                if k == 0:
                    waits = pwait(rb) + [(s_w1b[wslot].h, w1b_n)]
                q.op("tensor", (lambda fl, k, tb, wslot, pb: lambda e: e.matmul(
                    pb, w1b_b[wslot][:, k * FG * 128 + fl * 128: k * FG * 128 + (fl + 1) * 128],
                    xn2B[:, k * S + tb * TBW: k * S + (tb + 1) * TBW],
                    start=(k == 0), stop=(k == KD - 1)))(fl, k, tb, wslot, pb),
                    waits=waits, inc=s_pe.inc() if k == KD - 1 else None)
            pe_b = s_pe.n
            bcol = e_i * NFL + fg * FG + fl
            q.op("scalar", (lambda bcol, pa: lambda e: e.activation(
                silu_sb[:], pa, AF.Silu, bias=b1a_sb[:, bcol:bcol + 1]))(bcol, pa),
                waits=[w(s_pe, pe_a), w(s_dv, state["silu_free"]), wcw], inc=s_ac.inc())
            pread(ra, "ac", s_ac.n)
            ac_silu = s_ac.n
            q.op("vector", (lambda bcol, pb, ys, fl: lambda e: e.scalar_tensor_tensor(
                yT[ys][:, fl * TBW:(fl + 1) * TBW], pb, b1b_sb[:, bcol:bcol + 1],
                silu_sb[:], ALU.add, ALU.mult))(bcol, pb, ys, fl),
                waits=[w(s_pe, pe_b), w(s_ac, ac_silu), wcw] +
                      ([w(s_pe, state["yT_free"][ys])] if fl == 0 else []),
                inc=s_dv.inc())
            pread(rb, "dv", s_dv.n)
            state["silu_free"] = s_dv.n
        state["yT_ready"][ys] = s_dv.n
        state["fl_last_pe"] = s_pe.n
        state["w1_free"][wslot] = s_pe.n

    def emit_o(ts, e_i, fg, tb, wslot, w2_n, unit):
        ys = unit % 2
        for ttl in range(TPB):
            tt = tb * TPB + ttl
            po = ps_o0 if ttl % 2 == 0 else ps_o1
            regs = ("o0h0", "o0h1") if ttl % 2 == 0 else ("o1h0", "o1h1")
            nmm = 0
            tot = FG * ((D + 511) // 512)
            for fl in range(FG):
                for h in range(0, D, 512):
                    hw = min(512, D - h)
                    nmm += 1
                    waits = []
                    if nmm == 1:
                        waits = pwait(*regs) + [w(s_dv, state["yT_ready"][ys]),
                                                (s_w2[wslot].h, w2_n)]
                    q.op("tensor", (lambda fl, h, hw, ttl, ys, wslot, po: lambda e: e.matmul(
                        po[:, h:h + hw],
                        yT[ys][:, fl * TBW + ttl * 128: fl * TBW + (ttl + 1) * 128],
                        w2_b[wslot][:, fl * D + h: fl * D + h + hw],
                        start=(fl == 0), stop=(fl == FG - 1)))(fl, h, hw, ttl, ys, wslot, po),
                        waits=waits,
                        inc=s_pe.inc() if nmm == tot else None)
            pe_o = s_pe.n
            state["w2_free"][wslot] = pe_o
            state["yT_free"][ys] = pe_o
            q.op("vector", (lambda ts, e_i, tt, po: lambda e: e.scalar_tensor_tensor(
                acc[:, tt * D:(tt + 1) * D], po[:, 0:D],
                we_all[:, (ts * NT + tt) * E + e_i:(ts * NT + tt) * E + e_i + 1],
                acc[:, tt * D:(tt + 1) * D], ALU.mult, ALU.add))(ts, e_i, tt, po),
                waits=[w(s_pe, pe_o), w(s_dv, state["acc_ready"][tt])], inc=s_dv.inc())
            for r in regs:
                pread(r, "dv", s_dv.n)
            state["acc_ready"][tt] = s_dv.n

    def emit_phase_c(ts):
        for tt in range(NT):
            sl = tt % 2
            row0 = ts * S + tt * 128
            q.op("sync", (lambda row0, sl: lambda e: e.dma_start(
                out=xcb[sl][:], in_=x_h[row0:row0 + 128, :]))(row0, sl),
                waits=[w(s_dv, state["xc_rec"][sl])] +
                      (list(barrier) if (ts == 0 and tt < 2) else []),
                inc=s_xc[sl].inc(16))
            xn_ = s_xc[sl].n
            q.op("sync", (lambda row0, sl: lambda e: e.dma_start(
                out=scb[sl][:], in_=s_h[row0:row0 + 128, :]))(row0, sl),
                waits=[w(s_dv, state["sc_rec"][sl])], inc=s_sc[sl].inc(16))
            sn_ = s_sc[sl].n
            q.op("scalar", (lambda tt: lambda e: e.activation(
                gsb[:], acc[:, tt * D:(tt + 1) * D], AF.Silu))(tt),
                waits=[w(s_dv, state["acc_ready"][tt]), w(s_dv, state["g_free"])],
                inc=s_ac.inc())
            ac_g = s_ac.n
            gw = []
            if cfg.debug:
                q.op("sync", (lambda row0: lambda e: e.dma_start(
                    out=go_h[row0:row0 + 128, :], in_=gsb[:]))(row0),
                    waits=[w(s_ac, ac_g)], inc=s_dbg.inc(16))
                gw = [(s_dbg.h, s_dbg.n)]
            q.op("vector", (lambda sl: lambda e: e.tensor_mul(gsb[:], gsb[:], scb[sl][:]))(sl),
                 waits=[w(s_ac, ac_g), (s_sc[sl].h, sn_)] + gw, inc=s_dv.inc())
            dv_gm = s_dv.n
            q.op("vector", (lambda sl: lambda e: e.tensor_add(
                osb[sl][:], gsb[:], xcb[sl][:]))(sl),
                waits=[(s_xc[sl].h, xn_), w(s_dv, dv_gm),
                       (s_out[sl].h, state["osb_rec"][sl]) if state["osb_rec"][sl] else None],
                inc=s_dv.inc())
            dv_o = s_dv.n
            state["sc_rec"][sl] = dv_o
            state["xc_rec"][sl] = dv_o
            state["g_free"] = dv_o
            q.op("sync", (lambda row0, sl: lambda e: e.dma_start(
                out=o_h[row0:row0 + 128, :], in_=osb[sl][:]))(row0, sl),
                waits=[w(s_dv, dv_o)], inc=s_out[sl].inc(16))
            state["osb_rec"][sl] = s_out[sl].n
        state["c_done"] = s_dv.n

    FGW = FG * 128
    DH = min(512, D)
    NH = D // DH
    sp = dict(tb_dv=[0, 0], tb_gx=[(0, 0), (0, 0)], tb_osc=[(0, 0), (0, 0)],
              xgf_pe=[0, 0], xgT_pe=[0, 0], yT_pe=[0, 0],
              w1s_pe=[0, 0], w1a_cn=[0] * 64, w1b_cn=[0] * 64,
              w2_pe=[0, 0], conv_dv=[0, 0], gx_n=[0, 0],
              silu_dv=0, osb_osc=[0, 0], first_tbl=True, first_gx=True,
              first_w=True, w2chunk=0, w2_n={}, tbn=[0, 0], xgT_dv=[0, 0],
              yT_dv=[0, 0], b1_n=[0, 0], b1_pe=[0, 0])

    def emit_w1_chunk(ts, g, fg):
        wsl = fg % 2
        icol = widx_i[:, g * NFG + fg:g * NFG + fg + 1]
        wext = []
        if sp["first_w"]:
            wext = list(barrier)
            sp["first_w"] = False
        q.op("gpsimd", (lambda wsl, icol: lambda e: e.indirect_dma_start(
            out=w1a_b[wsl][:], out_offset=None, in_=w1a_sw,
            in_offset=bass.IndirectOffsetOnAxis(ap=icol, axis=0),
            bounds_check=BND["w1"], oob_is_err=False))(wsl, icol),
            waits=wext + [w(s_pe, sp["w1s_pe"][wsl])], inc=s_w1a[wsl].inc(16))
        sp["w1a_cn"][fg] = s_w1a[wsl].n
        q.op("gpsimd", (lambda wsl, icol: lambda e: e.indirect_dma_start(
            out=w1b_b[wsl][:], out_offset=None, in_=w1b_sw,
            in_offset=bass.IndirectOffsetOnAxis(ap=icol, axis=0),
            bounds_check=BND["w1"], oob_is_err=False))(wsl, icol),
            inc=s_w1b[wsl].inc(16))
        sp["w1b_cn"][fg] = s_w1b[wsl].n

    def emit_w2_chunk(ts, g, half, fg):
        c = sp["w2chunk"]
        sl2 = c % 2
        sp["w2chunk"] += 1
        col = g * NH * NFG + half * NFG + fg
        icol = w2x_i[:, col:col + 1]
        q.op("gpsimd", (lambda sl2, icol: lambda e: e.indirect_dma_start(
            out=w2_b[sl2][:], out_offset=None, in_=w2_sw,
            in_offset=bass.IndirectOffsetOnAxis(ap=icol, axis=0),
            bounds_check=BND["w2"], oob_is_err=False))(sl2, icol),
            waits=[w(s_pe, sp["w2_pe"][sl2])], inc=s_w2[sl2].inc(16))
        sp["w2_n"][(ts, g, half, fg)] = (sl2, s_w2[sl2].n)

    def emit_sp_tblconv(ts, g):
        gsl = g % 2
        waits = [w(s_dv, sp["tb_dv"][gsl]),
                 (s_gx[0].h, sp["tb_gx"][gsl][0]) if sp["tb_gx"][gsl][0] else None,
                 (s_gx[1].h, sp["tb_gx"][gsl][1]) if sp["tb_gx"][gsl][1] else None,
                 (s_osc[0].h, sp["tb_osc"][gsl][0]) if sp["tb_osc"][gsl][0] else None,
                 (s_osc[1].h, sp["tb_osc"][gsl][1]) if sp["tb_osc"][gsl][1] else None,
                 (s_tbl.h, TBL_DONE[0])]
        if sp["first_tbl"]:
            waits += list(barrier)
            sp["first_tbl"] = False
        flat = tbl_h.rearrange("a b -> (a b)")
        q.op("sync", (lambda flat, g, gsl: lambda e: e.dma_start(
            out=tb_sb[gsl][:],
            in_=flat[g * GRP * 4:(g + 1) * GRP * 4].rearrange("(p j) -> p j", p=128)))(
                flat, g, gsl),
            waits=waits, inc=s_tb2s[gsl].inc(16))
        sp["tbn"][gsl] = s_tb2s[gsl].n
        tb4 = tb_sb[gsl][:].rearrange("p (q f) -> p q f", f=4)
        q.op("vector", (lambda gsl, tb4: lambda e: e.tensor_copy(
            gi32[gsl][:], tb4[:, :, 0:1].squeeze(2)))(gsl, tb4),
            waits=[(s_tb2s[gsl].h, sp["tbn"][gsl])], inc=s_dv.inc())
        dvx = s_dv.n
        q.op("vector", (lambda gsl, tb4: lambda e: e.tensor_copy(
            si32[gsl][:], tb4[:, :, 1:2].squeeze(2)))(gsl, tb4),
            waits=[w(s_dv, dvx)], inc=s_dv.inc())
        sp["conv_dv"][gsl] = s_dv.n

    def emit_gx(gsl, ti):
        xs = ti % 2
        waits = [w(s_dv, sp["conv_dv"][gsl]), w(s_pe, sp["xgf_pe"][xs])]
        if sp["first_gx"]:
            waits += list(barrier)
            sp["first_gx"] = False
        q.op("gpsimd", (lambda gsl, ti, xs: lambda e: e.indirect_dma_start(
            out=xgf[xs][:], out_offset=None,
            in_=xn2n_h,
            in_offset=bass.IndirectOffsetOnAxis(ap=gi32[gsl][:, ti:ti + 1], axis=0),
            bounds_check=BND["gath"], oob_is_err=False))(gsl, ti, xs),
            waits=waits, inc=s_gx[xs].inc(16))
        sp["gx_n"][xs] = s_gx[xs].n

    def emit_sp_loads(ts, g):
        gsl = g % 2
        for ti in range(min(NGT, 2)):
            emit_gx(gsl, ti)
        sp["tb_gx"][gsl] = (s_gx[0].n, s_gx[1].n)
        bcol_i = b1x_i[:, g:g + 1]
        q.op("gpsimd", (lambda gsl, bcol_i: lambda e: e.indirect_dma_start(
            out=b1ag[gsl][:], out_offset=None, in_=b1ag_h,
            in_offset=bass.IndirectOffsetOnAxis(ap=bcol_i, axis=0),
            bounds_check=BND["b1"], oob_is_err=False))(gsl, bcol_i),
            waits=[w(s_pe, sp["b1_pe"][gsl])], inc=s_wqkv.inc(16))
        q.op("gpsimd", (lambda gsl, bcol_i: lambda e: e.indirect_dma_start(
            out=b1bg[gsl][:], out_offset=None, in_=b1bg_h,
            in_offset=bass.IndirectOffsetOnAxis(ap=bcol_i, axis=0),
            bounds_check=BND["b1"], oob_is_err=False))(gsl, bcol_i),
            inc=s_wqkv.inc(16))
        sp["b1_n"][gsl] = s_wqkv.n
        emit_w1_chunk(ts, g, 0)
        if NFG > 1:
            emit_w1_chunk(ts, g, 1)

    def emit_sp_trans(ts, g):
        gsl = g % 2
        for ti in range(NGT):
            xs = ti % 2
            par = ti % 2
            regs = ("ba0", "bb0") if par == 0 else ("ba1", "bb1")
            pbase = par * 1024
            for k in range(KD):
                q.op("tensor", (lambda xs, k, pbase: lambda e: e.transpose(
                    ps_big[:, pbase + k * 128: pbase + (k + 1) * 128],
                    xgf[xs][:, k * 128:(k + 1) * 128], ident_sb[:]))(xs, k, pbase),
                    waits=([(s_gx[xs].h, sp["gx_n"][xs])] + pwait(*regs) + [wcw])
                    if k == 0 else [],
                    inc=s_pe.inc() if k == KD - 1 else None)
            pe_t = s_pe.n
            sp["xgf_pe"][xs] = pe_t
            if ti + 2 < NGT:
                emit_gx(gsl, ti + 2)
            q.op("vector", (lambda gsl, ti, pbase: lambda e: e.tensor_copy(
                xgT[gsl][:].rearrange("p (k c) -> p k c", k=KD)[
                    :, :, ti * 128:(ti + 1) * 128],
                ps_big[:, pbase:pbase + KD * 128].rearrange(
                    "p (k c) -> p k c", k=KD)))(gsl, ti, pbase),
                waits=[w(s_pe, pe_t)] +
                      ([w(s_pe, sp["xgT_pe"][gsl])] if ti == 0 else []),
                inc=s_dv.inc())
            for r in regs:
                pread(r, "dv", s_dv.n)
        sp["xgT_dv"][gsl] = s_dv.n
        sp["tb_gx"][gsl] = (s_gx[0].n, s_gx[1].n)

    def emit_sp_fl(ts, g):
        gsl = g % 2
        ys = g % 2
        for fg in range(NFG):
            wsl = fg % 2
            w1a_n, w1b_n = sp["w1a_cn"][fg], sp["w1b_cn"][fg]
            for fl in range(FG):
                par = fl % 2
                ra, rb = "ba%d" % par, "bb%d" % par
                pa = ps_big[:, par * 1024: par * 1024 + GRP]
                pb = ps_big[:, par * 1024 + 512: par * 1024 + 512 + GRP]
                for k in range(KD):
                    waits = []
                    if k == 0:
                        waits = pwait(ra) + [(s_w1a[wsl].h, w1a_n)]
                        if fg == 0 and fl == 0:
                            waits += [w(s_dv, sp["xgT_dv"][gsl])]
                    q.op("tensor", (lambda fl, k, gsl, wsl, pa: lambda e: e.matmul(
                        pa, w1a_b[wsl][:, k * FGW + fl * 128: k * FGW + (fl + 1) * 128],
                        xgT[gsl][:, k * GRP:(k + 1) * GRP],
                        start=(k == 0), stop=(k == KD - 1)))(fl, k, gsl, wsl, pa),
                        waits=waits, inc=s_pe.inc() if k == KD - 1 else None)
                pe_a = s_pe.n
                for k in range(KD):
                    waits = []
                    if k == 0:
                        waits = pwait(rb) + [(s_w1b[wsl].h, w1b_n)]
                    q.op("tensor", (lambda fl, k, gsl, wsl, pb: lambda e: e.matmul(
                        pb, w1b_b[wsl][:, k * FGW + fl * 128: k * FGW + (fl + 1) * 128],
                        xgT[gsl][:, k * GRP:(k + 1) * GRP],
                        start=(k == 0), stop=(k == KD - 1)))(fl, k, gsl, wsl, pb),
                        waits=waits, inc=s_pe.inc() if k == KD - 1 else None)
                pe_b = s_pe.n
                bcol = fg * FG + fl
                q.op("scalar", (lambda gsl, bcol, pa: lambda e: e.activation(
                    silu_sb[:], pa, AF.Silu, bias=b1ag[gsl][:, bcol:bcol + 1]))(
                        gsl, bcol, pa),
                    waits=[w(s_pe, pe_a), w(s_dv, sp["silu_dv"]),
                           (s_wqkv.h, sp["b1_n"][gsl])], inc=s_ac.inc())
                pread(ra, "ac", s_ac.n)
                ac_s = s_ac.n
                q.op("vector", (lambda gsl, bcol, pb, ys: lambda e: e.scalar_tensor_tensor(
                    yT[ys][:, bcol * GRP:(bcol + 1) * GRP], pb,
                    b1bg[gsl][:, bcol:bcol + 1], silu_sb[:], ALU.add, ALU.mult))(
                        gsl, bcol, pb, ys),
                    waits=[w(s_pe, pe_b), w(s_ac, ac_s)] +
                          ([w(s_pe, sp["yT_pe"][ys])] if (fg == 0 and fl == 0) else []),
                    inc=s_dv.inc())
                pread(rb, "dv", s_dv.n)
                sp["silu_dv"] = s_dv.n
            sp["w1s_pe"][wsl] = s_pe.n
            if fg + 2 < NFG:
                emit_w1_chunk(ts, g, fg + 2)
        sp["yT_dv"][ys] = s_dv.n
        sp["xgT_pe"][gsl] = s_pe.n
        sp["b1_pe"][gsl] = s_pe.n

    def emit_sp_o(ts, g):
        gsl = g % 2
        ys = g % 2
        PO = (ps_o0, ps_o0, ps_o1, ps_o1)
        RG = ("o0h0", "o0h1", "o1h0", "o1h1")
        emit_w2_chunk(ts, g, 0, 0)
        if NFG > 1:
            emit_w2_chunk(ts, g, 0, 1)
        elif NH > 1:
            emit_w2_chunk(ts, g, 1, 0)
        ck = 0
        for half in range(NH):
            for fg in range(NFG):
                sl2, w2n = sp["w2_n"][(ts, g, half, fg)]
                for ti in range(NGT):
                    po = PO[ti][:, (ti % 2) * 512:(ti % 2) * 512 + DH]
                    for fl in range(FG):
                        waits = []
                        if fl == 0 and ti == 0:
                            waits = [(s_w2[sl2].h, w2n)]
                            if fg == 0:
                                waits = waits + [w(s_dv, sp["yT_dv"][ys])]
                        if fg == 0 and fl == 0:
                            waits = waits + pwait(RG[ti])
                        fidx = fg * FG + fl
                        q.op("tensor", (lambda po, fidx, ti, fl, ys, sl2: lambda e: e.matmul(
                            po, yT[ys][:, fidx * GRP + ti * 128: fidx * GRP + (ti + 1) * 128],
                            w2_b[sl2][:, fl * DH:(fl + 1) * DH],
                            start=(fidx == 0), stop=(fidx == NFL - 1)))(
                                po, fidx, ti, fl, ys, sl2),
                            waits=waits,
                            inc=s_pe.inc() if (ti == NGT - 1 and fl == FG - 1) else None)
                sp["w2_pe"][sl2] = s_pe.n
                ck += 1
                nxt = ck + 1
                if nxt < NH * NFG:
                    emit_w2_chunk(ts, g, nxt // NFG, nxt % NFG)
            pe_o = s_pe.n
            for ti in range(NGT):
                po = PO[ti][:, (ti % 2) * 512:(ti % 2) * 512 + DH]
                osl = ti % 2
                waits = [w(s_pe, pe_o)]
                if sp["osb_osc"][osl]:
                    waits.append((s_osc[osl].h, sp["osb_osc"][osl]))
                if state["osb_rec"][osl]:
                    waits.append((s_out[osl].h, state["osb_rec"][osl]))
                q.op("vector", (lambda po, gsl, ti, osl, half: lambda e: e.tensor_scalar_mul(
                    osb[osl][:, half * DH:(half + 1) * DH], po,
                    tb_sb[gsl][:, ti * 4 + 2:ti * 4 + 3]))(po, gsl, ti, osl, half),
                    waits=waits, inc=s_dv.inc())
                pread(RG[ti], "dv", s_dv.n)
                dv_e = s_dv.n
                q.op("gpsimd", (lambda gsl, ti, osl, half: lambda e: e.indirect_dma_start(
                    out=obuf_hh[half],
                    out_offset=bass.IndirectOffsetOnAxis(
                        ap=si32[gsl][:, ti:ti + 1], axis=0),
                    in_=osb[osl][:, half * DH:(half + 1) * DH], in_offset=None,
                    bounds_check=BND["scat"], oob_is_err=False))(gsl, ti, osl, half),
                    waits=[w(s_dv, dv_e)], inc=s_osc[osl].inc(16))
                sp["osb_osc"][osl] = s_osc[osl].n
        sp["tb_dv"][gsl] = s_dv.n
        sp["tb_osc"][gsl] = (s_osc[0].n, s_osc[1].n)
        sp["yT_pe"][ys] = s_pe.n

    def emit_phase_c_sp(ts):
        osc_tot = (s_osc[0].n, s_osc[1].n)
        for tt in range(NT):
            sl = tt % 2
            row0 = ts * S + tt * 128
            q.op("sync", (lambda row0, sl: lambda e: e.dma_start(
                out=xcb[sl][:], in_=x_h[row0:row0 + 128, :]))(row0, sl),
                waits=[w(s_dv, state["xc_rec"][sl])], inc=s_xc[sl].inc(16))
            xn_ = s_xc[sl].n
            q.op("sync", (lambda row0, sl: lambda e: e.dma_start(
                out=scb[sl][:], in_=s_h[row0:row0 + 128, :]))(row0, sl),
                waits=[w(s_dv, state["sc_rec"][sl])], inc=s_sc[sl].inc(16))
            sn_ = s_sc[sl].n
            for h in range(NH):
                q.op("sync", (lambda row0, h: lambda e: e.dma_start(
                    out=o1b[:, h * DH:(h + 1) * DH],
                    in_=obuf_hh[h][row0:row0 + 128, :]))(row0, h),
                    waits=([w(s_dv, state["o1_rec"]),
                            (s_osc[0].h, osc_tot[0]) if tt < 2 else None,
                            (s_osc[1].h, osc_tot[1]) if tt < 2 else None]
                           if h == 0 else []),
                    inc=s_xc[sl].inc(16))
                q.op("sync", (lambda row0, h: lambda e: e.dma_start(
                    out=o2b[:, h * DH:(h + 1) * DH],
                    in_=obuf_hh[h][NBS + row0:NBS + row0 + 128, :]))(row0, h),
                    inc=s_xc[sl].inc(16))
            on_ = s_xc[sl].n
            po = ps_o0 if tt % 2 == 0 else ps_o1
            regs = ("o0h0", "o0h1") if tt % 2 == 0 else ("o1h0", "o1h1")
            nh = (D + 511) // 512
            for hi, h in enumerate(range(0, D, 512)):
                hw = min(512, D - h)
                waits = []
                if hi == 0:
                    waits = pwait(*regs) + [w(s_dv, we_done[ts]), wcw]
                q.op("tensor", (lambda ts, tt, h, hw, po: lambda e: e.matmul(
                    po[:, h:h + hw],
                    weT_all[0:E, ts * S + tt * 128: ts * S + (tt + 1) * 128],
                    b2_sb[0:E, h:h + hw],
                    start=True, stop=True))(ts, tt, h, hw, po),
                    waits=waits, inc=s_pe.inc() if hi == nh - 1 else None)
            pe_b2 = s_pe.n
            q.op("vector", (lambda sl: lambda e: e.tensor_add(
                gsb[:], o1b[:], o2b[:]))(sl),
                waits=[(s_xc[sl].h, on_), w(s_dv, state["g_free"])], inc=s_dv.inc())
            dvx = s_dv.n
            state["o1_rec"] = dvx
            q.op("vector", (lambda po: lambda e: e.tensor_add(
                gsb[:], gsb[:], po[:, 0:D]))(po),
                waits=[w(s_dv, dvx), w(s_pe, pe_b2)], inc=s_dv.inc())
            dvx = s_dv.n
            for r in regs:
                pread(r, "dv", dvx)
            q.op("scalar", lambda e: e.activation(gsb[:], gsb[:], AF.Silu),
                 waits=[w(s_dv, dvx)], inc=s_ac.inc())
            ac_g = s_ac.n
            q.op("vector", (lambda sl: lambda e: e.tensor_mul(gsb[:], gsb[:], scb[sl][:]))(sl),
                 waits=[w(s_ac, ac_g), (s_sc[sl].h, sn_)], inc=s_dv.inc())
            dv_gm = s_dv.n
            q.op("vector", (lambda sl: lambda e: e.tensor_add(
                osb[sl][:], gsb[:], xcb[sl][:]))(sl),
                waits=[(s_xc[sl].h, on_), w(s_dv, dv_gm),
                       (s_out[sl].h, state["osb_rec"][sl]) if state["osb_rec"][sl] else None,
                       (s_osc[sl].h, sp["osb_osc"][sl]) if sp["osb_osc"][sl] else None],
                inc=s_dv.inc())
            dv_o = s_dv.n
            sp["osb_osc"][sl] = 0
            state["sc_rec"][sl] = dv_o
            state["xc_rec"][sl] = dv_o
            state["g_free"] = dv_o
            q.op("sync", (lambda row0, sl: lambda e: e.dma_start(
                out=o_h[row0:row0 + 128, :], in_=osb[sl][:]))(row0, sl),
                waits=[w(s_dv, dv_o)], inc=s_out[sl].inc(16))
            state["osb_rec"][sl] = s_out[sl].n
        state["c_done"] = s_dv.n

    if cfg.sparse and not cfg.only_a:
        q.op("gpsimd", _setup_bounds)
        NGL = G if cfg.ng_limit is None else min(G, cfg.ng_limit)
        for g in range(NGL):
            if g == 0:
                emit_sp_tblconv(0, 0)
            emit_sp_loads(0, g)
            emit_sp_trans(0, g)
            emit_sp_fl(0, g)
            if g > 0:
                emit_sp_o(0, g - 1)
            if g + 1 < NGL:
                emit_sp_tblconv(0, g + 1)
        emit_sp_o(0, NGL - 1)
        if not cfg.skip_c:
            for ts in range(NB):
                emit_phase_c_sp(ts)

    for ts in range(NB if not (cfg.only_a or cfg.sparse) else 0):
        if ts == 0:
            waits0 = list(barrier)
        else:
            waits0 = [w(s_pe, state["fl_last_pe"])]
        for k in range(KD):
            q.op("sync", (lambda ts, k: lambda e: e.dma_start(
                out=xn2B[:, k * S:(k + 1) * S],
                in_=xn2s_h[k, :, ts * S:(ts + 1) * S]))(ts, k),
                waits=waits0 if k == 0 else [], inc=s_xn2r.inc(16))
        xn2_n = s_xn2r.n

        for tt in range(NT):
            po = ps_o0 if tt % 2 == 0 else ps_o1
            regs = ("o0h0", "o0h1") if tt % 2 == 0 else ("o1h0", "o1h1")
            nh = (D + 511) // 512
            for hi, h in enumerate(range(0, D, 512)):
                hw = min(512, D - h)
                waits = []
                if hi == 0:
                    waits = pwait(*regs) + [w(s_dv, we_done[ts]), wcw]
                    if ts == 0 and tt == 0:
                        waits = waits + list(barrier)
                q.op("tensor", (lambda ts, tt, h, hw, po: lambda e: e.matmul(
                    po[:, h:h + hw],
                    weT_all[0:E, ts * S + tt * 128: ts * S + (tt + 1) * 128],
                    b2_sb[0:E, h:h + hw],
                    start=True, stop=True))(ts, tt, h, hw, po),
                    waits=waits, inc=s_pe.inc() if hi == nh - 1 else None)
            pe_b2 = s_pe.n
            q.op("vector", (lambda tt, po: lambda e: e.tensor_copy(
                acc[:, tt * D:(tt + 1) * D], po[:, 0:D]))(tt, po),
                waits=[w(s_pe, pe_b2), w(s_dv, state["c_done"])], inc=s_dv.inc())
            for r in regs:
                pread(r, "dv", s_dv.n)
            state["acc_ready"][tt] = s_dv.n

        pending = None
        for e_i in range(E):
            for fg in range(NFG):
                wslot, w1a_n, w1b_n, w2_n = emit_w_loads(ts, e_i, fg)
                for tb in range(NTB):
                    unit = (e_i * NFG + fg) * NTB + tb
                    emit_fl(ts, e_i, fg, tb, wslot, w1a_n, w1b_n, xn2_n, unit)
                    if pending is not None:
                        emit_o(*pending)
                    pending = (ts, e_i, fg, tb, wslot, w2_n, unit)
        emit_o(*pending)

        emit_phase_c(ts)

    if cfg.debug:
        q.op("sync", lambda e: e.dma_start(out=weo_h[:], in_=we_all[:]),
             waits=[(s_dv.h, s_dv.n)], inc=s_dbg.inc(16))
        q.op("sync", lambda e: e.wait_ge(s_dbg.h, s_dbg.n))

    if not cfg.only_a:
        q.op("sync", lambda e: e.wait_ge(s_out[0].h, s_out[0].n),
             waits=[(s_out[1].h, s_out[1].n)])
    else:
        q.op("sync", lambda e: e.wait_ge(s_spx.h, s_spx.n),
             waits=[(s_se[0].h, s_se[0].n), (s_se[1].h, s_se[1].n),
                    (s_dv.h, s_dv.n), (s_ac.h, s_ac.n), (s_pe.h, s_pe.n)] +
                   ([(s_dbg.h, s_dbg.n)] if cfg.debug else []))

    q.run_block(nc)
    stB.close()
    st.close()
    return nc


_CFG = Cfg()
_NC_CACHE = {}


def _get_nc(cfg):
    key = id(cfg)
    if key not in _NC_CACHE:
        _NC_CACHE[key] = build_program(cfg)
    return _NC_CACHE[key]


def make_in_maps(cfg, n_cores, x, wq, wk, wvd, wvu, lnw, wg, bg, w1a, b1a, w1b, b1b, w2, b2):
    B, S, D = x.shape
    per = B // n_cores
    ident = np.eye(128, dtype=np.float32)
    amask = np.triu(np.ones((128, 128), dtype=np.uint8), 1)
    f32 = lambda a: np.ascontiguousarray(np.asarray(a, dtype=np.float32))

    def btr(bv):
        bv = f32(bv)
        E_, F_ = bv.shape
        return np.ascontiguousarray(
            bv.reshape(E_, F_ // 128, 128).transpose(2, 0, 1).reshape(128, E_ * (F_ // 128)))

    def btg(bv):
        bv = f32(bv)
        E_, F_ = bv.shape
        return np.ascontiguousarray(
            bv.reshape(E_, F_ // 128, 128).transpose(0, 2, 1).reshape(E_ * 128, F_ // 128))

    shared = dict(wq=f32(wq), wk=f32(wk), wvd=f32(wvd), wvu=f32(wvu), lnw=f32(lnw),
                  wg=f32(wg), bg=f32(bg), w1a=f32(w1a), b1a=btr(b1a), w1b=f32(w1b),
                  b1b=btr(b1b), w2=f32(w2), b2=f32(b2), ident=ident, amask=amask)
    if cfg.sparse:
        shared["b1agh"] = btg(b1a)
        shared["b1bgh"] = btg(b1b)
    in_maps = []
    for c in range(n_cores):
        m = dict(shared)
        m["x"] = np.ascontiguousarray(x[c * per:(c + 1) * per].reshape(per * S, D))
        in_maps.append(m)
    return in_maps


def kernel(L0_attn_inputs, L0_attn_Wq, L0_attn_Wk, L0_attn_Wvdown, L0_attn_Wvup,
           L0_ffn_ln_w, L0_ffn_Wg, L0_ffn_bg, L0_ffn_W1a, L0_ffn_b1a,
           L0_ffn_W1b, L0_ffn_b1b, L0_ffn_W2, L0_ffn_b2):
    cfg = _CFG
    n_cores = 8
    x = np.ascontiguousarray(np.asarray(L0_attn_inputs, dtype=np.float32))
    B, S, D = x.shape
    per = B // n_cores
    nc = _get_nc(cfg)
    in_maps = make_in_maps(cfg, n_cores, x, L0_attn_Wq, L0_attn_Wk, L0_attn_Wvdown,
                           L0_attn_Wvup, L0_ffn_ln_w, L0_ffn_Wg, L0_ffn_bg,
                           L0_ffn_W1a, L0_ffn_b1a, L0_ffn_W1b, L0_ffn_b1b,
                           L0_ffn_W2, L0_ffn_b2)
    res = run_bass_kernel_spmd(nc, in_maps, core_ids=list(range(n_cores)))
    out = np.concatenate([np.asarray(r["o"]).reshape(per, S, D) for r in res.results], axis=0)
    return out.astype(np.float32)


# version 07
# speedup vs baseline: 1.8180x; 1.0054x over previous
import numpy as np
from contextlib import ExitStack

import concourse.bass as bass
import concourse.mybir as mybir
from concourse.bass_utils import run_bass_kernel_spmd

F32 = mybir.dt.float32
BF16 = mybir.dt.bfloat16
U8 = mybir.dt.uint8
AF = mybir.ActivationFunctionType
ALU = mybir.AluOpType
AX = mybir.AxisListType

ENGS = ("tensor", "vector", "scalar", "gpsimd", "sync")


class Cfg:
    def __init__(self, NB=4, S=2048, D=1024, DQ=256, E=8, F=4096, FG=4, TBW=512,
                 GRP=512):
        self.NB, self.S, self.D, self.DQ, self.E, self.F = NB, S, D, DQ, E, F
        self.KD = D // 128
        self.KQ = DQ // 128
        self.NT = S // 128
        self.NFL = F // 128
        self.FG = FG
        self.NFG = self.NFL // FG
        self.TBW = TBW
        self.NTB = S // TBW
        self.TPB = TBW // 128
        self.eps_ln = 1e-5
        self.eps_rms = float(np.finfo(np.float32).eps)
        self.scale = 1.0 / float(np.sqrt(DQ))
        self.debug = False
        self.only_a = False
        self.only_ln = False
        self.sparse = True
        self.ng_limit = None
        self.skip_c = False
        self.GRP = GRP
        self.NGT = self.GRP // 128
        self.G = E + (2 * NB * S) // self.GRP
        self.LG = int(np.log2(self.GRP))


class Sem:
    def __init__(self, h):
        self.h = h
        self.n = 0

    def inc(self, amt=1):
        self.n += amt
        return (self.h, amt)


class Q:
    def __init__(self):
        self.ops = {e: [] for e in ENGS}

    def op(self, eng, fn, waits=(), inc=None):
        self.ops[eng].append((fn, [x for x in waits if x is not None], inc))

    def run_block(self, nc):
        with nc.Block() as block:
            for eng in ENGS:
                ops = self.ops[eng]
                if not ops:
                    continue

                def make(ops):
                    def prog(e):
                        for fn, waits, inc in ops:
                            for (sem, val) in waits:
                                e.wait_ge(sem, val)
                            ins = fn(e)
                            if inc is not None:
                                ins.then_inc(inc[0], inc[1])
                    return prog

                getattr(block, eng)(make(ops))
        self.ops = {e: [] for e in ENGS}


def build_program(cfg: Cfg):
    nc = bass.Bass("TRN2", target_bir_lowering=False)
    NB, S, D, DQ, E, F = cfg.NB, cfg.S, cfg.D, cfg.DQ, cfg.E, cfg.F
    KD, KQ, NT, NFL, FG, NFG = cfg.KD, cfg.KQ, cfg.NT, cfg.NFL, cfg.FG, cfg.NFG
    TBW, NTB, TPB = cfg.TBW, cfg.NTB, cfg.TPB

    x_h = nc.dram_tensor("x", [NB * S, D], F32, kind="ExternalInput").ap()
    wq_h = nc.dram_tensor("wq", [D, DQ], F32, kind="ExternalInput").ap()
    wk_h = nc.dram_tensor("wk", [D, DQ], F32, kind="ExternalInput").ap()
    wvd_h = nc.dram_tensor("wvd", [D, DQ], F32, kind="ExternalInput").ap()
    wvu_h = nc.dram_tensor("wvu", [DQ, D], F32, kind="ExternalInput").ap()
    lnw_h = nc.dram_tensor("lnw", [D], F32, kind="ExternalInput").ap()
    wg_h = nc.dram_tensor("wg", [D, E], F32, kind="ExternalInput").ap()
    bg_h = nc.dram_tensor("bg", [E], F32, kind="ExternalInput").ap()
    w1a_h = nc.dram_tensor("w1a", [E, D, F], F32, kind="ExternalInput").ap()
    b1a_h = nc.dram_tensor("b1a", [128, E * NFL], F32, kind="ExternalInput").ap()
    w1b_h = nc.dram_tensor("w1b", [E, D, F], F32, kind="ExternalInput").ap()
    b1b_h = nc.dram_tensor("b1b", [128, E * NFL], F32, kind="ExternalInput").ap()
    w2_h = nc.dram_tensor("w2", [E, F, D], F32, kind="ExternalInput").ap()
    b2_h = nc.dram_tensor("b2", [E, D], F32, kind="ExternalInput").ap()
    ident_h = nc.dram_tensor("ident", [128, 128], F32, kind="ExternalInput").ap()
    amask_h = nc.dram_tensor("amask", [128, 128], U8, kind="ExternalInput").ap()
    o_h = nc.dram_tensor("o", [NB * S, D], F32, kind="ExternalOutput").ap()
    dbgk = "ExternalOutput" if cfg.debug else "Internal"
    xn2s_h = nc.dram_tensor("xn2s", [KD, 128, NB * S], BF16, kind=dbgk).ap()
    s_h = nc.dram_tensor("sscr", [NB * S, D], F32, kind=dbgk).ap()
    GRP, NGT, G = cfg.GRP, cfg.NGT, cfg.G
    NBS = NB * S
    if cfg.sparse:
        xn2n_h = nc.dram_tensor("xn2n", [NBS, D], F32, kind="Internal").ap()
        rt_h = nc.dram_tensor("rt", [NB, 4 * NT, 128], F32, kind="Internal").ap()
        tbl_h = nc.dram_tensor("tbl", [G * GRP, 4], F32, kind="Internal").ap()
        FGWc = FG * 128
        DHc = min(512, D)
        NHc = D // DHc
        obuf_hh = [nc.dram_tensor("obuf%d" % h, [2 * NBS, DHc], F32,
                                  kind="Internal").ap() for h in range(NHc)]
        w1a_sw = nc.dram_tensor("w1asw", [E * NFG * 128, KD * FGWc], BF16,
                                kind="Internal").ap()
        w1b_sw = nc.dram_tensor("w1bsw", [E * NFG * 128, KD * FGWc], BF16,
                                kind="Internal").ap()
        w2_sw = nc.dram_tensor("w2sw", [E * NHc * NFG * 128, FG * DHc], BF16,
                               kind="Internal").ap()
        b1ag_h = nc.dram_tensor("b1agh", [E * 128, NFL], F32, kind="ExternalInput").ap()
        b1bg_h = nc.dram_tensor("b1bgh", [E * 128, NFL], F32, kind="ExternalInput").ap()
    if cfg.debug:
        weo_h = nc.dram_tensor("weo", [128, NB * NT * E], F32, kind="ExternalOutput").ap()
        go_h = nc.dram_tensor("go", [NB * S, D], F32, kind="ExternalOutput").ap()
        ao_h = nc.dram_tensor("ao", [NB * S, S], F32, kind="ExternalOutput").ap()
        xnso_h = nc.dram_tensor("xnso", [KD, 128, NB * S], BF16, kind="ExternalOutput").ap()
        colso_h = nc.dram_tensor("colso", [NB * NT, 128, 8], F32, kind="ExternalOutput").ap()
        lgo_h = nc.dram_tensor("lgo", [NB, 128, NT * E], F32, kind="ExternalOutput").ap()
        vno_h = nc.dram_tensor("vno", [NB, 128, S // 128 * D], BF16, kind="ExternalOutput").ap()
        qo_h = nc.dram_tensor("qo", [NB, 128, (DQ // 128) * S], BF16, kind="ExternalOutput").ap()
        ko_h = nc.dram_tensor("ko", [NB, 128, (DQ // 128) * S], BF16, kind="ExternalOutput").ap()

    st = ExitStack()

    def sbuf(name, shape, dt):
        return st.enter_context(nc.sbuf_tensor(name, shape, dt))

    def psum(name, shape, dt=F32):
        return st.enter_context(nc.psum_tensor(name, shape, dt))

    def sem(name):
        return Sem(st.enter_context(nc.semaphore(name)))

    we_all = sbuf("we_all", [128, NB * NT * E], F32)
    weT_all = sbuf("weT_all", [E, NB * S], BF16)
    if not cfg.sparse:
        b1a_sb = sbuf("b1a_sb", [128, E * NFL], F32)
        b1b_sb = sbuf("b1b_sb", [128, E * NFL], F32)
    b2_sb = sbuf("b2_sb", [E, D], BF16)
    ident_sb = sbuf("ident_sb", [128, 128], F32)
    amask_sb = sbuf("amask_sb", [128, 128], U8)
    neg_sb = sbuf("neg_sb", [128, 128], F32)
    cols = sbuf("colsA", [128, 32], F32)
    epsln_c = sbuf("epsln_c", [128, 1], F32)
    epsrms_c = sbuf("epsrms_c", [128, 1], F32)
    if cfg.sparse:
        I32 = mybir.dt.int32
        SEGB = 2 * S // 128
        SEG = NB * SEGB
        sent_sb = sbuf("sent_sb", [128, 256], F32)
        pcv = [sbuf("pcv0", [128, 2048], BF16), sbuf("pcv1", [128, 2048], BF16)]
        egrp_i = sbuf("egrp_i", [1, G], I32)
        e_iota = sbuf("e_iota", [128, E], F32)
        rankc = sbuf("rankc", [128, 1], F32)
        rankcS = sbuf("rankcS", [128, 1], F32)
        rankcN = sbuf("rankcN", [128, 1], F32)
        pcol = sbuf("pcol", [128, 1], F32)
        ebase = sbuf("ebase", [128, G], F32)
        eb2 = sbuf("eb2", [128, G], F32)
        widx_i = sbuf("widx_i", [128, G * NFG], I32)
        w2x_i = sbuf("w2x_i", [128, G * (D // min(512, D)) * NFG], I32)
        b1x_i = sbuf("b1x_i", [128, G], I32)
        basec = sbuf("basec", [128, E], F32)
        etmp = sbuf("etmp", [1, G], F32)
        ones_r = sbuf("ones_r", [1, 128], F32)
        segoff = sbuf("segoff", [128, 1], F32)
        rowa = sbuf("rowa", [1, 128], F32)
        rowb = sbuf("rowb", [1, 128], F32)
        rzro = sbuf("rzro", [1, 128], F32)
        crow = sbuf("crow", [1, 64], F32)
        cri = sbuf("cri", [1, 16], I32)
        grow = sbuf("grow", [1, G], F32)
        egf = sbuf("egf", [1, G], F32)

    ps_big = psum("ps_big", [128, 2048])
    ps_o0 = psum("ps_o0", [128, 1024])
    ps_o1 = psum("ps_o1", [128, 1024])

    s_pe = sem("s_pe")
    s_dv = sem("s_dv")
    s_ac = sem("s_ac")
    s_wc = sem("s_wc")
    s_x = [sem("s_x0"), sem("s_x1")]
    s_se = [sem("s_se0"), sem("s_se1")]
    s_spx = sem("s_spx")
    s_w1a = [sem("s_w1a0"), sem("s_w1a1")]
    s_w1b = [sem("s_w1b0"), sem("s_w1b1")]
    s_w2 = [sem("s_w20"), sem("s_w21")]
    s_xn2r = sem("s_xn2r")
    s_tb2s = [sem("s_tb2a"), sem("s_tb2b")]
    s_xc = [sem("s_xc0"), sem("s_xc1")]
    s_sc = [sem("s_sc0"), sem("s_sc1")]
    s_out = [sem("s_out0"), sem("s_out1")]
    s_dbg = sem("s_dbg")
    s_wqkv = sem("s_wqkv")
    s_ms = sem("s_ms")
    s_spn = [sem("s_spn0"), sem("s_spn1")]
    s_tbl = sem("s_tbl")
    s_pre = sem("s_pre")
    s_rt = sem("s_rt")
    s_gx = [sem("s_gx0"), sem("s_gx1")]
    s_tb2 = sem("s_tb2")
    s_osc = [sem("s_osc0"), sem("s_osc1")]
    s_fill = sem("s_fill")
    s_pri = [sem("s_pri0"), sem("s_pri1")]
    s_pro = [sem("s_pro0"), sem("s_pro1")]

    q = Q()
    _regc = [0]

    def _rname(p):
        _regc[0] += 1
        return "%s%d" % (p, _regc[0])

    def w(s, val):
        return (s.h, val) if val > 0 else None

    psg = {r: {"dv": 0, "ac": 0} for r in
           ("bigA", "ba0", "bb0", "ba1", "bb1", "o0h0", "o0h1", "o1h0", "o1h1")}

    def pwait(*regions):
        rs = set(regions)
        quarters = {"ba0", "bb0", "ba1", "bb1"}
        if "bigA" in rs:
            rs |= quarters
        elif rs & quarters:
            rs.add("bigA")
        dv = max(psg[r]["dv"] for r in rs)
        ac = max(psg[r]["ac"] for r in rs)
        return [w(s_dv, dv), w(s_ac, ac)]

    def pread(region, kind, cnt):
        psg[region][kind] = max(psg[region][kind], cnt)

    stA = ExitStack()

    def sbufA(name, shape, dt):
        return stA.enter_context(nc.sbuf_tensor(name, shape, dt))

    assert KD * DQ == KQ * D
    wqkv_sb = sbufA("wqkv_sb", [128, KD * DQ], BF16)
    wg_sb = sbufA("wg_sb", [128, KD * E], BF16)
    lnw_bc = sbufA("lnw_bc", [128, D], F32)
    bg_bc = sbufA("bg_bc", [128, E], F32)
    xin = [sbufA("xin0", [128, D], F32), sbufA("xin1", [128, D], F32)]
    sqs = sbufA("sqs", [128, D], BF16)
    xnf = [sbufA("xnf0", [128, D], F32), sbufA("xnf1", [128, D], F32)]
    xn2f = [sbufA("xn2f0", [128, D], F32), sbufA("xn2f1", [128, D], F32)]
    xnT = sbufA("xnT", [128, KD * S], BF16)
    SW2 = 256 if cfg.sparse else S
    xn2T = sbufA("xn2T_A", [128, KD * SW2], BF16)
    logits = sbufA("logits", [128, NT * E], F32)
    glt = sbufA("glt", [128, 4 * NT], F32)
    geq = sbufA("geq", [128, 2 * NT * E], F32)
    qT_sb = sbufA("qT", [128, KQ * S], BF16)
    kT_sb = sbufA("kT", [128, KQ * S], BF16)
    vdT_sb = sbufA("vdT", [128, KQ * S], BF16)
    vnat = sbufA("vnat", [128, NT * D], BF16)
    expb = sbufA("expb", [128, S], F32)
    attnT = sbufA("attnT", [128, S], BF16)
    sev = [sbufA("sev0", [128, D], F32), sbufA("sev1", [128, D], F32)]
    if cfg.sparse:
        gsc = sbufA("gsc", [128, NT * E], F32)
        gidx = sbufA("gidx", [128, 2 * NT], F32)
        rtn = sbufA("rtn", [128, 4 * NT], F32)
        rtv = sbufA("rtv", [4 * NT, 128], F32)
        e12g = sbufA("e12g", [128, SEG], F32)
        w12g = sbufA("w12g", [128, SEG], F32)
        mskg = sbufA("mskg", [128, SEG], F32)
        scng = sbufA("scng", [128, SEG], F32)
        zrog = sbufA("zrog", [128, SEG], F32)
        slag = sbufA("slag", [128, SEG], F32)
        tmpg = sbufA("tmpg", [128, SEG], F32)
        ubase = sbufA("ubase", [128, SEG], F32)
        sloti = sbufA("sloti", [128, SEG], I32)
        pdg = sbufA("pdg", [128, SEG * 4], F32)

    def ld3(dst, src, kk):
        return lambda g: g.dma_start(
            out=dst[:].rearrange("p (k m) -> p k m", k=kk),
            in_=src.rearrange("(k p) m -> p k m", p=128))

    q.op("gpsimd", ld3(wg_sb, wg_h, KD), inc=s_wc.inc(16))
    q.op("gpsimd", lambda g: g.dma_start(out=lnw_bc[:], in_=lnw_h[None, :].broadcast_to([128, D])),
         inc=s_wc.inc(16))
    q.op("gpsimd", lambda g: g.dma_start(out=bg_bc[:], in_=bg_h[None, :].broadcast_to([128, E])),
         inc=s_wc.inc(16))
    if not cfg.sparse:
        q.op("gpsimd", lambda g: g.dma_start(out=b1a_sb[:], in_=b1a_h[:]), inc=s_wc.inc(16))
        q.op("gpsimd", lambda g: g.dma_start(out=b1b_sb[:], in_=b1b_h[:]), inc=s_wc.inc(16))
    q.op("gpsimd", lambda g: g.dma_start(out=b2_sb[:], in_=b2_h[:]), inc=s_wc.inc(16))
    q.op("gpsimd", lambda g: g.dma_start(out=ident_sb[:], in_=ident_h[:]), inc=s_wc.inc(16))
    q.op("gpsimd", lambda g: g.dma_start(out=amask_sb[:], in_=amask_h[:]), inc=s_wc.inc(16))
    q.op("gpsimd", lambda g: g.memset(neg_sb[:], -1e30), inc=s_ms.inc(1))
    q.op("gpsimd", lambda g: g.memset(epsln_c[:], float(cfg.eps_ln)), inc=s_ms.inc(1))
    q.op("gpsimd", lambda g: g.memset(epsrms_c[:], float(cfg.eps_rms)), inc=s_ms.inc(1))
    if cfg.sparse:
        q.op("gpsimd", lambda g: g.memset(sent_sb[:], float(2 * NBS)), inc=s_ms.inc(1))
        q.op("gpsimd", lambda g: g.iota(ubase[:], [[S, NB], [1, SEGB]],
                                        channel_multiplier=SEGB,
                                        allow_small_or_imprecise_dtypes=True), inc=s_ms.inc(1))
        q.op("gpsimd", lambda g: g.iota(e_iota[:], [[1, E]], channel_multiplier=0,
                                        allow_small_or_imprecise_dtypes=True), inc=s_ms.inc(1))
        q.op("gpsimd", lambda g: g.iota(rankc[:], [[0, 1]], channel_multiplier=1,
                                        allow_small_or_imprecise_dtypes=True), inc=s_ms.inc(1))
        q.op("gpsimd", lambda g: g.iota(pcol[:], [[0, 1]], channel_multiplier=1,
                                        allow_small_or_imprecise_dtypes=True), inc=s_ms.inc(1))
        q.op("gpsimd", lambda g: g.iota(grow[:], [[1, G]], channel_multiplier=0,
                                        allow_small_or_imprecise_dtypes=True), inc=s_ms.inc(1))
        q.op("gpsimd", lambda g: g.memset(zrog[:], 0.0), inc=s_ms.inc(1))
        q.op("gpsimd", lambda g: g.memset(rzro[:], 0.0), inc=s_ms.inc(1))
        q.op("gpsimd", lambda g: g.memset(pdg[:], 0.0), inc=s_ms.inc(1))
        q.op("gpsimd", lambda g: g.memset(ones_r[:], 1.0), inc=s_ms.inc(1))
        BND = {}
        _bnd_gen = [0]

        def _setup_bounds(e):
            vals = dict(tbl=G * GRP - 1, gath=NBS - 1, scat=2 * NBS - 1,
                        w1=E * NFG * 128 - 1,
                        w2=E * NHc * NFG * 128 - 1, b1=E * 128 - 1)
            gen = _bnd_gen[0]
            _bnd_gen[0] += 1
            ins = None
            for kk, vv in vals.items():
                r = e.register("bnd%d_%s" % (gen, kk)).__enter__()
                ins = e.reg_mov(r, vv)
                BND[kk] = e.snap(r)
            return ins
        q.op("gpsimd", _setup_bounds)
        ms_init = s_ms.n
        q.op("vector", lambda e: e.tensor_scalar(
            rankc[:], rankc[:], 63.5, 0.0, ALU.is_gt, ALU.add),
            waits=[(s_ms.h, ms_init)], inc=s_dv.inc())
        dvx = s_dv.n
        q.op("vector", lambda e: e.tensor_scalar_mul(rankcS[:], rankc[:], float(S)),
             waits=[w(s_dv, dvx)], inc=s_dv.inc())
        dvx = s_dv.n
        q.op("vector", lambda e: e.tensor_scalar_mul(rankcN[:], rankc[:], float(NBS)),
             waits=[w(s_dv, dvx)], inc=s_dv.inc())
        dvx = s_dv.n
        q.op("vector", lambda e: e.tensor_scalar_mul(grow[:], grow[:], float(GRP)),
             waits=[w(s_dv, dvx)], inc=s_dv.inc())
        RT_INIT_DV = s_dv.n
        nfill = (G * GRP * 4 + (128 * 256) - 1) // (128 * 256)
        flat = tbl_h.rearrange("a b -> (a b)")
        for fi in range(nfill):
            lo = fi * 128 * 256
            hi = min((fi + 1) * 128 * 256, G * GRP * 4)
            q.op("sync", (lambda flat, lo, hi: lambda e: e.dma_start(
                out=flat[lo:hi].rearrange("(p j) -> p j", p=128),
                in_=sent_sb[:, 0:(hi - lo) // 128]))(flat, lo, hi),
                waits=[(s_ms.h, ms_init)], inc=s_fill.inc(16))
        TBL_FILL = s_fill.n
        pre_jobs = []
        NKH = 2 if KD % 2 == 0 and KD * FGWc > 2048 else 1
        KH = KD // NKH
        for (srcw, dstw) in ((w1a_h, w1a_sw), (w1b_h, w1b_sw)):
            for ei in range(E):
                for fg in range(NFG):
                    for kh in range(NKH):
                        pre_jobs.append((
                            srcw[ei, kh * KH * 128:(kh + 1) * KH * 128,
                                 fg * FGWc:(fg + 1) * FGWc].rearrange(
                                "(k p) f -> p k f", p=128),
                            dstw[(ei * NFG + fg) * 128:(ei * NFG + fg + 1) * 128,
                                 kh * KH * FGWc:(kh + 1) * KH * FGWc],
                            KH * FGWc))
        for ei in range(E):
            for half in range(NHc):
                for fg in range(NFG):
                    row = ((ei * NHc + half) * NFG + fg) * 128
                    pre_jobs.append((
                        w2_h[ei, fg * FGWc:(fg + 1) * FGWc,
                             half * DHc:(half + 1) * DHc].rearrange(
                            "(fl p) d -> p fl d", p=128),
                        w2_sw[row:row + 128, :],
                        FG * DHc))
        pre_state = dict(free=[0, 0], n=[0, 0], i=0)

        def emit_pre(count):
            for _ in range(count):
                if pre_state["i"] >= len(pre_jobs):
                    return
                srcap, dstap, ncols = pre_jobs[pre_state["i"]]
                sl = pre_state["i"] % 2
                pre_state["i"] += 1
                q.op("gpsimd", (lambda srcap, ncols, sl: lambda e: e.dma_start(
                    out=pcv[sl][:, 0:ncols].rearrange(
                        "p (a b) -> p a b", a=srcap.shape[1]),
                    in_=srcap))(srcap, ncols, sl),
                    waits=[w(s_pro[sl], pre_state["free"][sl])], inc=s_pri[sl].inc(16))
                pre_state["n"][sl] = s_pri[sl].n
                q.op("sync", (lambda dstap, ncols, sl: lambda e: e.dma_start(
                    out=dstap, in_=pcv[sl][:, 0:ncols]))(dstap, ncols, sl),
                    waits=[(s_pri[sl].h, pre_state["n"][sl])], inc=s_pro[sl].inc(16))
                pre_state["free"][sl] = s_pro[sl].n
        PRE_TOTAL = None
    WC = s_wc.n
    wcw = (s_wc.h, WC)
    wcm = (s_ms.h, s_ms.n)

    dbg_cols = [0, 0]
    dbg_xnt = 0
    x_free = [0, 0]
    xnf_free = [0, 0]
    xn2f_free = [0, 0]
    xn2Tslot_free = [0, 0]
    sev_rec = [0, 0]
    we_done = [0] * NB
    smm_last = [0] * NB
    wqkv_carry = [0]

    TBL_DONE = [0]
    RT_SPILL_N = [0]
    SPN_END = [0, 0]

    def emit_rt_spill(b):
        dvc = [s_dv.n]

        def vq(fn, extra=()):
            q.op("vector", fn, waits=[w(s_dv, dvc[0])] + [x for x in extra if x], inc=s_dv.inc())
            dvc[0] = s_dv.n

        vq(lambda e: e.tensor_copy(rtn[:, 0:2 * NT], gidx[:]), extra=[w(s_dv, dv_gidx)])
        vq(lambda e: e.tensor_copy(rtn[:, 2 * NT:3 * NT], glt[:, 2 * NT:3 * NT]),
           extra=[w(s_dv, dv_we)])
        vq(lambda e: e.tensor_copy(rtn[:, 3 * NT:4 * NT], glt[:, 3 * NT:4 * NT]))
        rdy = dvc[0]
        q.op("tensor", lambda e: e.transpose(
            ps_o0[0:4 * NT, 0:128], rtn[:], ident_sb[:]),
            waits=[w(s_dv, rdy), wcw] + pwait("o0h0"), inc=s_pe.inc())
        pe_rt = s_pe.n
        vq(lambda e: e.tensor_copy(rtv[:], ps_o0[0:4 * NT, 0:128]),
           extra=[w(s_pe, pe_rt),
                  (s_rt.h, RT_SPILL_N[0]) if RT_SPILL_N[0] else None])
        pread("o0h0", "dv", s_dv.n)
        q.op("sync", (lambda b: lambda e: e.dma_start(
            out=rt_h[b], in_=rtv[:]))(b), waits=[w(s_dv, dvc[0])], inc=s_rt.inc(16))
        RT_SPILL_N[0] = s_rt.n

    def emit_routing():
        dvc = [s_dv.n]

        def vq(fn, extra=()):
            q.op("vector", fn, waits=[w(s_dv, dvc[0])] + [x for x in extra if x], inc=s_dv.inc())
            dvc[0] = s_dv.n

        for b in range(NB):
            flat = rt_h[b].rearrange("a b -> (a b)")
            q.op("sync", (lambda flat, b: lambda e: e.dma_start(
                out=e12g[:, b * SEGB:(b + 1) * SEGB],
                in_=flat[0:2 * S].rearrange("(p j) -> p j", p=128)))(flat, b),
                waits=[(s_rt.h, RT_SPILL_N[0])] if b == 0 else [], inc=s_rt.inc(16))
            q.op("sync", (lambda flat, b: lambda e: e.dma_start(
                out=w12g[:, b * SEGB:(b + 1) * SEGB],
                in_=flat[2 * S:4 * S].rearrange("(p j) -> p j", p=128)))(flat, b),
                inc=s_rt.inc(16))
        rt_ld = s_rt.n

        pdg4 = pdg[:].rearrange("p (j f) -> p j f", f=4)
        vq(lambda e: e.tensor_scalar(
            tmpg[:], ubase[:], rankcS[:], 0.0, ALU.subtract, ALU.add),
            extra=[(s_rt.h, rt_ld), (s_ms.h, s_ms.n), w(s_dv, RT_INIT_DV)])
        vq(lambda e: e.tensor_copy(pdg4[:, :, 0:1].squeeze(2), tmpg[:]))
        vq(lambda e: e.tensor_scalar(
            pdg4[:, :, 1:2].squeeze(2), tmpg[:], rankcN[:], 0.0, ALU.add, ALU.add))
        vq(lambda e: e.tensor_copy(pdg4[:, :, 2:3].squeeze(2), w12g[:]))

        vq(lambda e: e.memset(slag[:], 0.0))
        for ei in range(E):
            vq((lambda ei: lambda e: e.tensor_single_scalar(
                mskg[:], e12g[:], float(ei), ALU.is_equal))(ei))
            vq(lambda e: e.tensor_tensor_scan(
                scng[:], mskg[:], zrog[:], 0.0, ALU.add, ALU.add))
            rdy = dvc[0]
            q.op("tensor", lambda e: e.transpose(
                ps_o0[0:1, 0:128], scng[:, SEG - 1:SEG], ident_sb[:]),
                waits=[w(s_dv, rdy)] + pwait("o0h0"), inc=s_pe.inc())
            pe_t = s_pe.n
            vq(lambda e: e.tensor_copy(rowa[:], ps_o0[0:1, 0:128]),
               extra=[w(s_pe, pe_t)])
            pread("o0h0", "dv", s_dv.n)
            vq(lambda e: e.tensor_tensor_scan(
                rowb[:], rowa[:], rzro[:], 0.0, ALU.add, ALU.add))
            vq((lambda ei: lambda e: e.tensor_copy(
                crow[0:1, ei:ei + 1], rowb[0:1, 127:128]))(ei))
            vq(lambda e: e.tensor_sub(rowa[:], rowb[:], rowa[:]))
            rdy = dvc[0]
            q.op("tensor", lambda e: e.transpose(
                ps_o0[:, 128:129], rowa[:], ident_sb[0:1, 0:1]),
                waits=[w(s_dv, rdy)] + pwait("o0h0"), inc=s_pe.inc())
            pe_t = s_pe.n
            vq(lambda e: e.tensor_copy(segoff[:], ps_o0[:, 128:129]),
               extra=[w(s_pe, pe_t)])
            pread("o0h0", "dv", s_dv.n)
            vq(lambda e: e.tensor_scalar(
                tmpg[:], scng[:], segoff[:], -1.0, ALU.add, ALU.add))
            vq(lambda e: e.tensor_mul(tmpg[:], tmpg[:], mskg[:]))
            vq(lambda e: e.tensor_add(slag[:], slag[:], tmpg[:]))

        vq(lambda e: e.tensor_copy(cri[0:1, 0:E], crow[0:1, 0:E]))
        vq(lambda e: e.tensor_scalar_add(cri[0:1, 0:E], cri[0:1, 0:E], GRP - 1))
        vq(lambda e: e.tensor_single_scalar(
            cri[0:1, 0:E], cri[0:1, 0:E], cfg.LG, ALU.arith_shift_right))
        vq(lambda e: e.tensor_single_scalar(
            cri[0:1, 0:E], cri[0:1, 0:E], cfg.LG, ALU.arith_shift_left))
        vq(lambda e: e.tensor_copy(crow[0:1, 8:8 + E], cri[0:1, 0:E]))
        vq(lambda e: e.tensor_tensor_scan(
            crow[0:1, 16:16 + E], crow[0:1, 8:8 + E], rzro[0:1, 0:E], 0.0,
            ALU.add, ALU.add))
        vq(lambda e: e.tensor_sub(crow[0:1, 24:24 + E], crow[0:1, 16:16 + E],
                                  crow[0:1, 8:8 + E]))
        rdy = dvc[0]
        q.op("tensor", lambda e: e.matmul(
            ps_o0[:, 0:E], ones_r[:], crow[0:1, 24:24 + E], start=True, stop=True),
            waits=[w(s_dv, rdy), (s_ms.h, s_ms.n)] + pwait("o0h0"), inc=s_pe.inc())
        pe_t = s_pe.n
        vq(lambda e: e.tensor_copy(basec[:, 0:E], ps_o0[:, 0:E]), extra=[w(s_pe, pe_t)])
        pread("o0h0", "dv", s_dv.n)

        vq(lambda e: e.memset(egf[:], -1.0))
        for ei in range(E):
            vq((lambda ei: lambda e: e.tensor_single_scalar(
                etmp[:], grow[:], crow[0:1, 24 + ei:25 + ei], ALU.is_ge))(ei))
            vq(lambda e: e.tensor_add(egf[:], egf[:], etmp[:]))
        vq(lambda e: e.tensor_copy(egrp_i[0:1, 0:G], egf[:]))
        rdy = dvc[0]
        q.op("tensor", lambda e: e.matmul(
            ps_o0[:, 256:256 + G], ones_r[:], egf[:], start=True, stop=True),
            waits=[w(s_dv, rdy), (s_ms.h, s_ms.n)] + pwait("o0h0"), inc=s_pe.inc())
        pe_t = s_pe.n
        vq(lambda e: e.tensor_copy(ebase[:], ps_o0[:, 256:256 + G]),
           extra=[w(s_pe, pe_t)])
        pread("o0h0", "dv", s_dv.n)
        NHl = D // min(512, D)
        DHl = min(512, D)
        vq(lambda e: e.tensor_scalar(eb2[:], ebase[:], 128.0, pcol[:],
                                     ALU.mult, ALU.add))
        vq(lambda e: e.tensor_copy(b1x_i[:, 0:G], eb2[:]))
        vq(lambda e: e.tensor_scalar(eb2[:], ebase[:], float(NFG * 128), pcol[:],
                                     ALU.mult, ALU.add))
        wv = widx_i[:, 0:G * NFG].rearrange("p (g c) -> p g c", c=NFG)
        for fg in range(NFG):
            vq((lambda fg, wv: lambda e: e.tensor_scalar_add(
                wv[:, :, fg:fg + 1].squeeze(2), eb2[:], float(fg * 128)))(fg, wv))
        vq(lambda e: e.tensor_scalar(eb2[:], ebase[:], float(NHl * NFG * 128), pcol[:],
                                     ALU.mult, ALU.add))
        w2v = w2x_i[:, 0:G * NHl * NFG].rearrange("p (g c) -> p g c", c=NHl * NFG)
        for c in range(NHl * NFG):
            vq((lambda c, w2v: lambda e: e.tensor_scalar_add(
                w2v[:, :, c:c + 1].squeeze(2), eb2[:], float(c * 128)))(c, w2v))
        EGRP_DV = dvc[0]

        for ei in range(E):
            vq((lambda ei: lambda e: e.tensor_single_scalar(
                mskg[:], e12g[:], float(ei), ALU.is_equal))(ei))
            vq((lambda ei: lambda e: e.tensor_scalar_mul(
                tmpg[:], mskg[:], basec[:, ei:ei + 1]))(ei))
            vq(lambda e: e.tensor_add(slag[:], slag[:], tmpg[:]))
        vq(lambda e: e.tensor_copy(sloti[:], slag[:]))
        rdy = dvc[0]
        for j in range(SEG):
            q.op("gpsimd", (lambda j: lambda e: e.indirect_dma_start(
                out=tbl_h,
                out_offset=bass.IndirectOffsetOnAxis(ap=sloti[:, j:j + 1], axis=0),
                in_=pdg[:, j * 4:(j + 1) * 4], in_offset=None,
                bounds_check=BND["tbl"], oob_is_err=False))(j),
                waits=[(s_fill.h, TBL_FILL), w(s_dv, rdy)] if j == 0 else [],
                inc=s_tbl.inc(16))
        TBL_DONE[0] = s_tbl.n

    for b in range(NB):
        logit_guard = 0

        for tt in range(NT):
            sl = tt % 2
            par = tt % 2
            row0 = b * S + tt * 128
            cb = 16 * par
            q.op("sync", (lambda r, sl: lambda e: e.dma_start(
                out=xin[sl][:], in_=x_h[r:r + 128, :]))(r=row0, sl=sl),
                waits=[w(s_dv, x_free[sl])], inc=s_x[sl].inc(16))
            xw = [(s_x[sl].h, s_x[sl].n)]
            q.op("scalar", (lambda sl, cb: lambda e: e.activation(
                sqs[:], xin[sl][:], AF.Square, accum_out=cols[:, cb + 1:cb + 2]))(sl, cb),
                waits=xw + [w(s_dv, x_free[sl])] +
                ([w(s_dbg, dbg_cols[par])] if cfg.debug else []), inc=s_ac.inc())
            ac_sq = s_ac.n
            q.op("vector", (lambda sl, cb: lambda e: e.reduce_sum(
                cols[:, cb:cb + 1], xin[sl][:], axis=AX.X))(sl, cb),
                waits=xw + ([w(s_dbg, dbg_cols[par])] if cfg.debug else []),
                inc=s_dv.inc())
            dv_rs = s_dv.n
            q.op("vector", (lambda cb: lambda e: e.tensor_scalar_mul(
                cols[:, cb + 2:cb + 3], cols[:, cb:cb + 1], 1.0 / D))(cb),
                waits=[w(s_dv, dv_rs)], inc=s_dv.inc())
            dv_mean = s_dv.n
            q.op("vector", (lambda cb: lambda e: e.tensor_scalar_mul(
                cols[:, cb + 5:cb + 6], cols[:, cb + 1:cb + 2], 1.0 / D))(cb),
                waits=[w(s_ac, ac_sq)], inc=s_dv.inc())
            dv_ex2 = s_dv.n
            q.op("vector", (lambda cb: lambda e: e.scalar_tensor_tensor(
                cols[:, cb + 6:cb + 7], cols[:, cb + 2:cb + 3], cols[:, cb + 2:cb + 3],
                cols[:, cb + 5:cb + 6], ALU.mult, ALU.subtract))(cb),
                waits=[w(s_dv, dv_mean), w(s_dv, dv_ex2)], inc=s_dv.inc())
            dv_nv = s_dv.n
            q.op("scalar", (lambda cb: lambda e: e.activation(
                cols[:, cb + 3:cb + 4], cols[:, cb + 6:cb + 7], AF.Sqrt,
                scale=-1.0, bias=epsln_c[:]))(cb),
                waits=[w(s_dv, dv_nv), wcw, wcm], inc=s_ac.inc())
            ac_sd = s_ac.n
            q.op("vector", (lambda cb: lambda e: e.reciprocal(
                cols[:, cb + 3:cb + 4], cols[:, cb + 3:cb + 4]))(cb),
                waits=[w(s_ac, ac_sd)], inc=s_dv.inc())
            dv_rstd = s_dv.n
            q.op("vector", (lambda sl, par, cb: lambda e: e.tensor_scalar(
                xnf[par][:], xin[sl][:], cols[:, cb + 2:cb + 3], cols[:, cb + 3:cb + 4],
                ALU.subtract, ALU.mult))(sl, par, cb),
                waits=[w(s_pe, xnf_free[par]), w(s_dv, dv_rstd)], inc=s_dv.inc())
            dv_xn = s_dv.n
            x_free[sl] = dv_xn
            q.op("scalar", (lambda par, cb: lambda e: e.activation(
                sqs[:], xnf[par][:], AF.Square, accum_out=cols[:, cb + 7:cb + 8]))(par, cb),
                waits=[w(s_dv, dv_xn)], inc=s_ac.inc())
            ac_s2 = s_ac.n
            q.op("scalar", (lambda cb: lambda e: e.activation(
                cols[:, cb + 4:cb + 5], cols[:, cb + 7:cb + 8], AF.Sqrt,
                scale=1.0 / D, bias=epsrms_c[:]))(cb),
                waits=[w(s_ac, ac_s2)], inc=s_ac.inc())
            ac_sd2 = s_ac.n
            q.op("vector", (lambda cb: lambda e: e.reciprocal(
                cols[:, cb + 4:cb + 5], cols[:, cb + 4:cb + 5]))(cb),
                waits=[w(s_ac, ac_sd2)], inc=s_dv.inc())
            dv_rstd2 = s_dv.n
            q.op("vector", (lambda par, cb: lambda e: e.scalar_tensor_tensor(
                xn2f[par][:], xnf[par][:], cols[:, cb + 4:cb + 5], lnw_bc[:],
                ALU.mult, ALU.mult))(par, cb),
                waits=[wcw, wcm, w(s_pe, xn2f_free[par]), w(s_dv, dv_rstd2)] +
                      ([(s_spn[par].h, s_spn[par].n)] if cfg.sparse else []),
                inc=s_dv.inc())
            dv_xn2 = s_dv.n
            if cfg.sparse:
                q.op("sync", (lambda par, row0: lambda e: e.dma_start(
                    out=xn2n_h[row0:row0 + 128, :], in_=xn2f[par][:]))(par, row0),
                    waits=[w(s_dv, dv_xn2)], inc=s_spn[par].inc(16))

            if cfg.debug:
                q.op("sync", (lambda b, tt, cb: lambda e: e.dma_start(
                    out=colso_h[b * NT + tt], in_=cols[:, cb:cb + 8]))(b, tt, cb),
                    waits=[w(s_dv, dv_xn2)], inc=s_dbg.inc(16))
                dbg_cols[par] = s_dbg.n

            for which, srcf, dstT, dvready in ((0, xnf[par], xnT, dv_xn),
                                               (1, xn2f[par], xn2T, dv_xn2)):
                for c0 in range(0, KD, 4):
                    half = (c0 // 4) % 2
                    reg = "o0h%d" % half
                    n = min(4, KD - c0)
                    for j in range(n):
                        q.op("tensor", (lambda srcf, c0, j, half: lambda e: e.transpose(
                            ps_o0[:, half * 512 + j * 128: half * 512 + (j + 1) * 128],
                            srcf[:, (c0 + j) * 128: (c0 + j + 1) * 128], ident_sb[:]))(
                                srcf, c0, j, half),
                            waits=([w(s_dv, dvready), wcw, wcm] + pwait(reg) +
                                   ([w(s_dbg, dbg_xnt)] if cfg.debug else [])) if j == 0 else [],
                            inc=s_pe.inc() if j == n - 1 else None)
                    pe_tr = s_pe.n
                    extra = []
                    if which == 1 and c0 == 0 and not cfg.sparse:
                        extra = [w(s_spx, 16 * KD * b)]
                    if which == 1 and c0 == 0 and cfg.sparse:
                        extra = [w(s_pe, xn2Tslot_free[par])]
                    ttc = (tt % 2) if (which == 1 and cfg.sparse) else tt
                    q.op("vector", (lambda dstT, c0, n, half, ttc: lambda e: e.tensor_copy(
                        dstT[:].rearrange("p (k s) -> p k s", k=KD)[
                            :, c0:c0 + n, ttc * 128:(ttc + 1) * 128],
                        ps_o0[:, half * 512: half * 512 + n * 128].rearrange(
                            "p (k s) -> p k s", k=n)))(dstT, c0, n, half, ttc),
                        waits=[w(s_pe, pe_tr)] + extra, inc=s_dv.inc())
                    pread(reg, "dv", s_dv.n)
                if which == 0:
                    xnf_free[par] = pe_tr
                    xnT_done = s_dv.n
                else:
                    xn2f_free[par] = pe_tr
                    xn2T_done = s_dv.n

            x2b = (tt % 2) * 128 if cfg.sparse else tt * 128
            for k in range(KD):
                q.op("tensor", (lambda k, x2b: lambda e: e.matmul(
                    ps_o1[:, 0:E],
                    xn2T[:, k * SW2 + x2b: k * SW2 + x2b + 128],
                    wg_sb[:, k * E:(k + 1) * E],
                    start=(k == 0), stop=(k == KD - 1)))(k, x2b),
                    waits=([w(s_dv, xn2T_done)] + pwait("o1h0")) if k == 0 else [],
                    inc=s_pe.inc() if k == KD - 1 else None)
            pe_lg = s_pe.n
            xn2Tslot_free[tt % 2] = pe_lg
            q.op("vector", (lambda tt: lambda e: e.tensor_add(
                logits[:, tt * E:(tt + 1) * E], ps_o1[:, 0:E], bg_bc[:]))(tt),
                waits=[w(s_pe, pe_lg), wcw, wcm], inc=s_dv.inc())
            pread("o1h0", "dv", s_dv.n)
            lg_done = s_dv.n

        if cfg.debug:
            q.op("sync", (lambda b: lambda e: e.dma_start(
                out=lgo_h[b], in_=logits[:]))(b),
                waits=[w(s_dv, s_dv.n)], inc=s_dbg.inc(16))
            dbg_lg = s_dbg.n
        lg3 = logits[:].rearrange("p (t e) -> p t e", e=E)
        m1 = glt[:, 0 * NT:1 * NT]
        m2 = glt[:, 1 * NT:2 * NT]
        w1c = glt[:, 2 * NT:3 * NT]
        w2c = glt[:, 3 * NT:4 * NT]
        eq1 = geq[:, 0:NT * E]
        eq2 = geq[:, NT * E:2 * NT * E]
        eq13 = eq1.rearrange("p (t e) -> p t e", e=E)
        eq23 = eq2.rearrange("p (t e) -> p t e", e=E)
        q.op("vector", lambda e: e.reduce_max(m1, lg3, axis=AX.X),
             waits=[w(s_dv, lg_done)] + ([w(s_dbg, dbg_lg)] if cfg.debug else []),
             inc=s_dv.inc())
        dv_m1 = s_dv.n
        q.op("vector", lambda e: e.tensor_tensor(
            eq13, lg3, m1.unsqueeze(2).broadcast_to([128, NT, E]), ALU.is_equal),
            waits=[w(s_dv, dv_m1)], inc=s_dv.inc())
        dv_eq1 = s_dv.n
        q.op("vector", lambda e: e.scalar_tensor_tensor(
            eq2, eq1, -1e30, logits[:], ALU.mult, ALU.add),
            waits=[w(s_dv, dv_eq1)], inc=s_dv.inc())
        dv_msk = s_dv.n
        q.op("vector", lambda e: e.reduce_max(m2, eq23, axis=AX.X),
             waits=[w(s_dv, dv_msk)], inc=s_dv.inc())
        dv_m2 = s_dv.n
        q.op("vector", lambda e: e.tensor_sub(w1c, m1, m2), waits=[w(s_dv, dv_m2)],
             inc=s_dv.inc())
        dv_w1 = s_dv.n
        q.op("scalar", lambda e: e.activation(w1c, w1c, AF.Sigmoid),
             waits=[w(s_dv, dv_w1)], inc=s_ac.inc())
        ac_w1 = s_ac.n
        q.op("vector", lambda e: e.tensor_tensor(
            eq23, eq23, m2.unsqueeze(2).broadcast_to([128, NT, E]), ALU.is_equal),
            waits=[w(s_ac, ac_w1)], inc=s_dv.inc())
        dv_eq2 = s_dv.n
        if cfg.sparse:
            eb = e_iota[:].unsqueeze(1).broadcast_to([128, NT, E])
            gsc3 = gsc[:].rearrange("p (t e) -> p t e", e=E)
            q.op("vector", lambda e: e.tensor_tensor(gsc3, eq13, eb, ALU.mult),
                 waits=[w(s_dv, dv_eq2), (s_ms.h, s_ms.n)], inc=s_dv.inc())
            dvtmp = s_dv.n
            q.op("vector", lambda e: e.reduce_sum(gidx[:, 0:NT], gsc3, axis=AX.X),
                 waits=[w(s_dv, dvtmp)], inc=s_dv.inc())
            dvtmp = s_dv.n
            q.op("vector", lambda e: e.tensor_tensor(gsc3, eq23, eb, ALU.mult),
                 waits=[w(s_dv, dvtmp)], inc=s_dv.inc())
            dvtmp = s_dv.n
            q.op("vector", lambda e: e.reduce_sum(gidx[:, NT:2 * NT], gsc3, axis=AX.X),
                 waits=[w(s_dv, dvtmp)], inc=s_dv.inc())
            dv_gidx = s_dv.n
        q.op("vector", lambda e: e.tensor_scalar(w2c, w1c, -1.0, 1.0, ALU.mult, ALU.add),
             inc=s_dv.inc())
        dv_w2 = s_dv.n
        q.op("vector", lambda e: e.tensor_tensor(
            eq13, eq13, w1c.unsqueeze(2).broadcast_to([128, NT, E]), ALU.mult),
            waits=[w(s_dv, dv_eq2)], inc=s_dv.inc())
        dv_p1 = s_dv.n
        q.op("vector", lambda e: e.tensor_tensor(
            eq23, eq23, w2c.unsqueeze(2).broadcast_to([128, NT, E]), ALU.mult),
            waits=[w(s_dv, dv_w2)], inc=s_dv.inc())
        dv_p2 = s_dv.n
        q.op("vector", (lambda b: lambda e: e.tensor_add(
            we_all[:, b * NT * E:(b + 1) * NT * E], eq1, eq2))(b),
            waits=[w(s_dv, dv_p1), w(s_dv, dv_p2)], inc=s_dv.inc())
        dv_we = s_dv.n

        for c0 in range(0, NT, 4):
            half = (c0 // 4) % 2
            reg = "o0h%d" % half
            n = min(4, NT - c0)
            for j in range(n):
                q.op("tensor", (lambda b, c0, j, half: lambda e: e.transpose(
                    ps_o0[0:E, half * 512 + j * 128: half * 512 + (j + 1) * 128],
                    we_all[:, (b * NT + c0 + j) * E:(b * NT + c0 + j + 1) * E],
                    ident_sb[:]))(b, c0, j, half),
                    waits=([w(s_dv, dv_we)] + pwait(reg)) if j == 0 else [],
                    inc=s_pe.inc() if j == n - 1 else None)
            pe_wt = s_pe.n
            q.op("vector", (lambda b, c0, n, half: lambda e: e.tensor_copy(
                weT_all[0:E, b * S + c0 * 128: b * S + (c0 + n) * 128],
                ps_o0[0:E, half * 512: half * 512 + n * 128]))(b, c0, n, half),
                waits=[w(s_pe, pe_wt)], inc=s_dv.inc())
            pread(reg, "dv", s_dv.n)
        we_done[b] = s_dv.n

        if cfg.sparse:
            emit_rt_spill(b)
            if b == NB - 1:
                emit_routing()
            emit_pre((len(pre_jobs) + NB - 1) // NB)

        if cfg.debug:
            for k in range(KD):
                q.op("sync", (lambda b, k: lambda e: e.dma_start(
                    out=xnso_h[k, :, b * S:(b + 1) * S],
                    in_=xnT[:, k * S:(k + 1) * S]))(b, k),
                    waits=[w(s_dv, xnT_done)] if k == 0 else [], inc=s_dbg.inc(16))
            dbg_xnt = s_dbg.n
        if not cfg.sparse:
            for k in range(KD):
                q.op("sync", (lambda b, k: lambda e: e.dma_start(
                    out=xn2s_h[k, :, b * S:(b + 1) * S],
                    in_=xn2T[:, k * S:(k + 1) * S]))(b, k),
                    waits=[w(s_dv, xn2T_done)] if k == 0 else [], inc=s_spx.inc(16))

        wqkv_last_pe = wqkv_carry[0]
        if not cfg.only_ln:
            for (wsrc, kk, dstT) in ((wq_h, KD, qT_sb), (wk_h, KD, kT_sb), (wvd_h, KD, vdT_sb)):
                q.op("gpsimd", (lambda wsrc, kk: ld3(wqkv_sb, wsrc, kk))(wsrc, kk),
                     waits=[w(s_pe, wqkv_last_pe)], inc=s_wqkv.inc(16))
                wq_n = s_wqkv.n
                for m in range(KQ):
                    for c in range(0, S, 512):
                        cw = min(512, S - c)
                        half = (c // 512) % 2
                        reg = "o1h%d" % half
                        for k in range(KD):
                            q.op("tensor", (lambda m, c, cw, k, half: lambda e: e.matmul(
                                ps_o1[:, half * 512: half * 512 + cw],
                                wqkv_sb[:, k * DQ + m * 128: k * DQ + (m + 1) * 128],
                                xnT[:, k * S + c: k * S + c + cw],
                                start=(k == 0), stop=(k == KD - 1)))(m, c, cw, k, half),
                                waits=([w(s_dv, xnT_done), (s_wqkv.h, wq_n)] + pwait(reg))
                                if k == 0 else [],
                                inc=s_pe.inc() if k == KD - 1 else None)
                        pe_g = s_pe.n
                        q.op("vector", (lambda dstT, m, c, cw, half: lambda e: e.tensor_copy(
                            dstT[:, m * S + c: m * S + c + cw],
                            ps_o1[:, half * 512: half * 512 + cw]))(dstT, m, c, cw, half),
                            waits=[w(s_pe, pe_g)], inc=s_dv.inc())
                        pread(reg, "dv", s_dv.n)
                wqkv_last_pe = s_pe.n
        qkv_done = s_dv.n

        if not cfg.only_ln:
            q.op("gpsimd", ld3(wqkv_sb, wvu_h, KQ),
                 waits=[w(s_pe, wqkv_last_pe)], inc=s_wqkv.inc(16))
            wvu_n = s_wqkv.n
        for tt in range(0 if cfg.only_ln else NT):
            regs = ("o0h0", "o0h1") if tt % 2 == 0 else ("o1h0", "o1h1")
            pst = ps_o0 if tt % 2 == 0 else ps_o1
            nmm = 0
            tot = ((D + 511) // 512) * KQ
            for h in range(0, D, 512):
                hw = min(512, D - h)
                for kq in range(KQ):
                    nmm += 1
                    q.op("tensor", (lambda tt, h, hw, kq, pst: lambda e: e.matmul(
                        pst[:, h:h + hw],
                        vdT_sb[:, kq * S + tt * 128: kq * S + (tt + 1) * 128],
                        wqkv_sb[:, kq * D + h: kq * D + h + hw],
                        start=(kq == 0), stop=(kq == KQ - 1)))(tt, h, hw, kq, pst),
                        waits=(pwait(*regs) + [w(s_dv, qkv_done), (s_wqkv.h, wvu_n)])
                        if nmm == 1 else [],
                        inc=s_pe.inc() if nmm == tot else None)
            pe_v = s_pe.n
            q.op("vector", (lambda tt, pst: lambda e: e.tensor_copy(
                vnat[:, tt * D:(tt + 1) * D], pst[:, 0:D]))(tt, pst),
                waits=[w(s_pe, pe_v)], inc=s_dv.inc())
            for r in regs:
                pread(r, "dv", s_dv.n)
        if not cfg.only_ln:
            wqkv_carry[0] = s_pe.n

        expb_free = 0

        def emit_scores(tt):
            ncols = (tt + 1) * 128
            nblk = (ncols + 511) // 512
            for cbk in range(nblk):
                cw = min(512, ncols - cbk * 512)
                for kq in range(KQ):
                    q.op("tensor", (lambda tt, cbk, cw, kq: lambda e: e.matmul(
                        ps_big[:, cbk * 512: cbk * 512 + cw],
                        qT_sb[:, kq * S + tt * 128: kq * S + (tt + 1) * 128],
                        kT_sb[:, kq * S + cbk * 512: kq * S + cbk * 512 + cw],
                        start=(kq == 0), stop=(kq == KQ - 1)))(tt, cbk, cw, kq),
                        waits=pwait("bigA") if (cbk == 0 and kq == 0) else [],
                        inc=s_pe.inc() if (cbk == nblk - 1 and kq == KQ - 1) else None)
            return s_pe.n

        def emit_softmax(tt, pe_sc):
            ncols = (tt + 1) * 128
            q.op("vector", (lambda tt: lambda e: e.copy_predicated(
                ps_big[:, tt * 128:(tt + 1) * 128], amask_sb[:], neg_sb[:]))(tt),
                waits=[w(s_pe, pe_sc), wcw, wcm], inc=s_dv.inc())
            dv_mask = s_dv.n
            pread("bigA", "dv", dv_mask)
            q.op("vector", (lambda ncols: lambda e: e.reduce_max(
                cols[:, 16:17], ps_big[:, 0:ncols], axis=AX.X, negate=True))(ncols),
                waits=[w(s_dv, dv_mask)], inc=s_dv.inc())
            dv_mx = s_dv.n
            pread("bigA", "dv", dv_mx)
            q.op("scalar", lambda e: e.activation(
                cols[:, 17:18], cols[:, 16:17], AF.Copy, scale=float(cfg.scale)),
                waits=[w(s_dv, dv_mx)], inc=s_ac.inc())
            ac_cp = s_ac.n
            q.op("scalar", (lambda ncols: lambda e: e.activation(
                expb[:, 0:ncols], ps_big[:, 0:ncols], AF.Exp,
                bias=cols[:, 17:18], scale=float(cfg.scale),
                accum_out=cols[:, 18:19]))(ncols),
                waits=[w(s_pe, expb_free), w(s_ac, ac_cp)] +
                      ([(s_dbg.h, s_dbg.n)] if cfg.debug else []), inc=s_ac.inc())
            ac_exp = s_ac.n
            pread("bigA", "ac", ac_exp)
            q.op("vector", lambda e: e.reciprocal(cols[:, 19:20], cols[:, 18:19]),
                 waits=[w(s_ac, ac_exp)], inc=s_dv.inc())
            dv_rc = s_dv.n
            q.op("vector", (lambda ncols: lambda e: e.tensor_scalar_mul(
                expb[:, 0:ncols], expb[:, 0:ncols], cols[:, 19:20]))(ncols),
                waits=[w(s_dv, dv_rc)], inc=s_dv.inc())
            return s_dv.n

        def emit_trans_smm(tt, dv_attn):
            nonlocal expb_free
            n_tk = tt + 1
            if cfg.debug:
                rr = b * S + tt * 128
                q.op("sync", (lambda rr, n_tk: lambda e: e.dma_start(
                    out=ao_h[rr:rr + 128, 0:n_tk * 128], in_=expb[:, 0:n_tk * 128]))(rr, n_tk),
                    waits=[w(s_dv, dv_attn)], inc=s_dbg.inc(16))
            for c0 in range(0, n_tk, 4):
                half = (c0 // 4) % 2
                reg = "o0h%d" % half
                n = min(4, n_tk - c0)
                for j in range(n):
                    q.op("tensor", (lambda c0, j, half: lambda e: e.transpose(
                        ps_o0[:, half * 512 + j * 128: half * 512 + (j + 1) * 128],
                        expb[:, (c0 + j) * 128: (c0 + j + 1) * 128], ident_sb[:]))(c0, j, half),
                        waits=([w(s_dv, dv_attn)] + pwait(reg)) if j == 0 else [],
                        inc=s_pe.inc() if j == n - 1 else None)
                pe_tr = s_pe.n
                q.op("vector", (lambda c0, n, half: lambda e: e.tensor_copy(
                    attnT[:, c0 * 128:(c0 + n) * 128],
                    ps_o0[:, half * 512: half * 512 + n * 128]))(c0, n, half),
                    waits=[w(s_pe, pe_tr)], inc=s_dv.inc())
                pread(reg, "dv", s_dv.n)
            expb_free = s_pe.n
            attnT_done = s_dv.n
            nmm = 0
            tot = n_tk * ((D + 511) // 512)
            for tk in range(n_tk):
                for h in range(0, D, 512):
                    hw = min(512, D - h)
                    nmm += 1
                    q.op("tensor", (lambda tk, h, hw, n_tk: lambda e: e.matmul(
                        ps_o1[:, h:h + hw],
                        attnT[:, tk * 128:(tk + 1) * 128],
                        vnat[:, tk * D + h: tk * D + h + hw],
                        start=(tk == 0), stop=(tk == n_tk - 1)))(tk, h, hw, n_tk),
                        waits=([w(s_dv, attnT_done)] + pwait("o1h0", "o1h1"))
                        if nmm == 1 else [],
                        inc=s_pe.inc() if nmm == tot else None)
            pe_s = s_pe.n
            sl = tt % 2
            q.op("vector", (lambda sl: lambda e: e.tensor_copy(
                sev[sl][:], ps_o1[:, 0:D]))(sl),
                waits=[w(s_pe, pe_s), (s_se[sl].h, sev_rec[sl]) if sev_rec[sl] else None],
                inc=s_dv.inc())
            pread("o1h0", "dv", s_dv.n)
            pread("o1h1", "dv", s_dv.n)
            dv_sev = s_dv.n
            row0 = b * S + tt * 128
            q.op("sync", (lambda sl, row0: lambda e: e.dma_start(
                out=s_h[row0:row0 + 128, :], in_=sev[sl][:]))(sl, row0),
                waits=[w(s_dv, dv_sev)], inc=s_se[sl].inc(16))
            sev_rec[sl] = s_se[sl].n
            return pe_s

        prev = None
        for tt in range(0 if cfg.only_ln else NT):
            pe_sc = emit_scores(tt)
            if prev is not None:
                smm_last[b] = emit_trans_smm(prev[0], prev[1])
            dv_attn = emit_softmax(tt, pe_sc)
            prev = (tt, dv_attn)
        if prev is not None:
            smm_last[b] = emit_trans_smm(prev[0], prev[1])
        if cfg.debug:
            vnat_dv = s_dv.n
            q.op("sync", (lambda b: lambda e: e.dma_start(
                out=vno_h[b], in_=vnat[:]))(b), waits=[w(s_dv, vnat_dv)], inc=s_dbg.inc(16))
            q.op("sync", (lambda b: lambda e: e.dma_start(
                out=qo_h[b], in_=qT_sb[:]))(b), inc=s_dbg.inc(16))
            q.op("sync", (lambda b: lambda e: e.dma_start(
                out=ko_h[b], in_=kT_sb[:]))(b), inc=s_dbg.inc(16))

    PE_END_A = s_pe.n
    DV_END_A = s_dv.n
    AC_END_A = s_ac.n
    SE_END = (s_se[0].n, s_se[1].n)

    q.run_block(nc)
    stA.close()

    stB = ExitStack()

    def sbufB(name, shape, dt):
        return stB.enter_context(nc.sbuf_tensor(name, shape, dt))

    w1a_b = [sbufB("w1a_b0", [128, KD * FG * 128], BF16),
             sbufB("w1a_b1", [128, KD * FG * 128], BF16)]
    w1b_b = [sbufB("w1b_b0", [128, KD * FG * 128], BF16),
             sbufB("w1b_b1", [128, KD * FG * 128], BF16)]
    xcb = [sbufB("xc0", [128, D], F32), sbufB("xc1", [128, D], F32)]
    if cfg.sparse:
        scb1 = sbufB("scb", [128, D], F32)
        scb = [scb1, scb1]
    else:
        scb = [sbufB("scb0", [128, D], F32), sbufB("scb1", [128, D], F32)]
    gsb = sbufB("gsb", [128, D], F32)
    osb = [sbufB("osb0", [128, D], F32), sbufB("osb1", [128, D], F32)]
    if not cfg.sparse:
        xn2B = sbufB("xn2B", [128, KD * S], BF16)
        acc = sbufB("acc", [128, NT * D], F32)
        w2_b = [sbufB("w2_b0", [128, FG * D], BF16),
                sbufB("w2_b1", [128, FG * D], BF16)]
        yT = [sbufB("yT0", [128, FG * TBW], BF16), sbufB("yT1", [128, FG * TBW], BF16)]
        silu_sb = sbufB("silu_sb", [128, TBW], BF16)
    else:
        DHp = min(512, D)
        w2_b = [sbufB("w2_b0", [128, FG * DHp], BF16),
                sbufB("w2_b1", [128, FG * DHp], BF16)]
        yT = [sbufB("yT0", [128, NFL * GRP], BF16), sbufB("yT1", [128, NFL * GRP], BF16)]
        silu_sb = sbufB("silu_sb", [128, GRP], BF16)
        xgT = [sbufB("xgT0", [128, KD * GRP], BF16), sbufB("xgT1", [128, KD * GRP], BF16)]
        xgf = [sbufB("xgf0", [128, D], F32), sbufB("xgf1", [128, D], F32)]
        tb_sb = [sbufB("tb0", [128, NGT * 4], F32), sbufB("tb1", [128, NGT * 4], F32)]
        gi32 = [sbufB("gi0", [128, NGT], mybir.dt.int32),
                sbufB("gi1", [128, NGT], mybir.dt.int32)]
        si32 = [sbufB("si0", [128, NGT], mybir.dt.int32),
                sbufB("si1", [128, NGT], mybir.dt.int32)]
        b1ag = [sbufB("b1ag0", [128, NFL], F32), sbufB("b1ag1", [128, NFL], F32)]
        b1bg = [sbufB("b1bg0", [128, NFL], F32), sbufB("b1bg1", [128, NFL], F32)]
        o1b = [sbufB("o1b0", [128, D], F32), sbufB("o1b1", [128, D], F32)]
        o2b = [sbufB("o2b0", [128, D], F32), sbufB("o2b1", [128, D], F32)]


    barrier = [(s_pe.h, PE_END_A), (s_dv.h, DV_END_A), (s_ac.h, AC_END_A),
               (s_spx.h, 16 * KD * NB) if not cfg.sparse else None,
               (s_se[0].h, SE_END[0]) if SE_END[0] else None,
               (s_se[1].h, SE_END[1]) if SE_END[1] else None]
    barrier = [x for x in barrier if x is not None]
    if cfg.sparse:
        barrier = barrier + [(s_spn[0].h, s_spn[0].n) if s_spn[0].n else None,
                             (s_spn[1].h, s_spn[1].n) if s_spn[1].n else None,
                             (s_pro[0].h, s_pro[0].n), (s_pro[1].h, s_pro[1].n),
                             (s_tbl.h, TBL_DONE[0]) if TBL_DONE[0] else None]
        barrier = [x for x in barrier if x is not None]

    state = dict(w1_free=[0, 0], w2_free=[0, 0], yT_free=[0, 0], yT_ready=[0, 0],
                 silu_free=0, fl_last_pe=0, c_done=0, g_free=0,
                 xc_rec=[0, 0], sc_rec=[0, 0], osb_rec=[0, 0], o1_rec=[0, 0],
                 acc_ready=[0] * NT, first=True)

    def emit_w_loads(ts, e_i, fg):
        gidx = (ts * E + e_i) * NFG + fg
        slot = gidx % 2
        extra = list(barrier) if state["first"] else []
        state["first"] = False
        q.op("gpsimd", (lambda e_i, fg, slot: lambda e: e.dma_start(
            out=w1a_b[slot][:].rearrange("p (k f) -> p k f", k=KD),
            in_=w1a_h[e_i, :, fg * FG * 128:(fg + 1) * FG * 128].rearrange(
                "(k p) f -> p k f", p=128)))(e_i, fg, slot),
            waits=extra + [w(s_pe, state["w1_free"][slot])], inc=s_w1a[slot].inc(16))
        q.op("gpsimd", (lambda e_i, fg, slot: lambda e: e.dma_start(
            out=w1b_b[slot][:].rearrange("p (k f) -> p k f", k=KD),
            in_=w1b_h[e_i, :, fg * FG * 128:(fg + 1) * FG * 128].rearrange(
                "(k p) f -> p k f", p=128)))(e_i, fg, slot),
            inc=s_w1b[slot].inc(16))
        q.op("gpsimd", (lambda e_i, fg, slot: lambda e: e.dma_start(
            out=w2_b[slot][:].rearrange("p (f d) -> p f d", f=FG),
            in_=w2_h[e_i, fg * FG * 128:(fg + 1) * FG * 128, :].rearrange(
                "(f p) d -> p f d", p=128)))(e_i, fg, slot),
            waits=[w(s_pe, state["w2_free"][slot])], inc=s_w2[slot].inc(16))
        return slot, s_w1a[slot].n, s_w1b[slot].n, s_w2[slot].n

    def emit_fl(ts, e_i, fg, tb, wslot, w1a_n, w1b_n, xn2_n, unit):
        ys = unit % 2
        for fl in range(FG):
            par = fl % 2
            ra, rb = "ba%d" % par, "bb%d" % par
            pa = ps_big[:, par * 1024: par * 1024 + TBW]
            pb = ps_big[:, par * 1024 + 512: par * 1024 + 512 + TBW]
            for k in range(KD):
                waits = []
                if k == 0:
                    waits = pwait(ra) + [(s_w1a[wslot].h, w1a_n), (s_xn2r.h, xn2_n)]
                q.op("tensor", (lambda fl, k, tb, wslot, pa: lambda e: e.matmul(
                    pa, w1a_b[wslot][:, k * FG * 128 + fl * 128: k * FG * 128 + (fl + 1) * 128],
                    xn2B[:, k * S + tb * TBW: k * S + (tb + 1) * TBW],
                    start=(k == 0), stop=(k == KD - 1)))(fl, k, tb, wslot, pa),
                    waits=waits, inc=s_pe.inc() if k == KD - 1 else None)
            pe_a = s_pe.n
            for k in range(KD):
                waits = []
                if k == 0:
                    waits = pwait(rb) + [(s_w1b[wslot].h, w1b_n)]
                q.op("tensor", (lambda fl, k, tb, wslot, pb: lambda e: e.matmul(
                    pb, w1b_b[wslot][:, k * FG * 128 + fl * 128: k * FG * 128 + (fl + 1) * 128],
                    xn2B[:, k * S + tb * TBW: k * S + (tb + 1) * TBW],
                    start=(k == 0), stop=(k == KD - 1)))(fl, k, tb, wslot, pb),
                    waits=waits, inc=s_pe.inc() if k == KD - 1 else None)
            pe_b = s_pe.n
            bcol = e_i * NFL + fg * FG + fl
            q.op("scalar", (lambda bcol, pa: lambda e: e.activation(
                silu_sb[:], pa, AF.Silu, bias=b1a_sb[:, bcol:bcol + 1]))(bcol, pa),
                waits=[w(s_pe, pe_a), w(s_dv, state["silu_free"]), wcw], inc=s_ac.inc())
            pread(ra, "ac", s_ac.n)
            ac_silu = s_ac.n
            q.op("vector", (lambda bcol, pb, ys, fl: lambda e: e.scalar_tensor_tensor(
                yT[ys][:, fl * TBW:(fl + 1) * TBW], pb, b1b_sb[:, bcol:bcol + 1],
                silu_sb[:], ALU.add, ALU.mult))(bcol, pb, ys, fl),
                waits=[w(s_pe, pe_b), w(s_ac, ac_silu), wcw] +
                      ([w(s_pe, state["yT_free"][ys])] if fl == 0 else []),
                inc=s_dv.inc())
            pread(rb, "dv", s_dv.n)
            state["silu_free"] = s_dv.n
        state["yT_ready"][ys] = s_dv.n
        state["fl_last_pe"] = s_pe.n
        state["w1_free"][wslot] = s_pe.n

    def emit_o(ts, e_i, fg, tb, wslot, w2_n, unit):
        ys = unit % 2
        for ttl in range(TPB):
            tt = tb * TPB + ttl
            po = ps_o0 if ttl % 2 == 0 else ps_o1
            regs = ("o0h0", "o0h1") if ttl % 2 == 0 else ("o1h0", "o1h1")
            nmm = 0
            tot = FG * ((D + 511) // 512)
            for fl in range(FG):
                for h in range(0, D, 512):
                    hw = min(512, D - h)
                    nmm += 1
                    waits = []
                    if nmm == 1:
                        waits = pwait(*regs) + [w(s_dv, state["yT_ready"][ys]),
                                                (s_w2[wslot].h, w2_n)]
                    q.op("tensor", (lambda fl, h, hw, ttl, ys, wslot, po: lambda e: e.matmul(
                        po[:, h:h + hw],
                        yT[ys][:, fl * TBW + ttl * 128: fl * TBW + (ttl + 1) * 128],
                        w2_b[wslot][:, fl * D + h: fl * D + h + hw],
                        start=(fl == 0), stop=(fl == FG - 1)))(fl, h, hw, ttl, ys, wslot, po),
                        waits=waits,
                        inc=s_pe.inc() if nmm == tot else None)
            pe_o = s_pe.n
            state["w2_free"][wslot] = pe_o
            state["yT_free"][ys] = pe_o
            q.op("vector", (lambda ts, e_i, tt, po: lambda e: e.scalar_tensor_tensor(
                acc[:, tt * D:(tt + 1) * D], po[:, 0:D],
                we_all[:, (ts * NT + tt) * E + e_i:(ts * NT + tt) * E + e_i + 1],
                acc[:, tt * D:(tt + 1) * D], ALU.mult, ALU.add))(ts, e_i, tt, po),
                waits=[w(s_pe, pe_o), w(s_dv, state["acc_ready"][tt])], inc=s_dv.inc())
            for r in regs:
                pread(r, "dv", s_dv.n)
            state["acc_ready"][tt] = s_dv.n

    def emit_phase_c(ts):
        for tt in range(NT):
            sl = tt % 2
            row0 = ts * S + tt * 128
            q.op("sync", (lambda row0, sl: lambda e: e.dma_start(
                out=xcb[sl][:], in_=x_h[row0:row0 + 128, :]))(row0, sl),
                waits=[w(s_dv, state["xc_rec"][sl])] +
                      (list(barrier) if (ts == 0 and tt < 2) else []),
                inc=s_xc[sl].inc(16))
            xn_ = s_xc[sl].n
            q.op("sync", (lambda row0, sl: lambda e: e.dma_start(
                out=scb[sl][:], in_=s_h[row0:row0 + 128, :]))(row0, sl),
                waits=[w(s_dv, state["sc_rec"][sl])], inc=s_sc[sl].inc(16))
            sn_ = s_sc[sl].n
            q.op("scalar", (lambda tt: lambda e: e.activation(
                gsb[:], acc[:, tt * D:(tt + 1) * D], AF.Silu))(tt),
                waits=[w(s_dv, state["acc_ready"][tt]), w(s_dv, state["g_free"])],
                inc=s_ac.inc())
            ac_g = s_ac.n
            gw = []
            if cfg.debug:
                q.op("sync", (lambda row0: lambda e: e.dma_start(
                    out=go_h[row0:row0 + 128, :], in_=gsb[:]))(row0),
                    waits=[w(s_ac, ac_g)], inc=s_dbg.inc(16))
                gw = [(s_dbg.h, s_dbg.n)]
            q.op("vector", (lambda sl: lambda e: e.tensor_mul(gsb[:], gsb[:], scb[sl][:]))(sl),
                 waits=[w(s_ac, ac_g), (s_sc[sl].h, sn_)] + gw, inc=s_dv.inc())
            dv_gm = s_dv.n
            q.op("vector", (lambda sl: lambda e: e.tensor_add(
                osb[sl][:], gsb[:], xcb[sl][:]))(sl),
                waits=[(s_xc[sl].h, xn_), w(s_dv, dv_gm),
                       (s_out[sl].h, state["osb_rec"][sl]) if state["osb_rec"][sl] else None],
                inc=s_dv.inc())
            dv_o = s_dv.n
            state["sc_rec"][sl] = dv_o
            state["xc_rec"][sl] = dv_o
            state["g_free"] = dv_o
            q.op("sync", (lambda row0, sl: lambda e: e.dma_start(
                out=o_h[row0:row0 + 128, :], in_=osb[sl][:]))(row0, sl),
                waits=[w(s_dv, dv_o)], inc=s_out[sl].inc(16))
            state["osb_rec"][sl] = s_out[sl].n
        state["c_done"] = s_dv.n

    FGW = FG * 128
    DH = min(512, D)
    NH = D // DH
    sp = dict(tb_dv=[0, 0], tb_gx=[(0, 0), (0, 0)], tb_osc=[(0, 0), (0, 0)],
              xgf_pe=[0, 0], xgT_pe=[0, 0], yT_pe=[0, 0],
              w1s_pe=[0, 0], w1a_cn=[0] * 64, w1b_cn=[0] * 64,
              w2_pe=[0, 0], conv_dv=[0, 0], gx_n=[0, 0],
              silu_dv=0, osb_osc=[0, 0], first_tbl=True, first_gx=True,
              first_w=True, w2chunk=0, w2_n={}, tbn=[0, 0], xgT_dv=[0, 0],
              yT_dv=[0, 0], b1_n=[0, 0], b1_pe=[0, 0])

    def emit_w1_chunk(ts, g, fg):
        wsl = fg % 2
        icol = widx_i[:, g * NFG + fg:g * NFG + fg + 1]
        wext = []
        if sp["first_w"]:
            wext = list(barrier)
            sp["first_w"] = False
        q.op("gpsimd", (lambda wsl, icol: lambda e: e.indirect_dma_start(
            out=w1a_b[wsl][:], out_offset=None, in_=w1a_sw,
            in_offset=bass.IndirectOffsetOnAxis(ap=icol, axis=0),
            bounds_check=BND["w1"], oob_is_err=False))(wsl, icol),
            waits=wext + [w(s_pe, sp["w1s_pe"][wsl])], inc=s_w1a[wsl].inc(16))
        sp["w1a_cn"][fg] = s_w1a[wsl].n
        q.op("gpsimd", (lambda wsl, icol: lambda e: e.indirect_dma_start(
            out=w1b_b[wsl][:], out_offset=None, in_=w1b_sw,
            in_offset=bass.IndirectOffsetOnAxis(ap=icol, axis=0),
            bounds_check=BND["w1"], oob_is_err=False))(wsl, icol),
            inc=s_w1b[wsl].inc(16))
        sp["w1b_cn"][fg] = s_w1b[wsl].n

    def emit_w2_chunk(ts, g, half, fg):
        c = sp["w2chunk"]
        sl2 = c % 2
        sp["w2chunk"] += 1
        col = g * NH * NFG + half * NFG + fg
        icol = w2x_i[:, col:col + 1]
        q.op("gpsimd", (lambda sl2, icol: lambda e: e.indirect_dma_start(
            out=w2_b[sl2][:], out_offset=None, in_=w2_sw,
            in_offset=bass.IndirectOffsetOnAxis(ap=icol, axis=0),
            bounds_check=BND["w2"], oob_is_err=False))(sl2, icol),
            waits=[w(s_pe, sp["w2_pe"][sl2])], inc=s_w2[sl2].inc(16))
        sp["w2_n"][(ts, g, half, fg)] = (sl2, s_w2[sl2].n)

    def emit_sp_tblconv(ts, g):
        gsl = g % 2
        waits = [w(s_dv, sp["tb_dv"][gsl]),
                 (s_gx[0].h, sp["tb_gx"][gsl][0]) if sp["tb_gx"][gsl][0] else None,
                 (s_gx[1].h, sp["tb_gx"][gsl][1]) if sp["tb_gx"][gsl][1] else None,
                 (s_osc[0].h, sp["tb_osc"][gsl][0]) if sp["tb_osc"][gsl][0] else None,
                 (s_osc[1].h, sp["tb_osc"][gsl][1]) if sp["tb_osc"][gsl][1] else None,
                 (s_tbl.h, TBL_DONE[0])]
        if sp["first_tbl"]:
            waits += list(barrier)
            sp["first_tbl"] = False
        flat = tbl_h.rearrange("a b -> (a b)")
        q.op("sync", (lambda flat, g, gsl: lambda e: e.dma_start(
            out=tb_sb[gsl][:],
            in_=flat[g * GRP * 4:(g + 1) * GRP * 4].rearrange("(p j) -> p j", p=128)))(
                flat, g, gsl),
            waits=waits, inc=s_tb2s[gsl].inc(16))
        sp["tbn"][gsl] = s_tb2s[gsl].n
        tb4 = tb_sb[gsl][:].rearrange("p (q f) -> p q f", f=4)
        q.op("vector", (lambda gsl, tb4: lambda e: e.tensor_copy(
            gi32[gsl][:], tb4[:, :, 0:1].squeeze(2)))(gsl, tb4),
            waits=[(s_tb2s[gsl].h, sp["tbn"][gsl])], inc=s_dv.inc())
        dvx = s_dv.n
        q.op("vector", (lambda gsl, tb4: lambda e: e.tensor_copy(
            si32[gsl][:], tb4[:, :, 1:2].squeeze(2)))(gsl, tb4),
            waits=[w(s_dv, dvx)], inc=s_dv.inc())
        sp["conv_dv"][gsl] = s_dv.n

    def emit_gx(gsl, ti):
        xs = ti % 2
        waits = [w(s_dv, sp["conv_dv"][gsl]), w(s_pe, sp["xgf_pe"][xs])]
        if sp["first_gx"]:
            waits += list(barrier)
            sp["first_gx"] = False
        q.op("gpsimd", (lambda gsl, ti, xs: lambda e: e.indirect_dma_start(
            out=xgf[xs][:], out_offset=None,
            in_=xn2n_h,
            in_offset=bass.IndirectOffsetOnAxis(ap=gi32[gsl][:, ti:ti + 1], axis=0),
            bounds_check=BND["gath"], oob_is_err=False))(gsl, ti, xs),
            waits=waits, inc=s_gx[xs].inc(16))
        sp["gx_n"][xs] = s_gx[xs].n

    def emit_sp_loads(ts, g):
        gsl = g % 2
        for ti in range(min(NGT, 2)):
            emit_gx(gsl, ti)
        sp["tb_gx"][gsl] = (s_gx[0].n, s_gx[1].n)
        bcol_i = b1x_i[:, g:g + 1]
        q.op("gpsimd", (lambda gsl, bcol_i: lambda e: e.indirect_dma_start(
            out=b1ag[gsl][:], out_offset=None, in_=b1ag_h,
            in_offset=bass.IndirectOffsetOnAxis(ap=bcol_i, axis=0),
            bounds_check=BND["b1"], oob_is_err=False))(gsl, bcol_i),
            waits=[w(s_pe, sp["b1_pe"][gsl])], inc=s_wqkv.inc(16))
        q.op("gpsimd", (lambda gsl, bcol_i: lambda e: e.indirect_dma_start(
            out=b1bg[gsl][:], out_offset=None, in_=b1bg_h,
            in_offset=bass.IndirectOffsetOnAxis(ap=bcol_i, axis=0),
            bounds_check=BND["b1"], oob_is_err=False))(gsl, bcol_i),
            inc=s_wqkv.inc(16))
        sp["b1_n"][gsl] = s_wqkv.n
        emit_w1_chunk(ts, g, 0)
        if NFG > 1:
            emit_w1_chunk(ts, g, 1)

    def emit_sp_trans(ts, g):
        gsl = g % 2
        for ti in range(NGT):
            xs = ti % 2
            par = ti % 2
            regs = ("ba0", "bb0") if par == 0 else ("ba1", "bb1")
            pbase = par * 1024
            for k in range(KD):
                q.op("tensor", (lambda xs, k, pbase: lambda e: e.transpose(
                    ps_big[:, pbase + k * 128: pbase + (k + 1) * 128],
                    xgf[xs][:, k * 128:(k + 1) * 128], ident_sb[:]))(xs, k, pbase),
                    waits=([(s_gx[xs].h, sp["gx_n"][xs])] + pwait(*regs) + [wcw])
                    if k == 0 else [],
                    inc=s_pe.inc() if k == KD - 1 else None)
            pe_t = s_pe.n
            sp["xgf_pe"][xs] = pe_t
            if ti + 2 < NGT:
                emit_gx(gsl, ti + 2)
            q.op("vector", (lambda gsl, ti, pbase: lambda e: e.tensor_copy(
                xgT[gsl][:].rearrange("p (k c) -> p k c", k=KD)[
                    :, :, ti * 128:(ti + 1) * 128],
                ps_big[:, pbase:pbase + KD * 128].rearrange(
                    "p (k c) -> p k c", k=KD)))(gsl, ti, pbase),
                waits=[w(s_pe, pe_t)] +
                      ([w(s_pe, sp["xgT_pe"][gsl])] if ti == 0 else []),
                inc=s_dv.inc())
            for r in regs:
                pread(r, "dv", s_dv.n)
        sp["xgT_dv"][gsl] = s_dv.n
        sp["tb_gx"][gsl] = (s_gx[0].n, s_gx[1].n)

    def emit_sp_fl(ts, g):
        gsl = g % 2
        ys = g % 2
        for fg in range(NFG):
            wsl = fg % 2
            w1a_n, w1b_n = sp["w1a_cn"][fg], sp["w1b_cn"][fg]
            for fl in range(FG):
                par = fl % 2
                ra, rb = "ba%d" % par, "bb%d" % par
                pa = ps_big[:, par * 1024: par * 1024 + GRP]
                pb = ps_big[:, par * 1024 + 512: par * 1024 + 512 + GRP]
                for k in range(KD):
                    waits = []
                    if k == 0:
                        waits = pwait(ra) + [(s_w1a[wsl].h, w1a_n)]
                        if fg == 0 and fl == 0:
                            waits += [w(s_dv, sp["xgT_dv"][gsl])]
                    q.op("tensor", (lambda fl, k, gsl, wsl, pa: lambda e: e.matmul(
                        pa, w1a_b[wsl][:, k * FGW + fl * 128: k * FGW + (fl + 1) * 128],
                        xgT[gsl][:, k * GRP:(k + 1) * GRP],
                        start=(k == 0), stop=(k == KD - 1)))(fl, k, gsl, wsl, pa),
                        waits=waits, inc=s_pe.inc() if k == KD - 1 else None)
                pe_a = s_pe.n
                for k in range(KD):
                    waits = []
                    if k == 0:
                        waits = pwait(rb) + [(s_w1b[wsl].h, w1b_n)]
                    q.op("tensor", (lambda fl, k, gsl, wsl, pb: lambda e: e.matmul(
                        pb, w1b_b[wsl][:, k * FGW + fl * 128: k * FGW + (fl + 1) * 128],
                        xgT[gsl][:, k * GRP:(k + 1) * GRP],
                        start=(k == 0), stop=(k == KD - 1)))(fl, k, gsl, wsl, pb),
                        waits=waits, inc=s_pe.inc() if k == KD - 1 else None)
                pe_b = s_pe.n
                bcol = fg * FG + fl
                q.op("scalar", (lambda gsl, bcol, pa: lambda e: e.activation(
                    silu_sb[:], pa, AF.Silu, bias=b1ag[gsl][:, bcol:bcol + 1]))(
                        gsl, bcol, pa),
                    waits=[w(s_pe, pe_a), w(s_dv, sp["silu_dv"]),
                           (s_wqkv.h, sp["b1_n"][gsl])], inc=s_ac.inc())
                pread(ra, "ac", s_ac.n)
                ac_s = s_ac.n
                q.op("vector", (lambda gsl, bcol, pb, ys: lambda e: e.scalar_tensor_tensor(
                    yT[ys][:, bcol * GRP:(bcol + 1) * GRP], pb,
                    b1bg[gsl][:, bcol:bcol + 1], silu_sb[:], ALU.add, ALU.mult))(
                        gsl, bcol, pb, ys),
                    waits=[w(s_pe, pe_b), w(s_ac, ac_s)] +
                          ([w(s_pe, sp["yT_pe"][ys])] if (fg == 0 and fl == 0) else []),
                    inc=s_dv.inc())
                pread(rb, "dv", s_dv.n)
                sp["silu_dv"] = s_dv.n
            sp["w1s_pe"][wsl] = s_pe.n
            if fg + 2 < NFG:
                emit_w1_chunk(ts, g, fg + 2)
        sp["yT_dv"][ys] = s_dv.n
        sp["xgT_pe"][gsl] = s_pe.n
        sp["b1_pe"][gsl] = s_pe.n

    def emit_sp_o(ts, g):
        gsl = g % 2
        ys = g % 2
        PO = (ps_o0, ps_o0, ps_o1, ps_o1)
        RG = ("o0h0", "o0h1", "o1h0", "o1h1")
        emit_w2_chunk(ts, g, 0, 0)
        if NFG > 1:
            emit_w2_chunk(ts, g, 0, 1)
        elif NH > 1:
            emit_w2_chunk(ts, g, 1, 0)
        ck = 0
        for half in range(NH):
            for fg in range(NFG):
                sl2, w2n = sp["w2_n"][(ts, g, half, fg)]
                for ti in range(NGT):
                    po = PO[ti][:, (ti % 2) * 512:(ti % 2) * 512 + DH]
                    for fl in range(FG):
                        waits = []
                        if fl == 0 and ti == 0:
                            waits = [(s_w2[sl2].h, w2n)]
                            if fg == 0:
                                waits = waits + [w(s_dv, sp["yT_dv"][ys])]
                        if fg == 0 and fl == 0:
                            waits = waits + pwait(RG[ti])
                        fidx = fg * FG + fl
                        q.op("tensor", (lambda po, fidx, ti, fl, ys, sl2: lambda e: e.matmul(
                            po, yT[ys][:, fidx * GRP + ti * 128: fidx * GRP + (ti + 1) * 128],
                            w2_b[sl2][:, fl * DH:(fl + 1) * DH],
                            start=(fidx == 0), stop=(fidx == NFL - 1)))(
                                po, fidx, ti, fl, ys, sl2),
                            waits=waits,
                            inc=s_pe.inc() if (ti == NGT - 1 and fl == FG - 1) else None)
                sp["w2_pe"][sl2] = s_pe.n
                ck += 1
                nxt = ck + 1
                if nxt < NH * NFG:
                    emit_w2_chunk(ts, g, nxt // NFG, nxt % NFG)
            pe_o = s_pe.n
            for ti in range(NGT):
                po = PO[ti][:, (ti % 2) * 512:(ti % 2) * 512 + DH]
                osl = ti % 2
                waits = [w(s_pe, pe_o)]
                if sp["osb_osc"][osl]:
                    waits.append((s_osc[osl].h, sp["osb_osc"][osl]))
                if state["osb_rec"][osl]:
                    waits.append((s_out[osl].h, state["osb_rec"][osl]))
                q.op("vector", (lambda po, gsl, ti, osl, half: lambda e: e.tensor_scalar_mul(
                    osb[osl][:, half * DH:(half + 1) * DH], po,
                    tb_sb[gsl][:, ti * 4 + 2:ti * 4 + 3]))(po, gsl, ti, osl, half),
                    waits=waits, inc=s_dv.inc())
                pread(RG[ti], "dv", s_dv.n)
                dv_e = s_dv.n
                q.op("gpsimd", (lambda gsl, ti, osl, half: lambda e: e.indirect_dma_start(
                    out=obuf_hh[half],
                    out_offset=bass.IndirectOffsetOnAxis(
                        ap=si32[gsl][:, ti:ti + 1], axis=0),
                    in_=osb[osl][:, half * DH:(half + 1) * DH], in_offset=None,
                    bounds_check=BND["scat"], oob_is_err=False))(gsl, ti, osl, half),
                    waits=[w(s_dv, dv_e)], inc=s_osc[osl].inc(16))
                sp["osb_osc"][osl] = s_osc[osl].n
        sp["tb_dv"][gsl] = s_dv.n
        sp["tb_osc"][gsl] = (s_osc[0].n, s_osc[1].n)
        sp["yT_pe"][ys] = s_pe.n

    def emit_phase_c_sp(ts):
        osc_tot = (s_osc[0].n, s_osc[1].n)
        for tt in range(NT):
            sl = tt % 2
            row0 = ts * S + tt * 128
            q.op("scalar", (lambda row0, sl: lambda e: e.dma_start(
                out=xcb[sl][:], in_=x_h[row0:row0 + 128, :]))(row0, sl),
                waits=[w(s_dv, state["xc_rec"][sl])], inc=s_xc[sl].inc(16))
            xn_ = s_xc[sl].n
            q.op("scalar", (lambda row0, sl: lambda e: e.dma_start(
                out=scb[sl][:], in_=s_h[row0:row0 + 128, :]))(row0, sl),
                waits=[w(s_dv, state["sc_rec"][sl])], inc=s_sc[sl].inc(16))
            sn_ = s_sc[sl].n
            for h in range(NH):
                q.op("sync", (lambda row0, h, sl: lambda e: e.dma_start(
                    out=o1b[sl][:, h * DH:(h + 1) * DH],
                    in_=obuf_hh[h][row0:row0 + 128, :]))(row0, h, sl),
                    waits=([w(s_dv, state["o1_rec"][sl]),
                            (s_osc[0].h, osc_tot[0]) if tt < 2 else None,
                            (s_osc[1].h, osc_tot[1]) if tt < 2 else None]
                           if h == 0 else []),
                    inc=s_xc[sl].inc(16))
                q.op("sync", (lambda row0, h, sl: lambda e: e.dma_start(
                    out=o2b[sl][:, h * DH:(h + 1) * DH],
                    in_=obuf_hh[h][NBS + row0:NBS + row0 + 128, :]))(row0, h, sl),
                    inc=s_xc[sl].inc(16))
            on_ = s_xc[sl].n
            po = ps_o0 if tt % 2 == 0 else ps_o1
            regs = ("o0h0", "o0h1") if tt % 2 == 0 else ("o1h0", "o1h1")
            nh = (D + 511) // 512
            for hi, h in enumerate(range(0, D, 512)):
                hw = min(512, D - h)
                waits = []
                if hi == 0:
                    waits = pwait(*regs) + [w(s_dv, we_done[ts]), wcw]
                q.op("tensor", (lambda ts, tt, h, hw, po: lambda e: e.matmul(
                    po[:, h:h + hw],
                    weT_all[0:E, ts * S + tt * 128: ts * S + (tt + 1) * 128],
                    b2_sb[0:E, h:h + hw],
                    start=True, stop=True))(ts, tt, h, hw, po),
                    waits=waits, inc=s_pe.inc() if hi == nh - 1 else None)
            pe_b2 = s_pe.n
            q.op("vector", (lambda sl: lambda e: e.tensor_add(
                gsb[:], o1b[sl][:], o2b[sl][:]))(sl),
                waits=[(s_xc[sl].h, on_), w(s_dv, state["g_free"])], inc=s_dv.inc())
            dvx = s_dv.n
            state["o1_rec"][sl] = dvx
            q.op("vector", (lambda po: lambda e: e.tensor_add(
                gsb[:], gsb[:], po[:, 0:D]))(po),
                waits=[w(s_dv, dvx), w(s_pe, pe_b2)], inc=s_dv.inc())
            dvx = s_dv.n
            for r in regs:
                pread(r, "dv", dvx)
            q.op("scalar", lambda e: e.activation(gsb[:], gsb[:], AF.Silu),
                 waits=[w(s_dv, dvx)], inc=s_ac.inc())
            ac_g = s_ac.n
            q.op("vector", (lambda sl: lambda e: e.tensor_mul(gsb[:], gsb[:], scb[sl][:]))(sl),
                 waits=[w(s_ac, ac_g), (s_sc[sl].h, sn_)], inc=s_dv.inc())
            dv_gm = s_dv.n
            q.op("vector", (lambda sl: lambda e: e.tensor_add(
                osb[sl][:], gsb[:], xcb[sl][:]))(sl),
                waits=[(s_xc[sl].h, on_), w(s_dv, dv_gm),
                       (s_out[sl].h, state["osb_rec"][sl]) if state["osb_rec"][sl] else None,
                       (s_osc[sl].h, sp["osb_osc"][sl]) if sp["osb_osc"][sl] else None],
                inc=s_dv.inc())
            dv_o = s_dv.n
            sp["osb_osc"][sl] = 0
            state["sc_rec"][0] = dv_gm
            state["sc_rec"][1] = dv_gm
            state["xc_rec"][sl] = dv_o
            state["g_free"] = dv_o
            q.op("sync", (lambda row0, sl: lambda e: e.dma_start(
                out=o_h[row0:row0 + 128, :], in_=osb[sl][:]))(row0, sl),
                waits=[w(s_dv, dv_o)], inc=s_out[sl].inc(16))
            state["osb_rec"][sl] = s_out[sl].n
        state["c_done"] = s_dv.n

    if cfg.sparse and not cfg.only_a:
        q.op("gpsimd", _setup_bounds)
        NGL = G if cfg.ng_limit is None else min(G, cfg.ng_limit)
        for g in range(NGL):
            if g == 0:
                emit_sp_tblconv(0, 0)
            emit_sp_loads(0, g)
            emit_sp_trans(0, g)
            emit_sp_fl(0, g)
            if g > 0:
                emit_sp_o(0, g - 1)
            if g + 1 < NGL:
                emit_sp_tblconv(0, g + 1)
        emit_sp_o(0, NGL - 1)
        if not cfg.skip_c:
            for ts in range(NB):
                emit_phase_c_sp(ts)

    for ts in range(NB if not (cfg.only_a or cfg.sparse) else 0):
        if ts == 0:
            waits0 = list(barrier)
        else:
            waits0 = [w(s_pe, state["fl_last_pe"])]
        for k in range(KD):
            q.op("sync", (lambda ts, k: lambda e: e.dma_start(
                out=xn2B[:, k * S:(k + 1) * S],
                in_=xn2s_h[k, :, ts * S:(ts + 1) * S]))(ts, k),
                waits=waits0 if k == 0 else [], inc=s_xn2r.inc(16))
        xn2_n = s_xn2r.n

        for tt in range(NT):
            po = ps_o0 if tt % 2 == 0 else ps_o1
            regs = ("o0h0", "o0h1") if tt % 2 == 0 else ("o1h0", "o1h1")
            nh = (D + 511) // 512
            for hi, h in enumerate(range(0, D, 512)):
                hw = min(512, D - h)
                waits = []
                if hi == 0:
                    waits = pwait(*regs) + [w(s_dv, we_done[ts]), wcw]
                    if ts == 0 and tt == 0:
                        waits = waits + list(barrier)
                q.op("tensor", (lambda ts, tt, h, hw, po: lambda e: e.matmul(
                    po[:, h:h + hw],
                    weT_all[0:E, ts * S + tt * 128: ts * S + (tt + 1) * 128],
                    b2_sb[0:E, h:h + hw],
                    start=True, stop=True))(ts, tt, h, hw, po),
                    waits=waits, inc=s_pe.inc() if hi == nh - 1 else None)
            pe_b2 = s_pe.n
            q.op("vector", (lambda tt, po: lambda e: e.tensor_copy(
                acc[:, tt * D:(tt + 1) * D], po[:, 0:D]))(tt, po),
                waits=[w(s_pe, pe_b2), w(s_dv, state["c_done"])], inc=s_dv.inc())
            for r in regs:
                pread(r, "dv", s_dv.n)
            state["acc_ready"][tt] = s_dv.n

        pending = None
        for e_i in range(E):
            for fg in range(NFG):
                wslot, w1a_n, w1b_n, w2_n = emit_w_loads(ts, e_i, fg)
                for tb in range(NTB):
                    unit = (e_i * NFG + fg) * NTB + tb
                    emit_fl(ts, e_i, fg, tb, wslot, w1a_n, w1b_n, xn2_n, unit)
                    if pending is not None:
                        emit_o(*pending)
                    pending = (ts, e_i, fg, tb, wslot, w2_n, unit)
        emit_o(*pending)

        emit_phase_c(ts)

    if cfg.debug:
        q.op("sync", lambda e: e.dma_start(out=weo_h[:], in_=we_all[:]),
             waits=[(s_dv.h, s_dv.n)], inc=s_dbg.inc(16))
        q.op("sync", lambda e: e.wait_ge(s_dbg.h, s_dbg.n))

    if not cfg.only_a:
        q.op("sync", lambda e: e.wait_ge(s_out[0].h, s_out[0].n),
             waits=[(s_out[1].h, s_out[1].n)])
    else:
        q.op("sync", lambda e: e.wait_ge(s_spx.h, s_spx.n),
             waits=[(s_se[0].h, s_se[0].n), (s_se[1].h, s_se[1].n),
                    (s_dv.h, s_dv.n), (s_ac.h, s_ac.n), (s_pe.h, s_pe.n)] +
                   ([(s_dbg.h, s_dbg.n)] if cfg.debug else []))

    q.run_block(nc)
    stB.close()
    st.close()
    return nc


_CFG = Cfg()
_NC_CACHE = {}


def _get_nc(cfg):
    key = id(cfg)
    if key not in _NC_CACHE:
        _NC_CACHE[key] = build_program(cfg)
    return _NC_CACHE[key]


def make_in_maps(cfg, n_cores, x, wq, wk, wvd, wvu, lnw, wg, bg, w1a, b1a, w1b, b1b, w2, b2):
    B, S, D = x.shape
    per = B // n_cores
    ident = np.eye(128, dtype=np.float32)
    amask = np.triu(np.ones((128, 128), dtype=np.uint8), 1)
    f32 = lambda a: np.ascontiguousarray(np.asarray(a, dtype=np.float32))

    def btr(bv):
        bv = f32(bv)
        E_, F_ = bv.shape
        return np.ascontiguousarray(
            bv.reshape(E_, F_ // 128, 128).transpose(2, 0, 1).reshape(128, E_ * (F_ // 128)))

    def btg(bv):
        bv = f32(bv)
        E_, F_ = bv.shape
        return np.ascontiguousarray(
            bv.reshape(E_, F_ // 128, 128).transpose(0, 2, 1).reshape(E_ * 128, F_ // 128))

    shared = dict(wq=f32(wq), wk=f32(wk), wvd=f32(wvd), wvu=f32(wvu), lnw=f32(lnw),
                  wg=f32(wg), bg=f32(bg), w1a=f32(w1a), b1a=btr(b1a), w1b=f32(w1b),
                  b1b=btr(b1b), w2=f32(w2), b2=f32(b2), ident=ident, amask=amask)
    if cfg.sparse:
        shared["b1agh"] = btg(b1a)
        shared["b1bgh"] = btg(b1b)
    in_maps = []
    for c in range(n_cores):
        m = dict(shared)
        m["x"] = np.ascontiguousarray(x[c * per:(c + 1) * per].reshape(per * S, D))
        in_maps.append(m)
    return in_maps


def kernel(L0_attn_inputs, L0_attn_Wq, L0_attn_Wk, L0_attn_Wvdown, L0_attn_Wvup,
           L0_ffn_ln_w, L0_ffn_Wg, L0_ffn_bg, L0_ffn_W1a, L0_ffn_b1a,
           L0_ffn_W1b, L0_ffn_b1b, L0_ffn_W2, L0_ffn_b2):
    cfg = _CFG
    n_cores = 8
    x = np.ascontiguousarray(np.asarray(L0_attn_inputs, dtype=np.float32))
    B, S, D = x.shape
    per = B // n_cores
    nc = _get_nc(cfg)
    in_maps = make_in_maps(cfg, n_cores, x, L0_attn_Wq, L0_attn_Wk, L0_attn_Wvdown,
                           L0_attn_Wvup, L0_ffn_ln_w, L0_ffn_Wg, L0_ffn_bg,
                           L0_ffn_W1a, L0_ffn_b1a, L0_ffn_W1b, L0_ffn_b1b,
                           L0_ffn_W2, L0_ffn_b2)
    res = run_bass_kernel_spmd(nc, in_maps, core_ids=list(range(n_cores)))
    out = np.concatenate([np.asarray(r["o"]).reshape(per, S, D) for r in res.results], axis=0)
    return out.astype(np.float32)
